# version 07
# baseline (speedup 1.0000x reference)
import numpy as np
import ml_dtypes
import concourse.bass as bass
import concourse.mybir as mybir
from concourse.bass_utils import run_bass_kernel_spmd

F32 = mybir.dt.float32
BF16 = mybir.dt.bfloat16
U8 = mybir.dt.uint8
AX = mybir.AxisListType
ALU = mybir.AluOpType
ACTF = mybir.ActivationFunctionType

P = 128
C = 2048
H = 16
D = 128
NKC = C // P
NE = 32
LAT = 1024
INTER = 1024
SI = 4096
KB = 1024
PGSZ = 2048

CQ = 1.0 / (127.0 * float(np.sqrt(np.float64(128.0))))
CK = 1.0 / 127.0
INV_N = 1.0 / (2048.0 * 2048.0)
M32 = float(2 ** 23 * 1.5)
M64 = float(2 ** 52 * 1.5)


class Buf:
    __slots__ = ("ap", "keys")

    def __init__(self, ap, keys):
        self.ap = ap
        self.keys = frozenset(keys)


class Prog:
    ENGS = ("T", "V", "A", "G", "S")

    def __init__(self, sems, lane_queue):
        self.sems = sems
        self.lane_queue = lane_queue
        self.cnt = {k: 0 for k in sems}
        self.steps = {e: [] for e in self.ENGS}
        self.seen = {e: {} for e in self.ENGS}
        self.pages = {}
        self.dirty = {e: set() for e in self.ENGS}
        self.rdirty = {e: set() for e in self.ENGS}

    def _rec(self, k):
        r = self.pages.get(k)
        if r is None:
            r = [None, []]
            self.pages[k] = r
        return r

    @staticmethod
    def _keys(bufs):
        s = set()
        for b in bufs:
            s |= b.keys if isinstance(b, Buf) else {b}
        return s

    def op(self, eng, fn, reads=(), writes=(), lane=None):
        rk = self._keys(reads)
        wk = self._keys(writes)
        waits = []
        seen = self.seen[eng]

        def need(sem, val, agent):
            if agent != eng and agent in self.lane_queue:
                val = self.cnt[agent]
            if val > 0 and seen.get(sem, 0) < val:
                waits.append((self.sems[sem], val))
                seen[sem] = val

        drain = False
        for k in rk | wk:
            rec = self.pages.get(k)
            if rec is None:
                continue
            w = rec[0]
            if w is not None:
                sem, val, agent = w
                if agent == eng:
                    if k in self.dirty[eng]:
                        drain = True
                else:
                    need(sem, val, agent)
            if k in wk:
                for sem, val, agent in rec[1]:
                    if agent == eng:
                        if k in self.rdirty[eng]:
                            drain = True
                    else:
                        need(sem, val, agent)

        if lane is None:
            sem_name, inc = eng, 1
            agent = eng
        else:
            sem_name, inc = lane, 16
            agent = lane
        self.cnt[sem_name] += inc
        val = self.cnt[sem_name]

        if drain:
            self.dirty[eng].clear()
            self.rdirty[eng].clear()
        for k in wk:
            rec = self._rec(k)
            rec[0] = (sem_name, val, agent)
            rec[1] = []
        for k in rk:
            self._rec(k)[1].append((sem_name, val, agent))
        if lane is None:
            self.dirty[eng] |= wk
            self.rdirty[eng] |= rk
        self.steps[eng].append((tuple(waits), drain, fn, self.sems[sem_name], inc))
        return val

    def finish(self):
        waits = [(self.sems[k], v) for k, v in self.cnt.items() if v > 0]
        self.steps["S"].append((tuple(waits), False, None, None, 0))

    def realize(self, block):
        names = {"T": "tensor", "V": "vector", "A": "scalar", "G": "gpsimd", "S": "sync"}
        for e in self.ENGS:
            steps = self.steps[e]

            def mk(steps=steps):
                def body(eng):
                    for waits, drain, fn, sem, inc in steps:
                        for s, v in waits:
                            eng.wait_ge(s, v)
                        if drain:
                            eng.drain()
                        if fn is not None:
                            inst = fn(eng)
                            if sem is not None:
                                inst.then_inc(sem, inc)
                return body
            getattr(block, names[e])(mk())


class Builder:
    def __init__(self, nc, prog, NT):
        self.nc = nc
        self.p = prog
        self.NT = NT
        self.T = NT * P

    def T_(self, fname, *args, reads=(), writes=(), **kw):
        self.p.op("T", (lambda e, a=args, k=kw, f=fname: getattr(e, f)(*a, **k)),
                  reads, writes)

    def V_(self, fname, *args, reads=(), writes=(), **kw):
        self.p.op("V", (lambda e, a=args, k=kw, f=fname: getattr(e, f)(*a, **k)),
                  reads, writes)

    def A_(self, fname, *args, reads=(), writes=(), **kw):
        self.p.op("A", (lambda e, a=args, k=kw, f=fname: getattr(e, f)(*a, **k)),
                  reads, writes)

    def dma(self, lane, out_ap, in_ap, reads=(), writes=()):
        self.p.op(self.p.lane_queue[lane],
                  (lambda e, o=out_ap, i=in_ap: e.dma_start(out=o, in_=i)),
                  reads, writes, lane=lane)

    def transpose(self, lane, out_ap, in_ap, reads=(), writes=()):
        self.p.op("S", (lambda e, o=out_ap, i=in_ap: e.dma_start_transpose(o, i)),
                  reads, writes, lane=lane)

    def AR(self, off_kb, shape, dtype):
        nel = int(np.prod(shape[1:]))
        nby = nel * mybir.dt.size(dtype)
        if dtype == BF16:
            arena, tag, epk = self.arena16, "A6", 512
        else:
            arena, tag, epk = self.arena32, "A2", 256
        ap = arena[:, off_kb * epk: off_kb * epk + nel]
        if len(shape) == 3:
            ap = ap.rearrange("p (a b) -> p a b", b=shape[2])
        keys = {(tag, g) for g in range(off_kb * KB // PGSZ,
                                        (off_kb * KB + nby + PGSZ - 1) // PGSZ)}
        return Buf(ap, keys)

    def SB(self, name, shape, dtype=F32, tdest=False):
        h = self.nc.alloc_sbuf_tensor(name, shape, dtype, is_transpose_dest=tdest)
        return Buf(h[:], {name})

    def PSR(self, which, lo, hi):
        t = self.psA if which == "A" else self.psB
        keys = {("ps" + which, b) for b in range(lo // 512, (hi + 511) // 512)}
        return Buf(t[:, lo:hi], keys)

    def declare(self):
        nc = self.nc
        NT, T = self.NT, self.T
        dr = lambda n, s, dt=F32, kind="ExternalInput": nc.dram_tensor(n, s, dt, kind=kind).ap()
        self.x = dr("x", [T, C])
        self.wqT = dr("wqT", [C, C]); self.wkT = dr("wkT", [C, C])
        self.wvT = dr("wvT", [C, C]); self.woT = dr("woT", [C, C])
        self.cc = dr("cc", [T, D]); self.ss = dr("ss", [T, D])
        self.cbm = dr("cbm", [P, P]); self.onesd = dr("onesd", [P, P])
        self.gbiasd = dr("gbiasd", [P, NE])
        self.gwT = dr("gwT", [C, NE])
        self.fc1Td = dr("fc1Td", [C, LAT])
        self.dfT = dr("dfT", [SI + LAT, C])
        self.w1Td = dr("w1Td", [NE, LAT, INTER])
        self.w2Td = dr("w2Td", [NE, INTER, LAT])
        self.upTd = dr("upTd", [C, SI])
        self.out = dr("out", [T, C], F32, kind="ExternalOutput")
        idr = lambda n, s, dt: nc.dram_tensor(n, s, dt, kind="Internal").ap()
        self.xqT_s = idr("xqT_s", [NT, P, NKC, P], BF16)
        self.qrT_s = idr("qrT_s", [NT, P, NKC, P], BF16)
        self.krT_s = idr("krT_s", [H, P, T], BF16)
        self.v_s = idr("v_s", [4, T, 512], BF16)
        self.yqT_s = idr("yqT_s", [NT, P, NKC, P], BF16)
        self.x1_s = idr("x1_s", [NT, P, C], F32)
        self.ln2T_s = idr("ln2T_s", [NT, P, NKC, P], BF16)
        self.ylT_s = idr("ylT_s", [P, 8, T], BF16)

        self.psA = nc.alloc_psum_tensor("psA", [P, 2048], F32)
        self.psB = nc.alloc_psum_tensor("psB", [P, 2048], F32)

        S = self.SB
        self.tr = [S("tr0", [P, NKC, P], BF16, tdest=True),
                   S("tr1", [P, NKC, P], BF16, tdest=True)]
        self.pT = S("pT", [P, NT, P], BF16, tdest=True)
        self.ylT = S("ylT", [P, 8, P], BF16, tdest=True)
        self.cb = S("cb", [P, P]); self.ones = S("ones", [P, P])
        self.cct = S("cct", [P, D]); self.sst = S("sst", [P, D])
        self.gbias = S("gbias", [P, NE])
        self.gw_sb = S("gw_sb", [P, NKC, NE], BF16)
        self.samc = S("samc", [P, NT]); self.samc_y = S("samc_y", [P, NT])
        self.cw = S("cw", [P, NT, NE])
        self.bst = S("bst", [P, 4, 6]); self.mv = S("mv", [P, 2])
        self.cs16 = S("cs16", [P, NKC]); self.cstot = S("cstot", [P, 1])
        for n in ("mclip_q", "mclip_k", "mclip_v", "mclip_o", "thr", "nthr",
                  "lnv", "rstd", "rsq", "bq", "rr", "sfold", "am", "amc",
                  "rmc", "qsc", "qsc2", "sc1", "s2t", "vv", "gv",
                  "nm", "den", "rden", "ss2v",
                  "g1v", "g2v", "m1g", "m2g", "th8", "s8", "rs8", "mx1"):
            setattr(self, n, S(n, [P, 1]))
        self.c_eps5 = S("c_eps5", [P, 1]); self.c_eps6 = S("c_eps6", [P, 1])
        self.c_2565 = S("c_2565", [P, 1])
        self.c_m256 = S("c_m256", [P, 1]); self.c_n1e30 = S("c_n1e30", [P, 1])
        self.c_one = S("c_one", [P, 1]); self.gdisc = S("gdisc", [P, 1])
        self.mqs = S("mqs", [P, 1]); self.rs25 = S("rs25", [P, 1])
        self.et = S("et", [P, NE]); self.d1 = S("d1", [P, NE])
        self.scrs = S("scrs", [P, NE]); self.sbv = S("sbv", [P, NE])
        self.msb = S("msb", [P, NE]); self.cur = S("cur", [P, NE])
        self.mk32 = S("mk32", [P, NE]); self.wsl = S("wsl", [P, NE])
        self.selm = S("selm", [P, NE])
        self.gsv = S("gsv", [P, 4]); self.mask4 = S("mask4", [P, 4])
        self.t4b = S("t4b", [P, 4]); self.madd4 = S("madd4", [P, 4])
        self.msk8 = S("msk8", [P, 8]); self.t8b = S("t8b", [P, 8])

        self.arena16 = nc.alloc_sbuf_tensor("arena16", [P, 108 * 512], BF16)
        self.arena32 = nc.alloc_sbuf_tensor("arena32", [P, 72 * 256], F32)
        A = self.AR
        self.wtern = A(0, [P, NKC, C], BF16)
        self.xqb = A(64, [P, C], BF16)
        self.vbb = A(68, [P, C], BF16)
        self.ld = [A(72, [P, NKC, P], BF16), A(76, [P, NKC, P], BF16)]
        self.pb = A(80, [P, 2048], BF16)
        self.ks = [A(84, [P, 2048], BF16), A(88, [P, 2048], BF16)]
        self.vs = A(92, [P, NT, 512], BF16)
        self.chunk = [A(0, [P, C], F32), A(8, [P, C], F32)]
        self.xt = [A(16, [P, C], F32), A(24, [P, C], F32)]
        self.xn = A(32, [P, C], F32)
        self.qf = A(40, [P, C], F32)
        self.uu = A(48, [P, C], F32)
        self.trt = A(56, [P, C], F32)
        self.y_scr = [A(64, [P, C], F32), A(64, [P, C], F32)]
        self.fc1T = A(76, [P, NKC, LAT], BF16)
        self.x_latT = A(0, [P, 8, T], BF16)
        self.slab = [A(32, [P, NKC, 512], BF16), A(48, [P, NKC, 512], BF16)]
        self.w1b = [A(32, [P, 8, INTER], BF16), A(48, [P, 8, INTER], BF16)]
        self.w2b = [A(64, [P, 8, LAT], BF16), A(80, [P, 8, LAT], BF16)]
        self.h_all = A(96, [P, 8, 512], BF16)
        self.ylb = A(104, [P, LAT], BF16)
        self.dnb = A(0, [P, 8, C], BF16)
        self.slab3 = A(32, [P, NKC, 512], BF16)
        self.sTs = A(48, [P, 8, 512], BF16)
        self.upTb = A(56, [P, NKC, 1024], BF16)
        self.yl3 = A(88, [P, 8, 512], BF16)
        self.x1ld = [A(0, [P, C], F32), A(8, [P, C], F32)]
        nhalf_t = max(NT // 2, 4)
        self.y_lat = A(16, [P, nhalf_t, LAT], F32)
        self.rtmp2 = [A(48, [P, 512], F32), A(50, [P, 512], F32)]
        self.rout = A(0, [P, 4, C], F32)
        self.x1l3 = A(32, [P, C], F32)
        self.outb = A(40, [P, C], F32)
        self.rtmp3 = [A(48, [P, 512], F32), A(50, [P, 512], F32)]

    def act(self, out, in_, func, bias=0.0, scale=1.0, accum=None,
            reads=(), writes=()):
        kw = dict(bias=bias, scale=scale)
        if accum is not None:
            kw["accum_out"] = accum
        self.A_("activation", out, in_, func, reads=reads, writes=writes, **kw)

    def bn(self, src_buf, src_ap2048, extra_reads=()):
        for i in range(4):
            self.V_("bn_stats", self.bst.ap[:, i, :], src_ap2048[:, i * 512:(i + 1) * 512],
                    reads=[src_buf, *extra_reads], writes=[self.bst])
        self.V_("bn_aggr", self.mv.ap, self.bst.ap,
                reads=[self.bst], writes=[self.mv])

    def ln_rstd(self, var_ap, eps_buf, out_buf, reads):
        self.act(self.lnv.ap, var_ap, ACTF.Ln, bias=eps_buf.ap, scale=1.0,
                 reads=list(reads) + [eps_buf], writes=[self.lnv])
        self.act(out_buf.ap, self.lnv.ap, ACTF.Exp, bias=0.0, scale=-0.5,
                 reads=[self.lnv], writes=[out_buf])

    def const_loads(self):
        self.V_("memset", self.c_eps5.ap, 1e-5, writes=[self.c_eps5])
        self.V_("memset", self.c_eps6.ap, 1e-6, writes=[self.c_eps6])
        self.V_("memset", self.c_2565.ap, 256.5, writes=[self.c_2565])
        self.V_("memset", self.c_m256.ap, -256.0, writes=[self.c_m256])
        self.V_("memset", self.c_n1e30.ap, -1e30, writes=[self.c_n1e30])
        self.V_("memset", self.c_one.ap, 1.0, writes=[self.c_one])
        self.V_("tensor_single_scalar", self.gdisc.ap, self.c_one.ap, M64,
                ALU.add, reads=[self.c_one], writes=[self.gdisc])
        self.V_("tensor_single_scalar", self.gdisc.ap, self.gdisc.ap, M64,
                ALU.subtract, reads=[self.gdisc], writes=[self.gdisc])
        self.dma("cb", self.cb.ap, self.cbm, writes=[self.cb])
        self.dma("ones", self.ones.ap, self.onesd, writes=[self.ones])
        self.dma("gbias", self.gbias.ap, self.gbiasd, writes=[self.gbias])
        self.dma("gw", self.gw_sb.ap,
                 self.gwT.rearrange("(a p) n -> p a n", p=P), writes=[self.gw_sb])
        from concourse.hw_specs import get_activation_tables
        idx = None
        for i, funcs in enumerate(get_activation_tables(self.nc.m.arch).values()):
            if ACTF.Exp in funcs and ACTF.Ln in funcs:
                idx = i
                break
        if idx is not None:
            nc = self.nc
            def fn(e, idx=idx, nc=nc):
                return e.add_instruction(mybir.InstLoadActFuncSet(
                    name=nc.get_next_instruction_name(),
                    act_func_set_id=idx, ins=[], outs=[]))
            self.p.op("A", fn, (), ())

    def tern(self, wdram, mclip, cmul):
        NCH = NKC
        for kc in range(NCH):
            ch = self.chunk[kc % 2]
            self.dma("ch%d" % (kc % 2), ch.ap, wdram[kc * P:(kc + 1) * P, :],
                     reads=[ch], writes=[ch])
            self.V_("reduce_sum", self.cs16.ap[:, kc:kc + 1], ch.ap, AX.X,
                    apply_absolute_value=True, reads=[ch], writes=[self.cs16])
        self.V_("reduce_sum", self.cstot.ap, self.cs16.ap, AX.X,
                reads=[self.cs16], writes=[self.cstot])
        ps1 = self.PSR("A", 0, 512)
        self.T_("matmul", self.psA[:, 0:1], self.ones.ap, self.cstot.ap,
                start=True, stop=True, reads=[self.ones, self.cstot], writes=[ps1])
        self.V_("tensor_scalar", mclip.ap, self.psA[:, 0:1], float(INV_N), 1e-5,
                ALU.mult, ALU.max, reads=[ps1], writes=[mclip])
        self.V_("tensor_scalar", self.thr.ap, mclip.ap, 0.5, None, ALU.mult,
                reads=[mclip], writes=[self.thr])
        self.V_("tensor_scalar", self.nthr.ap, mclip.ap, -0.5, None, ALU.mult,
                reads=[mclip], writes=[self.nthr])
        self.V_("tensor_scalar", self.mqs.ap, mclip.ap, float(cmul), None,
                ALU.mult, reads=[mclip], writes=[self.mqs])
        for kc in range(NCH):
            ch = self.chunk[kc % 2]
            self.dma("ch%d" % (kc % 2), ch.ap, wdram[kc * P:(kc + 1) * P, :],
                     reads=[ch], writes=[ch])
            self.V_("tensor_scalar", self.qf.ap, ch.ap, self.nthr.ap, None,
                    ALU.is_lt, reads=[ch, self.nthr], writes=[self.qf])
            self.V_("scalar_tensor_tensor", self.wtern.ap[:, kc, :], ch.ap,
                    self.thr.ap, self.qf.ap, ALU.is_gt, ALU.subtract,
                    reads=[ch, self.thr, self.qf], writes=[self.wtern])

    def a0(self):
        for t in range(self.NT):
            xt = self.xt[t % 2]
            self.dma("xt%d" % (t % 2), xt.ap, self.x[t * P:(t + 1) * P, :],
                     reads=[xt], writes=[xt])
            self.bn(xt, xt.ap)
            self.ln_rstd(self.mv.ap[:, 1:2], self.c_eps5, self.rstd, [self.mv])
            self.V_("tensor_tensor", self.rsq.ap, self.rstd.ap, self.rstd.ap,
                    ALU.mult, reads=[self.rstd], writes=[self.rsq])
            self.V_("tensor_tensor", self.bq.ap, self.mv.ap[:, 1:2], self.rsq.ap,
                    ALU.mult, reads=[self.mv, self.rsq], writes=[self.bq])
            self.act(self.lnv.ap, self.bq.ap, ACTF.Ln, bias=self.c_eps6.ap,
                     reads=[self.bq, self.c_eps6], writes=[self.lnv])
            self.act(self.rr.ap, self.lnv.ap, ACTF.Exp, scale=-0.5,
                     reads=[self.lnv], writes=[self.rr])
            self.V_("tensor_tensor", self.sfold.ap, self.rstd.ap, self.rr.ap,
                    ALU.mult, reads=[self.rstd, self.rr], writes=[self.sfold])
            self.V_("tensor_scalar", self.xn.ap, xt.ap, self.mv.ap[:, 0:1],
                    self.sfold.ap, ALU.subtract, ALU.mult,
                    reads=[xt, self.mv, self.sfold], writes=[self.xn])
            self.quant_store(self.xn, self.samc, t, self.xqT_s, "xqT_s",
                             scale_pre=None)

    def quant_store(self, src, samc, t, dst_dram, dst_key, scale_pre):
        self.V_("reduce_max", self.am.ap, src.ap, AX.X, apply_absolute_value=True,
                reads=[src], writes=[self.am])
        if scale_pre is not None:
            self.V_("tensor_tensor", self.amc.ap, self.am.ap, scale_pre.ap,
                    ALU.mult, reads=[self.am, scale_pre], writes=[self.amc])
            amsrc = self.amc
        else:
            amsrc = self.am
        self.V_("tensor_single_scalar", samc.ap[:, t:t + 1], amsrc.ap, 1e-5,
                ALU.max, reads=[amsrc], writes=[samc])
        self.V_("reciprocal", self.rmc.ap, samc.ap[:, t:t + 1],
                reads=[samc], writes=[self.rmc])
        self.V_("tensor_scalar", self.qsc.ap, self.rmc.ap, 127.0, None, ALU.mult,
                reads=[self.rmc], writes=[self.qsc])
        if scale_pre is not None:
            self.V_("tensor_tensor", self.qsc2.ap, self.qsc.ap, scale_pre.ap,
                    ALU.mult, reads=[self.qsc, scale_pre], writes=[self.qsc2])
            qs = self.qsc2
        else:
            qs = self.qsc
        self.V_("tensor_scalar", self.qf.ap, src.ap, qs.ap, None, ALU.mult,
                reads=[src, qs], writes=[self.qf])
        self.V_("tensor_single_scalar", self.uu.ap, self.qf.ap, M32, ALU.add,
                reads=[self.qf], writes=[self.uu])
        self.V_("tensor_single_scalar", self.uu.ap, self.uu.ap, M32,
                ALU.subtract, reads=[self.uu], writes=[self.uu])
        self.V_("tensor_single_scalar", self.trt.ap, self.qf.ap, M64, ALU.add,
                reads=[self.qf], writes=[self.trt])
        self.V_("tensor_single_scalar", self.trt.ap, self.trt.ap, M64,
                ALU.subtract, reads=[self.trt], writes=[self.trt])
        self.V_("tensor_tensor", self.trt.ap, self.trt.ap, self.uu.ap,
                ALU.subtract, reads=[self.trt, self.uu], writes=[self.trt])
        self.V_("scalar_tensor_tensor", self.xqb.ap, self.trt.ap,
                self.gdisc.ap, self.uu.ap, ALU.mult, ALU.add,
                reads=[self.trt, self.uu, self.gdisc], writes=[self.xqb])
        tr = self.tr[t % 2]
        self.transpose("trp%d" % (t % 2), tr.ap, self.xqb.ap,
                       reads=[self.xqb, tr], writes=[tr])
        self.dma("s" + dst_key, dst_dram[t], tr.ap, reads=[tr],
                 writes=[(dst_key, t)])

    def a1(self, kind):
        NT = self.NT
        for t in range(NT):
            ld = self.ld[t % 2]
            self.dma("ld%d" % (t % 2), ld.ap, self.xqT_s[t],
                     reads=[ld, ("xqT_s", t)], writes=[ld])
            psn = "A" if t % 2 == 0 else "B"
            pst = self.psA if psn == "A" else self.psB
            psb = self.PSR(psn, 0, 2048)
            for kc in range(NKC):
                for n in range(4):
                    self.T_("matmul", pst[:, n * 512:(n + 1) * 512],
                            ld.ap[:, kc, :], self.wtern.ap[:, kc, n * 512:(n + 1) * 512],
                            start=(kc == 0), stop=(kc == NKC - 1),
                            reads=[ld, self.wtern], writes=[psb])
            self.V_("tensor_scalar", self.sc1.ap, self.samc.ap[:, t:t + 1],
                    self.mqs.ap, None, ALU.mult,
                    reads=[self.samc, self.mqs], writes=[self.sc1])
            if kind == "v":
                self.V_("tensor_scalar", self.vbb.ap, pst[:, :], self.sc1.ap,
                        None, ALU.mult, reads=[psb, self.sc1], writes=[self.vbb])
                for hg in range(4):
                    self.dma("sv%d" % hg, self.v_s[hg, t * P:(t + 1) * P, :],
                             self.vbb.ap[:, hg * 512:(hg + 1) * 512],
                             reads=[self.vbb], writes=[("v_s", hg, t)])
                continue
            self.V_("tensor_scalar", self.qf.ap, pst[:, :], self.sc1.ap, None,
                    ALU.mult, reads=[psb, self.sc1], writes=[self.qf])
            self.dma("cc", self.cct.ap, self.cc[t * P:(t + 1) * P, :],
                     reads=[self.cct], writes=[self.cct])
            self.dma("ss", self.sst.ap, self.ss[t * P:(t + 1) * P, :],
                     reads=[self.sst], writes=[self.sst])
            qf3 = self.qf.ap.rearrange("p (h d) -> p h d", d=D)
            tr3 = self.trt.ap.rearrange("p (h d) -> p h d", d=D)
            ccb = self.cct.ap.unsqueeze(1).broadcast_to([P, H, D])
            uu3 = self.uu.ap.rearrange("p (h d) -> p h d", d=D)
            self.V_("tensor_tensor", uu3, qf3, ccb, ALU.mult,
                    reads=[self.qf, self.cct], writes=[self.uu])
            sslo = self.sst.ap[:, 0:64].unsqueeze(1).broadcast_to([P, H, 64])
            sshi = self.sst.ap[:, 64:128].unsqueeze(1).broadcast_to([P, H, 64])
            self.V_("tensor_tensor", tr3[:, :, 0:64], qf3[:, :, 64:128], sslo,
                    ALU.mult, reads=[self.qf, self.sst], writes=[self.trt])
            self.V_("tensor_tensor", tr3[:, :, 64:128], qf3[:, :, 0:64], sshi,
                    ALU.mult, reads=[self.qf, self.sst], writes=[self.trt])
            self.V_("tensor_tensor", self.xqb.ap, self.uu.ap, self.trt.ap,
                    ALU.add, reads=[self.uu, self.trt], writes=[self.xqb])
            tr = self.tr[t % 2]
            self.transpose("trp%d" % (t % 2), tr.ap, self.xqb.ap,
                           reads=[self.xqb, tr], writes=[tr])
            if kind == "q":
                self.dma("sqrT", self.qrT_s[t], tr.ap, reads=[tr],
                         writes=[("qrT_s", t)])
            else:
                for h in range(H):
                    self.dma("skr%d" % h, self.krT_s[h, :, t * P:(t + 1) * P],
                             tr.ap[:, h, :], reads=[tr], writes=[("krT_s", h)])

    def a2(self):
        NT = self.NT
        for qt in range(NT):
            L = (qt + 1) * P
            ld = self.ld[qt % 2]
            self.dma("ld%d" % (qt % 2), ld.ap, self.qrT_s[qt],
                     reads=[ld, ("qrT_s", qt)], writes=[ld])
            ysc = self.y_scr[qt % 2]
            for hg in range(4):
                for ktq in range(0, qt + 1, 4):
                    nkt = min(4, qt + 1 - ktq)
                    src = self.v_s[hg].rearrange("(k p) f -> p k f", p=P)
                    self.dma("vsl", self.vs.ap[:, ktq:ktq + nkt, :],
                             src[:, ktq:ktq + nkt, :],
                             reads=[("v_s", hg, tt) for tt in range(ktq, ktq + nkt)],
                             writes=[self.vs])
                for h4 in range(4):
                    h = hg * 4 + h4
                    Rn = "A" if h % 2 == 0 else "B"
                    Rt = self.psA if Rn == "A" else self.psB
                    Rb = self.PSR(Rn, 0, L)
                    ks = self.ks[h % 2]
                    self.dma("ks%d" % (h % 2), ks.ap[:, 0:L], self.krT_s[h, :, 0:L],
                             reads=[ks, ("krT_s", h)], writes=[ks])
                    nch = (L + 511) // 512
                    for cc_ in range(nch):
                        n = min(512, L - cc_ * 512)
                        self.T_("matmul", Rt[:, cc_ * 512:cc_ * 512 + n],
                                ld.ap[:, h, :], ks.ap[:, cc_ * 512:cc_ * 512 + n],
                                start=True, stop=True,
                                reads=[ld, ks], writes=[self.PSR(Rn, cc_ * 512, cc_ * 512 + n)])
                    self.V_("tensor_tensor", Rt[:, qt * P:L], Rt[:, qt * P:L],
                            self.cb.ap, ALU.add,
                            reads=[self.PSR(Rn, qt * P, L), self.cb],
                            writes=[self.PSR(Rn, qt * P, L)])
                    self.V_("reduce_max", self.nm.ap, Rt[:, 0:L], AX.X,
                            negate=True, reads=[Rb], writes=[self.nm])
                    self.act(self.pb.ap[:, 0:L], Rt[:, 0:L], ACTF.Exp,
                             bias=self.nm.ap, scale=1.0, accum=self.den.ap,
                             reads=[Rb, self.nm], writes=[self.pb, self.den])
                    self.V_("reciprocal", self.rden.ap, self.den.ap,
                            reads=[self.den], writes=[self.rden])
                    self.transpose("trpP", self.pT.ap[:, 0:qt + 1, :],
                                   self.pb.ap[:, 0:L],
                                   reads=[self.pb, self.pT], writes=[self.pT])
                    Yn = "B" if h % 2 == 0 else "A"
                    Yt = self.psB if Yn == "B" else self.psA
                    Yb = self.PSR(Yn, 1920, 2048)
                    for kt in range(qt + 1):
                        self.T_("matmul", Yt[:, 1920:2048], self.pT.ap[:, kt, :],
                                self.vs.ap[:, kt, h4 * P:(h4 + 1) * P],
                                start=(kt == 0), stop=(kt == qt),
                                reads=[self.pT, self.vs], writes=[Yb])
                    self.V_("tensor_scalar", ysc.ap[:, h * P:(h + 1) * P],
                            Yt[:, 1920:2048], self.rden.ap, None, ALU.mult,
                            reads=[Yb, self.rden], writes=[ysc])
            self.V_("tensor_tensor", self.xn.ap, ysc.ap, ysc.ap, ALU.mult,
                    reads=[ysc], writes=[self.xn])
            self.V_("reduce_sum", self.ss2v.ap, self.xn.ap, AX.X,
                    reads=[self.xn], writes=[self.ss2v])
            self.act(self.lnv.ap, self.ss2v.ap, ACTF.Ln, bias=self.c_eps6.ap,
                     scale=float(1.0 / C), reads=[self.ss2v, self.c_eps6],
                     writes=[self.lnv])
            self.act(self.rr.ap, self.lnv.ap, ACTF.Exp, scale=-0.5,
                     reads=[self.lnv], writes=[self.rr])
            self.quant_store(ysc, self.samc_y, qt, self.yqT_s, "yqT_s",
                             scale_pre=self.rr)

    def a3(self):
        NT = self.NT
        for t in range(NT):
            ld = self.ld[t % 2]
            self.dma("ld%d" % (t % 2), ld.ap, self.yqT_s[t],
                     reads=[ld, ("yqT_s", t)], writes=[ld])
            xt = self.xt[t % 2]
            self.dma("xt%d" % (t % 2), xt.ap, self.x[t * P:(t + 1) * P, :],
                     reads=[xt], writes=[xt])
            psn = "A" if t % 2 == 0 else "B"
            pst = self.psA if psn == "A" else self.psB
            psb = self.PSR(psn, 0, 2048)
            for kc in range(NKC):
                for n in range(4):
                    self.T_("matmul", pst[:, n * 512:(n + 1) * 512],
                            ld.ap[:, kc, :], self.wtern.ap[:, kc, n * 512:(n + 1) * 512],
                            start=(kc == 0), stop=(kc == NKC - 1),
                            reads=[ld, self.wtern], writes=[psb])
            self.V_("tensor_scalar", self.sc1.ap, self.samc_y.ap[:, t:t + 1],
                    self.mqs.ap, None, ALU.mult,
                    reads=[self.samc_y, self.mqs], writes=[self.sc1])
            self.bn(psb, pst)
            self.V_("tensor_tensor", self.s2t.ap, self.sc1.ap, self.sc1.ap,
                    ALU.mult, reads=[self.sc1], writes=[self.s2t])
            self.V_("tensor_tensor", self.vv.ap, self.mv.ap[:, 1:2], self.s2t.ap,
                    ALU.mult, reads=[self.mv, self.s2t], writes=[self.vv])
            self.ln_rstd(self.vv.ap, self.c_eps5, self.rstd, [self.vv])
            self.V_("tensor_tensor", self.gv.ap, self.sc1.ap, self.rstd.ap,
                    ALU.mult, reads=[self.sc1, self.rstd], writes=[self.gv])
            self.V_("tensor_scalar", self.xn.ap, pst[:, :], self.mv.ap[:, 0:1],
                    self.gv.ap, ALU.subtract, ALU.mult,
                    reads=[psb, self.mv, self.gv], writes=[self.xn])
            self.V_("tensor_tensor", self.qf.ap, self.xn.ap, xt.ap, ALU.add,
                    reads=[self.xn, xt], writes=[self.qf])
            self.dma("sx1", self.x1_s[t], self.qf.ap, reads=[self.qf],
                     writes=[("x1_s", t)])

    def b0(self):
        NT = self.NT
        for kc in range(NKC):
            self.dma("fc1", self.fc1T.ap[:, kc, :],
                     self.fc1Td[kc * P:(kc + 1) * P, :],
                     reads=[self.fc1T], writes=[self.fc1T])
        for t in range(NT):
            x1 = self.x1ld[t % 2]
            self.dma("x1ld%d" % (t % 2), x1.ap, self.x1_s[t],
                     reads=[x1, ("x1_s", t)], writes=[x1])
            self.bn(x1, x1.ap)
            self.ln_rstd(self.mv.ap[:, 1:2], self.c_eps5, self.rstd, [self.mv])
            self.V_("tensor_scalar", self.xqb.ap, x1.ap, self.mv.ap[:, 0:1],
                    self.rstd.ap, ALU.subtract, ALU.mult,
                    reads=[x1, self.mv, self.rstd], writes=[self.xqb])
            tr = self.tr[t % 2]
            self.transpose("trp%d" % (t % 2), tr.ap, self.xqb.ap,
                           reads=[self.xqb, tr], writes=[tr])
            self.dma("sln2T", self.ln2T_s[t], tr.ap, reads=[tr],
                     writes=[("ln2T_s", t)])
            psg = self.PSR("B", 0, 512)
            for kc in range(NKC):
                self.T_("matmul", self.psB[:, 0:NE], tr.ap[:, kc, :],
                        self.gw_sb.ap[:, kc, :],
                        start=(kc == 0), stop=(kc == NKC - 1),
                        reads=[tr, self.gw_sb], writes=[psg])
            self.act(self.et.ap, self.psB[:, 0:NE], ACTF.Exp, scale=-1.0,
                     reads=[psg], writes=[self.et])
            self.V_("tensor_single_scalar", self.d1.ap, self.et.ap, 1.0, ALU.add,
                    reads=[self.et], writes=[self.d1])
            self.V_("reciprocal", self.scrs.ap, self.d1.ap,
                    reads=[self.d1], writes=[self.scrs])
            self.V_("tensor_tensor", self.sbv.ap, self.scrs.ap, self.gbias.ap,
                    ALU.add, reads=[self.scrs, self.gbias], writes=[self.sbv])
            for g in range(4):
                sg = self.sbv.ap[:, g * 8:(g + 1) * 8]
                self.V_("reduce_max", self.m1g.ap, sg, AX.X,
                        reads=[self.sbv], writes=[self.m1g])
                self.V_("tensor_scalar", self.msk8.ap, sg, self.m1g.ap, None,
                        ALU.is_ge, reads=[self.sbv, self.m1g], writes=[self.msk8])
                self.V_("scalar_tensor_tensor", self.t8b.ap, self.msk8.ap,
                        self.c_n1e30.ap, sg, ALU.mult, ALU.add,
                        reads=[self.msk8, self.sbv, self.c_n1e30],
                        writes=[self.t8b])
                self.V_("reduce_max", self.m2g.ap, self.t8b.ap, AX.X,
                        reads=[self.t8b], writes=[self.m2g])
                self.V_("tensor_tensor", self.gsv.ap[:, g:g + 1], self.m1g.ap,
                        self.m2g.ap, ALU.add,
                        reads=[self.m1g, self.m2g], writes=[self.gsv])
            self.V_("reduce_max", self.g1v.ap, self.gsv.ap, AX.X,
                    reads=[self.gsv], writes=[self.g1v])
            self.V_("tensor_scalar", self.mask4.ap, self.gsv.ap, self.g1v.ap,
                    None, ALU.is_ge, reads=[self.gsv, self.g1v], writes=[self.mask4])
            self.V_("scalar_tensor_tensor", self.t4b.ap, self.mask4.ap,
                    self.c_n1e30.ap, self.gsv.ap, ALU.mult, ALU.add,
                    reads=[self.mask4, self.gsv, self.c_n1e30],
                    writes=[self.t4b])
            self.V_("reduce_max", self.g2v.ap, self.t4b.ap, AX.X,
                    reads=[self.t4b], writes=[self.g2v])
            self.V_("tensor_scalar", self.mask4.ap, self.gsv.ap, self.g2v.ap,
                    None, ALU.is_ge, reads=[self.gsv, self.g2v], writes=[self.mask4])
            self.V_("tensor_scalar", self.madd4.ap, self.mask4.ap, -1.0, 1e30,
                    ALU.add, ALU.mult, reads=[self.mask4], writes=[self.madd4])
            sb3 = self.sbv.ap.rearrange("p (g e) -> p g e", e=8)
            msb3 = self.msb.ap.rearrange("p (g e) -> p g e", e=8)
            self.V_("tensor_tensor", msb3, sb3,
                    self.madd4.ap.unsqueeze(2).broadcast_to([P, 4, 8]), ALU.add,
                    reads=[self.sbv, self.madd4], writes=[self.msb])
            self.V_("tensor_copy", self.cur.ap, self.msb.ap,
                    reads=[self.msb], writes=[self.cur])
            for _ in range(7):
                self.V_("reduce_max", self.mx1.ap, self.cur.ap, AX.X,
                        reads=[self.cur], writes=[self.mx1])
                self.V_("tensor_scalar", self.mk32.ap, self.cur.ap, self.mx1.ap,
                        None, ALU.is_ge, reads=[self.cur, self.mx1],
                        writes=[self.mk32])
                self.V_("scalar_tensor_tensor", self.cur.ap, self.mk32.ap,
                        self.c_n1e30.ap, self.cur.ap, ALU.mult, ALU.add,
                        reads=[self.mk32, self.cur, self.c_n1e30],
                        writes=[self.cur])
            self.V_("reduce_max", self.th8.ap, self.cur.ap, AX.X,
                    reads=[self.cur], writes=[self.th8])
            self.V_("tensor_scalar", self.selm.ap, self.msb.ap, self.th8.ap,
                    None, ALU.is_ge, reads=[self.msb, self.th8], writes=[self.selm])
            self.V_("tensor_tensor", self.wsl.ap, self.selm.ap, self.scrs.ap,
                    ALU.mult, reads=[self.selm, self.scrs], writes=[self.wsl])
            self.V_("reduce_sum", self.s8.ap, self.wsl.ap, AX.X,
                    reads=[self.wsl], writes=[self.s8])
            self.V_("reciprocal", self.rs8.ap, self.s8.ap,
                    reads=[self.s8], writes=[self.rs8])
            self.V_("tensor_scalar", self.rs25.ap, self.rs8.ap, 2.5, None,
                    ALU.mult, reads=[self.rs8], writes=[self.rs25])
            self.V_("tensor_scalar", self.cw.ap[:, t, :], self.wsl.ap,
                    self.rs25.ap, None, ALU.mult,
                    reads=[self.wsl, self.rs25], writes=[self.cw])

    def load_slab(self, lane, slab, blk):
        for j in range(4):
            t = blk * 4 + j
            self.dma(lane, slab.ap[:, :, j * P:(j + 1) * P], self.ln2T_s[t],
                     reads=[("ln2T_s", t)], writes=[slab])

    def b1(self):
        NSLAB = self.T // 512
        for blk in range(NSLAB):
            slab = self.slab[blk % 2]
            self.load_slab("slab%d" % (blk % 2), slab, blk)
            for lc in range(8):
                bank = lc % 4
                psb = self.PSR("A", bank * 512, (bank + 1) * 512)
                for kc in range(NKC):
                    self.T_("matmul", self.psA[:, bank * 512:(bank + 1) * 512],
                            self.fc1T.ap[:, kc, lc * P:(lc + 1) * P],
                            slab.ap[:, kc, :],
                            start=(kc == 0), stop=(kc == NKC - 1),
                            reads=[self.fc1T, slab], writes=[psb])
                self.V_("tensor_copy", self.x_latT.ap[:, lc, blk * 512:(blk + 1) * 512],
                        self.psA[:, bank * 512:(bank + 1) * 512],
                        reads=[psb], writes=[self.x_latT])

    def b2(self):
        NT = self.NT
        nhalves = 2 if NT >= 8 else 1
        tiles_ph = NT // nhalves
        slabs_ph = tiles_ph // 4
        for half in range(nhalves):
            for e in range(NE):
                w1 = self.w1b[e % 2]
                w2 = self.w2b[e % 2]
                for lc in range(8):
                    self.dma("w1b%d" % (e % 2), w1.ap[:, lc, :],
                             self.w1Td[e, lc * P:(lc + 1) * P, :],
                             reads=[w1], writes=[w1])
                for ic in range(8):
                    self.dma("w2b%d" % (e % 2), w2.ap[:, ic, :],
                             self.w2Td[e, ic * P:(ic + 1) * P, :],
                             reads=[w2], writes=[w2])
                for blk in range(slabs_ph):
                    toff = half * tiles_ph * P + blk * 512
                    for ic in range(8):
                        bank = ic % 4
                        psb = self.PSR("A", bank * 512, (bank + 1) * 512)
                        for lc in range(8):
                            self.T_("matmul",
                                    self.psA[:, bank * 512:(bank + 1) * 512],
                                    w1.ap[:, lc, ic * P:(ic + 1) * P],
                                    self.x_latT.ap[:, lc, toff:toff + 512],
                                    start=(lc == 0), stop=(lc == 7),
                                    reads=[w1, self.x_latT], writes=[psb])
                        rt = self.rtmp2[ic % 2]
                        self.V_("tensor_scalar", rt.ap,
                                self.psA[:, bank * 512:(bank + 1) * 512],
                                0.0, None, ALU.max, reads=[psb], writes=[rt])
                        self.A_("activation", self.h_all.ap[:, ic, :], rt.ap,
                                ACTF.Square, reads=[rt], writes=[self.h_all])
                    for lh in range(2):
                        for tt in range(4):
                            psb = self.PSR("B", tt * 512, (tt + 1) * 512)
                            for ic in range(8):
                                self.T_("matmul",
                                        self.psB[:, tt * 512:(tt + 1) * 512],
                                        self.h_all.ap[:, ic, tt * P:(tt + 1) * P],
                                        w2.ap[:, ic, lh * 512:(lh + 1) * 512],
                                        start=(ic == 0), stop=(ic == 7),
                                        reads=[self.h_all, w2], writes=[psb])
                            th = blk * 4 + tt
                            tg = half * tiles_ph + th
                            cwap = self.cw.ap[:, tg, e:e + 1]
                            ydst = self.y_lat.ap[:, th, lh * 512:(lh + 1) * 512]
                            if e == 0:
                                self.V_("tensor_scalar", ydst,
                                        self.psB[:, tt * 512:(tt + 1) * 512],
                                        cwap, None, ALU.mult,
                                        reads=[psb, self.cw], writes=[self.y_lat])
                            else:
                                self.V_("scalar_tensor_tensor", ydst,
                                        self.psB[:, tt * 512:(tt + 1) * 512],
                                        cwap, ydst, ALU.mult, ALU.add,
                                        reads=[psb, self.cw, self.y_lat],
                                        writes=[self.y_lat])
            for th in range(tiles_ph):
                tg = half * tiles_ph + th
                self.V_("tensor_copy", self.ylb.ap, self.y_lat.ap[:, th, :],
                        reads=[self.y_lat], writes=[self.ylb])
                self.transpose("trpY", self.ylT.ap, self.ylb.ap,
                               reads=[self.ylb, self.ylT], writes=[self.ylT])
                self.dma("sylT", self.ylT_s[:, :, tg * P:(tg + 1) * P], self.ylT.ap,
                         reads=[self.ylT], writes=[("ylT_s", tg)])

    def b3(self):
        NSLAB = self.T // 512
        for blk in range(NSLAB):
            slab = self.slab3
            self.load_slab("slab3", slab, blk)
            self.dma("yl3", self.yl3.ap,
                     self.ylT_s[:, :, blk * 512:(blk + 1) * 512],
                     reads=[self.yl3] + [("ylT_s", blk * 4 + j) for j in range(4)],
                     writes=[self.yl3])
            for sig in range(5):
                if sig < 4:
                    for kc in range(NKC):
                        self.dma("upw", self.upTb.ap[:, kc, :],
                                 self.upTd[kc * P:(kc + 1) * P,
                                           sig * 1024:(sig + 1) * 1024],
                                 reads=[self.upTb], writes=[self.upTb])
                    for sc in range(8):
                        self.dma("dnw", self.dnb.ap[:, sc, :],
                                 self.dfT[sig * 1024 + sc * P:
                                          sig * 1024 + (sc + 1) * P, :],
                                 reads=[self.dnb], writes=[self.dnb])
                    for si in range(8):
                        bank = si % 4
                        psb = self.PSR("A", bank * 512, (bank + 1) * 512)
                        for kc in range(NKC):
                            self.T_("matmul",
                                    self.psA[:, bank * 512:(bank + 1) * 512],
                                    self.upTb.ap[:, kc, si * P:(si + 1) * P],
                                    slab.ap[:, kc, :],
                                    start=(kc == 0), stop=(kc == NKC - 1),
                                    reads=[self.upTb, slab], writes=[psb])
                        rt = self.rtmp3[si % 2]
                        self.V_("tensor_scalar", rt.ap,
                                self.psA[:, bank * 512:(bank + 1) * 512],
                                0.0, None, ALU.max, reads=[psb], writes=[rt])
                        self.A_("activation", self.sTs.ap[:, si, :], rt.ap,
                                ACTF.Square, reads=[rt], writes=[self.sTs])
                else:
                    for sc in range(8):
                        self.dma("dnw", self.dnb.ap[:, sc, :],
                                 self.dfT[SI + sc * P:SI + (sc + 1) * P, :],
                                 reads=[self.dnb], writes=[self.dnb])
                for t in range(4):
                    tg = blk * 4 + t
                    psb = self.PSR("B", 0, 2048)
                    for sc in range(8):
                        if sig < 4:
                            lhs = self.sTs.ap[:, sc, t * P:(t + 1) * P]
                            lrd = self.sTs
                        else:
                            lhs = self.yl3.ap[:, sc, t * P:(t + 1) * P]
                            lrd = self.yl3
                        for co in range(4):
                            self.T_("matmul", self.psB[:, co * 512:(co + 1) * 512],
                                    lhs, self.dnb.ap[:, sc, co * 512:(co + 1) * 512],
                                    start=(sc == 0), stop=(sc == 7),
                                    reads=[lrd, self.dnb],
                                    writes=[self.PSR("B", co * 512, (co + 1) * 512)])
                    rdst = self.rout.ap[:, t, :]
                    if sig == 0:
                        self.V_("tensor_copy", rdst, self.psB[:, :],
                                reads=[psb], writes=[self.rout])
                    else:
                        self.V_("tensor_tensor", rdst, rdst, self.psB[:, :],
                                ALU.add, reads=[psb, self.rout], writes=[self.rout])
            for t in range(4):
                tg = blk * 4 + t
                rsrc = self.rout.ap[:, t, :]
                self.bn(self.rout, rsrc)
                self.ln_rstd(self.mv.ap[:, 1:2], self.c_eps5, self.rstd, [self.mv])
                self.dma("x1l3", self.x1l3.ap, self.x1_s[tg],
                         reads=[("x1_s", tg)], writes=[self.x1l3])
                self.V_("tensor_scalar", self.outb.ap, rsrc, self.mv.ap[:, 0:1],
                        self.rstd.ap, ALU.subtract, ALU.mult,
                        reads=[self.rout, self.mv, self.rstd], writes=[self.outb])
                self.V_("tensor_tensor", self.outb.ap, self.outb.ap, self.x1l3.ap,
                        ALU.add, reads=[self.outb, self.x1l3], writes=[self.outb])
                self.dma("sout", self.out[tg * P:(tg + 1) * P, :], self.outb.ap,
                         reads=[self.outb], writes=[("out", tg)])

    def build_all(self):
        self.declare()
        self.const_loads()
        self.a0()
        self.tern(self.wqT, self.mclip_q, CQ)
        self.a1("q")
        self.tern(self.wvT, self.mclip_v, CK)
        self.a1("v")
        self.tern(self.wkT, self.mclip_k, CK)
        self.a1("k")
        self.a2()
        self.tern(self.woT, self.mclip_o, CK)
        self.a3()
        self.b0()
        self.b1()
        self.b2()
        self.b3()
        self.p.finish()


LANES = {}
for _l in (["xt0", "xt1", "ld0", "ld1", "ks0", "ks1", "vsl", "x1ld0", "x1ld1",
            "slab0", "slab1", "slab3", "x1l3", "sxqT_s", "sqrT", "syqT_s",
            "sx1", "sln2T", "sout", "sylT", "yl3", "trp0", "trp1", "trpP", "trpY"]
           + ["skr%d" % h for h in range(16)] + ["sv%d" % g for g in range(4)]):
    LANES[_l] = "S"
for _l in ("cb", "ones", "gbias", "cc", "ss", "ch0", "ch1"):
    LANES[_l] = "A"
for _l in ("gw", "fc1", "w1b0", "w1b1", "w2b0", "w2b1", "upw", "dnw"):
    LANES[_l] = "G"


def build_nc(NT=16):
    nc = bass.Bass("TRN2", target_bir_lowering=False)
    from contextlib import ExitStack
    with ExitStack() as st:
        sems = {}
        for name in ("T", "V", "A", "G"):
            sems[name] = st.enter_context(nc.semaphore("sem" + name))
        for name in LANES:
            sems[name] = st.enter_context(nc.semaphore("q" + name))
        prog = Prog(sems, LANES)
        b = Builder(nc, prog, NT)
        b.build_all()
        block = st.enter_context(nc.Block())
        prog.realize(block)
    return nc


def _host_prep(L0_attn_Wq, L0_attn_Wk, L0_attn_Wv, L0_attn_Wo, L0_ffn_gate_w,
               L0_ffn_gate_bias, L0_ffn_fc1_w, L0_ffn_fc2_w, L0_ffn_w1,
               L0_ffn_w2, L0_ffn_shared_up_w, L0_ffn_shared_down_w, T=2048):
    cont = np.ascontiguousarray
    perm = np.concatenate([
        np.concatenate([np.arange(h * 128, (h + 1) * 128, 2),
                        np.arange(h * 128 + 1, (h + 1) * 128, 2)])
        for h in range(16)])
    shared = {
        "wqT": cont(L0_attn_Wq[perm].T),
        "wkT": cont(L0_attn_Wk[perm].T),
        "wvT": cont(L0_attn_Wv.T),
        "woT": cont(L0_attn_Wo.T),
        "gwT": cont(L0_ffn_gate_w.T),
        "gbiasd": cont(np.tile(L0_ffn_gate_bias[None, :].astype(np.float32),
                               (128, 1))),
        "fc1Td": cont(L0_ffn_fc1_w.T),
        "dfT": cont(np.concatenate([L0_ffn_shared_down_w.T,
                                    L0_ffn_fc2_w.T], axis=0)),
        "w1Td": cont(L0_ffn_w1.transpose(0, 2, 1)),
        "w2Td": cont(L0_ffn_w2.transpose(0, 2, 1)),
        "upTd": cont(L0_ffn_shared_up_w.T),
    }
    import jax
    import jax.numpy as jnp
    with jax.default_device(jax.devices("cpu")[0]):
        freqs = 1.0 / (10000.0 ** (jnp.arange(0, 128, 2, dtype=jnp.float32) / 128))
        ang = jnp.arange(T, dtype=jnp.float32)[:, None] * freqs[None, :]
        cos = np.asarray(jnp.cos(ang), dtype=np.float32)
        sin = np.asarray(jnp.sin(ang), dtype=np.float32)
    shared["cc"] = cont(np.concatenate([cos, cos], axis=1))
    shared["ss"] = cont(np.concatenate([-sin, sin], axis=1))
    ii = np.arange(128)
    shared["cbm"] = np.where(ii[:, None] >= ii[None, :], 0.0,
                             -1e30).astype(np.float32)
    shared["onesd"] = np.ones((128, 128), np.float32)
    return shared


def kernel(L0_attn_x, L0_attn_Wq, L0_attn_Wk, L0_attn_Wv, L0_attn_Wo,
           L0_ffn_gate_w, L0_ffn_gate_bias, L0_ffn_fc1_w, L0_ffn_fc2_w,
           L0_ffn_w1, L0_ffn_w2, L0_ffn_shared_up_w, L0_ffn_shared_down_w):
    shared = _host_prep(L0_attn_Wq, L0_attn_Wk, L0_attn_Wv, L0_attn_Wo,
                        L0_ffn_gate_w, L0_ffn_gate_bias, L0_ffn_fc1_w,
                        L0_ffn_fc2_w, L0_ffn_w1, L0_ffn_w2,
                        L0_ffn_shared_up_w, L0_ffn_shared_down_w)
    nc = build_nc(NT=16)
    in_maps = []
    for c in range(8):
        m = dict(shared)
        m["x"] = np.ascontiguousarray(L0_attn_x[c])
        in_maps.append(m)
    res = run_bass_kernel_spmd(nc, in_maps, core_ids=list(range(8)))
    return np.stack([res.results[c]["out"] for c in range(8)], axis=0)


# version 08
# speedup vs baseline: 1.6358x; 1.6358x over previous
import numpy as np
import ml_dtypes
import concourse.bass as bass
import concourse.mybir as mybir
from concourse.bass_utils import run_bass_kernel_spmd

F32 = mybir.dt.float32
BF16 = mybir.dt.bfloat16
U8 = mybir.dt.uint8
AX = mybir.AxisListType
ALU = mybir.AluOpType
ACTF = mybir.ActivationFunctionType

P = 128
C = 2048
H = 16
D = 128
NKC = C // P
NE = 32
LAT = 1024
INTER = 1024
SI = 4096
KB = 1024
PGSZ = 2048

CQ = 1.0 / (127.0 * float(np.sqrt(np.float64(128.0))))
CK = 1.0 / 127.0
INV_N = 1.0 / (2048.0 * 2048.0)
M32 = float(2 ** 23 * 1.5)
M64 = float(2 ** 52 * 1.5)


class Buf:
    __slots__ = ("ap", "keys")

    def __init__(self, ap, keys):
        self.ap = ap
        self.keys = frozenset(keys)


class Prog:
    ENGS = ("T", "V", "A", "G", "S")

    def __init__(self, sems, lane_queue):
        self.sems = sems
        self.lane_queue = lane_queue
        self.cnt = {k: 0 for k in sems}
        self.steps = {e: [] for e in self.ENGS}
        self.seen = {e: {} for e in self.ENGS}
        self.pages = {}
        self.dirty = {e: set() for e in self.ENGS}
        self.rdirty = {e: set() for e in self.ENGS}

    def _rec(self, k):
        r = self.pages.get(k)
        if r is None:
            r = [None, []]
            self.pages[k] = r
        return r

    @staticmethod
    def _keys(bufs):
        s = set()
        for b in bufs:
            s |= b.keys if isinstance(b, Buf) else {b}
        return s

    def op(self, eng, fn, reads=(), writes=(), lane=None):
        rk = self._keys(reads)
        wk = self._keys(writes)
        waits = []
        seen = self.seen[eng]

        def need(sem, val, agent):
            if agent != eng and agent in self.lane_queue:
                val = self.cnt[agent]
            if val > 0 and seen.get(sem, 0) < val:
                waits.append((self.sems[sem], val))
                seen[sem] = val

        drain = False
        for k in rk | wk:
            rec = self.pages.get(k)
            if rec is None:
                continue
            w = rec[0]
            if w is not None:
                sem, val, agent = w
                if agent == eng:
                    if k in self.dirty[eng]:
                        drain = True
                else:
                    need(sem, val, agent)
            if k in wk:
                for sem, val, agent in rec[1]:
                    if agent == eng:
                        if k in self.rdirty[eng]:
                            drain = True
                    else:
                        need(sem, val, agent)
        if eng == "T":
            drain = False

        if lane is None:
            sem_name, inc = eng, 1
            agent = eng
        else:
            sem_name, inc = lane, 16
            agent = lane
        self.cnt[sem_name] += inc
        val = self.cnt[sem_name]

        if drain:
            self.dirty[eng].clear()
            self.rdirty[eng].clear()
        for k in wk:
            rec = self._rec(k)
            rec[0] = (sem_name, val, agent)
            rec[1] = []
        for k in rk:
            self._rec(k)[1].append((sem_name, val, agent))
        if lane is None:
            self.dirty[eng] |= wk
            self.rdirty[eng] |= rk
        self.steps[eng].append((tuple(waits), drain, fn, self.sems[sem_name], inc))
        return val

    def finish(self):
        waits = [(self.sems[k], v) for k, v in self.cnt.items() if v > 0]
        self.steps["S"].append((tuple(waits), False, None, None, 0))

    def realize(self, block):
        names = {"T": "tensor", "V": "vector", "A": "scalar", "G": "gpsimd", "S": "sync"}
        for e in self.ENGS:
            steps = self.steps[e]

            def mk(steps=steps):
                def body(eng):
                    for waits, drain, fn, sem, inc in steps:
                        for s, v in waits:
                            eng.wait_ge(s, v)
                        if drain:
                            eng.drain()
                        if fn is not None:
                            inst = fn(eng)
                            if sem is not None:
                                inst.then_inc(sem, inc)
                return body
            getattr(block, names[e])(mk())


class Builder:
    def __init__(self, nc, prog, NT):
        self.nc = nc
        self.p = prog
        self.NT = NT
        self.T = NT * P

    def T_(self, fname, *args, reads=(), writes=(), **kw):
        self.p.op("T", (lambda e, a=args, k=kw, f=fname: getattr(e, f)(*a, **k)),
                  reads, writes)

    def V_(self, fname, *args, reads=(), writes=(), **kw):
        self.p.op("V", (lambda e, a=args, k=kw, f=fname: getattr(e, f)(*a, **k)),
                  reads, writes)

    def A_(self, fname, *args, reads=(), writes=(), **kw):
        self.p.op("A", (lambda e, a=args, k=kw, f=fname: getattr(e, f)(*a, **k)),
                  reads, writes)

    def dma(self, lane, out_ap, in_ap, reads=(), writes=()):
        self.p.op(self.p.lane_queue[lane],
                  (lambda e, o=out_ap, i=in_ap: e.dma_start(out=o, in_=i)),
                  reads, writes, lane=lane)

    def transpose(self, lane, out_ap, in_ap, reads=(), writes=()):
        self.p.op("S", (lambda e, o=out_ap, i=in_ap: e.dma_start_transpose(o, i)),
                  reads, writes, lane=lane)

    def AR(self, off_kb, shape, dtype):
        nel = int(np.prod(shape[1:]))
        nby = nel * mybir.dt.size(dtype)
        if dtype == BF16:
            arena, tag, epk = self.arena16, "A6", 512
        else:
            arena, tag, epk = self.arena32, "A2", 256
        ap = arena[:, off_kb * epk: off_kb * epk + nel]
        if len(shape) == 3:
            ap = ap.rearrange("p (a b) -> p a b", b=shape[2])
        keys = {(tag, g) for g in range(off_kb * KB // PGSZ,
                                        (off_kb * KB + nby + PGSZ - 1) // PGSZ)}
        return Buf(ap, keys)

    def SB(self, name, shape, dtype=F32, tdest=False):
        h = self.nc.alloc_sbuf_tensor(name, shape, dtype, is_transpose_dest=tdest)
        return Buf(h[:], {name})

    def PSR(self, which, lo, hi):
        t = self.psA if which == "A" else self.psB
        keys = {("ps" + which, b) for b in range(lo // 512, (hi + 511) // 512)}
        return Buf(t[:, lo:hi], keys)

    def declare(self):
        nc = self.nc
        NT, T = self.NT, self.T
        dr = lambda n, s, dt=F32, kind="ExternalInput": nc.dram_tensor(n, s, dt, kind=kind).ap()
        self.x = dr("x", [T, C])
        self.wqT = dr("wqT", [C, C]); self.wkT = dr("wkT", [C, C])
        self.wvT = dr("wvT", [C, C]); self.woT = dr("woT", [C, C])
        self.cc = dr("cc", [T, D]); self.ss = dr("ss", [T, D])
        self.cbm = dr("cbm", [P, P]); self.onesd = dr("onesd", [P, P])
        self.gbiasd = dr("gbiasd", [P, NE])
        self.gwT = dr("gwT", [C, NE])
        self.fc1Td = dr("fc1Td", [C, LAT])
        self.dfT = dr("dfT", [SI + LAT, C])
        self.w1Td = dr("w1Td", [NE, LAT, INTER])
        self.w2Td = dr("w2Td", [NE, INTER, LAT])
        self.upTd = dr("upTd", [C, SI])
        self.out = dr("out", [T, C], F32, kind="ExternalOutput")
        idr = lambda n, s, dt: nc.dram_tensor(n, s, dt, kind="Internal").ap()
        self.xqT_s = idr("xqT_s", [NT, P, NKC, P], BF16)
        self.qrT_s = idr("qrT_s", [NT, P, NKC, P], BF16)
        self.krT_s = idr("krT_s", [H, P, T], BF16)
        self.v_s = idr("v_s", [4, T, 512], BF16)
        self.yqT_s = idr("yqT_s", [NT, P, NKC, P], BF16)
        self.x1_s = idr("x1_s", [NT, P, C], F32)
        self.ln2T_s = idr("ln2T_s", [NT, P, NKC, P], BF16)
        self.ylT_s = idr("ylT_s", [P, 8, T], BF16)

        self.psA = nc.alloc_psum_tensor("psA", [P, 2048], F32)
        self.psB = nc.alloc_psum_tensor("psB", [P, 2048], F32)

        S = self.SB
        self.tr = [S("tr0", [P, NKC, P], BF16, tdest=True),
                   S("tr1", [P, NKC, P], BF16, tdest=True)]
        self.pT = S("pT", [P, NT, P], BF16, tdest=True)
        self.ylT = S("ylT", [P, 8, P], BF16, tdest=True)
        self.cb = S("cb", [P, P]); self.ones = S("ones", [P, P])
        self.cct = S("cct", [P, D]); self.sst = S("sst", [P, D])
        self.gbias = S("gbias", [P, NE])
        self.gw_sb = S("gw_sb", [P, NKC, NE], BF16)
        self.samc = S("samc", [P, NT]); self.samc_y = S("samc_y", [P, NT])
        self.cw = S("cw", [P, NT, NE])
        self.bst = S("bst", [P, 4, 6]); self.mv = S("mv", [P, 2])
        self.cs16 = S("cs16", [P, NKC]); self.cstot = S("cstot", [P, 1])
        for n in ("mclip_q", "mclip_k", "mclip_v", "mclip_o", "thr", "nthr",
                  "lnv", "rstd", "rsq", "bq", "rr", "sfold", "am", "amc",
                  "rmc", "qsc", "qsc2", "sc1", "s2t", "vv", "gv",
                  "nm", "den", "rden", "ss2v",
                  "g1v", "g2v", "m1g", "m2g", "th8", "s8", "rs8", "mx1"):
            setattr(self, n, S(n, [P, 1]))
        self.c_eps5 = S("c_eps5", [P, 1]); self.c_eps6 = S("c_eps6", [P, 1])
        self.c_2565 = S("c_2565", [P, 1])
        self.c_m256 = S("c_m256", [P, 1]); self.c_n1e30 = S("c_n1e30", [P, 1])
        self.c_one = S("c_one", [P, 1]); self.gdisc = S("gdisc", [P, 1])
        self.mqs = S("mqs", [P, 1]); self.rs25 = S("rs25", [P, 1])
        self.et = S("et", [P, NE]); self.d1 = S("d1", [P, NE])
        self.scrs = S("scrs", [P, NE]); self.sbv = S("sbv", [P, NE])
        self.msb = S("msb", [P, NE]); self.cur = S("cur", [P, NE])
        self.mk32 = S("mk32", [P, NE]); self.wsl = S("wsl", [P, NE])
        self.selm = S("selm", [P, NE])
        self.gsv = S("gsv", [P, 4]); self.mask4 = S("mask4", [P, 4])
        self.t4b = S("t4b", [P, 4]); self.madd4 = S("madd4", [P, 4])
        self.msk8 = S("msk8", [P, 8]); self.t8b = S("t8b", [P, 8])

        self.arena16 = nc.alloc_sbuf_tensor("arena16", [P, 108 * 512], BF16)
        self.arena32 = nc.alloc_sbuf_tensor("arena32", [P, 72 * 256], F32)
        A = self.AR
        self.wtern = A(0, [P, NKC, C], BF16)
        self.xqb = A(64, [P, C], BF16)
        self.vbb = A(68, [P, C], BF16)
        self.ld = [A(72, [P, NKC, P], BF16), A(76, [P, NKC, P], BF16)]
        self.pb = A(80, [P, 2048], BF16)
        self.ks = [A(84, [P, 2048], BF16), A(88, [P, 2048], BF16)]
        self.vs = A(92, [P, NT, 512], BF16)
        self.chunk = [A(0, [P, C], F32), A(8, [P, C], F32)]
        self.xt = [A(16, [P, C], F32), A(24, [P, C], F32)]
        self.xn = A(32, [P, C], F32)
        self.qf = A(40, [P, C], F32)
        self.uu = A(48, [P, C], F32)
        self.trt = A(56, [P, C], F32)
        self.y_scr = [A(64, [P, C], F32), A(64, [P, C], F32)]
        self.fc1T = A(76, [P, NKC, LAT], BF16)
        self.x_latT = A(0, [P, 8, T], BF16)
        self.slab = [A(32, [P, NKC, 512], BF16), A(48, [P, NKC, 512], BF16)]
        self.w1b = [A(32, [P, 8, INTER], BF16), A(48, [P, 8, INTER], BF16)]
        self.w2b = [A(64, [P, 8, LAT], BF16), A(80, [P, 8, LAT], BF16)]
        self.h_all = A(96, [P, 8, 512], BF16)
        self.ylb = A(104, [P, LAT], BF16)
        self.dnb = A(0, [P, 8, C], BF16)
        self.slab3 = A(32, [P, NKC, 512], BF16)
        self.sTs = A(48, [P, 8, 512], BF16)
        self.upTb = A(56, [P, NKC, 1024], BF16)
        self.yl3 = A(88, [P, 8, 512], BF16)
        self.x1ld = [A(0, [P, C], F32), A(8, [P, C], F32)]
        nhalf_t = max(NT // 2, 4)
        self.y_lat = A(16, [P, nhalf_t, LAT], F32)
        self.rtmp2 = [A(48, [P, 512], F32), A(50, [P, 512], F32)]
        self.rout = A(0, [P, 4, C], F32)
        self.x1l3 = A(32, [P, C], F32)
        self.outb = A(40, [P, C], F32)
        self.rtmp3 = [A(48, [P, 512], F32), A(50, [P, 512], F32)]

    def act(self, out, in_, func, bias=0.0, scale=1.0, accum=None,
            reads=(), writes=()):
        kw = dict(bias=bias, scale=scale)
        if accum is not None:
            kw["accum_out"] = accum
        self.A_("activation", out, in_, func, reads=reads, writes=writes, **kw)

    def bn(self, src_buf, src_ap2048, extra_reads=()):
        for i in range(4):
            self.V_("bn_stats", self.bst.ap[:, i, :], src_ap2048[:, i * 512:(i + 1) * 512],
                    reads=[src_buf, *extra_reads], writes=[self.bst])
        self.V_("bn_aggr", self.mv.ap, self.bst.ap,
                reads=[self.bst], writes=[self.mv])

    def ln_rstd(self, var_ap, eps_buf, out_buf, reads):
        self.act(self.lnv.ap, var_ap, ACTF.Ln, bias=eps_buf.ap, scale=1.0,
                 reads=list(reads) + [eps_buf], writes=[self.lnv])
        self.act(out_buf.ap, self.lnv.ap, ACTF.Exp, bias=0.0, scale=-0.5,
                 reads=[self.lnv], writes=[out_buf])

    def const_loads(self):
        self.V_("memset", self.c_eps5.ap, 1e-5, writes=[self.c_eps5])
        self.V_("memset", self.c_eps6.ap, 1e-6, writes=[self.c_eps6])
        self.V_("memset", self.c_2565.ap, 256.5, writes=[self.c_2565])
        self.V_("memset", self.c_m256.ap, -256.0, writes=[self.c_m256])
        self.V_("memset", self.c_n1e30.ap, -1e30, writes=[self.c_n1e30])
        self.V_("memset", self.c_one.ap, 1.0, writes=[self.c_one])
        self.V_("tensor_single_scalar", self.gdisc.ap, self.c_one.ap, M64,
                ALU.add, reads=[self.c_one], writes=[self.gdisc])
        self.V_("tensor_single_scalar", self.gdisc.ap, self.gdisc.ap, M64,
                ALU.subtract, reads=[self.gdisc], writes=[self.gdisc])
        self.dma("cb", self.cb.ap, self.cbm, writes=[self.cb])
        self.dma("ones", self.ones.ap, self.onesd, writes=[self.ones])
        self.dma("gbias", self.gbias.ap, self.gbiasd, writes=[self.gbias])
        self.dma("gw", self.gw_sb.ap,
                 self.gwT.rearrange("(a p) n -> p a n", p=P), writes=[self.gw_sb])
        from concourse.hw_specs import get_activation_tables
        idx = None
        for i, funcs in enumerate(get_activation_tables(self.nc.m.arch).values()):
            if ACTF.Exp in funcs and ACTF.Ln in funcs:
                idx = i
                break
        if idx is not None:
            nc = self.nc
            def fn(e, idx=idx, nc=nc):
                return e.add_instruction(mybir.InstLoadActFuncSet(
                    name=nc.get_next_instruction_name(),
                    act_func_set_id=idx, ins=[], outs=[]))
            self.p.op("A", fn, (), ())

    def tern(self, wdram, mclip, cmul):
        NCH = NKC
        for kc in range(NCH):
            ch = self.chunk[kc % 2]
            self.dma("ch%d" % (kc % 2), ch.ap, wdram[kc * P:(kc + 1) * P, :],
                     reads=[ch], writes=[ch])
            self.V_("reduce_sum", self.cs16.ap[:, kc:kc + 1], ch.ap, AX.X,
                    apply_absolute_value=True, reads=[ch], writes=[self.cs16])
        self.V_("reduce_sum", self.cstot.ap, self.cs16.ap, AX.X,
                reads=[self.cs16], writes=[self.cstot])
        ps1 = self.PSR("A", 0, 512)
        self.T_("matmul", self.psA[:, 0:1], self.ones.ap, self.cstot.ap,
                start=True, stop=True, reads=[self.ones, self.cstot], writes=[ps1])
        self.V_("tensor_scalar", mclip.ap, self.psA[:, 0:1], float(INV_N), 1e-5,
                ALU.mult, ALU.max, reads=[ps1], writes=[mclip])
        self.V_("tensor_scalar", self.thr.ap, mclip.ap, 0.5, None, ALU.mult,
                reads=[mclip], writes=[self.thr])
        self.V_("tensor_scalar", self.nthr.ap, mclip.ap, -0.5, None, ALU.mult,
                reads=[mclip], writes=[self.nthr])
        self.V_("tensor_scalar", self.mqs.ap, mclip.ap, float(cmul), None,
                ALU.mult, reads=[mclip], writes=[self.mqs])
        for kc in range(NCH):
            ch = self.chunk[kc % 2]
            self.dma("ch%d" % (kc % 2), ch.ap, wdram[kc * P:(kc + 1) * P, :],
                     reads=[ch], writes=[ch])
            self.V_("tensor_scalar", self.qf.ap, ch.ap, self.nthr.ap, None,
                    ALU.is_lt, reads=[ch, self.nthr], writes=[self.qf])
            self.V_("scalar_tensor_tensor", self.wtern.ap[:, kc, :], ch.ap,
                    self.thr.ap, self.qf.ap, ALU.is_gt, ALU.subtract,
                    reads=[ch, self.thr, self.qf], writes=[self.wtern])

    def a0(self):
        for t in range(self.NT):
            xt = self.xt[t % 2]
            self.dma("xt%d" % (t % 2), xt.ap, self.x[t * P:(t + 1) * P, :],
                     reads=[xt], writes=[xt])
            self.bn(xt, xt.ap)
            self.ln_rstd(self.mv.ap[:, 1:2], self.c_eps5, self.rstd, [self.mv])
            self.V_("tensor_tensor", self.rsq.ap, self.rstd.ap, self.rstd.ap,
                    ALU.mult, reads=[self.rstd], writes=[self.rsq])
            self.V_("tensor_tensor", self.bq.ap, self.mv.ap[:, 1:2], self.rsq.ap,
                    ALU.mult, reads=[self.mv, self.rsq], writes=[self.bq])
            self.act(self.lnv.ap, self.bq.ap, ACTF.Ln, bias=self.c_eps6.ap,
                     reads=[self.bq, self.c_eps6], writes=[self.lnv])
            self.act(self.rr.ap, self.lnv.ap, ACTF.Exp, scale=-0.5,
                     reads=[self.lnv], writes=[self.rr])
            self.V_("tensor_tensor", self.sfold.ap, self.rstd.ap, self.rr.ap,
                    ALU.mult, reads=[self.rstd, self.rr], writes=[self.sfold])
            self.V_("tensor_scalar", self.xn.ap, xt.ap, self.mv.ap[:, 0:1],
                    self.sfold.ap, ALU.subtract, ALU.mult,
                    reads=[xt, self.mv, self.sfold], writes=[self.xn])
            self.quant_store(self.xn, self.samc, t, self.xqT_s, "xqT_s",
                             scale_pre=None)

    def quant_store(self, src, samc, t, dst_dram, dst_key, scale_pre):
        self.V_("reduce_max", self.am.ap, src.ap, AX.X, apply_absolute_value=True,
                reads=[src], writes=[self.am])
        if scale_pre is not None:
            self.V_("tensor_tensor", self.amc.ap, self.am.ap, scale_pre.ap,
                    ALU.mult, reads=[self.am, scale_pre], writes=[self.amc])
            amsrc = self.amc
        else:
            amsrc = self.am
        self.V_("tensor_single_scalar", samc.ap[:, t:t + 1], amsrc.ap, 1e-5,
                ALU.max, reads=[amsrc], writes=[samc])
        self.V_("reciprocal", self.rmc.ap, samc.ap[:, t:t + 1],
                reads=[samc], writes=[self.rmc])
        self.V_("tensor_scalar", self.qsc.ap, self.rmc.ap, 127.0, None, ALU.mult,
                reads=[self.rmc], writes=[self.qsc])
        if scale_pre is not None:
            self.V_("tensor_tensor", self.qsc2.ap, self.qsc.ap, scale_pre.ap,
                    ALU.mult, reads=[self.qsc, scale_pre], writes=[self.qsc2])
            qs = self.qsc2
        else:
            qs = self.qsc
        self.V_("tensor_scalar", self.qf.ap, src.ap, qs.ap, None, ALU.mult,
                reads=[src, qs], writes=[self.qf])
        self.V_("tensor_single_scalar", self.uu.ap, self.qf.ap, M32, ALU.add,
                reads=[self.qf], writes=[self.uu])
        self.V_("tensor_single_scalar", self.uu.ap, self.uu.ap, M32,
                ALU.subtract, reads=[self.uu], writes=[self.uu])
        self.V_("tensor_single_scalar", self.trt.ap, self.qf.ap, M64, ALU.add,
                reads=[self.qf], writes=[self.trt])
        self.V_("tensor_single_scalar", self.trt.ap, self.trt.ap, M64,
                ALU.subtract, reads=[self.trt], writes=[self.trt])
        self.V_("tensor_tensor", self.trt.ap, self.trt.ap, self.uu.ap,
                ALU.subtract, reads=[self.trt, self.uu], writes=[self.trt])
        self.V_("scalar_tensor_tensor", self.xqb.ap, self.trt.ap,
                self.gdisc.ap, self.uu.ap, ALU.mult, ALU.add,
                reads=[self.trt, self.uu, self.gdisc], writes=[self.xqb])
        tr = self.tr[t % 2]
        self.transpose("trp%d" % (t % 2), tr.ap, self.xqb.ap,
                       reads=[self.xqb, tr], writes=[tr])
        self.dma("s" + dst_key, dst_dram[t], tr.ap, reads=[tr],
                 writes=[(dst_key, t)])

    def a1(self, kind):
        NT = self.NT
        for t in range(NT):
            ld = self.ld[t % 2]
            self.dma("ld%d" % (t % 2), ld.ap, self.xqT_s[t],
                     reads=[ld, ("xqT_s", t)], writes=[ld])
            psn = "A" if t % 2 == 0 else "B"
            pst = self.psA if psn == "A" else self.psB
            psb = self.PSR(psn, 0, 2048)
            for kc in range(NKC):
                for n in range(4):
                    self.T_("matmul", pst[:, n * 512:(n + 1) * 512],
                            ld.ap[:, kc, :], self.wtern.ap[:, kc, n * 512:(n + 1) * 512],
                            start=(kc == 0), stop=(kc == NKC - 1),
                            reads=[ld, self.wtern], writes=[psb])
            self.V_("tensor_scalar", self.sc1.ap, self.samc.ap[:, t:t + 1],
                    self.mqs.ap, None, ALU.mult,
                    reads=[self.samc, self.mqs], writes=[self.sc1])
            if kind == "v":
                self.V_("tensor_scalar", self.vbb.ap, pst[:, :], self.sc1.ap,
                        None, ALU.mult, reads=[psb, self.sc1], writes=[self.vbb])
                for hg in range(4):
                    self.dma("sv%d" % hg, self.v_s[hg, t * P:(t + 1) * P, :],
                             self.vbb.ap[:, hg * 512:(hg + 1) * 512],
                             reads=[self.vbb], writes=[("v_s", hg, t)])
                continue
            self.V_("tensor_scalar", self.qf.ap, pst[:, :], self.sc1.ap, None,
                    ALU.mult, reads=[psb, self.sc1], writes=[self.qf])
            self.dma("cc", self.cct.ap, self.cc[t * P:(t + 1) * P, :],
                     reads=[self.cct], writes=[self.cct])
            self.dma("ss", self.sst.ap, self.ss[t * P:(t + 1) * P, :],
                     reads=[self.sst], writes=[self.sst])
            qf3 = self.qf.ap.rearrange("p (h d) -> p h d", d=D)
            tr3 = self.trt.ap.rearrange("p (h d) -> p h d", d=D)
            ccb = self.cct.ap.unsqueeze(1).broadcast_to([P, H, D])
            uu3 = self.uu.ap.rearrange("p (h d) -> p h d", d=D)
            self.V_("tensor_tensor", uu3, qf3, ccb, ALU.mult,
                    reads=[self.qf, self.cct], writes=[self.uu])
            sslo = self.sst.ap[:, 0:64].unsqueeze(1).broadcast_to([P, H, 64])
            sshi = self.sst.ap[:, 64:128].unsqueeze(1).broadcast_to([P, H, 64])
            self.V_("tensor_tensor", tr3[:, :, 0:64], qf3[:, :, 64:128], sslo,
                    ALU.mult, reads=[self.qf, self.sst], writes=[self.trt])
            self.V_("tensor_tensor", tr3[:, :, 64:128], qf3[:, :, 0:64], sshi,
                    ALU.mult, reads=[self.qf, self.sst], writes=[self.trt])
            self.V_("tensor_tensor", self.xqb.ap, self.uu.ap, self.trt.ap,
                    ALU.add, reads=[self.uu, self.trt], writes=[self.xqb])
            tr = self.tr[t % 2]
            self.transpose("trp%d" % (t % 2), tr.ap, self.xqb.ap,
                           reads=[self.xqb, tr], writes=[tr])
            if kind == "q":
                self.dma("sqrT", self.qrT_s[t], tr.ap, reads=[tr],
                         writes=[("qrT_s", t)])
            else:
                for h in range(H):
                    self.dma("skr%d" % h, self.krT_s[h, :, t * P:(t + 1) * P],
                             tr.ap[:, h, :], reads=[tr], writes=[("krT_s", h)])

    def a2(self):
        NT = self.NT
        for qt in range(NT):
            L = (qt + 1) * P
            ld = self.ld[qt % 2]
            self.dma("ld%d" % (qt % 2), ld.ap, self.qrT_s[qt],
                     reads=[ld, ("qrT_s", qt)], writes=[ld])
            ysc = self.y_scr[qt % 2]
            for hg in range(4):
                for ktq in range(0, qt + 1, 4):
                    nkt = min(4, qt + 1 - ktq)
                    src = self.v_s[hg].rearrange("(k p) f -> p k f", p=P)
                    self.dma("vsl", self.vs.ap[:, ktq:ktq + nkt, :],
                             src[:, ktq:ktq + nkt, :],
                             reads=[("v_s", hg, tt) for tt in range(ktq, ktq + nkt)],
                             writes=[self.vs])
                for h4 in range(4):
                    h = hg * 4 + h4
                    Rn = "A" if h % 2 == 0 else "B"
                    Rt = self.psA if Rn == "A" else self.psB
                    Rb = self.PSR(Rn, 0, L)
                    ks = self.ks[h % 2]
                    self.dma("ks%d" % (h % 2), ks.ap[:, 0:L], self.krT_s[h, :, 0:L],
                             reads=[ks, ("krT_s", h)], writes=[ks])
                    nch = (L + 511) // 512
                    for cc_ in range(nch):
                        n = min(512, L - cc_ * 512)
                        self.T_("matmul", Rt[:, cc_ * 512:cc_ * 512 + n],
                                ld.ap[:, h, :], ks.ap[:, cc_ * 512:cc_ * 512 + n],
                                start=True, stop=True,
                                reads=[ld, ks], writes=[self.PSR(Rn, cc_ * 512, cc_ * 512 + n)])
                    self.V_("tensor_tensor", Rt[:, qt * P:L], Rt[:, qt * P:L],
                            self.cb.ap, ALU.add,
                            reads=[self.PSR(Rn, qt * P, L), self.cb],
                            writes=[self.PSR(Rn, qt * P, L)])
                    self.V_("reduce_max", self.nm.ap, Rt[:, 0:L], AX.X,
                            negate=True, reads=[Rb], writes=[self.nm])
                    self.act(self.pb.ap[:, 0:L], Rt[:, 0:L], ACTF.Exp,
                             bias=self.nm.ap, scale=1.0, accum=self.den.ap,
                             reads=[Rb, self.nm], writes=[self.pb, self.den])
                    self.V_("reciprocal", self.rden.ap, self.den.ap,
                            reads=[self.den], writes=[self.rden])
                    self.transpose("trpP", self.pT.ap[:, 0:qt + 1, :],
                                   self.pb.ap[:, 0:L],
                                   reads=[self.pb, self.pT], writes=[self.pT])
                    Yn = "B" if h % 2 == 0 else "A"
                    Yt = self.psB if Yn == "B" else self.psA
                    Yb = self.PSR(Yn, 1920, 2048)
                    for kt in range(qt + 1):
                        self.T_("matmul", Yt[:, 1920:2048], self.pT.ap[:, kt, :],
                                self.vs.ap[:, kt, h4 * P:(h4 + 1) * P],
                                start=(kt == 0), stop=(kt == qt),
                                reads=[self.pT, self.vs], writes=[Yb])
                    self.V_("tensor_scalar", ysc.ap[:, h * P:(h + 1) * P],
                            Yt[:, 1920:2048], self.rden.ap, None, ALU.mult,
                            reads=[Yb, self.rden], writes=[ysc])
            self.V_("tensor_tensor", self.xn.ap, ysc.ap, ysc.ap, ALU.mult,
                    reads=[ysc], writes=[self.xn])
            self.V_("reduce_sum", self.ss2v.ap, self.xn.ap, AX.X,
                    reads=[self.xn], writes=[self.ss2v])
            self.act(self.lnv.ap, self.ss2v.ap, ACTF.Ln, bias=self.c_eps6.ap,
                     scale=float(1.0 / C), reads=[self.ss2v, self.c_eps6],
                     writes=[self.lnv])
            self.act(self.rr.ap, self.lnv.ap, ACTF.Exp, scale=-0.5,
                     reads=[self.lnv], writes=[self.rr])
            self.quant_store(ysc, self.samc_y, qt, self.yqT_s, "yqT_s",
                             scale_pre=self.rr)

    def a3(self):
        NT = self.NT
        for t in range(NT):
            ld = self.ld[t % 2]
            self.dma("ld%d" % (t % 2), ld.ap, self.yqT_s[t],
                     reads=[ld, ("yqT_s", t)], writes=[ld])
            xt = self.xt[t % 2]
            self.dma("xt%d" % (t % 2), xt.ap, self.x[t * P:(t + 1) * P, :],
                     reads=[xt], writes=[xt])
            psn = "A" if t % 2 == 0 else "B"
            pst = self.psA if psn == "A" else self.psB
            psb = self.PSR(psn, 0, 2048)
            for kc in range(NKC):
                for n in range(4):
                    self.T_("matmul", pst[:, n * 512:(n + 1) * 512],
                            ld.ap[:, kc, :], self.wtern.ap[:, kc, n * 512:(n + 1) * 512],
                            start=(kc == 0), stop=(kc == NKC - 1),
                            reads=[ld, self.wtern], writes=[psb])
            self.V_("tensor_scalar", self.sc1.ap, self.samc_y.ap[:, t:t + 1],
                    self.mqs.ap, None, ALU.mult,
                    reads=[self.samc_y, self.mqs], writes=[self.sc1])
            self.bn(psb, pst)
            self.V_("tensor_tensor", self.s2t.ap, self.sc1.ap, self.sc1.ap,
                    ALU.mult, reads=[self.sc1], writes=[self.s2t])
            self.V_("tensor_tensor", self.vv.ap, self.mv.ap[:, 1:2], self.s2t.ap,
                    ALU.mult, reads=[self.mv, self.s2t], writes=[self.vv])
            self.ln_rstd(self.vv.ap, self.c_eps5, self.rstd, [self.vv])
            self.V_("tensor_tensor", self.gv.ap, self.sc1.ap, self.rstd.ap,
                    ALU.mult, reads=[self.sc1, self.rstd], writes=[self.gv])
            self.V_("tensor_scalar", self.xn.ap, pst[:, :], self.mv.ap[:, 0:1],
                    self.gv.ap, ALU.subtract, ALU.mult,
                    reads=[psb, self.mv, self.gv], writes=[self.xn])
            self.V_("tensor_tensor", self.qf.ap, self.xn.ap, xt.ap, ALU.add,
                    reads=[self.xn, xt], writes=[self.qf])
            self.dma("sx1", self.x1_s[t], self.qf.ap, reads=[self.qf],
                     writes=[("x1_s", t)])

    def b0(self):
        NT = self.NT
        for kc in range(NKC):
            self.dma("fc1", self.fc1T.ap[:, kc, :],
                     self.fc1Td[kc * P:(kc + 1) * P, :],
                     reads=[self.fc1T], writes=[self.fc1T])
        for t in range(NT):
            x1 = self.x1ld[t % 2]
            self.dma("x1ld%d" % (t % 2), x1.ap, self.x1_s[t],
                     reads=[x1, ("x1_s", t)], writes=[x1])
            self.bn(x1, x1.ap)
            self.ln_rstd(self.mv.ap[:, 1:2], self.c_eps5, self.rstd, [self.mv])
            self.V_("tensor_scalar", self.xqb.ap, x1.ap, self.mv.ap[:, 0:1],
                    self.rstd.ap, ALU.subtract, ALU.mult,
                    reads=[x1, self.mv, self.rstd], writes=[self.xqb])
            tr = self.tr[t % 2]
            self.transpose("trp%d" % (t % 2), tr.ap, self.xqb.ap,
                           reads=[self.xqb, tr], writes=[tr])
            self.dma("sln2T", self.ln2T_s[t], tr.ap, reads=[tr],
                     writes=[("ln2T_s", t)])
            psg = self.PSR("B", 0, 512)
            for kc in range(NKC):
                self.T_("matmul", self.psB[:, 0:NE], tr.ap[:, kc, :],
                        self.gw_sb.ap[:, kc, :],
                        start=(kc == 0), stop=(kc == NKC - 1),
                        reads=[tr, self.gw_sb], writes=[psg])
            self.act(self.et.ap, self.psB[:, 0:NE], ACTF.Exp, scale=-1.0,
                     reads=[psg], writes=[self.et])
            self.V_("tensor_single_scalar", self.d1.ap, self.et.ap, 1.0, ALU.add,
                    reads=[self.et], writes=[self.d1])
            self.V_("reciprocal", self.scrs.ap, self.d1.ap,
                    reads=[self.d1], writes=[self.scrs])
            self.V_("tensor_tensor", self.sbv.ap, self.scrs.ap, self.gbias.ap,
                    ALU.add, reads=[self.scrs, self.gbias], writes=[self.sbv])
            for g in range(4):
                sg = self.sbv.ap[:, g * 8:(g + 1) * 8]
                self.V_("reduce_max", self.m1g.ap, sg, AX.X,
                        reads=[self.sbv], writes=[self.m1g])
                self.V_("tensor_scalar", self.msk8.ap, sg, self.m1g.ap, None,
                        ALU.is_ge, reads=[self.sbv, self.m1g], writes=[self.msk8])
                self.V_("scalar_tensor_tensor", self.t8b.ap, self.msk8.ap,
                        self.c_n1e30.ap, sg, ALU.mult, ALU.add,
                        reads=[self.msk8, self.sbv, self.c_n1e30],
                        writes=[self.t8b])
                self.V_("reduce_max", self.m2g.ap, self.t8b.ap, AX.X,
                        reads=[self.t8b], writes=[self.m2g])
                self.V_("tensor_tensor", self.gsv.ap[:, g:g + 1], self.m1g.ap,
                        self.m2g.ap, ALU.add,
                        reads=[self.m1g, self.m2g], writes=[self.gsv])
            self.V_("reduce_max", self.g1v.ap, self.gsv.ap, AX.X,
                    reads=[self.gsv], writes=[self.g1v])
            self.V_("tensor_scalar", self.mask4.ap, self.gsv.ap, self.g1v.ap,
                    None, ALU.is_ge, reads=[self.gsv, self.g1v], writes=[self.mask4])
            self.V_("scalar_tensor_tensor", self.t4b.ap, self.mask4.ap,
                    self.c_n1e30.ap, self.gsv.ap, ALU.mult, ALU.add,
                    reads=[self.mask4, self.gsv, self.c_n1e30],
                    writes=[self.t4b])
            self.V_("reduce_max", self.g2v.ap, self.t4b.ap, AX.X,
                    reads=[self.t4b], writes=[self.g2v])
            self.V_("tensor_scalar", self.mask4.ap, self.gsv.ap, self.g2v.ap,
                    None, ALU.is_ge, reads=[self.gsv, self.g2v], writes=[self.mask4])
            self.V_("tensor_scalar", self.madd4.ap, self.mask4.ap, -1.0, 1e30,
                    ALU.add, ALU.mult, reads=[self.mask4], writes=[self.madd4])
            sb3 = self.sbv.ap.rearrange("p (g e) -> p g e", e=8)
            msb3 = self.msb.ap.rearrange("p (g e) -> p g e", e=8)
            self.V_("tensor_tensor", msb3, sb3,
                    self.madd4.ap.unsqueeze(2).broadcast_to([P, 4, 8]), ALU.add,
                    reads=[self.sbv, self.madd4], writes=[self.msb])
            self.V_("tensor_copy", self.cur.ap, self.msb.ap,
                    reads=[self.msb], writes=[self.cur])
            for _ in range(7):
                self.V_("reduce_max", self.mx1.ap, self.cur.ap, AX.X,
                        reads=[self.cur], writes=[self.mx1])
                self.V_("tensor_scalar", self.mk32.ap, self.cur.ap, self.mx1.ap,
                        None, ALU.is_ge, reads=[self.cur, self.mx1],
                        writes=[self.mk32])
                self.V_("scalar_tensor_tensor", self.cur.ap, self.mk32.ap,
                        self.c_n1e30.ap, self.cur.ap, ALU.mult, ALU.add,
                        reads=[self.mk32, self.cur, self.c_n1e30],
                        writes=[self.cur])
            self.V_("reduce_max", self.th8.ap, self.cur.ap, AX.X,
                    reads=[self.cur], writes=[self.th8])
            self.V_("tensor_scalar", self.selm.ap, self.msb.ap, self.th8.ap,
                    None, ALU.is_ge, reads=[self.msb, self.th8], writes=[self.selm])
            self.V_("tensor_tensor", self.wsl.ap, self.selm.ap, self.scrs.ap,
                    ALU.mult, reads=[self.selm, self.scrs], writes=[self.wsl])
            self.V_("reduce_sum", self.s8.ap, self.wsl.ap, AX.X,
                    reads=[self.wsl], writes=[self.s8])
            self.V_("reciprocal", self.rs8.ap, self.s8.ap,
                    reads=[self.s8], writes=[self.rs8])
            self.V_("tensor_scalar", self.rs25.ap, self.rs8.ap, 2.5, None,
                    ALU.mult, reads=[self.rs8], writes=[self.rs25])
            self.V_("tensor_scalar", self.cw.ap[:, t, :], self.wsl.ap,
                    self.rs25.ap, None, ALU.mult,
                    reads=[self.wsl, self.rs25], writes=[self.cw])

    def load_slab(self, lane, slab, blk):
        for j in range(4):
            t = blk * 4 + j
            self.dma(lane, slab.ap[:, :, j * P:(j + 1) * P], self.ln2T_s[t],
                     reads=[("ln2T_s", t)], writes=[slab])

    def b1(self):
        NSLAB = self.T // 512
        for blk in range(NSLAB):
            slab = self.slab[blk % 2]
            self.load_slab("slab%d" % (blk % 2), slab, blk)
            for lc in range(8):
                bank = lc % 4
                psb = self.PSR("A", bank * 512, (bank + 1) * 512)
                for kc in range(NKC):
                    self.T_("matmul", self.psA[:, bank * 512:(bank + 1) * 512],
                            self.fc1T.ap[:, kc, lc * P:(lc + 1) * P],
                            slab.ap[:, kc, :],
                            start=(kc == 0), stop=(kc == NKC - 1),
                            reads=[self.fc1T, slab], writes=[psb])
                self.V_("tensor_copy", self.x_latT.ap[:, lc, blk * 512:(blk + 1) * 512],
                        self.psA[:, bank * 512:(bank + 1) * 512],
                        reads=[psb], writes=[self.x_latT])

    def b2(self):
        NT = self.NT
        nhalves = 2 if NT >= 8 else 1
        tiles_ph = NT // nhalves
        slabs_ph = tiles_ph // 4
        for half in range(nhalves):
            for e in range(NE):
                w1 = self.w1b[e % 2]
                w2 = self.w2b[e % 2]
                for lc in range(8):
                    self.dma("w1b%d" % (e % 2), w1.ap[:, lc, :],
                             self.w1Td[e, lc * P:(lc + 1) * P, :],
                             reads=[w1], writes=[w1])
                for ic in range(8):
                    self.dma("w2b%d" % (e % 2), w2.ap[:, ic, :],
                             self.w2Td[e, ic * P:(ic + 1) * P, :],
                             reads=[w2], writes=[w2])
                for blk in range(slabs_ph):
                    toff = half * tiles_ph * P + blk * 512
                    for ic in range(8):
                        bank = ic % 4
                        psb = self.PSR("A", bank * 512, (bank + 1) * 512)
                        for lc in range(8):
                            self.T_("matmul",
                                    self.psA[:, bank * 512:(bank + 1) * 512],
                                    w1.ap[:, lc, ic * P:(ic + 1) * P],
                                    self.x_latT.ap[:, lc, toff:toff + 512],
                                    start=(lc == 0), stop=(lc == 7),
                                    reads=[w1, self.x_latT], writes=[psb])
                        rt = self.rtmp2[ic % 2]
                        self.V_("tensor_scalar", rt.ap,
                                self.psA[:, bank * 512:(bank + 1) * 512],
                                0.0, None, ALU.max, reads=[psb], writes=[rt])
                        self.A_("activation", self.h_all.ap[:, ic, :], rt.ap,
                                ACTF.Square, reads=[rt], writes=[self.h_all])
                    for lh in range(2):
                        for tt in range(4):
                            psb = self.PSR("B", tt * 512, (tt + 1) * 512)
                            for ic in range(8):
                                self.T_("matmul",
                                        self.psB[:, tt * 512:(tt + 1) * 512],
                                        self.h_all.ap[:, ic, tt * P:(tt + 1) * P],
                                        w2.ap[:, ic, lh * 512:(lh + 1) * 512],
                                        start=(ic == 0), stop=(ic == 7),
                                        reads=[self.h_all, w2], writes=[psb])
                            th = blk * 4 + tt
                            tg = half * tiles_ph + th
                            cwap = self.cw.ap[:, tg, e:e + 1]
                            ydst = self.y_lat.ap[:, th, lh * 512:(lh + 1) * 512]
                            if e == 0:
                                self.V_("tensor_scalar", ydst,
                                        self.psB[:, tt * 512:(tt + 1) * 512],
                                        cwap, None, ALU.mult,
                                        reads=[psb, self.cw], writes=[self.y_lat])
                            else:
                                self.V_("scalar_tensor_tensor", ydst,
                                        self.psB[:, tt * 512:(tt + 1) * 512],
                                        cwap, ydst, ALU.mult, ALU.add,
                                        reads=[psb, self.cw, self.y_lat],
                                        writes=[self.y_lat])
            for th in range(tiles_ph):
                tg = half * tiles_ph + th
                self.V_("tensor_copy", self.ylb.ap, self.y_lat.ap[:, th, :],
                        reads=[self.y_lat], writes=[self.ylb])
                self.transpose("trpY", self.ylT.ap, self.ylb.ap,
                               reads=[self.ylb, self.ylT], writes=[self.ylT])
                self.dma("sylT", self.ylT_s[:, :, tg * P:(tg + 1) * P], self.ylT.ap,
                         reads=[self.ylT], writes=[("ylT_s", tg)])

    def b3(self):
        NSLAB = self.T // 512
        for blk in range(NSLAB):
            slab = self.slab3
            self.load_slab("slab3", slab, blk)
            self.dma("yl3", self.yl3.ap,
                     self.ylT_s[:, :, blk * 512:(blk + 1) * 512],
                     reads=[self.yl3] + [("ylT_s", blk * 4 + j) for j in range(4)],
                     writes=[self.yl3])
            for sig in range(5):
                if sig < 4:
                    for kc in range(NKC):
                        self.dma("upw", self.upTb.ap[:, kc, :],
                                 self.upTd[kc * P:(kc + 1) * P,
                                           sig * 1024:(sig + 1) * 1024],
                                 reads=[self.upTb], writes=[self.upTb])
                    for sc in range(8):
                        self.dma("dnw", self.dnb.ap[:, sc, :],
                                 self.dfT[sig * 1024 + sc * P:
                                          sig * 1024 + (sc + 1) * P, :],
                                 reads=[self.dnb], writes=[self.dnb])
                    for si in range(8):
                        bank = si % 4
                        psb = self.PSR("A", bank * 512, (bank + 1) * 512)
                        for kc in range(NKC):
                            self.T_("matmul",
                                    self.psA[:, bank * 512:(bank + 1) * 512],
                                    self.upTb.ap[:, kc, si * P:(si + 1) * P],
                                    slab.ap[:, kc, :],
                                    start=(kc == 0), stop=(kc == NKC - 1),
                                    reads=[self.upTb, slab], writes=[psb])
                        rt = self.rtmp3[si % 2]
                        self.V_("tensor_scalar", rt.ap,
                                self.psA[:, bank * 512:(bank + 1) * 512],
                                0.0, None, ALU.max, reads=[psb], writes=[rt])
                        self.A_("activation", self.sTs.ap[:, si, :], rt.ap,
                                ACTF.Square, reads=[rt], writes=[self.sTs])
                else:
                    for sc in range(8):
                        self.dma("dnw", self.dnb.ap[:, sc, :],
                                 self.dfT[SI + sc * P:SI + (sc + 1) * P, :],
                                 reads=[self.dnb], writes=[self.dnb])
                for t in range(4):
                    tg = blk * 4 + t
                    psb = self.PSR("B", 0, 2048)
                    for sc in range(8):
                        if sig < 4:
                            lhs = self.sTs.ap[:, sc, t * P:(t + 1) * P]
                            lrd = self.sTs
                        else:
                            lhs = self.yl3.ap[:, sc, t * P:(t + 1) * P]
                            lrd = self.yl3
                        for co in range(4):
                            self.T_("matmul", self.psB[:, co * 512:(co + 1) * 512],
                                    lhs, self.dnb.ap[:, sc, co * 512:(co + 1) * 512],
                                    start=(sc == 0), stop=(sc == 7),
                                    reads=[lrd, self.dnb],
                                    writes=[self.PSR("B", co * 512, (co + 1) * 512)])
                    rdst = self.rout.ap[:, t, :]
                    if sig == 0:
                        self.V_("tensor_copy", rdst, self.psB[:, :],
                                reads=[psb], writes=[self.rout])
                    else:
                        self.V_("tensor_tensor", rdst, rdst, self.psB[:, :],
                                ALU.add, reads=[psb, self.rout], writes=[self.rout])
            for t in range(4):
                tg = blk * 4 + t
                rsrc = self.rout.ap[:, t, :]
                self.bn(self.rout, rsrc)
                self.ln_rstd(self.mv.ap[:, 1:2], self.c_eps5, self.rstd, [self.mv])
                self.dma("x1l3", self.x1l3.ap, self.x1_s[tg],
                         reads=[("x1_s", tg)], writes=[self.x1l3])
                self.V_("tensor_scalar", self.outb.ap, rsrc, self.mv.ap[:, 0:1],
                        self.rstd.ap, ALU.subtract, ALU.mult,
                        reads=[self.rout, self.mv, self.rstd], writes=[self.outb])
                self.V_("tensor_tensor", self.outb.ap, self.outb.ap, self.x1l3.ap,
                        ALU.add, reads=[self.outb, self.x1l3], writes=[self.outb])
                self.dma("sout", self.out[tg * P:(tg + 1) * P, :], self.outb.ap,
                         reads=[self.outb], writes=[("out", tg)])

    def build_all(self):
        self.declare()
        self.const_loads()
        self.a0()
        self.tern(self.wqT, self.mclip_q, CQ)
        self.a1("q")
        self.tern(self.wvT, self.mclip_v, CK)
        self.a1("v")
        self.tern(self.wkT, self.mclip_k, CK)
        self.a1("k")
        self.a2()
        self.tern(self.woT, self.mclip_o, CK)
        self.a3()
        self.b0()
        self.b1()
        self.b2()
        self.b3()
        self.p.finish()


LANES = {}
for _l in (["xt0", "xt1", "ld0", "ld1", "ks0", "ks1", "vsl", "x1ld0", "x1ld1",
            "slab0", "slab1", "slab3", "x1l3", "sxqT_s", "sqrT", "syqT_s",
            "sx1", "sln2T", "sout", "sylT", "yl3", "trp0", "trp1", "trpP", "trpY"]
           + ["skr%d" % h for h in range(16)] + ["sv%d" % g for g in range(4)]):
    LANES[_l] = "S"
for _l in ("cb", "ones", "gbias", "cc", "ss", "ch0", "ch1"):
    LANES[_l] = "A"
for _l in ("gw", "fc1", "w1b0", "w1b1", "w2b0", "w2b1", "upw", "dnw"):
    LANES[_l] = "G"


def build_nc(NT=16):
    nc = bass.Bass("TRN2", target_bir_lowering=False)
    from contextlib import ExitStack
    with ExitStack() as st:
        sems = {}
        for name in ("T", "V", "A", "G"):
            sems[name] = st.enter_context(nc.semaphore("sem" + name))
        for name in LANES:
            sems[name] = st.enter_context(nc.semaphore("q" + name))
        prog = Prog(sems, LANES)
        b = Builder(nc, prog, NT)
        b.build_all()
        block = st.enter_context(nc.Block())
        prog.realize(block)
    return nc


def _host_prep(L0_attn_Wq, L0_attn_Wk, L0_attn_Wv, L0_attn_Wo, L0_ffn_gate_w,
               L0_ffn_gate_bias, L0_ffn_fc1_w, L0_ffn_fc2_w, L0_ffn_w1,
               L0_ffn_w2, L0_ffn_shared_up_w, L0_ffn_shared_down_w, T=2048):
    cont = np.ascontiguousarray
    perm = np.concatenate([
        np.concatenate([np.arange(h * 128, (h + 1) * 128, 2),
                        np.arange(h * 128 + 1, (h + 1) * 128, 2)])
        for h in range(16)])
    shared = {
        "wqT": cont(L0_attn_Wq[perm].T),
        "wkT": cont(L0_attn_Wk[perm].T),
        "wvT": cont(L0_attn_Wv.T),
        "woT": cont(L0_attn_Wo.T),
        "gwT": cont(L0_ffn_gate_w.T),
        "gbiasd": cont(np.tile(L0_ffn_gate_bias[None, :].astype(np.float32),
                               (128, 1))),
        "fc1Td": cont(L0_ffn_fc1_w.T),
        "dfT": cont(np.concatenate([L0_ffn_shared_down_w.T,
                                    L0_ffn_fc2_w.T], axis=0)),
        "w1Td": cont(L0_ffn_w1.transpose(0, 2, 1)),
        "w2Td": cont(L0_ffn_w2.transpose(0, 2, 1)),
        "upTd": cont(L0_ffn_shared_up_w.T),
    }
    import jax
    import jax.numpy as jnp
    with jax.default_device(jax.devices("cpu")[0]):
        freqs = 1.0 / (10000.0 ** (jnp.arange(0, 128, 2, dtype=jnp.float32) / 128))
        ang = jnp.arange(T, dtype=jnp.float32)[:, None] * freqs[None, :]
        cos = np.asarray(jnp.cos(ang), dtype=np.float32)
        sin = np.asarray(jnp.sin(ang), dtype=np.float32)
    shared["cc"] = cont(np.concatenate([cos, cos], axis=1))
    shared["ss"] = cont(np.concatenate([-sin, sin], axis=1))
    ii = np.arange(128)
    shared["cbm"] = np.where(ii[:, None] >= ii[None, :], 0.0,
                             -1e30).astype(np.float32)
    shared["onesd"] = np.ones((128, 128), np.float32)
    return shared


def kernel(L0_attn_x, L0_attn_Wq, L0_attn_Wk, L0_attn_Wv, L0_attn_Wo,
           L0_ffn_gate_w, L0_ffn_gate_bias, L0_ffn_fc1_w, L0_ffn_fc2_w,
           L0_ffn_w1, L0_ffn_w2, L0_ffn_shared_up_w, L0_ffn_shared_down_w):
    shared = _host_prep(L0_attn_Wq, L0_attn_Wk, L0_attn_Wv, L0_attn_Wo,
                        L0_ffn_gate_w, L0_ffn_gate_bias, L0_ffn_fc1_w,
                        L0_ffn_fc2_w, L0_ffn_w1, L0_ffn_w2,
                        L0_ffn_shared_up_w, L0_ffn_shared_down_w)
    nc = build_nc(NT=16)
    in_maps = []
    for c in range(8):
        m = dict(shared)
        m["x"] = np.ascontiguousarray(L0_attn_x[c])
        in_maps.append(m)
    res = run_bass_kernel_spmd(nc, in_maps, core_ids=list(range(8)))
    return np.stack([res.results[c]["out"] for c in range(8)], axis=0)


# version 10
# speedup vs baseline: 1.7544x; 1.0725x over previous
import numpy as np
import ml_dtypes
import concourse.bass as bass
import concourse.mybir as mybir
from concourse.bass_utils import run_bass_kernel_spmd

F32 = mybir.dt.float32
BF16 = mybir.dt.bfloat16
U8 = mybir.dt.uint8
AX = mybir.AxisListType
ALU = mybir.AluOpType
ACTF = mybir.ActivationFunctionType

P = 128
C = 2048
H = 16
D = 128
NKC = C // P
NE = 32
LAT = 1024
INTER = 1024
SI = 4096
KB = 1024
PGSZ = 2048

CQ = 1.0 / (127.0 * float(np.sqrt(np.float64(128.0))))
CK = 1.0 / 127.0
INV_N = 1.0 / (2048.0 * 2048.0)
M32 = float(2 ** 23 * 1.5)
M64 = float(2 ** 52 * 1.5)


class Buf:
    __slots__ = ("ap", "keys")

    def __init__(self, ap, keys):
        self.ap = ap
        self.keys = frozenset(keys)


class Prog:
    ENGS = ("T", "V", "A", "G", "S")

    def __init__(self, sems, lane_queue):
        self.sems = sems
        self.lane_queue = lane_queue
        self.cnt = {k: 0 for k in sems}
        self.steps = {e: [] for e in self.ENGS}
        self.seen = {e: {} for e in self.ENGS}
        self.pages = {}
        self.dirty = {e: set() for e in self.ENGS}
        self.rdirty = {e: set() for e in self.ENGS}

    def _rec(self, k):
        r = self.pages.get(k)
        if r is None:
            r = [None, []]
            self.pages[k] = r
        return r

    @staticmethod
    def _keys(bufs):
        s = set()
        for b in bufs:
            s |= b.keys if isinstance(b, Buf) else {b}
        return s

    def op(self, eng, fn, reads=(), writes=(), lane=None):
        rk = self._keys(reads)
        wk = self._keys(writes)
        waits = []
        seen = self.seen[eng]

        def need(sem, val, agent):
            if agent != eng and agent in self.lane_queue:
                val = self.cnt[agent]
            if val > 0 and seen.get(sem, 0) < val:
                waits.append((self.sems[sem], val))
                seen[sem] = val

        drain = False
        for k in rk | wk:
            rec = self.pages.get(k)
            if rec is None:
                continue
            w = rec[0]
            if w is not None:
                sem, val, agent = w
                if agent == eng:
                    if k in self.dirty[eng]:
                        drain = True
                else:
                    need(sem, val, agent)
            if k in wk:
                for sem, val, agent in rec[1]:
                    if agent == eng:
                        if k in self.rdirty[eng]:
                            drain = True
                    else:
                        need(sem, val, agent)
        if eng == "T":
            drain = False

        if lane is None:
            sem_name, inc = eng, 1
            agent = eng
        else:
            sem_name, inc = lane, 16
            agent = lane
        self.cnt[sem_name] += inc
        val = self.cnt[sem_name]

        if drain:
            self.dirty[eng].clear()
            self.rdirty[eng].clear()
        for k in wk:
            rec = self._rec(k)
            rec[0] = (sem_name, val, agent)
            rec[1] = []
        for k in rk:
            self._rec(k)[1].append((sem_name, val, agent))
        if lane is None:
            self.dirty[eng] |= wk
            self.rdirty[eng] |= rk
        self.steps[eng].append((tuple(waits), drain, fn, self.sems[sem_name], inc))
        return val

    def finish(self):
        waits = [(self.sems[k], v) for k, v in self.cnt.items() if v > 0]
        self.steps["S"].append((tuple(waits), False, None, None, 0))

    def realize(self, block):
        names = {"T": "tensor", "V": "vector", "A": "scalar", "G": "gpsimd", "S": "sync"}
        for e in self.ENGS:
            steps = self.steps[e]

            def mk(steps=steps):
                def body(eng):
                    for waits, drain, fn, sem, inc in steps:
                        for s, v in waits:
                            eng.wait_ge(s, v)
                        if drain:
                            eng.drain()
                        if fn is not None:
                            inst = fn(eng)
                            if sem is not None:
                                inst.then_inc(sem, inc)
                return body
            getattr(block, names[e])(mk())


class Builder:
    def __init__(self, nc, prog, NT):
        self.nc = nc
        self.p = prog
        self.NT = NT
        self.T = NT * P

    def T_(self, fname, *args, reads=(), writes=(), **kw):
        self.p.op("T", (lambda e, a=args, k=kw, f=fname: getattr(e, f)(*a, **k)),
                  reads, writes)

    def V_(self, fname, *args, reads=(), writes=(), **kw):
        self.p.op("V", (lambda e, a=args, k=kw, f=fname: getattr(e, f)(*a, **k)),
                  reads, writes)

    def A_(self, fname, *args, reads=(), writes=(), **kw):
        self.p.op("A", (lambda e, a=args, k=kw, f=fname: getattr(e, f)(*a, **k)),
                  reads, writes)

    def dma(self, lane, out_ap, in_ap, reads=(), writes=()):
        self.p.op(self.p.lane_queue[lane],
                  (lambda e, o=out_ap, i=in_ap: e.dma_start(out=o, in_=i)),
                  reads, writes, lane=lane)

    def transpose(self, lane, out_ap, in_ap, reads=(), writes=()):
        self.p.op("S", (lambda e, o=out_ap, i=in_ap: e.dma_start_transpose(o, i)),
                  reads, writes, lane=lane)

    def AR(self, off_kb, shape, dtype):
        nel = int(np.prod(shape[1:]))
        nby = nel * mybir.dt.size(dtype)
        if dtype == BF16:
            arena, tag, epk = self.arena16, "A6", 512
        else:
            arena, tag, epk = self.arena32, "A2", 256
        ap = arena[:, off_kb * epk: off_kb * epk + nel]
        if len(shape) == 3:
            ap = ap.rearrange("p (a b) -> p a b", b=shape[2])
        keys = {(tag, g) for g in range(off_kb * KB // PGSZ,
                                        (off_kb * KB + nby + PGSZ - 1) // PGSZ)}
        return Buf(ap, keys)

    def SB(self, name, shape, dtype=F32, tdest=False):
        h = self.nc.alloc_sbuf_tensor(name, shape, dtype, is_transpose_dest=tdest)
        return Buf(h[:], {name})

    def PSR(self, which, lo, hi):
        t = self.psA if which == "A" else self.psB
        keys = {("ps" + which, b) for b in range(lo // 512, (hi + 511) // 512)}
        return Buf(t[:, lo:hi], keys)

    def declare(self):
        nc = self.nc
        NT, T = self.NT, self.T
        dr = lambda n, s, dt=F32, kind="ExternalInput": nc.dram_tensor(n, s, dt, kind=kind).ap()
        self.x = dr("x", [T, C])
        self.wqT = dr("wqT", [C, C]); self.wkT = dr("wkT", [C, C])
        self.wvT = dr("wvT", [C, C]); self.woT = dr("woT", [C, C])
        self.cc = dr("cc", [T, D]); self.ss = dr("ss", [T, D])
        self.cbm = dr("cbm", [P, P]); self.onesd = dr("onesd", [P, P])
        self.gbiasd = dr("gbiasd", [P, NE])
        self.gwT = dr("gwT", [C, NE])
        self.fc1Td = dr("fc1Td", [C, LAT])
        self.dfT = dr("dfT", [SI + LAT, C])
        self.w1Td = dr("w1Td", [NE, LAT, INTER])
        self.w2Td = dr("w2Td", [NE, INTER, LAT])
        self.upTd = dr("upTd", [C, SI])
        self.out = dr("out", [T, C], F32, kind="ExternalOutput")
        idr = lambda n, s, dt: nc.dram_tensor(n, s, dt, kind="Internal").ap()
        self.xqT_s = idr("xqT_s", [NT, P, NKC, P], BF16)
        self.qrT_s = idr("qrT_s", [NT, P, NKC, P], BF16)
        self.krT_s = idr("krT_s", [H, P, T], BF16)
        self.v_s = idr("v_s", [4, T, 512], BF16)
        self.yqT_s = idr("yqT_s", [NT, P, NKC, P], BF16)
        self.x1_s = idr("x1_s", [NT, P, C], F32)
        self.ln2T_s = idr("ln2T_s", [NT, P, NKC, P], BF16)
        self.ylT_s = idr("ylT_s", [P, 8, T], BF16)

        self.psA = nc.alloc_psum_tensor("psA", [P, 2048], F32)
        self.psB = nc.alloc_psum_tensor("psB", [P, 2048], F32)

        S = self.SB
        self.tr = [S("tr0", [P, NKC, P], BF16, tdest=True),
                   S("tr1", [P, NKC, P], BF16, tdest=True)]
        self.pT = S("pT", [P, NT, P], BF16, tdest=True)
        self.ylT = S("ylT", [P, 8, P], BF16, tdest=True)
        self.cb = S("cb", [P, P]); self.ones = S("ones", [P, P])
        self.cct = S("cct", [P, D]); self.sst = S("sst", [P, D])
        self.gbias = S("gbias", [P, NE])
        self.gw_sb = S("gw_sb", [P, NKC, NE], BF16)
        self.samc = S("samc", [P, NT]); self.samc_y = S("samc_y", [P, NT])
        self.cw = S("cw", [P, NT, NE])
        self.bst = S("bst", [P, 4, 6]); self.mv = S("mv", [P, 2])
        self.cs16 = S("cs16", [P, NKC]); self.cstot = S("cstot", [P, 1])
        for n in ("mclip_q", "mclip_k", "mclip_v", "mclip_o", "thr", "nthr",
                  "lnv", "rstd", "rsq", "bq", "rr", "sfold", "am", "amc",
                  "rmc", "qsc", "qsc2", "sc1", "s2t", "vv", "gv",
                  "nm", "den", "rden", "ss2v",
                  "g1v", "g2v", "m1g", "m2g", "th8", "s8", "rs8", "mx1"):
            setattr(self, n, S(n, [P, 1]))
        self.c_eps5 = S("c_eps5", [P, 1]); self.c_eps6 = S("c_eps6", [P, 1])
        self.c_2565 = S("c_2565", [P, 1])
        self.c_m256 = S("c_m256", [P, 1]); self.c_n1e30 = S("c_n1e30", [P, 1])
        self.c_one = S("c_one", [P, 1]); self.gdisc = S("gdisc", [P, 1])
        self.c_zero = S("c_zero", [P, 1])
        self.mqs = S("mqs", [P, 1]); self.rs25 = S("rs25", [P, 1])
        self.et = S("et", [P, NE]); self.d1 = S("d1", [P, NE])
        self.scrs = S("scrs", [P, NE]); self.sbv = S("sbv", [P, NE])
        self.msb = S("msb", [P, NE]); self.cur = S("cur", [P, NE])
        self.mk32 = S("mk32", [P, NE]); self.wsl = S("wsl", [P, NE])
        self.selm = S("selm", [P, NE])
        self.gsv = S("gsv", [P, 4]); self.mask4 = S("mask4", [P, 4])
        self.t4b = S("t4b", [P, 4]); self.madd4 = S("madd4", [P, 4])
        self.msk8 = S("msk8", [P, 8]); self.t8b = S("t8b", [P, 8])

        self.arena16 = nc.alloc_sbuf_tensor("arena16", [P, 108 * 512], BF16)
        self.arena32 = nc.alloc_sbuf_tensor("arena32", [P, 72 * 256], F32)
        A = self.AR
        self.wtern = A(0, [P, NKC, C], BF16)
        self.xqb = A(64, [P, C], BF16)
        self.vbb = A(68, [P, C], BF16)
        self.ld = [A(72, [P, NKC, P], BF16), A(76, [P, NKC, P], BF16)]
        self.pb = A(80, [P, 2048], BF16)
        self.ks = [A(84, [P, 2048], BF16), A(88, [P, 2048], BF16)]
        self.vs = A(92, [P, NT, 512], BF16)
        self.chunk = [A(0, [P, C], F32), A(8, [P, C], F32)]
        self.xt = [A(16, [P, C], F32), A(24, [P, C], F32)]
        self.xn = A(32, [P, C], F32)
        self.qf = A(40, [P, C], F32)
        self.uu = A(48, [P, C], F32)
        self.trt = A(56, [P, C], F32)
        self.y_scr = [A(64, [P, C], F32), A(64, [P, C], F32)]
        self.fc1T = A(76, [P, NKC, LAT], BF16)
        self.x_latT = A(0, [P, 8, T], BF16)
        self.slab = [A(32, [P, NKC, 512], BF16), A(48, [P, NKC, 512], BF16)]
        self.w1b = [A(32, [P, 8, INTER], BF16), A(48, [P, 8, INTER], BF16)]
        self.w2b = [A(64, [P, 8, LAT], BF16), A(80, [P, 8, LAT], BF16)]
        self.h_all = A(96, [P, 8, 512], BF16)
        self.ylb = A(104, [P, LAT], BF16)
        self.dnb = A(0, [P, 8, C], BF16)
        self.slab3 = A(32, [P, NKC, 512], BF16)
        self.sTs = A(48, [P, 8, 512], BF16)
        self.upTb = A(56, [P, NKC, 1024], BF16)
        self.yl3 = A(88, [P, 8, 512], BF16)
        self.x1ld = [A(0, [P, C], F32), A(8, [P, C], F32)]
        self.y_lat = A(0, [P, NT, LAT], F32)
        self.rtmp2 = [A(64, [P, 512], F32), A(66, [P, 512], F32)]
        self.rout = A(0, [P, 4, C], F32)
        self.x1l3 = A(32, [P, C], F32)
        self.outb = A(40, [P, C], F32)
        self.rtmp3 = [A(48, [P, 512], F32), A(50, [P, 512], F32)]

    def act(self, out, in_, func, bias=0.0, scale=1.0, accum=None,
            reads=(), writes=()):
        kw = dict(bias=bias, scale=scale)
        if accum is not None:
            kw["accum_out"] = accum
        self.A_("activation", out, in_, func, reads=reads, writes=writes, **kw)

    def bn(self, src_buf, src_ap2048, extra_reads=()):
        for i in range(4):
            self.V_("bn_stats", self.bst.ap[:, i, :], src_ap2048[:, i * 512:(i + 1) * 512],
                    reads=[src_buf, *extra_reads], writes=[self.bst])
        self.V_("bn_aggr", self.mv.ap, self.bst.ap,
                reads=[self.bst], writes=[self.mv])

    def ln_rstd(self, var_ap, eps_buf, out_buf, reads):
        self.act(self.lnv.ap, var_ap, ACTF.Ln, bias=eps_buf.ap, scale=1.0,
                 reads=list(reads) + [eps_buf], writes=[self.lnv])
        self.act(out_buf.ap, self.lnv.ap, ACTF.Exp, bias=0.0, scale=-0.5,
                 reads=[self.lnv], writes=[out_buf])

    def const_loads(self):
        self.V_("memset", self.c_eps5.ap, 1e-5, writes=[self.c_eps5])
        self.V_("memset", self.c_eps6.ap, 1e-6, writes=[self.c_eps6])
        self.V_("memset", self.c_2565.ap, 256.5, writes=[self.c_2565])
        self.V_("memset", self.c_m256.ap, -256.0, writes=[self.c_m256])
        self.V_("memset", self.c_n1e30.ap, -1e30, writes=[self.c_n1e30])
        self.V_("memset", self.c_one.ap, 1.0, writes=[self.c_one])
        self.V_("memset", self.c_zero.ap, 0.0, writes=[self.c_zero])
        self.V_("tensor_single_scalar", self.gdisc.ap, self.c_one.ap, M64,
                ALU.add, reads=[self.c_one], writes=[self.gdisc])
        self.V_("tensor_single_scalar", self.gdisc.ap, self.gdisc.ap, M64,
                ALU.subtract, reads=[self.gdisc], writes=[self.gdisc])
        self.dma("cb", self.cb.ap, self.cbm, writes=[self.cb])
        self.dma("ones", self.ones.ap, self.onesd, writes=[self.ones])
        self.dma("gbias", self.gbias.ap, self.gbiasd, writes=[self.gbias])
        self.dma("gw", self.gw_sb.ap,
                 self.gwT.rearrange("(a p) n -> p a n", p=P), writes=[self.gw_sb])
        from concourse.hw_specs import get_activation_tables
        idx = None
        for i, funcs in enumerate(get_activation_tables(self.nc.m.arch).values()):
            if ACTF.Exp in funcs and ACTF.Ln in funcs:
                idx = i
                break
        if idx is not None:
            nc = self.nc
            def fn(e, idx=idx, nc=nc):
                return e.add_instruction(mybir.InstLoadActFuncSet(
                    name=nc.get_next_instruction_name(),
                    act_func_set_id=idx, ins=[], outs=[]))
            self.p.op("A", fn, (), ())

    def tern(self, wdram, mclip, cmul):
        NCH = NKC
        for kc in range(NCH):
            ch = self.chunk[kc % 2]
            self.dma("ch%d" % (kc % 2), ch.ap, wdram[kc * P:(kc + 1) * P, :],
                     reads=[ch], writes=[ch])
            self.V_("reduce_sum", self.cs16.ap[:, kc:kc + 1], ch.ap, AX.X,
                    apply_absolute_value=True, reads=[ch], writes=[self.cs16])
        self.V_("reduce_sum", self.cstot.ap, self.cs16.ap, AX.X,
                reads=[self.cs16], writes=[self.cstot])
        ps1 = self.PSR("A", 0, 512)
        self.T_("matmul", self.psA[:, 0:1], self.ones.ap, self.cstot.ap,
                start=True, stop=True, reads=[self.ones, self.cstot], writes=[ps1])
        self.V_("tensor_scalar", mclip.ap, self.psA[:, 0:1], float(INV_N), 1e-5,
                ALU.mult, ALU.max, reads=[ps1], writes=[mclip])
        self.V_("tensor_scalar", self.thr.ap, mclip.ap, 0.5, None, ALU.mult,
                reads=[mclip], writes=[self.thr])
        self.V_("tensor_scalar", self.nthr.ap, mclip.ap, -0.5, None, ALU.mult,
                reads=[mclip], writes=[self.nthr])
        self.V_("tensor_scalar", self.mqs.ap, mclip.ap, float(cmul), None,
                ALU.mult, reads=[mclip], writes=[self.mqs])
        for kc in range(NCH):
            ch = self.chunk[kc % 2]
            self.dma("ch%d" % (kc % 2), ch.ap, wdram[kc * P:(kc + 1) * P, :],
                     reads=[ch], writes=[ch])
            self.V_("tensor_scalar", self.qf.ap, ch.ap, self.nthr.ap, None,
                    ALU.is_lt, reads=[ch, self.nthr], writes=[self.qf])
            self.V_("scalar_tensor_tensor", self.wtern.ap[:, kc, :], ch.ap,
                    self.thr.ap, self.qf.ap, ALU.is_gt, ALU.subtract,
                    reads=[ch, self.thr, self.qf], writes=[self.wtern])

    def a0(self):
        for t in range(self.NT):
            xt = self.xt[t % 2]
            self.dma("xt%d" % (t % 2), xt.ap, self.x[t * P:(t + 1) * P, :],
                     reads=[xt], writes=[xt])
            self.bn(xt, xt.ap)
            self.ln_rstd(self.mv.ap[:, 1:2], self.c_eps5, self.rstd, [self.mv])
            self.V_("tensor_tensor", self.rsq.ap, self.rstd.ap, self.rstd.ap,
                    ALU.mult, reads=[self.rstd], writes=[self.rsq])
            self.V_("tensor_tensor", self.bq.ap, self.mv.ap[:, 1:2], self.rsq.ap,
                    ALU.mult, reads=[self.mv, self.rsq], writes=[self.bq])
            self.act(self.lnv.ap, self.bq.ap, ACTF.Ln, bias=self.c_eps6.ap,
                     reads=[self.bq, self.c_eps6], writes=[self.lnv])
            self.act(self.rr.ap, self.lnv.ap, ACTF.Exp, scale=-0.5,
                     reads=[self.lnv], writes=[self.rr])
            self.V_("tensor_tensor", self.sfold.ap, self.rstd.ap, self.rr.ap,
                    ALU.mult, reads=[self.rstd, self.rr], writes=[self.sfold])
            self.V_("tensor_scalar", self.xn.ap, xt.ap, self.mv.ap[:, 0:1],
                    self.sfold.ap, ALU.subtract, ALU.mult,
                    reads=[xt, self.mv, self.sfold], writes=[self.xn])
            self.quant_store(self.xn, self.samc, t, self.xqT_s, "xqT_s",
                             scale_pre=None)

    def quant_store(self, src, samc, t, dst_dram, dst_key, scale_pre):
        self.V_("reduce_max", self.am.ap, src.ap, AX.X, apply_absolute_value=True,
                reads=[src], writes=[self.am])
        if scale_pre is not None:
            self.V_("tensor_tensor", self.amc.ap, self.am.ap, scale_pre.ap,
                    ALU.mult, reads=[self.am, scale_pre], writes=[self.amc])
            amsrc = self.amc
        else:
            amsrc = self.am
        self.V_("tensor_single_scalar", samc.ap[:, t:t + 1], amsrc.ap, 1e-5,
                ALU.max, reads=[amsrc], writes=[samc])
        self.V_("reciprocal", self.rmc.ap, samc.ap[:, t:t + 1],
                reads=[samc], writes=[self.rmc])
        self.V_("tensor_scalar", self.qsc.ap, self.rmc.ap, 127.0, None, ALU.mult,
                reads=[self.rmc], writes=[self.qsc])
        if scale_pre is not None:
            self.V_("tensor_tensor", self.qsc2.ap, self.qsc.ap, scale_pre.ap,
                    ALU.mult, reads=[self.qsc, scale_pre], writes=[self.qsc2])
            qs = self.qsc2
        else:
            qs = self.qsc
        self.V_("tensor_scalar", self.qf.ap, src.ap, qs.ap, None, ALU.mult,
                reads=[src, qs], writes=[self.qf])
        self.V_("tensor_single_scalar", self.uu.ap, self.qf.ap, M32, ALU.add,
                reads=[self.qf], writes=[self.uu])
        self.V_("tensor_single_scalar", self.uu.ap, self.uu.ap, M32,
                ALU.subtract, reads=[self.uu], writes=[self.uu])
        self.V_("tensor_single_scalar", self.trt.ap, self.qf.ap, M64, ALU.add,
                reads=[self.qf], writes=[self.trt])
        self.V_("tensor_single_scalar", self.trt.ap, self.trt.ap, M64,
                ALU.subtract, reads=[self.trt], writes=[self.trt])
        self.V_("tensor_tensor", self.trt.ap, self.trt.ap, self.uu.ap,
                ALU.subtract, reads=[self.trt, self.uu], writes=[self.trt])
        self.V_("scalar_tensor_tensor", self.xqb.ap, self.trt.ap,
                self.gdisc.ap, self.uu.ap, ALU.mult, ALU.add,
                reads=[self.trt, self.uu, self.gdisc], writes=[self.xqb])
        tr = self.tr[t % 2]
        self.transpose("trp%d" % (t % 2), tr.ap, self.xqb.ap,
                       reads=[self.xqb, tr], writes=[tr])
        self.dma("s" + dst_key, dst_dram[t], tr.ap, reads=[tr],
                 writes=[(dst_key, t)])

    def a1(self, kind):
        NT = self.NT
        for t in range(NT):
            ld = self.ld[t % 2]
            self.dma("ld%d" % (t % 2), ld.ap, self.xqT_s[t],
                     reads=[ld, ("xqT_s", t)], writes=[ld])
            psn = "A" if t % 2 == 0 else "B"
            pst = self.psA if psn == "A" else self.psB
            psb = self.PSR(psn, 0, 2048)
            for kc in range(NKC):
                for n in range(4):
                    self.T_("matmul", pst[:, n * 512:(n + 1) * 512],
                            ld.ap[:, kc, :], self.wtern.ap[:, kc, n * 512:(n + 1) * 512],
                            start=(kc == 0), stop=(kc == NKC - 1),
                            reads=[ld, self.wtern], writes=[psb])
            self.V_("tensor_scalar", self.sc1.ap, self.samc.ap[:, t:t + 1],
                    self.mqs.ap, None, ALU.mult,
                    reads=[self.samc, self.mqs], writes=[self.sc1])
            if kind == "v":
                self.V_("tensor_scalar", self.vbb.ap, pst[:, :], self.sc1.ap,
                        None, ALU.mult, reads=[psb, self.sc1], writes=[self.vbb])
                for hg in range(4):
                    self.dma("sv%d" % hg, self.v_s[hg, t * P:(t + 1) * P, :],
                             self.vbb.ap[:, hg * 512:(hg + 1) * 512],
                             reads=[self.vbb], writes=[("v_s", hg, t)])
                continue
            self.V_("tensor_scalar", self.qf.ap, pst[:, :], self.sc1.ap, None,
                    ALU.mult, reads=[psb, self.sc1], writes=[self.qf])
            self.dma("cc", self.cct.ap, self.cc[t * P:(t + 1) * P, :],
                     reads=[self.cct], writes=[self.cct])
            self.dma("ss", self.sst.ap, self.ss[t * P:(t + 1) * P, :],
                     reads=[self.sst], writes=[self.sst])
            qf3 = self.qf.ap.rearrange("p (h d) -> p h d", d=D)
            tr3 = self.trt.ap.rearrange("p (h d) -> p h d", d=D)
            ccb = self.cct.ap.unsqueeze(1).broadcast_to([P, H, D])
            uu3 = self.uu.ap.rearrange("p (h d) -> p h d", d=D)
            self.V_("tensor_tensor", uu3, qf3, ccb, ALU.mult,
                    reads=[self.qf, self.cct], writes=[self.uu])
            sslo = self.sst.ap[:, 0:64].unsqueeze(1).broadcast_to([P, H, 64])
            sshi = self.sst.ap[:, 64:128].unsqueeze(1).broadcast_to([P, H, 64])
            self.V_("tensor_tensor", tr3[:, :, 0:64], qf3[:, :, 64:128], sslo,
                    ALU.mult, reads=[self.qf, self.sst], writes=[self.trt])
            self.V_("tensor_tensor", tr3[:, :, 64:128], qf3[:, :, 0:64], sshi,
                    ALU.mult, reads=[self.qf, self.sst], writes=[self.trt])
            self.V_("tensor_tensor", self.xqb.ap, self.uu.ap, self.trt.ap,
                    ALU.add, reads=[self.uu, self.trt], writes=[self.xqb])
            tr = self.tr[t % 2]
            self.transpose("trp%d" % (t % 2), tr.ap, self.xqb.ap,
                           reads=[self.xqb, tr], writes=[tr])
            if kind == "q":
                self.dma("sqrT", self.qrT_s[t], tr.ap, reads=[tr],
                         writes=[("qrT_s", t)])
            else:
                for h in range(H):
                    self.dma("skr%d" % h, self.krT_s[h, :, t * P:(t + 1) * P],
                             tr.ap[:, h, :], reads=[tr], writes=[("krT_s", h)])

    def a2(self):
        NT = self.NT
        for qt in range(NT):
            L = (qt + 1) * P
            ld = self.ld[qt % 2]
            self.dma("ld%d" % (qt % 2), ld.ap, self.qrT_s[qt],
                     reads=[ld, ("qrT_s", qt)], writes=[ld])
            ysc = self.y_scr[qt % 2]
            for hg in range(4):
                for ktq in range(0, qt + 1, 4):
                    nkt = min(4, qt + 1 - ktq)
                    src = self.v_s[hg].rearrange("(k p) f -> p k f", p=P)
                    self.dma("vsl", self.vs.ap[:, ktq:ktq + nkt, :],
                             src[:, ktq:ktq + nkt, :],
                             reads=[("v_s", hg, tt) for tt in range(ktq, ktq + nkt)],
                             writes=[self.vs])
                for h4 in range(4):
                    h = hg * 4 + h4
                    Rn = "A" if h % 2 == 0 else "B"
                    Rt = self.psA if Rn == "A" else self.psB
                    Rb = self.PSR(Rn, 0, L)
                    ks = self.ks[h % 2]
                    self.dma("ks%d" % (h % 2), ks.ap[:, 0:L], self.krT_s[h, :, 0:L],
                             reads=[ks, ("krT_s", h)], writes=[ks])
                    nch = (L + 511) // 512
                    for cc_ in range(nch):
                        n = min(512, L - cc_ * 512)
                        self.T_("matmul", Rt[:, cc_ * 512:cc_ * 512 + n],
                                ld.ap[:, h, :], ks.ap[:, cc_ * 512:cc_ * 512 + n],
                                start=True, stop=True,
                                reads=[ld, ks], writes=[self.PSR(Rn, cc_ * 512, cc_ * 512 + n)])
                    self.V_("tensor_tensor", Rt[:, qt * P:L], Rt[:, qt * P:L],
                            self.cb.ap, ALU.add,
                            reads=[self.PSR(Rn, qt * P, L), self.cb],
                            writes=[self.PSR(Rn, qt * P, L)])
                    self.V_("reduce_max", self.nm.ap, Rt[:, 0:L], AX.X,
                            negate=True, reads=[Rb], writes=[self.nm])
                    self.act(self.pb.ap[:, 0:L], Rt[:, 0:L], ACTF.Exp,
                             bias=self.nm.ap, scale=1.0, accum=self.den.ap,
                             reads=[Rb, self.nm], writes=[self.pb, self.den])
                    self.V_("reciprocal", self.rden.ap, self.den.ap,
                            reads=[self.den], writes=[self.rden])
                    self.transpose("trpP", self.pT.ap[:, 0:qt + 1, :],
                                   self.pb.ap[:, 0:L],
                                   reads=[self.pb, self.pT], writes=[self.pT])
                    Yn = "B" if h % 2 == 0 else "A"
                    Yt = self.psB if Yn == "B" else self.psA
                    Yb = self.PSR(Yn, 1920, 2048)
                    for kt in range(qt + 1):
                        self.T_("matmul", Yt[:, 1920:2048], self.pT.ap[:, kt, :],
                                self.vs.ap[:, kt, h4 * P:(h4 + 1) * P],
                                start=(kt == 0), stop=(kt == qt),
                                reads=[self.pT, self.vs], writes=[Yb])
                    self.V_("tensor_scalar", ysc.ap[:, h * P:(h + 1) * P],
                            Yt[:, 1920:2048], self.rden.ap, None, ALU.mult,
                            reads=[Yb, self.rden], writes=[ysc])
            self.V_("tensor_tensor", self.xn.ap, ysc.ap, ysc.ap, ALU.mult,
                    reads=[ysc], writes=[self.xn])
            self.V_("reduce_sum", self.ss2v.ap, self.xn.ap, AX.X,
                    reads=[self.xn], writes=[self.ss2v])
            self.act(self.lnv.ap, self.ss2v.ap, ACTF.Ln, bias=self.c_eps6.ap,
                     scale=float(1.0 / C), reads=[self.ss2v, self.c_eps6],
                     writes=[self.lnv])
            self.act(self.rr.ap, self.lnv.ap, ACTF.Exp, scale=-0.5,
                     reads=[self.lnv], writes=[self.rr])
            self.quant_store(ysc, self.samc_y, qt, self.yqT_s, "yqT_s",
                             scale_pre=self.rr)

    def a3(self):
        NT = self.NT
        for t in range(NT):
            ld = self.ld[t % 2]
            self.dma("ld%d" % (t % 2), ld.ap, self.yqT_s[t],
                     reads=[ld, ("yqT_s", t)], writes=[ld])
            xt = self.xt[t % 2]
            self.dma("xt%d" % (t % 2), xt.ap, self.x[t * P:(t + 1) * P, :],
                     reads=[xt], writes=[xt])
            psn = "A" if t % 2 == 0 else "B"
            pst = self.psA if psn == "A" else self.psB
            psb = self.PSR(psn, 0, 2048)
            for kc in range(NKC):
                for n in range(4):
                    self.T_("matmul", pst[:, n * 512:(n + 1) * 512],
                            ld.ap[:, kc, :], self.wtern.ap[:, kc, n * 512:(n + 1) * 512],
                            start=(kc == 0), stop=(kc == NKC - 1),
                            reads=[ld, self.wtern], writes=[psb])
            self.V_("tensor_scalar", self.sc1.ap, self.samc_y.ap[:, t:t + 1],
                    self.mqs.ap, None, ALU.mult,
                    reads=[self.samc_y, self.mqs], writes=[self.sc1])
            self.bn(psb, pst)
            self.V_("tensor_tensor", self.s2t.ap, self.sc1.ap, self.sc1.ap,
                    ALU.mult, reads=[self.sc1], writes=[self.s2t])
            self.V_("tensor_tensor", self.vv.ap, self.mv.ap[:, 1:2], self.s2t.ap,
                    ALU.mult, reads=[self.mv, self.s2t], writes=[self.vv])
            self.ln_rstd(self.vv.ap, self.c_eps5, self.rstd, [self.vv])
            self.V_("tensor_tensor", self.gv.ap, self.sc1.ap, self.rstd.ap,
                    ALU.mult, reads=[self.sc1, self.rstd], writes=[self.gv])
            self.V_("tensor_scalar", self.xn.ap, pst[:, :], self.mv.ap[:, 0:1],
                    self.gv.ap, ALU.subtract, ALU.mult,
                    reads=[psb, self.mv, self.gv], writes=[self.xn])
            self.V_("tensor_tensor", self.qf.ap, self.xn.ap, xt.ap, ALU.add,
                    reads=[self.xn, xt], writes=[self.qf])
            self.dma("sx1", self.x1_s[t], self.qf.ap, reads=[self.qf],
                     writes=[("x1_s", t)])

    def b0(self):
        NT = self.NT
        for kc in range(NKC):
            self.dma("fc1", self.fc1T.ap[:, kc, :],
                     self.fc1Td[kc * P:(kc + 1) * P, :],
                     reads=[self.fc1T], writes=[self.fc1T])
        for t in range(NT):
            self.b0_tile(t)
            if t % 4 == 3:
                self.b1_blk(t // 4)

    def b0_tile(self, t):
        if True:
            x1 = self.x1ld[t % 2]
            self.dma("x1ld%d" % (t % 2), x1.ap, self.x1_s[t],
                     reads=[x1, ("x1_s", t)], writes=[x1])
            self.bn(x1, x1.ap)
            self.ln_rstd(self.mv.ap[:, 1:2], self.c_eps5, self.rstd, [self.mv])
            self.V_("tensor_scalar", self.xqb.ap, x1.ap, self.mv.ap[:, 0:1],
                    self.rstd.ap, ALU.subtract, ALU.mult,
                    reads=[x1, self.mv, self.rstd], writes=[self.xqb])
            tr = self.tr[t % 2]
            self.transpose("trp%d" % (t % 2), tr.ap, self.xqb.ap,
                           reads=[self.xqb, tr], writes=[tr])
            self.dma("sln2T", self.ln2T_s[t], tr.ap, reads=[tr],
                     writes=[("ln2T_s", t)])
            psg = self.PSR("B", 0, 512)
            for kc in range(NKC):
                self.T_("matmul", self.psB[:, 0:NE], tr.ap[:, kc, :],
                        self.gw_sb.ap[:, kc, :],
                        start=(kc == 0), stop=(kc == NKC - 1),
                        reads=[tr, self.gw_sb], writes=[psg])
            self.act(self.et.ap, self.psB[:, 0:NE], ACTF.Exp, scale=-1.0,
                     reads=[psg], writes=[self.et])
            self.V_("tensor_single_scalar", self.d1.ap, self.et.ap, 1.0, ALU.add,
                    reads=[self.et], writes=[self.d1])
            self.V_("reciprocal", self.scrs.ap, self.d1.ap,
                    reads=[self.d1], writes=[self.scrs])
            self.V_("tensor_tensor", self.sbv.ap, self.scrs.ap, self.gbias.ap,
                    ALU.add, reads=[self.scrs, self.gbias], writes=[self.sbv])
            for g in range(4):
                sg = self.sbv.ap[:, g * 8:(g + 1) * 8]
                self.V_("reduce_max", self.m1g.ap, sg, AX.X,
                        reads=[self.sbv], writes=[self.m1g])
                self.V_("tensor_scalar", self.msk8.ap, sg, self.m1g.ap, None,
                        ALU.is_ge, reads=[self.sbv, self.m1g], writes=[self.msk8])
                self.V_("scalar_tensor_tensor", self.t8b.ap, self.msk8.ap,
                        self.c_n1e30.ap, sg, ALU.mult, ALU.add,
                        reads=[self.msk8, self.sbv, self.c_n1e30],
                        writes=[self.t8b])
                self.V_("reduce_max", self.m2g.ap, self.t8b.ap, AX.X,
                        reads=[self.t8b], writes=[self.m2g])
                self.V_("tensor_tensor", self.gsv.ap[:, g:g + 1], self.m1g.ap,
                        self.m2g.ap, ALU.add,
                        reads=[self.m1g, self.m2g], writes=[self.gsv])
            self.V_("reduce_max", self.g1v.ap, self.gsv.ap, AX.X,
                    reads=[self.gsv], writes=[self.g1v])
            self.V_("tensor_scalar", self.mask4.ap, self.gsv.ap, self.g1v.ap,
                    None, ALU.is_ge, reads=[self.gsv, self.g1v], writes=[self.mask4])
            self.V_("scalar_tensor_tensor", self.t4b.ap, self.mask4.ap,
                    self.c_n1e30.ap, self.gsv.ap, ALU.mult, ALU.add,
                    reads=[self.mask4, self.gsv, self.c_n1e30],
                    writes=[self.t4b])
            self.V_("reduce_max", self.g2v.ap, self.t4b.ap, AX.X,
                    reads=[self.t4b], writes=[self.g2v])
            self.V_("tensor_scalar", self.mask4.ap, self.gsv.ap, self.g2v.ap,
                    None, ALU.is_ge, reads=[self.gsv, self.g2v], writes=[self.mask4])
            self.V_("tensor_scalar", self.madd4.ap, self.mask4.ap, -1.0, 1e30,
                    ALU.add, ALU.mult, reads=[self.mask4], writes=[self.madd4])
            sb3 = self.sbv.ap.rearrange("p (g e) -> p g e", e=8)
            msb3 = self.msb.ap.rearrange("p (g e) -> p g e", e=8)
            self.V_("tensor_tensor", msb3, sb3,
                    self.madd4.ap.unsqueeze(2).broadcast_to([P, 4, 8]), ALU.add,
                    reads=[self.sbv, self.madd4], writes=[self.msb])
            self.V_("tensor_copy", self.cur.ap, self.msb.ap,
                    reads=[self.msb], writes=[self.cur])
            for _ in range(7):
                self.V_("reduce_max", self.mx1.ap, self.cur.ap, AX.X,
                        reads=[self.cur], writes=[self.mx1])
                self.V_("tensor_scalar", self.mk32.ap, self.cur.ap, self.mx1.ap,
                        None, ALU.is_ge, reads=[self.cur, self.mx1],
                        writes=[self.mk32])
                self.V_("scalar_tensor_tensor", self.cur.ap, self.mk32.ap,
                        self.c_n1e30.ap, self.cur.ap, ALU.mult, ALU.add,
                        reads=[self.mk32, self.cur, self.c_n1e30],
                        writes=[self.cur])
            self.V_("reduce_max", self.th8.ap, self.cur.ap, AX.X,
                    reads=[self.cur], writes=[self.th8])
            self.V_("tensor_scalar", self.selm.ap, self.msb.ap, self.th8.ap,
                    None, ALU.is_ge, reads=[self.msb, self.th8], writes=[self.selm])
            self.V_("tensor_tensor", self.wsl.ap, self.selm.ap, self.scrs.ap,
                    ALU.mult, reads=[self.selm, self.scrs], writes=[self.wsl])
            self.V_("reduce_sum", self.s8.ap, self.wsl.ap, AX.X,
                    reads=[self.wsl], writes=[self.s8])
            self.V_("reciprocal", self.rs8.ap, self.s8.ap,
                    reads=[self.s8], writes=[self.rs8])
            self.V_("tensor_scalar", self.rs25.ap, self.rs8.ap, 2.5, None,
                    ALU.mult, reads=[self.rs8], writes=[self.rs25])
            self.V_("tensor_scalar", self.cw.ap[:, t, :], self.wsl.ap,
                    self.rs25.ap, None, ALU.mult,
                    reads=[self.wsl, self.rs25], writes=[self.cw])

    def load_slab(self, lane, slab, blk):
        for j in range(4):
            t = blk * 4 + j
            self.dma(lane, slab.ap[:, :, j * P:(j + 1) * P], self.ln2T_s[t],
                     reads=[("ln2T_s", t)], writes=[slab])

    def b1_blk(self, blk):
        slab = self.slab[blk % 2]
        self.load_slab("slab%d" % (blk % 2), slab, blk)
        for lc in range(8):
            bank = lc % 4
            psb = self.PSR("A", bank * 512, (bank + 1) * 512)
            for kc in range(NKC):
                self.T_("matmul", self.psA[:, bank * 512:(bank + 1) * 512],
                        self.fc1T.ap[:, kc, lc * P:(lc + 1) * P],
                        slab.ap[:, kc, :],
                        start=(kc == 0), stop=(kc == NKC - 1),
                        reads=[self.fc1T, slab], writes=[psb])
            self.act(self.x_latT.ap[:, lc, blk * 512:(blk + 1) * 512],
                     self.psA[:, bank * 512:(bank + 1) * 512], ACTF.Copy,
                     reads=[psb], writes=[self.x_latT])

    def b2(self):
        NT = self.NT
        nblks = NT // 4
        for e in range(NE):
            w1 = self.w1b[e % 2]
            w2 = self.w2b[e % 2]
            for lc in range(8):
                self.dma("w1b%d" % (e % 2), w1.ap[:, lc, :],
                         self.w1Td[e, lc * P:(lc + 1) * P, :],
                         reads=[w1], writes=[w1])
            for ic in range(8):
                self.dma("w2b%d" % (e % 2), w2.ap[:, ic, :],
                         self.w2Td[e, ic * P:(ic + 1) * P, :],
                         reads=[w2], writes=[w2])
            for blk in range(nblks):
                toff = blk * 512
                for ic in range(8):
                    bank = ic % 4
                    psb = self.PSR("A", bank * 512, (bank + 1) * 512)
                    for lc in range(8):
                        self.T_("matmul",
                                self.psA[:, bank * 512:(bank + 1) * 512],
                                w1.ap[:, lc, ic * P:(ic + 1) * P],
                                self.x_latT.ap[:, lc, toff:toff + 512],
                                start=(lc == 0), stop=(lc == 7),
                                reads=[w1, self.x_latT], writes=[psb])
                    rt = self.rtmp2[ic % 2]
                    self.V_("tensor_scalar", rt.ap,
                            self.psA[:, bank * 512:(bank + 1) * 512],
                            0.0, None, ALU.max, reads=[psb], writes=[rt])
                    self.A_("activation", self.h_all.ap[:, ic, :], rt.ap,
                            ACTF.Square, reads=[rt], writes=[self.h_all])
                for lh in range(2):
                    for tt in range(4):
                        psb = self.PSR("B", tt * 512, (tt + 1) * 512)
                        for ic in range(8):
                            self.T_("matmul",
                                    self.psB[:, tt * 512:(tt + 1) * 512],
                                    self.h_all.ap[:, ic, tt * P:(tt + 1) * P],
                                    w2.ap[:, ic, lh * 512:(lh + 1) * 512],
                                    start=(ic == 0), stop=(ic == 7),
                                    reads=[self.h_all, w2], writes=[psb])
                        tg = blk * 4 + tt
                        cwap = self.cw.ap[:, tg, e:e + 1]
                        ydst = self.y_lat.ap[:, tg, lh * 512:(lh + 1) * 512]
                        if e == 0:
                            self.V_("tensor_scalar", ydst,
                                    self.psB[:, tt * 512:(tt + 1) * 512],
                                    cwap, None, ALU.mult,
                                    reads=[psb, self.cw], writes=[self.y_lat])
                        else:
                            self.V_("scalar_tensor_tensor", ydst,
                                    self.psB[:, tt * 512:(tt + 1) * 512],
                                    cwap, ydst, ALU.mult, ALU.add,
                                    reads=[psb, self.cw, self.y_lat],
                                    writes=[self.y_lat])
        for tg in range(NT):
            self.V_("tensor_copy", self.ylb.ap, self.y_lat.ap[:, tg, :],
                    reads=[self.y_lat], writes=[self.ylb])
            self.transpose("trpY", self.ylT.ap, self.ylb.ap,
                           reads=[self.ylb, self.ylT], writes=[self.ylT])
            self.dma("sylT", self.ylT_s[:, :, tg * P:(tg + 1) * P], self.ylT.ap,
                     reads=[self.ylT], writes=[("ylT_s", tg)])

    def b3(self):
        NSLAB = self.T // 512
        for blk in range(NSLAB):
            slab = self.slab3
            self.load_slab("slab3", slab, blk)
            self.dma("yl3", self.yl3.ap,
                     self.ylT_s[:, :, blk * 512:(blk + 1) * 512],
                     reads=[self.yl3] + [("ylT_s", blk * 4 + j) for j in range(4)],
                     writes=[self.yl3])
            for sig in range(5):
                if sig < 4:
                    for kc in range(NKC):
                        self.dma("upw", self.upTb.ap[:, kc, :],
                                 self.upTd[kc * P:(kc + 1) * P,
                                           sig * 1024:(sig + 1) * 1024],
                                 reads=[self.upTb], writes=[self.upTb])
                    for sc in range(8):
                        self.dma("dnw", self.dnb.ap[:, sc, :],
                                 self.dfT[sig * 1024 + sc * P:
                                          sig * 1024 + (sc + 1) * P, :],
                                 reads=[self.dnb], writes=[self.dnb])
                    for si in range(8):
                        bank = si % 4
                        psb = self.PSR("A", bank * 512, (bank + 1) * 512)
                        for kc in range(NKC):
                            self.T_("matmul",
                                    self.psA[:, bank * 512:(bank + 1) * 512],
                                    self.upTb.ap[:, kc, si * P:(si + 1) * P],
                                    slab.ap[:, kc, :],
                                    start=(kc == 0), stop=(kc == NKC - 1),
                                    reads=[self.upTb, slab], writes=[psb])
                        rt = self.rtmp3[si % 2]
                        self.V_("tensor_scalar", rt.ap,
                                self.psA[:, bank * 512:(bank + 1) * 512],
                                0.0, None, ALU.max, reads=[psb], writes=[rt])
                        self.A_("activation", self.sTs.ap[:, si, :], rt.ap,
                                ACTF.Square, reads=[rt], writes=[self.sTs])
                else:
                    for sc in range(8):
                        self.dma("dnw", self.dnb.ap[:, sc, :],
                                 self.dfT[SI + sc * P:SI + (sc + 1) * P, :],
                                 reads=[self.dnb], writes=[self.dnb])
                for t in range(4):
                    tg = blk * 4 + t
                    psb = self.PSR("B", 0, 2048)
                    for sc in range(8):
                        if sig < 4:
                            lhs = self.sTs.ap[:, sc, t * P:(t + 1) * P]
                            lrd = self.sTs
                        else:
                            lhs = self.yl3.ap[:, sc, t * P:(t + 1) * P]
                            lrd = self.yl3
                        for co in range(4):
                            self.T_("matmul", self.psB[:, co * 512:(co + 1) * 512],
                                    lhs, self.dnb.ap[:, sc, co * 512:(co + 1) * 512],
                                    start=(sc == 0), stop=(sc == 7),
                                    reads=[lrd, self.dnb],
                                    writes=[self.PSR("B", co * 512, (co + 1) * 512)])
                    rdst = self.rout.ap[:, t, :]
                    if sig == 0:
                        self.V_("tensor_copy", rdst, self.psB[:, :],
                                reads=[psb], writes=[self.rout])
                    else:
                        self.V_("tensor_tensor", rdst, rdst, self.psB[:, :],
                                ALU.add, reads=[psb, self.rout], writes=[self.rout])
            for t in range(4):
                tg = blk * 4 + t
                rsrc = self.rout.ap[:, t, :]
                self.bn(self.rout, rsrc)
                self.ln_rstd(self.mv.ap[:, 1:2], self.c_eps5, self.rstd, [self.mv])
                self.dma("x1l3", self.x1l3.ap, self.x1_s[tg],
                         reads=[("x1_s", tg)], writes=[self.x1l3])
                self.V_("tensor_scalar", self.outb.ap, rsrc, self.mv.ap[:, 0:1],
                        self.rstd.ap, ALU.subtract, ALU.mult,
                        reads=[self.rout, self.mv, self.rstd], writes=[self.outb])
                self.V_("tensor_tensor", self.outb.ap, self.outb.ap, self.x1l3.ap,
                        ALU.add, reads=[self.outb, self.x1l3], writes=[self.outb])
                self.dma("sout", self.out[tg * P:(tg + 1) * P, :], self.outb.ap,
                         reads=[self.outb], writes=[("out", tg)])

    def build_all(self):
        self.declare()
        self.const_loads()
        self.a0()
        self.tern(self.wqT, self.mclip_q, CQ)
        self.a1("q")
        self.tern(self.wvT, self.mclip_v, CK)
        self.a1("v")
        self.tern(self.wkT, self.mclip_k, CK)
        self.a1("k")
        self.a2()
        self.tern(self.woT, self.mclip_o, CK)
        self.a3()
        self.b0()
        self.b2()
        self.b3()
        self.p.finish()


LANES = {}
for _l in (["xt0", "xt1", "ld0", "ld1", "ks0", "ks1", "vsl", "x1ld0", "x1ld1",
            "slab0", "slab1", "slab3", "x1l3", "sxqT_s", "sqrT", "syqT_s",
            "sx1", "sln2T", "sout", "sylT", "yl3", "trp0", "trp1", "trpP", "trpY"]
           + ["skr%d" % h for h in range(16)] + ["sv%d" % g for g in range(4)]):
    LANES[_l] = "S"
for _l in ("cb", "ones", "gbias", "cc", "ss", "ch0"):
    LANES[_l] = "A"
LANES["ch1"] = "G"
for _l in ("gw", "fc1", "w1b0", "w1b1", "w2b0", "w2b1", "upw", "dnw"):
    LANES[_l] = "G"


def build_nc(NT=16):
    nc = bass.Bass("TRN2", target_bir_lowering=False)
    from contextlib import ExitStack
    with ExitStack() as st:
        sems = {}
        for name in ("T", "V", "A", "G"):
            sems[name] = st.enter_context(nc.semaphore("sem" + name))
        for name in LANES:
            sems[name] = st.enter_context(nc.semaphore("q" + name))
        prog = Prog(sems, LANES)
        b = Builder(nc, prog, NT)
        b.build_all()
        block = st.enter_context(nc.Block())
        prog.realize(block)
    return nc


def _host_prep(L0_attn_Wq, L0_attn_Wk, L0_attn_Wv, L0_attn_Wo, L0_ffn_gate_w,
               L0_ffn_gate_bias, L0_ffn_fc1_w, L0_ffn_fc2_w, L0_ffn_w1,
               L0_ffn_w2, L0_ffn_shared_up_w, L0_ffn_shared_down_w, T=2048):
    cont = np.ascontiguousarray
    perm = np.concatenate([
        np.concatenate([np.arange(h * 128, (h + 1) * 128, 2),
                        np.arange(h * 128 + 1, (h + 1) * 128, 2)])
        for h in range(16)])
    shared = {
        "wqT": cont(L0_attn_Wq[perm].T),
        "wkT": cont(L0_attn_Wk[perm].T),
        "wvT": cont(L0_attn_Wv.T),
        "woT": cont(L0_attn_Wo.T),
        "gwT": cont(L0_ffn_gate_w.T),
        "gbiasd": cont(np.tile(L0_ffn_gate_bias[None, :].astype(np.float32),
                               (128, 1))),
        "fc1Td": cont(L0_ffn_fc1_w.T),
        "dfT": cont(np.concatenate([L0_ffn_shared_down_w.T,
                                    L0_ffn_fc2_w.T], axis=0)),
        "w1Td": cont(L0_ffn_w1.transpose(0, 2, 1)),
        "w2Td": cont(L0_ffn_w2.transpose(0, 2, 1)),
        "upTd": cont(L0_ffn_shared_up_w.T),
    }
    import jax
    import jax.numpy as jnp
    with jax.default_device(jax.devices("cpu")[0]):
        freqs = 1.0 / (10000.0 ** (jnp.arange(0, 128, 2, dtype=jnp.float32) / 128))
        ang = jnp.arange(T, dtype=jnp.float32)[:, None] * freqs[None, :]
        cos = np.asarray(jnp.cos(ang), dtype=np.float32)
        sin = np.asarray(jnp.sin(ang), dtype=np.float32)
    shared["cc"] = cont(np.concatenate([cos, cos], axis=1))
    shared["ss"] = cont(np.concatenate([-sin, sin], axis=1))
    ii = np.arange(128)
    shared["cbm"] = np.where(ii[:, None] >= ii[None, :], 0.0,
                             -1e30).astype(np.float32)
    shared["onesd"] = np.ones((128, 128), np.float32)
    return shared


def kernel(L0_attn_x, L0_attn_Wq, L0_attn_Wk, L0_attn_Wv, L0_attn_Wo,
           L0_ffn_gate_w, L0_ffn_gate_bias, L0_ffn_fc1_w, L0_ffn_fc2_w,
           L0_ffn_w1, L0_ffn_w2, L0_ffn_shared_up_w, L0_ffn_shared_down_w):
    shared = _host_prep(L0_attn_Wq, L0_attn_Wk, L0_attn_Wv, L0_attn_Wo,
                        L0_ffn_gate_w, L0_ffn_gate_bias, L0_ffn_fc1_w,
                        L0_ffn_fc2_w, L0_ffn_w1, L0_ffn_w2,
                        L0_ffn_shared_up_w, L0_ffn_shared_down_w)
    nc = build_nc(NT=16)
    in_maps = []
    for c in range(8):
        m = dict(shared)
        m["x"] = np.ascontiguousarray(L0_attn_x[c])
        in_maps.append(m)
    res = run_bass_kernel_spmd(nc, in_maps, core_ids=list(range(8)))
    return np.stack([res.results[c]["out"] for c in range(8)], axis=0)


# version 11
# speedup vs baseline: 1.7636x; 1.0052x over previous
import numpy as np
import ml_dtypes
import concourse.bass as bass
import concourse.mybir as mybir
from concourse.bass_utils import run_bass_kernel_spmd

F32 = mybir.dt.float32
BF16 = mybir.dt.bfloat16
U8 = mybir.dt.uint8
AX = mybir.AxisListType
ALU = mybir.AluOpType
ACTF = mybir.ActivationFunctionType

P = 128
C = 2048
H = 16
D = 128
NKC = C // P
NE = 32
LAT = 1024
INTER = 1024
SI = 4096
KB = 1024
PGSZ = 2048

CQ = 1.0 / (127.0 * float(np.sqrt(np.float64(128.0))))
CK = 1.0 / 127.0
INV_N = 1.0 / (2048.0 * 2048.0)
M32 = float(2 ** 23 * 1.5)
M64 = float(2 ** 52 * 1.5)


class Buf:
    __slots__ = ("ap", "keys")

    def __init__(self, ap, keys):
        self.ap = ap
        self.keys = frozenset(keys)


class Prog:
    ENGS = ("T", "V", "A", "G", "S")

    def __init__(self, sems, lane_queue):
        self.sems = sems
        self.lane_queue = lane_queue
        self.cnt = {k: 0 for k in sems}
        self.steps = {e: [] for e in self.ENGS}
        self.seen = {e: {} for e in self.ENGS}
        self.pages = {}
        self.dirty = {e: set() for e in self.ENGS}
        self.rdirty = {e: set() for e in self.ENGS}

    def _rec(self, k):
        r = self.pages.get(k)
        if r is None:
            r = [None, []]
            self.pages[k] = r
        return r

    @staticmethod
    def _keys(bufs):
        s = set()
        for b in bufs:
            s |= b.keys if isinstance(b, Buf) else {b}
        return s

    def op(self, eng, fn, reads=(), writes=(), lane=None):
        rk = self._keys(reads)
        wk = self._keys(writes)
        waits = []
        seen = self.seen[eng]

        def need(sem, val, agent):
            if agent != eng and agent in self.lane_queue:
                val = self.cnt[agent]
            if val > 0 and seen.get(sem, 0) < val:
                waits.append((self.sems[sem], val))
                seen[sem] = val

        drain = False
        for k in rk | wk:
            rec = self.pages.get(k)
            if rec is None:
                continue
            w = rec[0]
            if w is not None:
                sem, val, agent = w
                if agent == eng:
                    if k in self.dirty[eng]:
                        drain = True
                else:
                    need(sem, val, agent)
            if k in wk:
                for sem, val, agent in rec[1]:
                    if agent == eng:
                        if k in self.rdirty[eng]:
                            drain = True
                    else:
                        need(sem, val, agent)
        if eng == "T":
            drain = False

        if lane is None:
            sem_name, inc = eng, 1
            agent = eng
        else:
            sem_name, inc = lane, 16
            agent = lane
        self.cnt[sem_name] += inc
        val = self.cnt[sem_name]

        if drain:
            self.dirty[eng].clear()
            self.rdirty[eng].clear()
        for k in wk:
            rec = self._rec(k)
            rec[0] = (sem_name, val, agent)
            rec[1] = []
        for k in rk:
            self._rec(k)[1].append((sem_name, val, agent))
        if lane is None:
            self.dirty[eng] |= wk
            self.rdirty[eng] |= rk
        self.steps[eng].append((tuple(waits), drain, fn, self.sems[sem_name], inc))
        return val

    def finish(self):
        waits = [(self.sems[k], v) for k, v in self.cnt.items() if v > 0]
        self.steps["S"].append((tuple(waits), False, None, None, 0))

    def realize(self, block):
        names = {"T": "tensor", "V": "vector", "A": "scalar", "G": "gpsimd", "S": "sync"}
        for e in self.ENGS:
            steps = self.steps[e]

            def mk(steps=steps):
                def body(eng):
                    for waits, drain, fn, sem, inc in steps:
                        for s, v in waits:
                            eng.wait_ge(s, v)
                        if drain:
                            eng.drain()
                        if fn is not None:
                            inst = fn(eng)
                            if sem is not None:
                                inst.then_inc(sem, inc)
                return body
            getattr(block, names[e])(mk())


class Builder:
    def __init__(self, nc, prog, NT):
        self.nc = nc
        self.p = prog
        self.NT = NT
        self.T = NT * P

    def T_(self, fname, *args, reads=(), writes=(), **kw):
        self.p.op("T", (lambda e, a=args, k=kw, f=fname: getattr(e, f)(*a, **k)),
                  reads, writes)

    def V_(self, fname, *args, reads=(), writes=(), **kw):
        self.p.op("V", (lambda e, a=args, k=kw, f=fname: getattr(e, f)(*a, **k)),
                  reads, writes)

    def A_(self, fname, *args, reads=(), writes=(), **kw):
        self.p.op("A", (lambda e, a=args, k=kw, f=fname: getattr(e, f)(*a, **k)),
                  reads, writes)

    def dma(self, lane, out_ap, in_ap, reads=(), writes=()):
        self.p.op(self.p.lane_queue[lane],
                  (lambda e, o=out_ap, i=in_ap: e.dma_start(out=o, in_=i)),
                  reads, writes, lane=lane)

    def transpose(self, lane, out_ap, in_ap, reads=(), writes=()):
        self.p.op("S", (lambda e, o=out_ap, i=in_ap: e.dma_start_transpose(o, i)),
                  reads, writes, lane=lane)

    def AR(self, off_kb, shape, dtype):
        nel = int(np.prod(shape[1:]))
        nby = nel * mybir.dt.size(dtype)
        if dtype == BF16:
            arena, tag, epk = self.arena16, "A6", 512
        else:
            arena, tag, epk = self.arena32, "A2", 256
        ap = arena[:, off_kb * epk: off_kb * epk + nel]
        if len(shape) == 3:
            ap = ap.rearrange("p (a b) -> p a b", b=shape[2])
        keys = {(tag, g) for g in range(off_kb * KB // PGSZ,
                                        (off_kb * KB + nby + PGSZ - 1) // PGSZ)}
        return Buf(ap, keys)

    def SB(self, name, shape, dtype=F32, tdest=False):
        h = self.nc.alloc_sbuf_tensor(name, shape, dtype, is_transpose_dest=tdest)
        return Buf(h[:], {name})

    def PSR(self, which, lo, hi):
        t = self.psA if which == "A" else self.psB
        keys = {("ps" + which, b) for b in range(lo // 512, (hi + 511) // 512)}
        return Buf(t[:, lo:hi], keys)

    def declare(self):
        nc = self.nc
        NT, T = self.NT, self.T
        dr = lambda n, s, dt=F32, kind="ExternalInput": nc.dram_tensor(n, s, dt, kind=kind).ap()
        self.x = dr("x", [T, C])
        self.wqT = dr("wqT", [C, C]); self.wkT = dr("wkT", [C, C])
        self.wvT = dr("wvT", [C, C]); self.woT = dr("woT", [C, C])
        self.cc = dr("cc", [T, D]); self.ss = dr("ss", [T, D])
        self.cbm = dr("cbm", [P, P]); self.onesd = dr("onesd", [P, P])
        self.gbiasd = dr("gbiasd", [P, NE])
        self.gwT = dr("gwT", [C, NE])
        self.fc1Td = dr("fc1Td", [C, LAT])
        self.dfT = dr("dfT", [SI + LAT, C])
        self.w1Td = dr("w1Td", [NE, LAT, INTER])
        self.w2Td = dr("w2Td", [NE, INTER, LAT])
        self.upTd = dr("upTd", [C, SI])
        self.out = dr("out", [T, C], F32, kind="ExternalOutput")
        idr = lambda n, s, dt: nc.dram_tensor(n, s, dt, kind="Internal").ap()
        self.xqT_s = idr("xqT_s", [NT, P, NKC, P], BF16)
        self.qrT_s = idr("qrT_s", [NT, P, NKC, P], BF16)
        self.krT_s = idr("krT_s", [H, P, T], BF16)
        self.v_s = idr("v_s", [4, T, 512], BF16)
        self.yqT_s = idr("yqT_s", [NT, P, NKC, P], BF16)
        self.x1_s = idr("x1_s", [NT, P, C], F32)
        self.ln2T_s = idr("ln2T_s", [NT, P, NKC, P], BF16)
        self.ylT_s = idr("ylT_s", [P, 8, T], BF16)

        self.psA = nc.alloc_psum_tensor("psA", [P, 2048], F32)
        self.psB = nc.alloc_psum_tensor("psB", [P, 2048], F32)

        S = self.SB
        self.tr = [S("tr0", [P, NKC, P], BF16, tdest=True),
                   S("tr1", [P, NKC, P], BF16, tdest=True)]
        self.pT = S("pT", [P, NT, P], BF16, tdest=True)
        self.ylT = S("ylT", [P, 8, P], BF16, tdest=True)
        self.cb = S("cb", [P, P]); self.ones = S("ones", [P, P])
        self.cct = S("cct", [P, D]); self.sst = S("sst", [P, D])
        self.gbias = S("gbias", [P, NE])
        self.gw_sb = S("gw_sb", [P, NKC, NE], BF16)
        self.samc = S("samc", [P, NT]); self.samc_y = S("samc_y", [P, NT])
        self.cw = S("cw", [P, NT, NE])
        self.bst = S("bst", [P, 4, 6]); self.mv = S("mv", [P, 2])
        self.cs16 = S("cs16", [P, NKC]); self.cstot = S("cstot", [P, 1])
        for n in ("mclip_q", "mclip_k", "mclip_v", "mclip_o", "thr", "nthr",
                  "lnv", "rstd", "rsq", "bq", "rr", "sfold", "am", "amc",
                  "rmc", "qsc", "qsc2", "sc1", "s2t", "vv", "gv",
                  "nm", "den", "rden", "ss2v",
                  "g1v", "g2v", "m1g", "m2g", "th8", "s8", "rs8", "mx1",
                  "nm2", "den2", "rden2"):
            setattr(self, n, S(n, [P, 1]))
        self.nmp = [self.nm, self.nm2]
        self.denp = [self.den, self.den2]
        self.rdenp = [self.rden, self.rden2]
        self.c_eps5 = S("c_eps5", [P, 1]); self.c_eps6 = S("c_eps6", [P, 1])
        self.c_2565 = S("c_2565", [P, 1])
        self.c_m256 = S("c_m256", [P, 1]); self.c_n1e30 = S("c_n1e30", [P, 1])
        self.c_one = S("c_one", [P, 1]); self.gdisc = S("gdisc", [P, 1])
        self.c_zero = S("c_zero", [P, 1])
        self.mqs = S("mqs", [P, 1]); self.rs25 = S("rs25", [P, 1])
        self.et = S("et", [P, NE]); self.d1 = S("d1", [P, NE])
        self.scrs = S("scrs", [P, NE]); self.sbv = S("sbv", [P, NE])
        self.msb = S("msb", [P, NE]); self.cur = S("cur", [P, NE])
        self.mk32 = S("mk32", [P, NE]); self.wsl = S("wsl", [P, NE])
        self.selm = S("selm", [P, NE])
        self.gsv = S("gsv", [P, 4]); self.mask4 = S("mask4", [P, 4])
        self.t4b = S("t4b", [P, 4]); self.madd4 = S("madd4", [P, 4])
        self.msk8 = S("msk8", [P, 8]); self.t8b = S("t8b", [P, 8])

        self.arena16 = nc.alloc_sbuf_tensor("arena16", [P, 108 * 512], BF16)
        self.arena32 = nc.alloc_sbuf_tensor("arena32", [P, 72 * 256], F32)
        A = self.AR
        self.wtern = A(0, [P, NKC, C], BF16)
        self.xqb = A(64, [P, C], BF16)
        self.vbb = A(68, [P, C], BF16)
        self.ld = [A(72, [P, NKC, P], BF16), A(76, [P, NKC, P], BF16)]
        self.pb = A(80, [P, 2048], BF16)
        self.ks = [A(84, [P, 2048], BF16), A(88, [P, 2048], BF16)]
        self.vsh = [A(92, [P, NT, 256], BF16),
                    A(100, [P, NT, 256], BF16)]
        self.pbp = [self.pb, self.vbb]
        self.chunk = [A(0, [P, C], F32), A(8, [P, C], F32)]
        self.xt = [A(16, [P, C], F32), A(24, [P, C], F32)]
        self.xn = A(32, [P, C], F32)
        self.qf = A(40, [P, C], F32)
        self.uu = A(48, [P, C], F32)
        self.trt = A(56, [P, C], F32)
        self.y_scr = [A(64, [P, C], F32), A(64, [P, C], F32)]
        self.fc1T = A(76, [P, NKC, LAT], BF16)
        self.x_latT = A(0, [P, 8, T], BF16)
        self.slab = [A(32, [P, NKC, 512], BF16), A(48, [P, NKC, 512], BF16)]
        self.w1b = [A(32, [P, 8, INTER], BF16), A(48, [P, 8, INTER], BF16)]
        self.w2b = [A(64, [P, 8, LAT], BF16), A(80, [P, 8, LAT], BF16)]
        self.h_all = A(96, [P, 8, 512], BF16)
        self.ylb = A(104, [P, LAT], BF16)
        self.dnb = A(0, [P, 8, C], BF16)
        self.slab3 = A(32, [P, NKC, 512], BF16)
        self.sTs = A(48, [P, 8, 512], BF16)
        self.upTb = A(56, [P, NKC, 1024], BF16)
        self.yl3 = A(88, [P, 8, 512], BF16)
        self.x1ld = [A(0, [P, C], F32), A(8, [P, C], F32)]
        self.y_lat = A(0, [P, NT, LAT], F32)
        self.rtmp2 = [A(64, [P, 512], F32), A(66, [P, 512], F32)]
        self.rout = A(0, [P, 4, C], F32)
        self.x1l3 = A(32, [P, C], F32)
        self.outb = A(40, [P, C], F32)
        self.rtmp3 = [A(48, [P, 512], F32), A(50, [P, 512], F32)]

    def act(self, out, in_, func, bias=0.0, scale=1.0, accum=None,
            reads=(), writes=()):
        kw = dict(bias=bias, scale=scale)
        if accum is not None:
            kw["accum_out"] = accum
        self.A_("activation", out, in_, func, reads=reads, writes=writes, **kw)

    def bn(self, src_buf, src_ap2048, extra_reads=()):
        for i in range(4):
            self.V_("bn_stats", self.bst.ap[:, i, :], src_ap2048[:, i * 512:(i + 1) * 512],
                    reads=[src_buf, *extra_reads], writes=[self.bst])
        self.V_("bn_aggr", self.mv.ap, self.bst.ap,
                reads=[self.bst], writes=[self.mv])

    def ln_rstd(self, var_ap, eps_buf, out_buf, reads):
        self.act(self.lnv.ap, var_ap, ACTF.Ln, bias=eps_buf.ap, scale=1.0,
                 reads=list(reads) + [eps_buf], writes=[self.lnv])
        self.act(out_buf.ap, self.lnv.ap, ACTF.Exp, bias=0.0, scale=-0.5,
                 reads=[self.lnv], writes=[out_buf])

    def const_loads(self):
        self.V_("memset", self.c_eps5.ap, 1e-5, writes=[self.c_eps5])
        self.V_("memset", self.c_eps6.ap, 1e-6, writes=[self.c_eps6])
        self.V_("memset", self.c_2565.ap, 256.5, writes=[self.c_2565])
        self.V_("memset", self.c_m256.ap, -256.0, writes=[self.c_m256])
        self.V_("memset", self.c_n1e30.ap, -1e30, writes=[self.c_n1e30])
        self.V_("memset", self.c_one.ap, 1.0, writes=[self.c_one])
        self.V_("memset", self.c_zero.ap, 0.0, writes=[self.c_zero])
        self.V_("tensor_single_scalar", self.gdisc.ap, self.c_one.ap, M64,
                ALU.add, reads=[self.c_one], writes=[self.gdisc])
        self.V_("tensor_single_scalar", self.gdisc.ap, self.gdisc.ap, M64,
                ALU.subtract, reads=[self.gdisc], writes=[self.gdisc])
        self.dma("cb", self.cb.ap, self.cbm, writes=[self.cb])
        self.dma("ones", self.ones.ap, self.onesd, writes=[self.ones])
        self.dma("gbias", self.gbias.ap, self.gbiasd, writes=[self.gbias])
        self.dma("gw", self.gw_sb.ap,
                 self.gwT.rearrange("(a p) n -> p a n", p=P), writes=[self.gw_sb])
        from concourse.hw_specs import get_activation_tables
        idx = None
        for i, funcs in enumerate(get_activation_tables(self.nc.m.arch).values()):
            if ACTF.Exp in funcs and ACTF.Ln in funcs:
                idx = i
                break
        if idx is not None:
            nc = self.nc
            def fn(e, idx=idx, nc=nc):
                return e.add_instruction(mybir.InstLoadActFuncSet(
                    name=nc.get_next_instruction_name(),
                    act_func_set_id=idx, ins=[], outs=[]))
            self.p.op("A", fn, (), ())

    def tern(self, wdram, mclip, cmul):
        NCH = NKC
        for kc in range(NCH):
            ch = self.chunk[kc % 2]
            self.dma("ch%d" % (kc % 2), ch.ap, wdram[kc * P:(kc + 1) * P, :],
                     reads=[ch], writes=[ch])
            self.V_("reduce_sum", self.cs16.ap[:, kc:kc + 1], ch.ap, AX.X,
                    apply_absolute_value=True, reads=[ch], writes=[self.cs16])
        self.V_("reduce_sum", self.cstot.ap, self.cs16.ap, AX.X,
                reads=[self.cs16], writes=[self.cstot])
        ps1 = self.PSR("A", 0, 512)
        self.T_("matmul", self.psA[:, 0:1], self.ones.ap, self.cstot.ap,
                start=True, stop=True, reads=[self.ones, self.cstot], writes=[ps1])
        self.V_("tensor_scalar", mclip.ap, self.psA[:, 0:1], float(INV_N), 1e-5,
                ALU.mult, ALU.max, reads=[ps1], writes=[mclip])
        self.V_("tensor_scalar", self.thr.ap, mclip.ap, 0.5, None, ALU.mult,
                reads=[mclip], writes=[self.thr])
        self.V_("tensor_scalar", self.nthr.ap, mclip.ap, -0.5, None, ALU.mult,
                reads=[mclip], writes=[self.nthr])
        self.V_("tensor_scalar", self.mqs.ap, mclip.ap, float(cmul), None,
                ALU.mult, reads=[mclip], writes=[self.mqs])
        for kc in range(NCH):
            ch = self.chunk[kc % 2]
            self.dma("ch%d" % (kc % 2), ch.ap, wdram[kc * P:(kc + 1) * P, :],
                     reads=[ch], writes=[ch])
            self.V_("tensor_scalar", self.qf.ap, ch.ap, self.nthr.ap, None,
                    ALU.is_lt, reads=[ch, self.nthr], writes=[self.qf])
            self.V_("scalar_tensor_tensor", self.wtern.ap[:, kc, :], ch.ap,
                    self.thr.ap, self.qf.ap, ALU.is_gt, ALU.subtract,
                    reads=[ch, self.thr, self.qf], writes=[self.wtern])

    def a0(self):
        for t in range(self.NT):
            xt = self.xt[t % 2]
            self.dma("xt%d" % (t % 2), xt.ap, self.x[t * P:(t + 1) * P, :],
                     reads=[xt], writes=[xt])
            self.bn(xt, xt.ap)
            self.ln_rstd(self.mv.ap[:, 1:2], self.c_eps5, self.rstd, [self.mv])
            self.V_("tensor_tensor", self.rsq.ap, self.rstd.ap, self.rstd.ap,
                    ALU.mult, reads=[self.rstd], writes=[self.rsq])
            self.V_("tensor_tensor", self.bq.ap, self.mv.ap[:, 1:2], self.rsq.ap,
                    ALU.mult, reads=[self.mv, self.rsq], writes=[self.bq])
            self.act(self.lnv.ap, self.bq.ap, ACTF.Ln, bias=self.c_eps6.ap,
                     reads=[self.bq, self.c_eps6], writes=[self.lnv])
            self.act(self.rr.ap, self.lnv.ap, ACTF.Exp, scale=-0.5,
                     reads=[self.lnv], writes=[self.rr])
            self.V_("tensor_tensor", self.sfold.ap, self.rstd.ap, self.rr.ap,
                    ALU.mult, reads=[self.rstd, self.rr], writes=[self.sfold])
            self.V_("tensor_scalar", self.xn.ap, xt.ap, self.mv.ap[:, 0:1],
                    self.sfold.ap, ALU.subtract, ALU.mult,
                    reads=[xt, self.mv, self.sfold], writes=[self.xn])
            self.quant_store(self.xn, self.samc, t, self.xqT_s, "xqT_s",
                             scale_pre=None)

    def quant_store(self, src, samc, t, dst_dram, dst_key, scale_pre):
        self.V_("reduce_max", self.am.ap, src.ap, AX.X, apply_absolute_value=True,
                reads=[src], writes=[self.am])
        if scale_pre is not None:
            self.V_("tensor_tensor", self.amc.ap, self.am.ap, scale_pre.ap,
                    ALU.mult, reads=[self.am, scale_pre], writes=[self.amc])
            amsrc = self.amc
        else:
            amsrc = self.am
        self.V_("tensor_single_scalar", samc.ap[:, t:t + 1], amsrc.ap, 1e-5,
                ALU.max, reads=[amsrc], writes=[samc])
        self.V_("reciprocal", self.rmc.ap, samc.ap[:, t:t + 1],
                reads=[samc], writes=[self.rmc])
        self.V_("tensor_scalar", self.qsc.ap, self.rmc.ap, 127.0, None, ALU.mult,
                reads=[self.rmc], writes=[self.qsc])
        if scale_pre is not None:
            self.V_("tensor_tensor", self.qsc2.ap, self.qsc.ap, scale_pre.ap,
                    ALU.mult, reads=[self.qsc, scale_pre], writes=[self.qsc2])
            qs = self.qsc2
        else:
            qs = self.qsc
        self.V_("tensor_scalar", self.qf.ap, src.ap, qs.ap, None, ALU.mult,
                reads=[src, qs], writes=[self.qf])
        self.V_("tensor_single_scalar", self.uu.ap, self.qf.ap, M32, ALU.add,
                reads=[self.qf], writes=[self.uu])
        self.V_("tensor_single_scalar", self.uu.ap, self.uu.ap, M32,
                ALU.subtract, reads=[self.uu], writes=[self.uu])
        self.V_("tensor_single_scalar", self.trt.ap, self.qf.ap, M64, ALU.add,
                reads=[self.qf], writes=[self.trt])
        self.V_("tensor_single_scalar", self.trt.ap, self.trt.ap, M64,
                ALU.subtract, reads=[self.trt], writes=[self.trt])
        self.V_("tensor_tensor", self.trt.ap, self.trt.ap, self.uu.ap,
                ALU.subtract, reads=[self.trt, self.uu], writes=[self.trt])
        self.V_("scalar_tensor_tensor", self.xqb.ap, self.trt.ap,
                self.gdisc.ap, self.uu.ap, ALU.mult, ALU.add,
                reads=[self.trt, self.uu, self.gdisc], writes=[self.xqb])
        tr = self.tr[t % 2]
        self.transpose("trp%d" % (t % 2), tr.ap, self.xqb.ap,
                       reads=[self.xqb, tr], writes=[tr])
        self.dma("s" + dst_key, dst_dram[t], tr.ap, reads=[tr],
                 writes=[(dst_key, t)])

    def a1(self, kind):
        NT = self.NT
        for t in range(NT):
            ld = self.ld[t % 2]
            self.dma("ld%d" % (t % 2), ld.ap, self.xqT_s[t],
                     reads=[ld, ("xqT_s", t)], writes=[ld])
            psn = "A" if t % 2 == 0 else "B"
            pst = self.psA if psn == "A" else self.psB
            psb = self.PSR(psn, 0, 2048)
            for kc in range(NKC):
                for n in range(4):
                    self.T_("matmul", pst[:, n * 512:(n + 1) * 512],
                            ld.ap[:, kc, :], self.wtern.ap[:, kc, n * 512:(n + 1) * 512],
                            start=(kc == 0), stop=(kc == NKC - 1),
                            reads=[ld, self.wtern], writes=[psb])
            self.V_("tensor_scalar", self.sc1.ap, self.samc.ap[:, t:t + 1],
                    self.mqs.ap, None, ALU.mult,
                    reads=[self.samc, self.mqs], writes=[self.sc1])
            if kind == "v":
                self.V_("tensor_scalar", self.vbb.ap, pst[:, :], self.sc1.ap,
                        None, ALU.mult, reads=[psb, self.sc1], writes=[self.vbb])
                for hg in range(4):
                    self.dma("sv%d" % hg, self.v_s[hg, t * P:(t + 1) * P, :],
                             self.vbb.ap[:, hg * 512:(hg + 1) * 512],
                             reads=[self.vbb], writes=[("v_s", hg, t)])
                continue
            self.V_("tensor_scalar", self.qf.ap, pst[:, :], self.sc1.ap, None,
                    ALU.mult, reads=[psb, self.sc1], writes=[self.qf])
            self.dma("cc", self.cct.ap, self.cc[t * P:(t + 1) * P, :],
                     reads=[self.cct], writes=[self.cct])
            self.dma("ss", self.sst.ap, self.ss[t * P:(t + 1) * P, :],
                     reads=[self.sst], writes=[self.sst])
            qf3 = self.qf.ap.rearrange("p (h d) -> p h d", d=D)
            tr3 = self.trt.ap.rearrange("p (h d) -> p h d", d=D)
            ccb = self.cct.ap.unsqueeze(1).broadcast_to([P, H, D])
            uu3 = self.uu.ap.rearrange("p (h d) -> p h d", d=D)
            self.V_("tensor_tensor", uu3, qf3, ccb, ALU.mult,
                    reads=[self.qf, self.cct], writes=[self.uu])
            sslo = self.sst.ap[:, 0:64].unsqueeze(1).broadcast_to([P, H, 64])
            sshi = self.sst.ap[:, 64:128].unsqueeze(1).broadcast_to([P, H, 64])
            self.V_("tensor_tensor", tr3[:, :, 0:64], qf3[:, :, 64:128], sslo,
                    ALU.mult, reads=[self.qf, self.sst], writes=[self.trt])
            self.V_("tensor_tensor", tr3[:, :, 64:128], qf3[:, :, 0:64], sshi,
                    ALU.mult, reads=[self.qf, self.sst], writes=[self.trt])
            self.V_("tensor_tensor", self.xqb.ap, self.uu.ap, self.trt.ap,
                    ALU.add, reads=[self.uu, self.trt], writes=[self.xqb])
            tr = self.tr[t % 2]
            self.transpose("trp%d" % (t % 2), tr.ap, self.xqb.ap,
                           reads=[self.xqb, tr], writes=[tr])
            if kind == "q":
                self.dma("sqrT", self.qrT_s[t], tr.ap, reads=[tr],
                         writes=[("qrT_s", t)])
            else:
                for h in range(H):
                    self.dma("skr%d" % h, self.krT_s[h, :, t * P:(t + 1) * P],
                             tr.ap[:, h, :], reads=[tr], writes=[("krT_s", h)])

    def a2(self):
        NT = self.NT
        for qt in range(NT):
            L = (qt + 1) * P
            ld = self.ld[qt % 2]
            self.dma("ld%d" % (qt % 2), ld.ap, self.qrT_s[qt],
                     reads=[ld, ("qrT_s", qt)], writes=[ld])
            ysc = self.y_scr[qt % 2]
            pTs = [self.pT, self.tr[qt % 2]]
            nch = (L + 511) // 512

            def qk(h):
                hg = h // 4
                if h % 2 == 0:
                    half = (h % 4) // 2
                    vb = self.vsh[half]
                    src = self.v_s[hg].rearrange("(k p) f -> p k f", p=P)
                    self.dma("vsl", vb.ap[:, 0:qt + 1, :],
                             src[:, 0:qt + 1, half * 256:(half + 1) * 256],
                             reads=[vb] + [("v_s", hg, tt) for tt in range(qt + 1)],
                             writes=[vb])
                ks = self.ks[h % 2]
                self.dma("ks%d" % (h % 2), ks.ap[:, 0:L], self.krT_s[h, :, 0:L],
                         reads=[ks, ("krT_s", h)], writes=[ks])
                Rn = "A" if h % 2 == 0 else "B"
                Rt = self.psA if Rn == "A" else self.psB
                for cc_ in range(nch):
                    n = min(512, L - cc_ * 512)
                    self.T_("matmul", Rt[:, cc_ * 512:cc_ * 512 + n],
                            ld.ap[:, h, :], ks.ap[:, cc_ * 512:cc_ * 512 + n],
                            start=True, stop=True,
                            reads=[ld, ks],
                            writes=[self.PSR(Rn, cc_ * 512, cc_ * 512 + n)])

            def smx(h):
                Rn = "A" if h % 2 == 0 else "B"
                Rt = self.psA if Rn == "A" else self.psB
                Rb = self.PSR(Rn, 0, L)
                nm = self.nmp[h % 2]; den = self.denp[h % 2]
                rden = self.rdenp[h % 2]; pb = self.pbp[h % 2]
                self.V_("tensor_tensor", Rt[:, qt * P:L], Rt[:, qt * P:L],
                        self.cb.ap, ALU.add,
                        reads=[self.PSR(Rn, qt * P, L), self.cb],
                        writes=[self.PSR(Rn, qt * P, L)])
                self.V_("reduce_max", nm.ap, Rt[:, 0:L], AX.X,
                        negate=True, reads=[Rb], writes=[nm])
                self.act(pb.ap[:, 0:L], Rt[:, 0:L], ACTF.Exp,
                         bias=nm.ap, scale=1.0, accum=den.ap,
                         reads=[Rb, nm], writes=[pb, den])
                self.V_("reciprocal", rden.ap, den.ap,
                        reads=[den], writes=[rden])
                pT = pTs[h % 2]
                self.transpose("trpP", pT.ap[:, 0:qt + 1, :], pb.ap[:, 0:L],
                               reads=[pb, pT], writes=[pT])

            def pv(h):
                h4 = h % 4
                Rn = "A" if h % 2 == 0 else "B"
                Yt = self.psA if Rn == "A" else self.psB
                Yb = self.PSR(Rn, 1920, 2048)
                pT = pTs[h % 2]; rden = self.rdenp[h % 2]
                vb = self.vsh[h4 // 2]
                for kt in range(qt + 1):
                    self.T_("matmul", Yt[:, 1920:2048], pT.ap[:, kt, :],
                            vb.ap[:, kt, (h4 % 2) * P:(h4 % 2 + 1) * P],
                            start=(kt == 0), stop=(kt == qt),
                            reads=[pT, vb], writes=[Yb])
                self.V_("tensor_scalar", ysc.ap[:, h * P:(h + 1) * P],
                        Yt[:, 1920:2048], rden.ap, None, ALU.mult,
                        reads=[Yb, rden], writes=[ysc])

            qk(0)
            smx(0)
            for h in range(16):
                if h + 1 < 16:
                    qk(h + 1)
                    smx(h + 1)
                pv(h)
            self.V_("tensor_tensor", self.xn.ap, ysc.ap, ysc.ap, ALU.mult,
                    reads=[ysc], writes=[self.xn])
            self.V_("reduce_sum", self.ss2v.ap, self.xn.ap, AX.X,
                    reads=[self.xn], writes=[self.ss2v])
            self.act(self.lnv.ap, self.ss2v.ap, ACTF.Ln, bias=self.c_eps6.ap,
                     scale=float(1.0 / C), reads=[self.ss2v, self.c_eps6],
                     writes=[self.lnv])
            self.act(self.rr.ap, self.lnv.ap, ACTF.Exp, scale=-0.5,
                     reads=[self.lnv], writes=[self.rr])
            self.quant_store(ysc, self.samc_y, qt, self.yqT_s, "yqT_s",
                             scale_pre=self.rr)

    def a3(self):
        NT = self.NT
        for t in range(NT):
            ld = self.ld[t % 2]
            self.dma("ld%d" % (t % 2), ld.ap, self.yqT_s[t],
                     reads=[ld, ("yqT_s", t)], writes=[ld])
            xt = self.xt[t % 2]
            self.dma("xt%d" % (t % 2), xt.ap, self.x[t * P:(t + 1) * P, :],
                     reads=[xt], writes=[xt])
            psn = "A" if t % 2 == 0 else "B"
            pst = self.psA if psn == "A" else self.psB
            psb = self.PSR(psn, 0, 2048)
            for kc in range(NKC):
                for n in range(4):
                    self.T_("matmul", pst[:, n * 512:(n + 1) * 512],
                            ld.ap[:, kc, :], self.wtern.ap[:, kc, n * 512:(n + 1) * 512],
                            start=(kc == 0), stop=(kc == NKC - 1),
                            reads=[ld, self.wtern], writes=[psb])
            self.V_("tensor_scalar", self.sc1.ap, self.samc_y.ap[:, t:t + 1],
                    self.mqs.ap, None, ALU.mult,
                    reads=[self.samc_y, self.mqs], writes=[self.sc1])
            self.bn(psb, pst)
            self.V_("tensor_tensor", self.s2t.ap, self.sc1.ap, self.sc1.ap,
                    ALU.mult, reads=[self.sc1], writes=[self.s2t])
            self.V_("tensor_tensor", self.vv.ap, self.mv.ap[:, 1:2], self.s2t.ap,
                    ALU.mult, reads=[self.mv, self.s2t], writes=[self.vv])
            self.ln_rstd(self.vv.ap, self.c_eps5, self.rstd, [self.vv])
            self.V_("tensor_tensor", self.gv.ap, self.sc1.ap, self.rstd.ap,
                    ALU.mult, reads=[self.sc1, self.rstd], writes=[self.gv])
            self.V_("tensor_scalar", self.xn.ap, pst[:, :], self.mv.ap[:, 0:1],
                    self.gv.ap, ALU.subtract, ALU.mult,
                    reads=[psb, self.mv, self.gv], writes=[self.xn])
            self.V_("tensor_tensor", self.qf.ap, self.xn.ap, xt.ap, ALU.add,
                    reads=[self.xn, xt], writes=[self.qf])
            self.dma("sx1", self.x1_s[t], self.qf.ap, reads=[self.qf],
                     writes=[("x1_s", t)])

    def b0(self):
        NT = self.NT
        for kc in range(NKC):
            self.dma("fc1", self.fc1T.ap[:, kc, :],
                     self.fc1Td[kc * P:(kc + 1) * P, :],
                     reads=[self.fc1T], writes=[self.fc1T])
        for t in range(NT):
            self.b0_tile(t)
            if t % 4 == 3:
                self.b1_blk(t // 4)

    def b0_tile(self, t):
        if True:
            x1 = self.x1ld[t % 2]
            self.dma("x1ld%d" % (t % 2), x1.ap, self.x1_s[t],
                     reads=[x1, ("x1_s", t)], writes=[x1])
            self.bn(x1, x1.ap)
            self.ln_rstd(self.mv.ap[:, 1:2], self.c_eps5, self.rstd, [self.mv])
            self.V_("tensor_scalar", self.xqb.ap, x1.ap, self.mv.ap[:, 0:1],
                    self.rstd.ap, ALU.subtract, ALU.mult,
                    reads=[x1, self.mv, self.rstd], writes=[self.xqb])
            tr = self.tr[t % 2]
            self.transpose("trp%d" % (t % 2), tr.ap, self.xqb.ap,
                           reads=[self.xqb, tr], writes=[tr])
            self.dma("sln2T", self.ln2T_s[t], tr.ap, reads=[tr],
                     writes=[("ln2T_s", t)])
            psg = self.PSR("B", 0, 512)
            for kc in range(NKC):
                self.T_("matmul", self.psB[:, 0:NE], tr.ap[:, kc, :],
                        self.gw_sb.ap[:, kc, :],
                        start=(kc == 0), stop=(kc == NKC - 1),
                        reads=[tr, self.gw_sb], writes=[psg])
            self.act(self.et.ap, self.psB[:, 0:NE], ACTF.Exp, scale=-1.0,
                     reads=[psg], writes=[self.et])
            self.V_("tensor_single_scalar", self.d1.ap, self.et.ap, 1.0, ALU.add,
                    reads=[self.et], writes=[self.d1])
            self.V_("reciprocal", self.scrs.ap, self.d1.ap,
                    reads=[self.d1], writes=[self.scrs])
            self.V_("tensor_tensor", self.sbv.ap, self.scrs.ap, self.gbias.ap,
                    ALU.add, reads=[self.scrs, self.gbias], writes=[self.sbv])
            for g in range(4):
                sg = self.sbv.ap[:, g * 8:(g + 1) * 8]
                self.V_("reduce_max", self.m1g.ap, sg, AX.X,
                        reads=[self.sbv], writes=[self.m1g])
                self.V_("tensor_scalar", self.msk8.ap, sg, self.m1g.ap, None,
                        ALU.is_ge, reads=[self.sbv, self.m1g], writes=[self.msk8])
                self.V_("scalar_tensor_tensor", self.t8b.ap, self.msk8.ap,
                        self.c_n1e30.ap, sg, ALU.mult, ALU.add,
                        reads=[self.msk8, self.sbv, self.c_n1e30],
                        writes=[self.t8b])
                self.V_("reduce_max", self.m2g.ap, self.t8b.ap, AX.X,
                        reads=[self.t8b], writes=[self.m2g])
                self.V_("tensor_tensor", self.gsv.ap[:, g:g + 1], self.m1g.ap,
                        self.m2g.ap, ALU.add,
                        reads=[self.m1g, self.m2g], writes=[self.gsv])
            self.V_("reduce_max", self.g1v.ap, self.gsv.ap, AX.X,
                    reads=[self.gsv], writes=[self.g1v])
            self.V_("tensor_scalar", self.mask4.ap, self.gsv.ap, self.g1v.ap,
                    None, ALU.is_ge, reads=[self.gsv, self.g1v], writes=[self.mask4])
            self.V_("scalar_tensor_tensor", self.t4b.ap, self.mask4.ap,
                    self.c_n1e30.ap, self.gsv.ap, ALU.mult, ALU.add,
                    reads=[self.mask4, self.gsv, self.c_n1e30],
                    writes=[self.t4b])
            self.V_("reduce_max", self.g2v.ap, self.t4b.ap, AX.X,
                    reads=[self.t4b], writes=[self.g2v])
            self.V_("tensor_scalar", self.mask4.ap, self.gsv.ap, self.g2v.ap,
                    None, ALU.is_ge, reads=[self.gsv, self.g2v], writes=[self.mask4])
            self.V_("tensor_scalar", self.madd4.ap, self.mask4.ap, -1.0, 1e30,
                    ALU.add, ALU.mult, reads=[self.mask4], writes=[self.madd4])
            sb3 = self.sbv.ap.rearrange("p (g e) -> p g e", e=8)
            msb3 = self.msb.ap.rearrange("p (g e) -> p g e", e=8)
            self.V_("tensor_tensor", msb3, sb3,
                    self.madd4.ap.unsqueeze(2).broadcast_to([P, 4, 8]), ALU.add,
                    reads=[self.sbv, self.madd4], writes=[self.msb])
            self.V_("tensor_copy", self.cur.ap, self.msb.ap,
                    reads=[self.msb], writes=[self.cur])
            for _ in range(7):
                self.V_("reduce_max", self.mx1.ap, self.cur.ap, AX.X,
                        reads=[self.cur], writes=[self.mx1])
                self.V_("tensor_scalar", self.mk32.ap, self.cur.ap, self.mx1.ap,
                        None, ALU.is_ge, reads=[self.cur, self.mx1],
                        writes=[self.mk32])
                self.V_("scalar_tensor_tensor", self.cur.ap, self.mk32.ap,
                        self.c_n1e30.ap, self.cur.ap, ALU.mult, ALU.add,
                        reads=[self.mk32, self.cur, self.c_n1e30],
                        writes=[self.cur])
            self.V_("reduce_max", self.th8.ap, self.cur.ap, AX.X,
                    reads=[self.cur], writes=[self.th8])
            self.V_("tensor_scalar", self.selm.ap, self.msb.ap, self.th8.ap,
                    None, ALU.is_ge, reads=[self.msb, self.th8], writes=[self.selm])
            self.V_("tensor_tensor", self.wsl.ap, self.selm.ap, self.scrs.ap,
                    ALU.mult, reads=[self.selm, self.scrs], writes=[self.wsl])
            self.V_("reduce_sum", self.s8.ap, self.wsl.ap, AX.X,
                    reads=[self.wsl], writes=[self.s8])
            self.V_("reciprocal", self.rs8.ap, self.s8.ap,
                    reads=[self.s8], writes=[self.rs8])
            self.V_("tensor_scalar", self.rs25.ap, self.rs8.ap, 2.5, None,
                    ALU.mult, reads=[self.rs8], writes=[self.rs25])
            self.V_("tensor_scalar", self.cw.ap[:, t, :], self.wsl.ap,
                    self.rs25.ap, None, ALU.mult,
                    reads=[self.wsl, self.rs25], writes=[self.cw])

    def load_slab(self, lane, slab, blk):
        for j in range(4):
            t = blk * 4 + j
            self.dma(lane, slab.ap[:, :, j * P:(j + 1) * P], self.ln2T_s[t],
                     reads=[("ln2T_s", t)], writes=[slab])

    def b1_blk(self, blk):
        slab = self.slab[blk % 2]
        self.load_slab("slab%d" % (blk % 2), slab, blk)
        for lc in range(8):
            bank = lc % 4
            psb = self.PSR("A", bank * 512, (bank + 1) * 512)
            for kc in range(NKC):
                self.T_("matmul", self.psA[:, bank * 512:(bank + 1) * 512],
                        self.fc1T.ap[:, kc, lc * P:(lc + 1) * P],
                        slab.ap[:, kc, :],
                        start=(kc == 0), stop=(kc == NKC - 1),
                        reads=[self.fc1T, slab], writes=[psb])
            self.act(self.x_latT.ap[:, lc, blk * 512:(blk + 1) * 512],
                     self.psA[:, bank * 512:(bank + 1) * 512], ACTF.Copy,
                     reads=[psb], writes=[self.x_latT])

    def b2(self):
        NT = self.NT
        nblks = NT // 4
        for e in range(NE):
            w1 = self.w1b[e % 2]
            w2 = self.w2b[e % 2]
            for lc in range(8):
                self.dma("w1b%d" % (e % 2), w1.ap[:, lc, :],
                         self.w1Td[e, lc * P:(lc + 1) * P, :],
                         reads=[w1], writes=[w1])
            for ic in range(8):
                self.dma("w2b%d" % (e % 2), w2.ap[:, ic, :],
                         self.w2Td[e, ic * P:(ic + 1) * P, :],
                         reads=[w2], writes=[w2])
            for blk in range(nblks):
                toff = blk * 512
                for ic in range(8):
                    bank = ic % 4
                    psb = self.PSR("A", bank * 512, (bank + 1) * 512)
                    for lc in range(8):
                        self.T_("matmul",
                                self.psA[:, bank * 512:(bank + 1) * 512],
                                w1.ap[:, lc, ic * P:(ic + 1) * P],
                                self.x_latT.ap[:, lc, toff:toff + 512],
                                start=(lc == 0), stop=(lc == 7),
                                reads=[w1, self.x_latT], writes=[psb])
                    rt = self.rtmp2[ic % 2]
                    self.V_("tensor_scalar", rt.ap,
                            self.psA[:, bank * 512:(bank + 1) * 512],
                            0.0, None, ALU.max, reads=[psb], writes=[rt])
                    self.A_("activation", self.h_all.ap[:, ic, :], rt.ap,
                            ACTF.Square, reads=[rt], writes=[self.h_all])
                for lh in range(2):
                    for tt in range(4):
                        psb = self.PSR("B", tt * 512, (tt + 1) * 512)
                        for ic in range(8):
                            self.T_("matmul",
                                    self.psB[:, tt * 512:(tt + 1) * 512],
                                    self.h_all.ap[:, ic, tt * P:(tt + 1) * P],
                                    w2.ap[:, ic, lh * 512:(lh + 1) * 512],
                                    start=(ic == 0), stop=(ic == 7),
                                    reads=[self.h_all, w2], writes=[psb])
                        tg = blk * 4 + tt
                        cwap = self.cw.ap[:, tg, e:e + 1]
                        ydst = self.y_lat.ap[:, tg, lh * 512:(lh + 1) * 512]
                        if e == 0:
                            self.V_("tensor_scalar", ydst,
                                    self.psB[:, tt * 512:(tt + 1) * 512],
                                    cwap, None, ALU.mult,
                                    reads=[psb, self.cw], writes=[self.y_lat])
                        else:
                            self.V_("scalar_tensor_tensor", ydst,
                                    self.psB[:, tt * 512:(tt + 1) * 512],
                                    cwap, ydst, ALU.mult, ALU.add,
                                    reads=[psb, self.cw, self.y_lat],
                                    writes=[self.y_lat])
        for tg in range(NT):
            self.V_("tensor_copy", self.ylb.ap, self.y_lat.ap[:, tg, :],
                    reads=[self.y_lat], writes=[self.ylb])
            self.transpose("trpY", self.ylT.ap, self.ylb.ap,
                           reads=[self.ylb, self.ylT], writes=[self.ylT])
            self.dma("sylT", self.ylT_s[:, :, tg * P:(tg + 1) * P], self.ylT.ap,
                     reads=[self.ylT], writes=[("ylT_s", tg)])

    def b3(self):
        NSLAB = self.T // 512
        for blk in range(NSLAB):
            slab = self.slab3
            self.load_slab("slab3", slab, blk)
            self.dma("yl3", self.yl3.ap,
                     self.ylT_s[:, :, blk * 512:(blk + 1) * 512],
                     reads=[self.yl3] + [("ylT_s", blk * 4 + j) for j in range(4)],
                     writes=[self.yl3])
            for sig in range(5):
                if sig < 4:
                    for kc in range(NKC):
                        self.dma("upw", self.upTb.ap[:, kc, :],
                                 self.upTd[kc * P:(kc + 1) * P,
                                           sig * 1024:(sig + 1) * 1024],
                                 reads=[self.upTb], writes=[self.upTb])
                    for sc in range(8):
                        self.dma("dnw", self.dnb.ap[:, sc, :],
                                 self.dfT[sig * 1024 + sc * P:
                                          sig * 1024 + (sc + 1) * P, :],
                                 reads=[self.dnb], writes=[self.dnb])
                    for si in range(8):
                        bank = si % 4
                        psb = self.PSR("A", bank * 512, (bank + 1) * 512)
                        for kc in range(NKC):
                            self.T_("matmul",
                                    self.psA[:, bank * 512:(bank + 1) * 512],
                                    self.upTb.ap[:, kc, si * P:(si + 1) * P],
                                    slab.ap[:, kc, :],
                                    start=(kc == 0), stop=(kc == NKC - 1),
                                    reads=[self.upTb, slab], writes=[psb])
                        rt = self.rtmp3[si % 2]
                        self.V_("tensor_scalar", rt.ap,
                                self.psA[:, bank * 512:(bank + 1) * 512],
                                0.0, None, ALU.max, reads=[psb], writes=[rt])
                        self.A_("activation", self.sTs.ap[:, si, :], rt.ap,
                                ACTF.Square, reads=[rt], writes=[self.sTs])
                else:
                    for sc in range(8):
                        self.dma("dnw", self.dnb.ap[:, sc, :],
                                 self.dfT[SI + sc * P:SI + (sc + 1) * P, :],
                                 reads=[self.dnb], writes=[self.dnb])
                for t in range(4):
                    tg = blk * 4 + t
                    psb = self.PSR("B", 0, 2048)
                    for sc in range(8):
                        if sig < 4:
                            lhs = self.sTs.ap[:, sc, t * P:(t + 1) * P]
                            lrd = self.sTs
                        else:
                            lhs = self.yl3.ap[:, sc, t * P:(t + 1) * P]
                            lrd = self.yl3
                        for co in range(4):
                            self.T_("matmul", self.psB[:, co * 512:(co + 1) * 512],
                                    lhs, self.dnb.ap[:, sc, co * 512:(co + 1) * 512],
                                    start=(sc == 0), stop=(sc == 7),
                                    reads=[lrd, self.dnb],
                                    writes=[self.PSR("B", co * 512, (co + 1) * 512)])
                    rdst = self.rout.ap[:, t, :]
                    if sig == 0:
                        self.V_("tensor_copy", rdst, self.psB[:, :],
                                reads=[psb], writes=[self.rout])
                    else:
                        self.V_("tensor_tensor", rdst, rdst, self.psB[:, :],
                                ALU.add, reads=[psb, self.rout], writes=[self.rout])
            for t in range(4):
                tg = blk * 4 + t
                rsrc = self.rout.ap[:, t, :]
                self.bn(self.rout, rsrc)
                self.ln_rstd(self.mv.ap[:, 1:2], self.c_eps5, self.rstd, [self.mv])
                self.dma("x1l3", self.x1l3.ap, self.x1_s[tg],
                         reads=[("x1_s", tg)], writes=[self.x1l3])
                self.V_("tensor_scalar", self.outb.ap, rsrc, self.mv.ap[:, 0:1],
                        self.rstd.ap, ALU.subtract, ALU.mult,
                        reads=[self.rout, self.mv, self.rstd], writes=[self.outb])
                self.V_("tensor_tensor", self.outb.ap, self.outb.ap, self.x1l3.ap,
                        ALU.add, reads=[self.outb, self.x1l3], writes=[self.outb])
                self.dma("sout", self.out[tg * P:(tg + 1) * P, :], self.outb.ap,
                         reads=[self.outb], writes=[("out", tg)])

    def build_all(self):
        self.declare()
        self.const_loads()
        self.a0()
        self.tern(self.wqT, self.mclip_q, CQ)
        self.a1("q")
        self.tern(self.wvT, self.mclip_v, CK)
        self.a1("v")
        self.tern(self.wkT, self.mclip_k, CK)
        self.a1("k")
        self.a2()
        self.tern(self.woT, self.mclip_o, CK)
        self.a3()
        self.b0()
        self.b2()
        self.b3()
        self.p.finish()


LANES = {}
for _l in (["xt0", "xt1", "ld0", "ld1", "x1ld0", "x1ld1",
            "slab0", "slab1", "slab3", "x1l3", "sxqT_s", "sqrT", "syqT_s",
            "sx1", "sln2T", "sout", "sylT", "yl3", "trp0", "trp1", "trpP", "trpY"]
           + ["skr%d" % h for h in range(16)] + ["sv%d" % g for g in range(4)]):
    LANES[_l] = "S"
for _l in ("cb", "ones", "gbias", "cc", "ss", "ch0", "vsl"):
    LANES[_l] = "A"
for _l in ("ch1", "ks0", "ks1"):
    LANES[_l] = "G"
for _l in ("gw", "fc1", "w1b0", "w1b1", "w2b0", "w2b1", "upw", "dnw"):
    LANES[_l] = "G"


def build_nc(NT=16):
    nc = bass.Bass("TRN2", target_bir_lowering=False)
    from contextlib import ExitStack
    with ExitStack() as st:
        sems = {}
        for name in ("T", "V", "A", "G"):
            sems[name] = st.enter_context(nc.semaphore("sem" + name))
        for name in LANES:
            sems[name] = st.enter_context(nc.semaphore("q" + name))
        prog = Prog(sems, LANES)
        b = Builder(nc, prog, NT)
        b.build_all()
        block = st.enter_context(nc.Block())
        prog.realize(block)
    return nc


def _host_prep(L0_attn_Wq, L0_attn_Wk, L0_attn_Wv, L0_attn_Wo, L0_ffn_gate_w,
               L0_ffn_gate_bias, L0_ffn_fc1_w, L0_ffn_fc2_w, L0_ffn_w1,
               L0_ffn_w2, L0_ffn_shared_up_w, L0_ffn_shared_down_w, T=2048):
    cont = np.ascontiguousarray
    perm = np.concatenate([
        np.concatenate([np.arange(h * 128, (h + 1) * 128, 2),
                        np.arange(h * 128 + 1, (h + 1) * 128, 2)])
        for h in range(16)])
    shared = {
        "wqT": cont(L0_attn_Wq[perm].T),
        "wkT": cont(L0_attn_Wk[perm].T),
        "wvT": cont(L0_attn_Wv.T),
        "woT": cont(L0_attn_Wo.T),
        "gwT": cont(L0_ffn_gate_w.T),
        "gbiasd": cont(np.tile(L0_ffn_gate_bias[None, :].astype(np.float32),
                               (128, 1))),
        "fc1Td": cont(L0_ffn_fc1_w.T),
        "dfT": cont(np.concatenate([L0_ffn_shared_down_w.T,
                                    L0_ffn_fc2_w.T], axis=0)),
        "w1Td": cont(L0_ffn_w1.transpose(0, 2, 1)),
        "w2Td": cont(L0_ffn_w2.transpose(0, 2, 1)),
        "upTd": cont(L0_ffn_shared_up_w.T),
    }
    import jax
    import jax.numpy as jnp
    with jax.default_device(jax.devices("cpu")[0]):
        freqs = 1.0 / (10000.0 ** (jnp.arange(0, 128, 2, dtype=jnp.float32) / 128))
        ang = jnp.arange(T, dtype=jnp.float32)[:, None] * freqs[None, :]
        cos = np.asarray(jnp.cos(ang), dtype=np.float32)
        sin = np.asarray(jnp.sin(ang), dtype=np.float32)
    shared["cc"] = cont(np.concatenate([cos, cos], axis=1))
    shared["ss"] = cont(np.concatenate([-sin, sin], axis=1))
    ii = np.arange(128)
    shared["cbm"] = np.where(ii[:, None] >= ii[None, :], 0.0,
                             -1e30).astype(np.float32)
    shared["onesd"] = np.ones((128, 128), np.float32)
    return shared


def kernel(L0_attn_x, L0_attn_Wq, L0_attn_Wk, L0_attn_Wv, L0_attn_Wo,
           L0_ffn_gate_w, L0_ffn_gate_bias, L0_ffn_fc1_w, L0_ffn_fc2_w,
           L0_ffn_w1, L0_ffn_w2, L0_ffn_shared_up_w, L0_ffn_shared_down_w):
    shared = _host_prep(L0_attn_Wq, L0_attn_Wk, L0_attn_Wv, L0_attn_Wo,
                        L0_ffn_gate_w, L0_ffn_gate_bias, L0_ffn_fc1_w,
                        L0_ffn_fc2_w, L0_ffn_w1, L0_ffn_w2,
                        L0_ffn_shared_up_w, L0_ffn_shared_down_w)
    nc = build_nc(NT=16)
    in_maps = []
    for c in range(8):
        m = dict(shared)
        m["x"] = np.ascontiguousarray(L0_attn_x[c])
        in_maps.append(m)
    res = run_bass_kernel_spmd(nc, in_maps, core_ids=list(range(8)))
    return np.stack([res.results[c]["out"] for c in range(8)], axis=0)


# version 12
# speedup vs baseline: 1.9390x; 1.0994x over previous
import numpy as np
import ml_dtypes
import concourse.bass as bass
import concourse.mybir as mybir
from concourse.bass_utils import run_bass_kernel_spmd

F32 = mybir.dt.float32
BF16 = mybir.dt.bfloat16
U8 = mybir.dt.uint8
AX = mybir.AxisListType
ALU = mybir.AluOpType
ACTF = mybir.ActivationFunctionType

P = 128
C = 2048
H = 16
D = 128
NKC = C // P
NE = 32
LAT = 1024
INTER = 1024
SI = 4096
KB = 1024
PGSZ = 2048

CQ = 1.0 / (127.0 * float(np.sqrt(np.float64(128.0))))
CK = 1.0 / 127.0
INV_N = 1.0 / (2048.0 * 2048.0)
M32 = float(2 ** 23 * 1.5)
M64 = float(2 ** 52 * 1.5)


class Buf:
    __slots__ = ("ap", "keys")

    def __init__(self, ap, keys):
        self.ap = ap
        self.keys = frozenset(keys)


class Prog:
    ENGS = ("T", "V", "A", "G", "S")

    def __init__(self, sems, lane_queue):
        self.sems = sems
        self.lane_queue = lane_queue
        self.cnt = {k: 0 for k in sems}
        self.steps = {e: [] for e in self.ENGS}
        self.seen = {e: {} for e in self.ENGS}
        self.pages = {}
        self.dirty = {e: set() for e in self.ENGS}
        self.rdirty = {e: set() for e in self.ENGS}

    def _rec(self, k):
        r = self.pages.get(k)
        if r is None:
            r = [None, []]
            self.pages[k] = r
        return r

    @staticmethod
    def _keys(bufs):
        s = set()
        for b in bufs:
            s |= b.keys if isinstance(b, Buf) else {b}
        return s

    def op(self, eng, fn, reads=(), writes=(), lane=None):
        rk = self._keys(reads)
        wk = self._keys(writes)
        waits = []
        seen = self.seen[eng]

        def need(sem, val, agent):
            if agent != eng and agent in self.lane_queue:
                val = self.cnt[agent]
            if val > 0 and seen.get(sem, 0) < val:
                waits.append((self.sems[sem], val))
                seen[sem] = val

        drain = False
        for k in rk | wk:
            rec = self.pages.get(k)
            if rec is None:
                continue
            w = rec[0]
            if w is not None:
                sem, val, agent = w
                if agent == eng:
                    if k in self.dirty[eng]:
                        drain = True
                else:
                    need(sem, val, agent)
            if k in wk:
                for sem, val, agent in rec[1]:
                    if agent == eng:
                        if k in self.rdirty[eng]:
                            drain = True
                    else:
                        need(sem, val, agent)
        if eng == "T":
            drain = False

        if lane is None:
            sem_name, inc = eng, 1
            agent = eng
        else:
            sem_name, inc = lane, 16
            agent = lane
        self.cnt[sem_name] += inc
        val = self.cnt[sem_name]

        if drain:
            self.dirty[eng].clear()
            self.rdirty[eng].clear()
        for k in wk:
            rec = self._rec(k)
            rec[0] = (sem_name, val, agent)
            rec[1] = []
        for k in rk:
            self._rec(k)[1].append((sem_name, val, agent))
        if lane is None:
            self.dirty[eng] |= wk
            self.rdirty[eng] |= rk
        self.steps[eng].append((tuple(waits), drain, fn, self.sems[sem_name], inc))
        return val

    def finish(self):
        waits = [(self.sems[k], v) for k, v in self.cnt.items() if v > 0]
        self.steps["S"].append((tuple(waits), False, None, None, 0))

    def realize(self, block):
        names = {"T": "tensor", "V": "vector", "A": "scalar", "G": "gpsimd", "S": "sync"}
        for e in self.ENGS:
            steps = self.steps[e]

            def mk(steps=steps):
                def body(eng):
                    for waits, drain, fn, sem, inc in steps:
                        for s, v in waits:
                            eng.wait_ge(s, v)
                        if drain:
                            eng.drain()
                        if fn is not None:
                            inst = fn(eng)
                            if sem is not None:
                                inst.then_inc(sem, inc)
                return body
            getattr(block, names[e])(mk())


class Builder:
    def __init__(self, nc, prog, NT):
        self.nc = nc
        self.p = prog
        self.NT = NT
        self.T = NT * P

    def T_(self, fname, *args, reads=(), writes=(), **kw):
        self.p.op("T", (lambda e, a=args, k=kw, f=fname: getattr(e, f)(*a, **k)),
                  reads, writes)

    def V_(self, fname, *args, reads=(), writes=(), **kw):
        self.p.op("V", (lambda e, a=args, k=kw, f=fname: getattr(e, f)(*a, **k)),
                  reads, writes)

    def A_(self, fname, *args, reads=(), writes=(), **kw):
        self.p.op("A", (lambda e, a=args, k=kw, f=fname: getattr(e, f)(*a, **k)),
                  reads, writes)

    def dma(self, lane, out_ap, in_ap, reads=(), writes=()):
        self.p.op(self.p.lane_queue[lane],
                  (lambda e, o=out_ap, i=in_ap: e.dma_start(out=o, in_=i)),
                  reads, writes, lane=lane)

    def transpose(self, lane, out_ap, in_ap, reads=(), writes=()):
        self.p.op("S", (lambda e, o=out_ap, i=in_ap: e.dma_start_transpose(o, i)),
                  reads, writes, lane=lane)

    def AR(self, off_kb, shape, dtype):
        nel = int(np.prod(shape[1:]))
        nby = nel * mybir.dt.size(dtype)
        if dtype == BF16:
            arena, tag, epk = self.arena16, "A6", 512
        else:
            arena, tag, epk = self.arena32, "A2", 256
        ap = arena[:, off_kb * epk: off_kb * epk + nel]
        if len(shape) == 3:
            ap = ap.rearrange("p (a b) -> p a b", b=shape[2])
        keys = {(tag, g) for g in range(off_kb * KB // PGSZ,
                                        (off_kb * KB + nby + PGSZ - 1) // PGSZ)}
        return Buf(ap, keys)

    def SB(self, name, shape, dtype=F32, tdest=False):
        h = self.nc.alloc_sbuf_tensor(name, shape, dtype, is_transpose_dest=tdest)
        return Buf(h[:], {name})

    def PSR(self, which, lo, hi):
        t = self.psA if which == "A" else self.psB
        keys = {("ps" + which, b) for b in range(lo // 512, (hi + 511) // 512)}
        return Buf(t[:, lo:hi], keys)

    def declare(self):
        nc = self.nc
        NT, T = self.NT, self.T
        dr = lambda n, s, dt=F32, kind="ExternalInput": nc.dram_tensor(n, s, dt, kind=kind).ap()
        self.x = dr("x", [T, C])
        self.wqT = dr("wqT", [C, C]); self.wkT = dr("wkT", [C, C])
        self.wvT = dr("wvT", [C, C]); self.woT = dr("woT", [C, C])
        self.cc = dr("cc", [T, D]); self.ss = dr("ss", [T, D])
        self.cbm = dr("cbm", [P, P]); self.onesd = dr("onesd", [P, P])
        self.gbiasd = dr("gbiasd", [P, NE])
        self.gwT = dr("gwT", [C, NE])
        self.fc1Td = dr("fc1Td", [C, LAT])
        self.dfT = dr("dfT", [SI + LAT, C])
        self.w1Td = dr("w1Td", [NE, LAT, INTER])
        self.w2Td = dr("w2Td", [NE, INTER, LAT])
        self.upTd = dr("upTd", [C, SI])
        self.out = dr("out", [T, C], F32, kind="ExternalOutput")
        idr = lambda n, s, dt: nc.dram_tensor(n, s, dt, kind="Internal").ap()
        self.xqT_s = idr("xqT_s", [NT, P, NKC, P], BF16)
        self.qrT_s = idr("qrT_s", [NT, P, NKC, P], BF16)
        self.krT_s = idr("krT_s", [H, P, T], BF16)
        self.v_s = idr("v_s", [4, T, 512], BF16)
        self.yqT_s = idr("yqT_s", [NT, P, NKC, P], BF16)
        self.x1_s = idr("x1_s", [NT, P, C], F32)
        self.ln2T_s = idr("ln2T_s", [NT, P, NKC, P], BF16)
        self.ylT_s = idr("ylT_s", [P, 8, T], BF16)

        self.psA = nc.alloc_psum_tensor("psA", [P, 2048], F32)
        self.psB = nc.alloc_psum_tensor("psB", [P, 2048], F32)

        S = self.SB
        self.tr = [S("tr0", [P, NKC, P], BF16, tdest=True),
                   S("tr1", [P, NKC, P], BF16, tdest=True)]
        self.pT = S("pT", [P, NT, P], BF16, tdest=True)
        self.ylT = S("ylT", [P, 8, P], BF16, tdest=True)
        self.cb = S("cb", [P, P]); self.ones = S("ones", [P, P])
        self.cct = S("cct", [P, D]); self.sst = S("sst", [P, D])
        self.gbias = S("gbias", [P, NE])
        self.gw_sb = S("gw_sb", [P, NKC, NE], BF16)
        self.samc = S("samc", [P, NT]); self.samc_y = S("samc_y", [P, NT])
        self.cw = S("cw", [P, NT, NE])
        self.bst = S("bst", [P, 4, 6]); self.mv = S("mv", [P, 2])
        self.cs16 = S("cs16", [P, NKC]); self.cstot = S("cstot", [P, 1])
        for n in ("mclip_q", "mclip_k", "mclip_v", "mclip_o", "thr", "nthr",
                  "lnv", "rstd", "rsq", "bq", "rr", "sfold", "am", "amc",
                  "rmc", "qsc", "qsc2", "sc1", "s2t", "vv", "gv",
                  "nm", "den", "rden", "ss2v",
                  "g1v", "g2v", "m1g", "m2g", "th8", "s8", "rs8", "mx1",
                  "nm2", "den2", "rden2"):
            setattr(self, n, S(n, [P, 1]))
        self.nmp = [self.nm, self.nm2]
        self.denp = [self.den, self.den2]
        self.rdenp = [self.rden, self.rden2]
        self.c_eps5 = S("c_eps5", [P, 1]); self.c_eps6 = S("c_eps6", [P, 1])
        self.c_2565 = S("c_2565", [P, 1])
        self.c_m256 = S("c_m256", [P, 1]); self.c_n1e30 = S("c_n1e30", [P, 1])
        self.c_one = S("c_one", [P, 1]); self.gdisc = S("gdisc", [P, 1])
        self.c_zero = S("c_zero", [P, 1])
        self.mqs = S("mqs", [P, 1]); self.rs25 = S("rs25", [P, 1])
        self.et = S("et", [P, NE]); self.d1 = S("d1", [P, NE])
        self.scrs = S("scrs", [P, NE]); self.sbv = S("sbv", [P, NE])
        self.msb = S("msb", [P, NE]); self.cur = S("cur", [P, NE])
        self.mk32 = S("mk32", [P, NE]); self.wsl = S("wsl", [P, NE])
        self.selm = S("selm", [P, NE])
        self.gsv = S("gsv", [P, 4]); self.mask4 = S("mask4", [P, 4])
        self.t4b = S("t4b", [P, 4]); self.madd4 = S("madd4", [P, 4])
        self.msk8 = S("msk8", [P, 8]); self.t8b = S("t8b", [P, 8])

        self.arena16 = nc.alloc_sbuf_tensor("arena16", [P, 108 * 512], BF16)
        self.arena32 = nc.alloc_sbuf_tensor("arena32", [P, 72 * 256], F32)
        A = self.AR
        self.wtern = A(0, [P, NKC, C], BF16)
        self.xqb = A(64, [P, C], BF16)
        self.vbb = A(68, [P, C], BF16)
        self.ld = [A(72, [P, NKC, P], BF16), A(76, [P, NKC, P], BF16)]
        self.pb = A(80, [P, 2048], BF16)
        self.ks = [A(84, [P, 2048], BF16), A(88, [P, 2048], BF16)]
        self.vsh = [A(92, [P, NT, 256], BF16),
                    A(100, [P, NT, 256], BF16)]
        self.pbp = [self.pb, self.vbb]
        self.chunk = [A(0, [P, C], F32), A(8, [P, C], F32)]
        self.xt = [A(16, [P, C], F32), A(24, [P, C], F32)]
        self.xn = A(32, [P, C], F32)
        self.qf = A(40, [P, C], F32)
        self.uu = A(48, [P, C], F32)
        self.trt = A(56, [P, C], F32)
        self.y_scr = [A(64, [P, C], F32), A(64, [P, C], F32)]
        self.fc1T = A(76, [P, NKC, LAT], BF16)
        self.x_latT = A(0, [P, 8, T], BF16)
        self.slab = [A(32, [P, NKC, 512], BF16), A(48, [P, NKC, 512], BF16)]
        self.w1b = [A(32, [P, 8, INTER], BF16), A(48, [P, 8, INTER], BF16)]
        self.w2b = [A(64, [P, 8, LAT], BF16), A(80, [P, 8, LAT], BF16)]
        self.h_all = A(96, [P, 8, 512], BF16)
        self.ylb = A(104, [P, LAT], BF16)
        self.dnb = A(0, [P, 8, C], BF16)
        self.slab3 = A(32, [P, NKC, 512], BF16)
        self.sTs = A(48, [P, 8, 512], BF16)
        self.upTb = A(56, [P, NKC, 1024], BF16)
        self.yl3 = A(88, [P, 8, 512], BF16)
        self.x1ld = [A(0, [P, C], F32), A(8, [P, C], F32)]
        self.y_lat = A(0, [P, NT, LAT], F32)
        self.rtmp2 = [A(64, [P, 512], F32), A(66, [P, 512], F32)]
        self.rout = A(0, [P, 4, C], F32)
        self.x1l3 = A(32, [P, C], F32)
        self.outb = A(40, [P, C], F32)
        self.rtmp3 = [A(48, [P, 512], F32), A(50, [P, 512], F32)]

    def act(self, out, in_, func, bias=0.0, scale=1.0, accum=None,
            reads=(), writes=()):
        kw = dict(bias=bias, scale=scale)
        if accum is not None:
            kw["accum_out"] = accum
        self.A_("activation", out, in_, func, reads=reads, writes=writes, **kw)

    def bn(self, src_buf, src_ap2048, extra_reads=()):
        for i in range(4):
            self.V_("bn_stats", self.bst.ap[:, i, :], src_ap2048[:, i * 512:(i + 1) * 512],
                    reads=[src_buf, *extra_reads], writes=[self.bst])
        self.V_("bn_aggr", self.mv.ap, self.bst.ap,
                reads=[self.bst], writes=[self.mv])

    def ln_rstd(self, var_ap, eps_buf, out_buf, reads):
        self.act(self.lnv.ap, var_ap, ACTF.Ln, bias=eps_buf.ap, scale=1.0,
                 reads=list(reads) + [eps_buf], writes=[self.lnv])
        self.act(out_buf.ap, self.lnv.ap, ACTF.Exp, bias=0.0, scale=-0.5,
                 reads=[self.lnv], writes=[out_buf])

    def const_loads(self):
        self.V_("memset", self.c_eps5.ap, 1e-5, writes=[self.c_eps5])
        self.V_("memset", self.c_eps6.ap, 1e-6, writes=[self.c_eps6])
        self.V_("memset", self.c_2565.ap, 256.5, writes=[self.c_2565])
        self.V_("memset", self.c_m256.ap, -256.0, writes=[self.c_m256])
        self.V_("memset", self.c_n1e30.ap, -1e30, writes=[self.c_n1e30])
        self.V_("memset", self.c_one.ap, 1.0, writes=[self.c_one])
        self.V_("memset", self.c_zero.ap, 0.0, writes=[self.c_zero])
        self.V_("tensor_single_scalar", self.gdisc.ap, self.c_one.ap, M64,
                ALU.add, reads=[self.c_one], writes=[self.gdisc])
        self.V_("tensor_single_scalar", self.gdisc.ap, self.gdisc.ap, M64,
                ALU.subtract, reads=[self.gdisc], writes=[self.gdisc])
        self.dma("cb", self.cb.ap, self.cbm, writes=[self.cb])
        self.dma("ones", self.ones.ap, self.onesd, writes=[self.ones])
        self.dma("gbias", self.gbias.ap, self.gbiasd, writes=[self.gbias])
        self.dma("gw", self.gw_sb.ap,
                 self.gwT.rearrange("(a p) n -> p a n", p=P), writes=[self.gw_sb])
        from concourse.hw_specs import get_activation_tables
        idx = None
        for i, funcs in enumerate(get_activation_tables(self.nc.m.arch).values()):
            if ACTF.Exp in funcs and ACTF.Ln in funcs:
                idx = i
                break
        if idx is not None:
            nc = self.nc
            def fn(e, idx=idx, nc=nc):
                return e.add_instruction(mybir.InstLoadActFuncSet(
                    name=nc.get_next_instruction_name(),
                    act_func_set_id=idx, ins=[], outs=[]))
            self.p.op("A", fn, (), ())

    def tern(self, wdram, mclip, cmul):
        NCH = NKC
        for kc in range(NCH):
            ch = self.chunk[kc % 2]
            self.dma("ch%d" % (kc % 2), ch.ap, wdram[kc * P:(kc + 1) * P, :],
                     reads=[ch], writes=[ch])
            self.V_("reduce_sum", self.cs16.ap[:, kc:kc + 1], ch.ap, AX.X,
                    apply_absolute_value=True, reads=[ch], writes=[self.cs16])
        self.V_("reduce_sum", self.cstot.ap, self.cs16.ap, AX.X,
                reads=[self.cs16], writes=[self.cstot])
        ps1 = self.PSR("A", 0, 512)
        self.T_("matmul", self.psA[:, 0:1], self.ones.ap, self.cstot.ap,
                start=True, stop=True, reads=[self.ones, self.cstot], writes=[ps1])
        self.V_("tensor_scalar", mclip.ap, self.psA[:, 0:1], float(INV_N), 1e-5,
                ALU.mult, ALU.max, reads=[ps1], writes=[mclip])
        self.V_("tensor_scalar", self.thr.ap, mclip.ap, 0.5, None, ALU.mult,
                reads=[mclip], writes=[self.thr])
        self.V_("tensor_scalar", self.nthr.ap, mclip.ap, -0.5, None, ALU.mult,
                reads=[mclip], writes=[self.nthr])
        self.V_("tensor_scalar", self.mqs.ap, mclip.ap, float(cmul), None,
                ALU.mult, reads=[mclip], writes=[self.mqs])
        for kc in range(NCH):
            ch = self.chunk[kc % 2]
            self.dma("ch%d" % (kc % 2), ch.ap, wdram[kc * P:(kc + 1) * P, :],
                     reads=[ch], writes=[ch])
            self.V_("tensor_scalar", self.qf.ap, ch.ap, self.nthr.ap, None,
                    ALU.is_lt, reads=[ch, self.nthr], writes=[self.qf])
            self.V_("scalar_tensor_tensor", self.wtern.ap[:, kc, :], ch.ap,
                    self.thr.ap, self.qf.ap, ALU.is_gt, ALU.subtract,
                    reads=[ch, self.thr, self.qf], writes=[self.wtern])

    def a0(self):
        for t in range(self.NT):
            xt = self.xt[t % 2]
            self.dma("xt%d" % (t % 2), xt.ap, self.x[t * P:(t + 1) * P, :],
                     reads=[xt], writes=[xt])
            self.bn(xt, xt.ap)
            self.ln_rstd(self.mv.ap[:, 1:2], self.c_eps5, self.rstd, [self.mv])
            self.V_("tensor_tensor", self.rsq.ap, self.rstd.ap, self.rstd.ap,
                    ALU.mult, reads=[self.rstd], writes=[self.rsq])
            self.V_("tensor_tensor", self.bq.ap, self.mv.ap[:, 1:2], self.rsq.ap,
                    ALU.mult, reads=[self.mv, self.rsq], writes=[self.bq])
            self.act(self.lnv.ap, self.bq.ap, ACTF.Ln, bias=self.c_eps6.ap,
                     reads=[self.bq, self.c_eps6], writes=[self.lnv])
            self.act(self.rr.ap, self.lnv.ap, ACTF.Exp, scale=-0.5,
                     reads=[self.lnv], writes=[self.rr])
            self.V_("tensor_tensor", self.sfold.ap, self.rstd.ap, self.rr.ap,
                    ALU.mult, reads=[self.rstd, self.rr], writes=[self.sfold])
            self.V_("tensor_scalar", self.xn.ap, xt.ap, self.mv.ap[:, 0:1],
                    self.sfold.ap, ALU.subtract, ALU.mult,
                    reads=[xt, self.mv, self.sfold], writes=[self.xn])
            self.quant_store(self.xn, self.samc, t, self.xqT_s, "xqT_s",
                             scale_pre=None)

    def quant_store(self, src, samc, t, dst_dram, dst_key, scale_pre):
        self.V_("reduce_max", self.am.ap, src.ap, AX.X, apply_absolute_value=True,
                reads=[src], writes=[self.am])
        if scale_pre is not None:
            self.V_("tensor_tensor", self.amc.ap, self.am.ap, scale_pre.ap,
                    ALU.mult, reads=[self.am, scale_pre], writes=[self.amc])
            amsrc = self.amc
        else:
            amsrc = self.am
        self.V_("tensor_single_scalar", samc.ap[:, t:t + 1], amsrc.ap, 1e-5,
                ALU.max, reads=[amsrc], writes=[samc])
        self.V_("reciprocal", self.rmc.ap, samc.ap[:, t:t + 1],
                reads=[samc], writes=[self.rmc])
        self.V_("tensor_scalar", self.qsc.ap, self.rmc.ap, 127.0, None, ALU.mult,
                reads=[self.rmc], writes=[self.qsc])
        if scale_pre is not None:
            self.V_("tensor_tensor", self.qsc2.ap, self.qsc.ap, scale_pre.ap,
                    ALU.mult, reads=[self.qsc, scale_pre], writes=[self.qsc2])
            qs = self.qsc2
        else:
            qs = self.qsc
        self.V_("tensor_scalar", self.qf.ap, src.ap, qs.ap, None, ALU.mult,
                reads=[src, qs], writes=[self.qf])
        self.V_("tensor_single_scalar", self.uu.ap, self.qf.ap, M32, ALU.add,
                reads=[self.qf], writes=[self.uu])
        self.V_("tensor_single_scalar", self.uu.ap, self.uu.ap, M32,
                ALU.subtract, reads=[self.uu], writes=[self.uu])
        self.V_("tensor_single_scalar", self.trt.ap, self.qf.ap, M64, ALU.add,
                reads=[self.qf], writes=[self.trt])
        self.V_("tensor_single_scalar", self.trt.ap, self.trt.ap, M64,
                ALU.subtract, reads=[self.trt], writes=[self.trt])
        self.V_("tensor_tensor", self.trt.ap, self.trt.ap, self.uu.ap,
                ALU.subtract, reads=[self.trt, self.uu], writes=[self.trt])
        self.V_("scalar_tensor_tensor", self.xqb.ap, self.trt.ap,
                self.gdisc.ap, self.uu.ap, ALU.mult, ALU.add,
                reads=[self.trt, self.uu, self.gdisc], writes=[self.xqb])
        tr = self.tr[t % 2]
        self.transpose("trp%d" % (t % 2), tr.ap, self.xqb.ap,
                       reads=[self.xqb, tr], writes=[tr])
        self.dma("s" + dst_key, dst_dram[t], tr.ap, reads=[tr],
                 writes=[(dst_key, t)])

    def a1(self, kind):
        NT = self.NT
        for t in range(NT):
            ld = self.ld[t % 2]
            self.dma("ld%d" % (t % 2), ld.ap, self.xqT_s[t],
                     reads=[ld, ("xqT_s", t)], writes=[ld])
            psn = "A" if t % 2 == 0 else "B"
            pst = self.psA if psn == "A" else self.psB
            psb = self.PSR(psn, 0, 2048)
            for kc in range(NKC):
                for n in range(4):
                    self.T_("matmul", pst[:, n * 512:(n + 1) * 512],
                            ld.ap[:, kc, :], self.wtern.ap[:, kc, n * 512:(n + 1) * 512],
                            start=(kc == 0), stop=(kc == NKC - 1),
                            reads=[ld, self.wtern], writes=[psb])
            self.V_("tensor_scalar", self.sc1.ap, self.samc.ap[:, t:t + 1],
                    self.mqs.ap, None, ALU.mult,
                    reads=[self.samc, self.mqs], writes=[self.sc1])
            if kind == "v":
                self.V_("tensor_scalar", self.vbb.ap, pst[:, :], self.sc1.ap,
                        None, ALU.mult, reads=[psb, self.sc1], writes=[self.vbb])
                for hg in range(4):
                    self.dma("sv%d" % hg, self.v_s[hg, t * P:(t + 1) * P, :],
                             self.vbb.ap[:, hg * 512:(hg + 1) * 512],
                             reads=[self.vbb], writes=[("v_s", hg, t)])
                continue
            self.V_("tensor_scalar", self.qf.ap, pst[:, :], self.sc1.ap, None,
                    ALU.mult, reads=[psb, self.sc1], writes=[self.qf])
            self.dma("cc", self.cct.ap, self.cc[t * P:(t + 1) * P, :],
                     reads=[self.cct], writes=[self.cct])
            self.dma("ss", self.sst.ap, self.ss[t * P:(t + 1) * P, :],
                     reads=[self.sst], writes=[self.sst])
            qf3 = self.qf.ap.rearrange("p (h d) -> p h d", d=D)
            tr3 = self.trt.ap.rearrange("p (h d) -> p h d", d=D)
            ccb = self.cct.ap.unsqueeze(1).broadcast_to([P, H, D])
            uu3 = self.uu.ap.rearrange("p (h d) -> p h d", d=D)
            self.V_("tensor_tensor", uu3, qf3, ccb, ALU.mult,
                    reads=[self.qf, self.cct], writes=[self.uu])
            sslo = self.sst.ap[:, 0:64].unsqueeze(1).broadcast_to([P, H, 64])
            sshi = self.sst.ap[:, 64:128].unsqueeze(1).broadcast_to([P, H, 64])
            self.V_("tensor_tensor", tr3[:, :, 0:64], qf3[:, :, 64:128], sslo,
                    ALU.mult, reads=[self.qf, self.sst], writes=[self.trt])
            self.V_("tensor_tensor", tr3[:, :, 64:128], qf3[:, :, 0:64], sshi,
                    ALU.mult, reads=[self.qf, self.sst], writes=[self.trt])
            self.V_("tensor_tensor", self.xqb.ap, self.uu.ap, self.trt.ap,
                    ALU.add, reads=[self.uu, self.trt], writes=[self.xqb])
            tr = self.tr[t % 2]
            self.transpose("trp%d" % (t % 2), tr.ap, self.xqb.ap,
                           reads=[self.xqb, tr], writes=[tr])
            if kind == "q":
                self.dma("sqrT", self.qrT_s[t], tr.ap, reads=[tr],
                         writes=[("qrT_s", t)])
            else:
                for h in range(H):
                    self.dma("skr%d" % h, self.krT_s[h, :, t * P:(t + 1) * P],
                             tr.ap[:, h, :], reads=[tr], writes=[("krT_s", h)])

    def a2(self):
        NT = self.NT
        for qt in range(NT):
            L = (qt + 1) * P
            ld = self.ld[qt % 2]
            self.dma("ld%d" % (qt % 2), ld.ap, self.qrT_s[qt],
                     reads=[ld, ("qrT_s", qt)], writes=[ld])
            ysc = self.y_scr[qt % 2]
            pTs = [self.pT, self.tr[qt % 2]]
            nch = (L + 511) // 512

            def qk(h):
                hg = h // 4
                if h % 2 == 0:
                    half = (h % 4) // 2
                    vb = self.vsh[half]
                    src = self.v_s[hg].rearrange("(k p) f -> p k f", p=P)
                    self.dma("vs%d" % half, vb.ap[:, 0:qt + 1, :],
                             src[:, 0:qt + 1, half * 256:(half + 1) * 256],
                             reads=[vb] + [("v_s", hg, tt) for tt in range(qt + 1)],
                             writes=[vb])
                ks = self.ks[h % 2]
                self.dma("ks%d" % (h % 2), ks.ap[:, 0:L], self.krT_s[h, :, 0:L],
                         reads=[ks, ("krT_s", h)], writes=[ks])
                Rn = "A" if h % 2 == 0 else "B"
                Rt = self.psA if Rn == "A" else self.psB
                for cc_ in range(nch):
                    n = min(512, L - cc_ * 512)
                    self.T_("matmul", Rt[:, cc_ * 512:cc_ * 512 + n],
                            ld.ap[:, h, :], ks.ap[:, cc_ * 512:cc_ * 512 + n],
                            start=True, stop=True,
                            reads=[ld, ks],
                            writes=[self.PSR(Rn, cc_ * 512, cc_ * 512 + n)])

            def smx(h):
                Rn = "A" if h % 2 == 0 else "B"
                Rt = self.psA if Rn == "A" else self.psB
                Rb = self.PSR(Rn, 0, L)
                nm = self.nmp[h % 2]; den = self.denp[h % 2]
                rden = self.rdenp[h % 2]; pb = self.pbp[h % 2]
                self.V_("tensor_tensor", Rt[:, qt * P:L], Rt[:, qt * P:L],
                        self.cb.ap, ALU.add,
                        reads=[self.PSR(Rn, qt * P, L), self.cb],
                        writes=[self.PSR(Rn, qt * P, L)])
                self.V_("reduce_max", nm.ap, Rt[:, 0:L], AX.X,
                        negate=True, reads=[Rb], writes=[nm])
                self.act(pb.ap[:, 0:L], Rt[:, 0:L], ACTF.Exp,
                         bias=nm.ap, scale=1.0, accum=den.ap,
                         reads=[Rb, nm], writes=[pb, den])
                self.V_("reciprocal", rden.ap, den.ap,
                        reads=[den], writes=[rden])
                pT = pTs[h % 2]
                self.transpose("trpP%d" % (h % 2), pT.ap[:, 0:qt + 1, :],
                               pb.ap[:, 0:L], reads=[pb, pT], writes=[pT])

            def pv(h):
                h4 = h % 4
                Rn = "A" if h % 2 == 0 else "B"
                Yt = self.psA if Rn == "A" else self.psB
                Yb = self.PSR(Rn, 1920, 2048)
                pT = pTs[h % 2]; rden = self.rdenp[h % 2]
                vb = self.vsh[h4 // 2]
                for kt in range(qt + 1):
                    self.T_("matmul", Yt[:, 1920:2048], pT.ap[:, kt, :],
                            vb.ap[:, kt, (h4 % 2) * P:(h4 % 2 + 1) * P],
                            start=(kt == 0), stop=(kt == qt),
                            reads=[pT, vb], writes=[Yb])
                self.V_("tensor_scalar", ysc.ap[:, h * P:(h + 1) * P],
                        Yt[:, 1920:2048], rden.ap, None, ALU.mult,
                        reads=[Yb, rden], writes=[ysc])

            qk(0)
            smx(0)
            for h in range(16):
                if h + 1 < 16:
                    qk(h + 1)
                    smx(h + 1)
                pv(h)
            self.V_("tensor_tensor", self.xn.ap, ysc.ap, ysc.ap, ALU.mult,
                    reads=[ysc], writes=[self.xn])
            self.V_("reduce_sum", self.ss2v.ap, self.xn.ap, AX.X,
                    reads=[self.xn], writes=[self.ss2v])
            self.act(self.lnv.ap, self.ss2v.ap, ACTF.Ln, bias=self.c_eps6.ap,
                     scale=float(1.0 / C), reads=[self.ss2v, self.c_eps6],
                     writes=[self.lnv])
            self.act(self.rr.ap, self.lnv.ap, ACTF.Exp, scale=-0.5,
                     reads=[self.lnv], writes=[self.rr])
            self.quant_store(ysc, self.samc_y, qt, self.yqT_s, "yqT_s",
                             scale_pre=self.rr)

    def a3(self):
        NT = self.NT
        for t in range(NT):
            ld = self.ld[t % 2]
            self.dma("ld%d" % (t % 2), ld.ap, self.yqT_s[t],
                     reads=[ld, ("yqT_s", t)], writes=[ld])
            xt = self.xt[t % 2]
            self.dma("xt%d" % (t % 2), xt.ap, self.x[t * P:(t + 1) * P, :],
                     reads=[xt], writes=[xt])
            psn = "A" if t % 2 == 0 else "B"
            pst = self.psA if psn == "A" else self.psB
            psb = self.PSR(psn, 0, 2048)
            for kc in range(NKC):
                for n in range(4):
                    self.T_("matmul", pst[:, n * 512:(n + 1) * 512],
                            ld.ap[:, kc, :], self.wtern.ap[:, kc, n * 512:(n + 1) * 512],
                            start=(kc == 0), stop=(kc == NKC - 1),
                            reads=[ld, self.wtern], writes=[psb])
            self.V_("tensor_scalar", self.sc1.ap, self.samc_y.ap[:, t:t + 1],
                    self.mqs.ap, None, ALU.mult,
                    reads=[self.samc_y, self.mqs], writes=[self.sc1])
            self.bn(psb, pst)
            self.V_("tensor_tensor", self.s2t.ap, self.sc1.ap, self.sc1.ap,
                    ALU.mult, reads=[self.sc1], writes=[self.s2t])
            self.V_("tensor_tensor", self.vv.ap, self.mv.ap[:, 1:2], self.s2t.ap,
                    ALU.mult, reads=[self.mv, self.s2t], writes=[self.vv])
            self.ln_rstd(self.vv.ap, self.c_eps5, self.rstd, [self.vv])
            self.V_("tensor_tensor", self.gv.ap, self.sc1.ap, self.rstd.ap,
                    ALU.mult, reads=[self.sc1, self.rstd], writes=[self.gv])
            self.V_("tensor_scalar", self.xn.ap, pst[:, :], self.mv.ap[:, 0:1],
                    self.gv.ap, ALU.subtract, ALU.mult,
                    reads=[psb, self.mv, self.gv], writes=[self.xn])
            self.V_("tensor_tensor", self.qf.ap, self.xn.ap, xt.ap, ALU.add,
                    reads=[self.xn, xt], writes=[self.qf])
            self.dma("sx1", self.x1_s[t], self.qf.ap, reads=[self.qf],
                     writes=[("x1_s", t)])

    def b0(self):
        NT = self.NT
        for kc in range(NKC):
            self.dma("fc1", self.fc1T.ap[:, kc, :],
                     self.fc1Td[kc * P:(kc + 1) * P, :],
                     reads=[self.fc1T], writes=[self.fc1T])
        for t in range(NT):
            self.b0_tile(t)
            if t % 4 == 3:
                self.b1_blk(t // 4)

    def b0_tile(self, t):
        if True:
            x1 = self.x1ld[t % 2]
            self.dma("x1ld%d" % (t % 2), x1.ap, self.x1_s[t],
                     reads=[x1, ("x1_s", t)], writes=[x1])
            self.bn(x1, x1.ap)
            self.ln_rstd(self.mv.ap[:, 1:2], self.c_eps5, self.rstd, [self.mv])
            self.V_("tensor_scalar", self.xqb.ap, x1.ap, self.mv.ap[:, 0:1],
                    self.rstd.ap, ALU.subtract, ALU.mult,
                    reads=[x1, self.mv, self.rstd], writes=[self.xqb])
            tr = self.tr[t % 2]
            self.transpose("trp%d" % (t % 2), tr.ap, self.xqb.ap,
                           reads=[self.xqb, tr], writes=[tr])
            self.dma("sln2T", self.ln2T_s[t], tr.ap, reads=[tr],
                     writes=[("ln2T_s", t)])
            psg = self.PSR("B", 0, 512)
            for kc in range(NKC):
                self.T_("matmul", self.psB[:, 0:NE], tr.ap[:, kc, :],
                        self.gw_sb.ap[:, kc, :],
                        start=(kc == 0), stop=(kc == NKC - 1),
                        reads=[tr, self.gw_sb], writes=[psg])
            self.act(self.et.ap, self.psB[:, 0:NE], ACTF.Exp, scale=-1.0,
                     reads=[psg], writes=[self.et])
            self.V_("tensor_single_scalar", self.d1.ap, self.et.ap, 1.0, ALU.add,
                    reads=[self.et], writes=[self.d1])
            self.V_("reciprocal", self.scrs.ap, self.d1.ap,
                    reads=[self.d1], writes=[self.scrs])
            self.V_("tensor_tensor", self.sbv.ap, self.scrs.ap, self.gbias.ap,
                    ALU.add, reads=[self.scrs, self.gbias], writes=[self.sbv])
            for g in range(4):
                sg = self.sbv.ap[:, g * 8:(g + 1) * 8]
                self.V_("reduce_max", self.m1g.ap, sg, AX.X,
                        reads=[self.sbv], writes=[self.m1g])
                self.V_("tensor_scalar", self.msk8.ap, sg, self.m1g.ap, None,
                        ALU.is_ge, reads=[self.sbv, self.m1g], writes=[self.msk8])
                self.V_("scalar_tensor_tensor", self.t8b.ap, self.msk8.ap,
                        self.c_n1e30.ap, sg, ALU.mult, ALU.add,
                        reads=[self.msk8, self.sbv, self.c_n1e30],
                        writes=[self.t8b])
                self.V_("reduce_max", self.m2g.ap, self.t8b.ap, AX.X,
                        reads=[self.t8b], writes=[self.m2g])
                self.V_("tensor_tensor", self.gsv.ap[:, g:g + 1], self.m1g.ap,
                        self.m2g.ap, ALU.add,
                        reads=[self.m1g, self.m2g], writes=[self.gsv])
            self.V_("reduce_max", self.g1v.ap, self.gsv.ap, AX.X,
                    reads=[self.gsv], writes=[self.g1v])
            self.V_("tensor_scalar", self.mask4.ap, self.gsv.ap, self.g1v.ap,
                    None, ALU.is_ge, reads=[self.gsv, self.g1v], writes=[self.mask4])
            self.V_("scalar_tensor_tensor", self.t4b.ap, self.mask4.ap,
                    self.c_n1e30.ap, self.gsv.ap, ALU.mult, ALU.add,
                    reads=[self.mask4, self.gsv, self.c_n1e30],
                    writes=[self.t4b])
            self.V_("reduce_max", self.g2v.ap, self.t4b.ap, AX.X,
                    reads=[self.t4b], writes=[self.g2v])
            self.V_("tensor_scalar", self.mask4.ap, self.gsv.ap, self.g2v.ap,
                    None, ALU.is_ge, reads=[self.gsv, self.g2v], writes=[self.mask4])
            self.V_("tensor_scalar", self.madd4.ap, self.mask4.ap, -1.0, 1e30,
                    ALU.add, ALU.mult, reads=[self.mask4], writes=[self.madd4])
            sb3 = self.sbv.ap.rearrange("p (g e) -> p g e", e=8)
            msb3 = self.msb.ap.rearrange("p (g e) -> p g e", e=8)
            self.V_("tensor_tensor", msb3, sb3,
                    self.madd4.ap.unsqueeze(2).broadcast_to([P, 4, 8]), ALU.add,
                    reads=[self.sbv, self.madd4], writes=[self.msb])
            self.V_("tensor_copy", self.cur.ap, self.msb.ap,
                    reads=[self.msb], writes=[self.cur])
            for _ in range(7):
                self.V_("reduce_max", self.mx1.ap, self.cur.ap, AX.X,
                        reads=[self.cur], writes=[self.mx1])
                self.V_("tensor_scalar", self.mk32.ap, self.cur.ap, self.mx1.ap,
                        None, ALU.is_ge, reads=[self.cur, self.mx1],
                        writes=[self.mk32])
                self.V_("scalar_tensor_tensor", self.cur.ap, self.mk32.ap,
                        self.c_n1e30.ap, self.cur.ap, ALU.mult, ALU.add,
                        reads=[self.mk32, self.cur, self.c_n1e30],
                        writes=[self.cur])
            self.V_("reduce_max", self.th8.ap, self.cur.ap, AX.X,
                    reads=[self.cur], writes=[self.th8])
            self.V_("tensor_scalar", self.selm.ap, self.msb.ap, self.th8.ap,
                    None, ALU.is_ge, reads=[self.msb, self.th8], writes=[self.selm])
            self.V_("tensor_tensor", self.wsl.ap, self.selm.ap, self.scrs.ap,
                    ALU.mult, reads=[self.selm, self.scrs], writes=[self.wsl])
            self.V_("reduce_sum", self.s8.ap, self.wsl.ap, AX.X,
                    reads=[self.wsl], writes=[self.s8])
            self.V_("reciprocal", self.rs8.ap, self.s8.ap,
                    reads=[self.s8], writes=[self.rs8])
            self.V_("tensor_scalar", self.rs25.ap, self.rs8.ap, 2.5, None,
                    ALU.mult, reads=[self.rs8], writes=[self.rs25])
            self.V_("tensor_scalar", self.cw.ap[:, t, :], self.wsl.ap,
                    self.rs25.ap, None, ALU.mult,
                    reads=[self.wsl, self.rs25], writes=[self.cw])

    def load_slab(self, lane, slab, blk):
        for j in range(4):
            t = blk * 4 + j
            self.dma(lane, slab.ap[:, :, j * P:(j + 1) * P], self.ln2T_s[t],
                     reads=[("ln2T_s", t)], writes=[slab])

    def b1_blk(self, blk):
        slab = self.slab[blk % 2]
        self.load_slab("slab%d" % (blk % 2), slab, blk)
        for lc in range(8):
            bank = lc % 4
            psb = self.PSR("A", bank * 512, (bank + 1) * 512)
            for kc in range(NKC):
                self.T_("matmul", self.psA[:, bank * 512:(bank + 1) * 512],
                        self.fc1T.ap[:, kc, lc * P:(lc + 1) * P],
                        slab.ap[:, kc, :],
                        start=(kc == 0), stop=(kc == NKC - 1),
                        reads=[self.fc1T, slab], writes=[psb])
            self.act(self.x_latT.ap[:, lc, blk * 512:(blk + 1) * 512],
                     self.psA[:, bank * 512:(bank + 1) * 512], ACTF.Copy,
                     reads=[psb], writes=[self.x_latT])

    def b2(self):
        NT = self.NT
        nblks = NT // 4
        for e in range(NE):
            w1 = self.w1b[e % 2]
            w2 = self.w2b[e % 2]
            for lc in range(8):
                self.dma("w1b%d" % (e % 2), w1.ap[:, lc, :],
                         self.w1Td[e, lc * P:(lc + 1) * P, :],
                         reads=[w1], writes=[w1])
            for ic in range(8):
                self.dma("w2b%d" % (e % 2), w2.ap[:, ic, :],
                         self.w2Td[e, ic * P:(ic + 1) * P, :],
                         reads=[w2], writes=[w2])
            for blk in range(nblks):
                toff = blk * 512
                for ic in range(8):
                    bank = ic % 4
                    psb = self.PSR("A", bank * 512, (bank + 1) * 512)
                    for lc in range(8):
                        self.T_("matmul",
                                self.psA[:, bank * 512:(bank + 1) * 512],
                                w1.ap[:, lc, ic * P:(ic + 1) * P],
                                self.x_latT.ap[:, lc, toff:toff + 512],
                                start=(lc == 0), stop=(lc == 7),
                                reads=[w1, self.x_latT], writes=[psb])
                    rt = self.rtmp2[ic % 2]
                    self.V_("tensor_scalar", rt.ap,
                            self.psA[:, bank * 512:(bank + 1) * 512],
                            0.0, None, ALU.max, reads=[psb], writes=[rt])
                    self.A_("activation", self.h_all.ap[:, ic, :], rt.ap,
                            ACTF.Square, reads=[rt], writes=[self.h_all])
                for lh in range(2):
                    for tt in range(4):
                        psb = self.PSR("B", tt * 512, (tt + 1) * 512)
                        for ic in range(8):
                            self.T_("matmul",
                                    self.psB[:, tt * 512:(tt + 1) * 512],
                                    self.h_all.ap[:, ic, tt * P:(tt + 1) * P],
                                    w2.ap[:, ic, lh * 512:(lh + 1) * 512],
                                    start=(ic == 0), stop=(ic == 7),
                                    reads=[self.h_all, w2], writes=[psb])
                        tg = blk * 4 + tt
                        cwap = self.cw.ap[:, tg, e:e + 1]
                        ydst = self.y_lat.ap[:, tg, lh * 512:(lh + 1) * 512]
                        if e == 0:
                            self.V_("tensor_scalar", ydst,
                                    self.psB[:, tt * 512:(tt + 1) * 512],
                                    cwap, None, ALU.mult,
                                    reads=[psb, self.cw], writes=[self.y_lat])
                        else:
                            self.V_("scalar_tensor_tensor", ydst,
                                    self.psB[:, tt * 512:(tt + 1) * 512],
                                    cwap, ydst, ALU.mult, ALU.add,
                                    reads=[psb, self.cw, self.y_lat],
                                    writes=[self.y_lat])
        for tg in range(NT):
            self.V_("tensor_copy", self.ylb.ap, self.y_lat.ap[:, tg, :],
                    reads=[self.y_lat], writes=[self.ylb])
            self.transpose("trpY", self.ylT.ap, self.ylb.ap,
                           reads=[self.ylb, self.ylT], writes=[self.ylT])
            self.dma("sylT", self.ylT_s[:, :, tg * P:(tg + 1) * P], self.ylT.ap,
                     reads=[self.ylT], writes=[("ylT_s", tg)])

    def b3(self):
        NSLAB = self.T // 512
        for blk in range(NSLAB):
            slab = self.slab3
            self.load_slab("slab3", slab, blk)
            self.dma("yl3", self.yl3.ap,
                     self.ylT_s[:, :, blk * 512:(blk + 1) * 512],
                     reads=[self.yl3] + [("ylT_s", blk * 4 + j) for j in range(4)],
                     writes=[self.yl3])
            for sig in range(5):
                if sig < 4:
                    for kc in range(NKC):
                        self.dma("upw", self.upTb.ap[:, kc, :],
                                 self.upTd[kc * P:(kc + 1) * P,
                                           sig * 1024:(sig + 1) * 1024],
                                 reads=[self.upTb], writes=[self.upTb])
                    for sc in range(8):
                        self.dma("dnw", self.dnb.ap[:, sc, :],
                                 self.dfT[sig * 1024 + sc * P:
                                          sig * 1024 + (sc + 1) * P, :],
                                 reads=[self.dnb], writes=[self.dnb])
                    for si in range(8):
                        bank = si % 4
                        psb = self.PSR("A", bank * 512, (bank + 1) * 512)
                        for kc in range(NKC):
                            self.T_("matmul",
                                    self.psA[:, bank * 512:(bank + 1) * 512],
                                    self.upTb.ap[:, kc, si * P:(si + 1) * P],
                                    slab.ap[:, kc, :],
                                    start=(kc == 0), stop=(kc == NKC - 1),
                                    reads=[self.upTb, slab], writes=[psb])
                        rt = self.rtmp3[si % 2]
                        self.V_("tensor_scalar", rt.ap,
                                self.psA[:, bank * 512:(bank + 1) * 512],
                                0.0, None, ALU.max, reads=[psb], writes=[rt])
                        self.A_("activation", self.sTs.ap[:, si, :], rt.ap,
                                ACTF.Square, reads=[rt], writes=[self.sTs])
                else:
                    for sc in range(8):
                        self.dma("dnw", self.dnb.ap[:, sc, :],
                                 self.dfT[SI + sc * P:SI + (sc + 1) * P, :],
                                 reads=[self.dnb], writes=[self.dnb])
                for t in range(4):
                    tg = blk * 4 + t
                    psb = self.PSR("B", 0, 2048)
                    for sc in range(8):
                        if sig < 4:
                            lhs = self.sTs.ap[:, sc, t * P:(t + 1) * P]
                            lrd = self.sTs
                        else:
                            lhs = self.yl3.ap[:, sc, t * P:(t + 1) * P]
                            lrd = self.yl3
                        for co in range(4):
                            self.T_("matmul", self.psB[:, co * 512:(co + 1) * 512],
                                    lhs, self.dnb.ap[:, sc, co * 512:(co + 1) * 512],
                                    start=(sc == 0), stop=(sc == 7),
                                    reads=[lrd, self.dnb],
                                    writes=[self.PSR("B", co * 512, (co + 1) * 512)])
                    rdst = self.rout.ap[:, t, :]
                    if sig == 0:
                        self.V_("tensor_copy", rdst, self.psB[:, :],
                                reads=[psb], writes=[self.rout])
                    else:
                        self.V_("tensor_tensor", rdst, rdst, self.psB[:, :],
                                ALU.add, reads=[psb, self.rout], writes=[self.rout])
            for t in range(4):
                tg = blk * 4 + t
                rsrc = self.rout.ap[:, t, :]
                self.bn(self.rout, rsrc)
                self.ln_rstd(self.mv.ap[:, 1:2], self.c_eps5, self.rstd, [self.mv])
                self.dma("x1l3", self.x1l3.ap, self.x1_s[tg],
                         reads=[("x1_s", tg)], writes=[self.x1l3])
                self.V_("tensor_scalar", self.outb.ap, rsrc, self.mv.ap[:, 0:1],
                        self.rstd.ap, ALU.subtract, ALU.mult,
                        reads=[self.rout, self.mv, self.rstd], writes=[self.outb])
                self.V_("tensor_tensor", self.outb.ap, self.outb.ap, self.x1l3.ap,
                        ALU.add, reads=[self.outb, self.x1l3], writes=[self.outb])
                self.dma("sout", self.out[tg * P:(tg + 1) * P, :], self.outb.ap,
                         reads=[self.outb], writes=[("out", tg)])

    def build_all(self):
        self.declare()
        self.const_loads()
        self.a0()
        self.tern(self.wqT, self.mclip_q, CQ)
        self.a1("q")
        self.tern(self.wvT, self.mclip_v, CK)
        self.a1("v")
        self.tern(self.wkT, self.mclip_k, CK)
        self.a1("k")
        self.a2()
        self.tern(self.woT, self.mclip_o, CK)
        self.a3()
        self.b0()
        self.b2()
        self.b3()
        self.p.finish()


LANES = {}
for _l in (["xt0", "xt1", "ld0", "ld1", "x1ld0", "x1ld1",
            "slab0", "slab1", "slab3", "x1l3", "sxqT_s", "sqrT", "syqT_s",
            "sx1", "sln2T", "sout", "sylT", "yl3", "trp0", "trp1",
            "trpP0", "trpP1", "trpY"]
           + ["skr%d" % h for h in range(16)] + ["sv%d" % g for g in range(4)]):
    LANES[_l] = "S"
for _l in ("cb", "ones", "gbias", "cc", "ss", "ch0", "vs0", "vs1"):
    LANES[_l] = "A"
for _l in ("ch1", "ks0", "ks1"):
    LANES[_l] = "G"
for _l in ("gw", "fc1", "w1b0", "w1b1", "w2b0", "w2b1", "upw", "dnw"):
    LANES[_l] = "G"


def build_nc(NT=16):
    nc = bass.Bass("TRN2", target_bir_lowering=False)
    from contextlib import ExitStack
    with ExitStack() as st:
        sems = {}
        for name in ("T", "V", "A", "G"):
            sems[name] = st.enter_context(nc.semaphore("sem" + name))
        for name in LANES:
            sems[name] = st.enter_context(nc.semaphore("q" + name))
        prog = Prog(sems, LANES)
        b = Builder(nc, prog, NT)
        b.build_all()
        block = st.enter_context(nc.Block())
        prog.realize(block)
    return nc


def _host_prep(L0_attn_Wq, L0_attn_Wk, L0_attn_Wv, L0_attn_Wo, L0_ffn_gate_w,
               L0_ffn_gate_bias, L0_ffn_fc1_w, L0_ffn_fc2_w, L0_ffn_w1,
               L0_ffn_w2, L0_ffn_shared_up_w, L0_ffn_shared_down_w, T=2048):
    cont = np.ascontiguousarray
    perm = np.concatenate([
        np.concatenate([np.arange(h * 128, (h + 1) * 128, 2),
                        np.arange(h * 128 + 1, (h + 1) * 128, 2)])
        for h in range(16)])
    shared = {
        "wqT": cont(L0_attn_Wq[perm].T),
        "wkT": cont(L0_attn_Wk[perm].T),
        "wvT": cont(L0_attn_Wv.T),
        "woT": cont(L0_attn_Wo.T),
        "gwT": cont(L0_ffn_gate_w.T),
        "gbiasd": cont(np.tile(L0_ffn_gate_bias[None, :].astype(np.float32),
                               (128, 1))),
        "fc1Td": cont(L0_ffn_fc1_w.T),
        "dfT": cont(np.concatenate([L0_ffn_shared_down_w.T,
                                    L0_ffn_fc2_w.T], axis=0)),
        "w1Td": cont(L0_ffn_w1.transpose(0, 2, 1)),
        "w2Td": cont(L0_ffn_w2.transpose(0, 2, 1)),
        "upTd": cont(L0_ffn_shared_up_w.T),
    }
    import jax
    import jax.numpy as jnp
    with jax.default_device(jax.devices("cpu")[0]):
        freqs = 1.0 / (10000.0 ** (jnp.arange(0, 128, 2, dtype=jnp.float32) / 128))
        ang = jnp.arange(T, dtype=jnp.float32)[:, None] * freqs[None, :]
        cos = np.asarray(jnp.cos(ang), dtype=np.float32)
        sin = np.asarray(jnp.sin(ang), dtype=np.float32)
    shared["cc"] = cont(np.concatenate([cos, cos], axis=1))
    shared["ss"] = cont(np.concatenate([-sin, sin], axis=1))
    ii = np.arange(128)
    shared["cbm"] = np.where(ii[:, None] >= ii[None, :], 0.0,
                             -1e30).astype(np.float32)
    shared["onesd"] = np.ones((128, 128), np.float32)
    return shared


def kernel(L0_attn_x, L0_attn_Wq, L0_attn_Wk, L0_attn_Wv, L0_attn_Wo,
           L0_ffn_gate_w, L0_ffn_gate_bias, L0_ffn_fc1_w, L0_ffn_fc2_w,
           L0_ffn_w1, L0_ffn_w2, L0_ffn_shared_up_w, L0_ffn_shared_down_w):
    shared = _host_prep(L0_attn_Wq, L0_attn_Wk, L0_attn_Wv, L0_attn_Wo,
                        L0_ffn_gate_w, L0_ffn_gate_bias, L0_ffn_fc1_w,
                        L0_ffn_fc2_w, L0_ffn_w1, L0_ffn_w2,
                        L0_ffn_shared_up_w, L0_ffn_shared_down_w)
    nc = build_nc(NT=16)
    in_maps = []
    for c in range(8):
        m = dict(shared)
        m["x"] = np.ascontiguousarray(L0_attn_x[c])
        in_maps.append(m)
    res = run_bass_kernel_spmd(nc, in_maps, core_ids=list(range(8)))
    return np.stack([res.results[c]["out"] for c in range(8)], axis=0)


# version 13
# speedup vs baseline: 2.0255x; 1.0446x over previous
import numpy as np
import ml_dtypes
import concourse.bass as bass
import concourse.mybir as mybir
from concourse.bass_utils import run_bass_kernel_spmd

F32 = mybir.dt.float32
BF16 = mybir.dt.bfloat16
U8 = mybir.dt.uint8
AX = mybir.AxisListType
ALU = mybir.AluOpType
ACTF = mybir.ActivationFunctionType

P = 128
C = 2048
H = 16
D = 128
NKC = C // P
NE = 32
LAT = 1024
INTER = 1024
SI = 4096
KB = 1024
PGSZ = 2048

CQ = 1.0 / (127.0 * float(np.sqrt(np.float64(128.0))))
CK = 1.0 / 127.0
INV_N = 1.0 / (2048.0 * 2048.0)
M32 = float(2 ** 23 * 1.5)
M64 = float(2 ** 52 * 1.5)


class Buf:
    __slots__ = ("ap", "keys")

    def __init__(self, ap, keys):
        self.ap = ap
        self.keys = frozenset(keys)


class Prog:
    ENGS = ("T", "V", "A", "G", "S")

    def __init__(self, sems, lane_queue):
        self.sems = sems
        self.lane_queue = lane_queue
        self.cnt = {k: 0 for k in sems}
        self.steps = {e: [] for e in self.ENGS}
        self.seen = {e: {} for e in self.ENGS}
        self.pages = {}
        self.dirty = {e: set() for e in self.ENGS}
        self.rdirty = {e: set() for e in self.ENGS}

    def _rec(self, k):
        r = self.pages.get(k)
        if r is None:
            r = [None, []]
            self.pages[k] = r
        return r

    @staticmethod
    def _keys(bufs):
        s = set()
        for b in bufs:
            s |= b.keys if isinstance(b, Buf) else {b}
        return s

    def op(self, eng, fn, reads=(), writes=(), lane=None):
        rk = self._keys(reads)
        wk = self._keys(writes)
        waits = []
        seen = self.seen[eng]

        def need(sem, val, agent):
            if agent != eng and agent in self.lane_queue:
                val = self.cnt[agent]
            if val > 0 and seen.get(sem, 0) < val:
                waits.append((self.sems[sem], val))
                seen[sem] = val

        drain = False
        for k in rk | wk:
            rec = self.pages.get(k)
            if rec is None:
                continue
            w = rec[0]
            if w is not None:
                sem, val, agent = w
                if agent == eng:
                    if k in self.dirty[eng]:
                        drain = True
                else:
                    need(sem, val, agent)
            if k in wk:
                for sem, val, agent in rec[1]:
                    if agent == eng:
                        if k in self.rdirty[eng]:
                            drain = True
                    else:
                        need(sem, val, agent)
        if eng == "T":
            drain = False

        if lane is None:
            sem_name, inc = eng, 1
            agent = eng
        else:
            sem_name, inc = lane, 16
            agent = lane
        self.cnt[sem_name] += inc
        val = self.cnt[sem_name]

        if drain:
            self.dirty[eng].clear()
            self.rdirty[eng].clear()
        for k in wk:
            rec = self._rec(k)
            rec[0] = (sem_name, val, agent)
            rec[1] = []
        for k in rk:
            self._rec(k)[1].append((sem_name, val, agent))
        if lane is None:
            self.dirty[eng] |= wk
            self.rdirty[eng] |= rk
        self.steps[eng].append((tuple(waits), drain, fn, self.sems[sem_name], inc))
        return val

    def finish(self):
        waits = [(self.sems[k], v) for k, v in self.cnt.items() if v > 0]
        self.steps["S"].append((tuple(waits), False, None, None, 0))

    def realize(self, block):
        names = {"T": "tensor", "V": "vector", "A": "scalar", "G": "gpsimd", "S": "sync"}
        for e in self.ENGS:
            steps = self.steps[e]

            def mk(steps=steps):
                def body(eng):
                    for waits, drain, fn, sem, inc in steps:
                        for s, v in waits:
                            eng.wait_ge(s, v)
                        if drain:
                            eng.drain()
                        if fn is not None:
                            inst = fn(eng)
                            if sem is not None:
                                inst.then_inc(sem, inc)
                return body
            getattr(block, names[e])(mk())


class Builder:
    def __init__(self, nc, prog, NT):
        self.nc = nc
        self.p = prog
        self.NT = NT
        self.T = NT * P

    def T_(self, fname, *args, reads=(), writes=(), **kw):
        self.p.op("T", (lambda e, a=args, k=kw, f=fname: getattr(e, f)(*a, **k)),
                  reads, writes)

    def V_(self, fname, *args, reads=(), writes=(), **kw):
        self.p.op("V", (lambda e, a=args, k=kw, f=fname: getattr(e, f)(*a, **k)),
                  reads, writes)

    def A_(self, fname, *args, reads=(), writes=(), **kw):
        self.p.op("A", (lambda e, a=args, k=kw, f=fname: getattr(e, f)(*a, **k)),
                  reads, writes)

    def dma(self, lane, out_ap, in_ap, reads=(), writes=()):
        self.p.op(self.p.lane_queue[lane],
                  (lambda e, o=out_ap, i=in_ap: e.dma_start(out=o, in_=i)),
                  reads, writes, lane=lane)

    def transpose(self, lane, out_ap, in_ap, reads=(), writes=()):
        self.p.op("S", (lambda e, o=out_ap, i=in_ap: e.dma_start_transpose(o, i)),
                  reads, writes, lane=lane)

    def AR(self, off_kb, shape, dtype):
        nel = int(np.prod(shape[1:]))
        nby = nel * mybir.dt.size(dtype)
        if dtype == BF16:
            arena, tag, epk = self.arena16, "A6", 512
        else:
            arena, tag, epk = self.arena32, "A2", 256
        ap = arena[:, off_kb * epk: off_kb * epk + nel]
        if len(shape) == 3:
            ap = ap.rearrange("p (a b) -> p a b", b=shape[2])
        keys = {(tag, g) for g in range(off_kb * KB // PGSZ,
                                        (off_kb * KB + nby + PGSZ - 1) // PGSZ)}
        return Buf(ap, keys)

    def SB(self, name, shape, dtype=F32, tdest=False):
        h = self.nc.alloc_sbuf_tensor(name, shape, dtype, is_transpose_dest=tdest)
        return Buf(h[:], {name})

    def PSR(self, which, lo, hi):
        t = self.psA if which == "A" else self.psB
        keys = {("ps" + which, b) for b in range(lo // 512, (hi + 511) // 512)}
        return Buf(t[:, lo:hi], keys)

    def declare(self):
        nc = self.nc
        NT, T = self.NT, self.T
        dr = lambda n, s, dt=F32, kind="ExternalInput": nc.dram_tensor(n, s, dt, kind=kind).ap()
        self.x = dr("x", [T, C])
        self.wqT = dr("wqT", [C, C]); self.wkT = dr("wkT", [C, C])
        self.wvT = dr("wvT", [C, C]); self.woT = dr("woT", [C, C])
        self.cc = dr("cc", [T, D]); self.ss = dr("ss", [T, D])
        self.cbm = dr("cbm", [P, P]); self.onesd = dr("onesd", [P, P])
        self.gbiasd = dr("gbiasd", [P, NE])
        self.gwT = dr("gwT", [C, NE])
        self.fc1Td = dr("fc1Td", [C, LAT])
        self.dfT = dr("dfT", [SI + LAT, C])
        self.w1Td = dr("w1Td", [NE, LAT, INTER])
        self.w2Td = dr("w2Td", [NE, INTER, LAT])
        self.upTd = dr("upTd", [C, SI])
        self.out = dr("out", [T, C], F32, kind="ExternalOutput")
        idr = lambda n, s, dt: nc.dram_tensor(n, s, dt, kind="Internal").ap()
        self.xqT_s = idr("xqT_s", [NT, P, NKC, P], BF16)
        self.qrT_s = idr("qrT_s", [NT, P, NKC, P], BF16)
        self.krT_s = idr("krT_s", [H, P, T], BF16)
        self.v_s = idr("v_s", [4, T, 512], BF16)
        self.yqT_s = idr("yqT_s", [NT, P, NKC, P], BF16)
        self.x1_s = idr("x1_s", [NT, P, C], F32)
        self.ln2T_s = idr("ln2T_s", [NT, P, NKC, P], BF16)
        self.ylT_s = idr("ylT_s", [P, 8, T], BF16)

        self.psA = nc.alloc_psum_tensor("psA", [P, 2048], F32)
        self.psB = nc.alloc_psum_tensor("psB", [P, 2048], F32)

        S = self.SB
        self.tr = [S("tr0", [P, NKC, P], BF16, tdest=True),
                   S("tr1", [P, NKC, P], BF16, tdest=True)]
        self.pT = S("pT", [P, NT, P], BF16, tdest=True)
        self.ylT = S("ylT", [P, 8, P], BF16, tdest=True)
        self.cb = S("cb", [P, P]); self.ones = S("ones", [P, P])
        self.cct = S("cct", [P, D]); self.sst = S("sst", [P, D])
        self.gbias = S("gbias", [P, NE])
        self.gw_sb = S("gw_sb", [P, NKC, NE], BF16)
        self.samc = S("samc", [P, NT]); self.samc_y = S("samc_y", [P, NT])
        self.cw = S("cw", [P, NT, NE])
        self.bst = S("bst", [P, 4, 6]); self.mv = S("mv", [P, 2])
        self.cs16 = S("cs16", [P, NKC]); self.cstot = S("cstot", [P, 1])
        for n in ("mclip_q", "mclip_k", "mclip_v", "mclip_o", "thr", "nthr",
                  "lnv", "rstd", "rsq", "bq", "rr", "sfold", "am", "amc",
                  "rmc", "qsc", "qsc2", "sc1", "s2t", "vv", "gv",
                  "nm", "den", "rden", "ss2v",
                  "g1v", "g2v", "m1g", "m2g", "th8", "s8", "rs8", "mx1",
                  "nm2", "den2", "rden2"):
            setattr(self, n, S(n, [P, 1]))
        self.nmp = [self.nm, self.nm2]
        self.denp = [self.den, self.den2]
        self.rdenp = [self.rden, self.rden2]
        self.c_eps5 = S("c_eps5", [P, 1]); self.c_eps6 = S("c_eps6", [P, 1])
        self.c_2565 = S("c_2565", [P, 1])
        self.c_m256 = S("c_m256", [P, 1]); self.c_n1e30 = S("c_n1e30", [P, 1])
        self.c_one = S("c_one", [P, 1]); self.gdisc = S("gdisc", [P, 1])
        self.c_zero = S("c_zero", [P, 1])
        self.mqs = S("mqs", [P, 1]); self.rs25 = S("rs25", [P, 1])
        self.et = S("et", [P, NE]); self.d1 = S("d1", [P, NE])
        self.scrs = S("scrs", [P, NE]); self.sbv = S("sbv", [P, NE])
        self.msb = S("msb", [P, NE]); self.cur = S("cur", [P, NE])
        self.mk32 = S("mk32", [P, NE]); self.wsl = S("wsl", [P, NE])
        self.selm = S("selm", [P, NE])
        self.gsv = S("gsv", [P, 4]); self.mask4 = S("mask4", [P, 4])
        self.t4b = S("t4b", [P, 4]); self.madd4 = S("madd4", [P, 4])
        self.msk8 = S("msk8", [P, 8]); self.t8b = S("t8b", [P, 8])

        self.arena16 = nc.alloc_sbuf_tensor("arena16", [P, 108 * 512], BF16)
        self.arena32 = nc.alloc_sbuf_tensor("arena32", [P, 76 * 256], F32)
        A = self.AR
        self.wtern = A(0, [P, NKC, C], BF16)
        self.xqb = A(64, [P, C], BF16)
        self.vbb = A(68, [P, C], BF16)
        self.ld = [A(72, [P, NKC, P], BF16), A(76, [P, NKC, P], BF16)]
        self.pb = A(80, [P, 2048], BF16)
        self.ks = [A(84, [P, 2048], BF16), A(88, [P, 2048], BF16)]
        self.vsh = [A(92, [P, NT, 256], BF16),
                    A(100, [P, NT, 256], BF16)]
        self.pbp = [self.pb, self.vbb]
        self.chunk = [A(0, [P, C], F32), A(8, [P, C], F32)]
        self.xt = [A(16, [P, C], F32), A(24, [P, C], F32)]
        self.xn = A(32, [P, C], F32)
        self.qf = A(40, [P, C], F32)
        self.uu = A(48, [P, C], F32)
        self.trt = A(56, [P, C], F32)
        self.y_scr = [A(64, [P, C], F32), A(64, [P, C], F32)]
        self.fc1T = A(76, [P, NKC, LAT], BF16)
        self.x_latT = A(0, [P, 8, T], BF16)
        self.slab = [A(32, [P, NKC, 512], BF16), A(48, [P, NKC, 512], BF16)]
        self.w1b = [A(32, [P, 8, INTER], BF16), A(48, [P, 8, INTER], BF16)]
        self.w2b = [A(64, [P, 8, LAT], BF16), A(80, [P, 8, LAT], BF16)]
        self.h_all = A(96, [P, 8, 512], BF16)
        self.ylb = A(104, [P, LAT], BF16)
        self.dnb = A(0, [P, 8, C], BF16)
        self.slab3 = A(32, [P, NKC, 1024], BF16)
        self.sTs = A(64, [P, 8, 1024], BF16)
        self.upb = [A(80, [P, NKC, P], BF16),
                    A(84, [P, NKC, P], BF16)]
        self.yl3 = A(88, [P, 8, 1024], BF16)
        self.x1ld = [A(0, [P, C], F32), A(8, [P, C], F32)]
        self.y_lat = A(0, [P, NT, LAT], F32)
        self.rtmp2 = [A(64, [P, 512], F32), A(66, [P, 512], F32)]
        self.rout = A(0, [P, 8, C], F32)
        self.x1l3 = A(64, [P, C], F32)
        self.rtmp3 = [A(72, [P, 512], F32), A(74, [P, 512], F32)]

    def act(self, out, in_, func, bias=0.0, scale=1.0, accum=None,
            reads=(), writes=()):
        kw = dict(bias=bias, scale=scale)
        if accum is not None:
            kw["accum_out"] = accum
        self.A_("activation", out, in_, func, reads=reads, writes=writes, **kw)

    def bn(self, src_buf, src_ap2048, extra_reads=()):
        for i in range(4):
            self.V_("bn_stats", self.bst.ap[:, i, :], src_ap2048[:, i * 512:(i + 1) * 512],
                    reads=[src_buf, *extra_reads], writes=[self.bst])
        self.V_("bn_aggr", self.mv.ap, self.bst.ap,
                reads=[self.bst], writes=[self.mv])

    def ln_rstd(self, var_ap, eps_buf, out_buf, reads):
        self.act(self.lnv.ap, var_ap, ACTF.Ln, bias=eps_buf.ap, scale=1.0,
                 reads=list(reads) + [eps_buf], writes=[self.lnv])
        self.act(out_buf.ap, self.lnv.ap, ACTF.Exp, bias=0.0, scale=-0.5,
                 reads=[self.lnv], writes=[out_buf])

    def const_loads(self):
        self.V_("memset", self.c_eps5.ap, 1e-5, writes=[self.c_eps5])
        self.V_("memset", self.c_eps6.ap, 1e-6, writes=[self.c_eps6])
        self.V_("memset", self.c_2565.ap, 256.5, writes=[self.c_2565])
        self.V_("memset", self.c_m256.ap, -256.0, writes=[self.c_m256])
        self.V_("memset", self.c_n1e30.ap, -1e30, writes=[self.c_n1e30])
        self.V_("memset", self.c_one.ap, 1.0, writes=[self.c_one])
        self.V_("memset", self.c_zero.ap, 0.0, writes=[self.c_zero])
        self.V_("tensor_single_scalar", self.gdisc.ap, self.c_one.ap, M64,
                ALU.add, reads=[self.c_one], writes=[self.gdisc])
        self.V_("tensor_single_scalar", self.gdisc.ap, self.gdisc.ap, M64,
                ALU.subtract, reads=[self.gdisc], writes=[self.gdisc])
        self.dma("cb", self.cb.ap, self.cbm, writes=[self.cb])
        self.dma("ones", self.ones.ap, self.onesd, writes=[self.ones])
        self.dma("gbias", self.gbias.ap, self.gbiasd, writes=[self.gbias])
        self.dma("gw", self.gw_sb.ap,
                 self.gwT.rearrange("(a p) n -> p a n", p=P), writes=[self.gw_sb])
        from concourse.hw_specs import get_activation_tables
        idx = None
        for i, funcs in enumerate(get_activation_tables(self.nc.m.arch).values()):
            if ACTF.Exp in funcs and ACTF.Ln in funcs:
                idx = i
                break
        if idx is not None:
            nc = self.nc
            def fn(e, idx=idx, nc=nc):
                return e.add_instruction(mybir.InstLoadActFuncSet(
                    name=nc.get_next_instruction_name(),
                    act_func_set_id=idx, ins=[], outs=[]))
            self.p.op("A", fn, (), ())

    def tern(self, wdram, mclip, cmul):
        NCH = NKC
        for kc in range(NCH):
            ch = self.chunk[kc % 2]
            self.dma("ch%d" % (kc % 2), ch.ap, wdram[kc * P:(kc + 1) * P, :],
                     reads=[ch], writes=[ch])
            self.V_("reduce_sum", self.cs16.ap[:, kc:kc + 1], ch.ap, AX.X,
                    apply_absolute_value=True, reads=[ch], writes=[self.cs16])
        self.V_("reduce_sum", self.cstot.ap, self.cs16.ap, AX.X,
                reads=[self.cs16], writes=[self.cstot])
        ps1 = self.PSR("A", 0, 512)
        self.T_("matmul", self.psA[:, 0:1], self.ones.ap, self.cstot.ap,
                start=True, stop=True, reads=[self.ones, self.cstot], writes=[ps1])
        self.V_("tensor_scalar", mclip.ap, self.psA[:, 0:1], float(INV_N), 1e-5,
                ALU.mult, ALU.max, reads=[ps1], writes=[mclip])
        self.V_("tensor_scalar", self.thr.ap, mclip.ap, 0.5, None, ALU.mult,
                reads=[mclip], writes=[self.thr])
        self.V_("tensor_scalar", self.nthr.ap, mclip.ap, -0.5, None, ALU.mult,
                reads=[mclip], writes=[self.nthr])
        self.V_("tensor_scalar", self.mqs.ap, mclip.ap, float(cmul), None,
                ALU.mult, reads=[mclip], writes=[self.mqs])
        for kc in range(NCH):
            ch = self.chunk[kc % 2]
            self.dma("ch%d" % (kc % 2), ch.ap, wdram[kc * P:(kc + 1) * P, :],
                     reads=[ch], writes=[ch])
            self.V_("tensor_scalar", self.qf.ap, ch.ap, self.nthr.ap, None,
                    ALU.is_lt, reads=[ch, self.nthr], writes=[self.qf])
            self.V_("scalar_tensor_tensor", self.wtern.ap[:, kc, :], ch.ap,
                    self.thr.ap, self.qf.ap, ALU.is_gt, ALU.subtract,
                    reads=[ch, self.thr, self.qf], writes=[self.wtern])

    def a0(self):
        for t in range(self.NT):
            xt = self.xt[t % 2]
            self.dma("xt%d" % (t % 2), xt.ap, self.x[t * P:(t + 1) * P, :],
                     reads=[xt], writes=[xt])
            self.bn(xt, xt.ap)
            self.ln_rstd(self.mv.ap[:, 1:2], self.c_eps5, self.rstd, [self.mv])
            self.V_("tensor_tensor", self.rsq.ap, self.rstd.ap, self.rstd.ap,
                    ALU.mult, reads=[self.rstd], writes=[self.rsq])
            self.V_("tensor_tensor", self.bq.ap, self.mv.ap[:, 1:2], self.rsq.ap,
                    ALU.mult, reads=[self.mv, self.rsq], writes=[self.bq])
            self.act(self.lnv.ap, self.bq.ap, ACTF.Ln, bias=self.c_eps6.ap,
                     reads=[self.bq, self.c_eps6], writes=[self.lnv])
            self.act(self.rr.ap, self.lnv.ap, ACTF.Exp, scale=-0.5,
                     reads=[self.lnv], writes=[self.rr])
            self.V_("tensor_tensor", self.sfold.ap, self.rstd.ap, self.rr.ap,
                    ALU.mult, reads=[self.rstd, self.rr], writes=[self.sfold])
            self.V_("tensor_scalar", self.xn.ap, xt.ap, self.mv.ap[:, 0:1],
                    self.sfold.ap, ALU.subtract, ALU.mult,
                    reads=[xt, self.mv, self.sfold], writes=[self.xn])
            self.quant_store(self.xn, self.samc, t, self.xqT_s, "xqT_s",
                             scale_pre=None)

    def quant_store(self, src, samc, t, dst_dram, dst_key, scale_pre):
        self.V_("reduce_max", self.am.ap, src.ap, AX.X, apply_absolute_value=True,
                reads=[src], writes=[self.am])
        if scale_pre is not None:
            self.V_("tensor_tensor", self.amc.ap, self.am.ap, scale_pre.ap,
                    ALU.mult, reads=[self.am, scale_pre], writes=[self.amc])
            amsrc = self.amc
        else:
            amsrc = self.am
        self.V_("tensor_single_scalar", samc.ap[:, t:t + 1], amsrc.ap, 1e-5,
                ALU.max, reads=[amsrc], writes=[samc])
        self.V_("reciprocal", self.rmc.ap, samc.ap[:, t:t + 1],
                reads=[samc], writes=[self.rmc])
        self.V_("tensor_scalar", self.qsc.ap, self.rmc.ap, 127.0, None, ALU.mult,
                reads=[self.rmc], writes=[self.qsc])
        if scale_pre is not None:
            self.V_("tensor_tensor", self.qsc2.ap, self.qsc.ap, scale_pre.ap,
                    ALU.mult, reads=[self.qsc, scale_pre], writes=[self.qsc2])
            qs = self.qsc2
        else:
            qs = self.qsc
        self.V_("tensor_scalar", self.qf.ap, src.ap, qs.ap, None, ALU.mult,
                reads=[src, qs], writes=[self.qf])
        self.V_("tensor_single_scalar", self.uu.ap, self.qf.ap, M32, ALU.add,
                reads=[self.qf], writes=[self.uu])
        self.V_("tensor_single_scalar", self.uu.ap, self.uu.ap, M32,
                ALU.subtract, reads=[self.uu], writes=[self.uu])
        self.V_("tensor_single_scalar", self.trt.ap, self.qf.ap, M64, ALU.add,
                reads=[self.qf], writes=[self.trt])
        self.V_("tensor_single_scalar", self.trt.ap, self.trt.ap, M64,
                ALU.subtract, reads=[self.trt], writes=[self.trt])
        self.V_("tensor_tensor", self.trt.ap, self.trt.ap, self.uu.ap,
                ALU.subtract, reads=[self.trt, self.uu], writes=[self.trt])
        self.V_("scalar_tensor_tensor", self.xqb.ap, self.trt.ap,
                self.gdisc.ap, self.uu.ap, ALU.mult, ALU.add,
                reads=[self.trt, self.uu, self.gdisc], writes=[self.xqb])
        tr = self.tr[t % 2]
        self.transpose("trp%d" % (t % 2), tr.ap, self.xqb.ap,
                       reads=[self.xqb, tr], writes=[tr])
        self.dma("s" + dst_key, dst_dram[t], tr.ap, reads=[tr],
                 writes=[(dst_key, t)])

    def a1(self, kind):
        NT = self.NT
        for t in range(NT):
            ld = self.ld[t % 2]
            self.dma("ld%d" % (t % 2), ld.ap, self.xqT_s[t],
                     reads=[ld, ("xqT_s", t)], writes=[ld])
            psn = "A" if t % 2 == 0 else "B"
            pst = self.psA if psn == "A" else self.psB
            psb = self.PSR(psn, 0, 2048)
            for kc in range(NKC):
                for n in range(4):
                    self.T_("matmul", pst[:, n * 512:(n + 1) * 512],
                            ld.ap[:, kc, :], self.wtern.ap[:, kc, n * 512:(n + 1) * 512],
                            start=(kc == 0), stop=(kc == NKC - 1),
                            reads=[ld, self.wtern], writes=[psb])
            self.V_("tensor_scalar", self.sc1.ap, self.samc.ap[:, t:t + 1],
                    self.mqs.ap, None, ALU.mult,
                    reads=[self.samc, self.mqs], writes=[self.sc1])
            if kind == "v":
                self.V_("tensor_scalar", self.vbb.ap, pst[:, :], self.sc1.ap,
                        None, ALU.mult, reads=[psb, self.sc1], writes=[self.vbb])
                for hg in range(4):
                    self.dma("sv%d" % hg, self.v_s[hg, t * P:(t + 1) * P, :],
                             self.vbb.ap[:, hg * 512:(hg + 1) * 512],
                             reads=[self.vbb], writes=[("v_s", hg, t)])
                continue
            self.V_("tensor_scalar", self.qf.ap, pst[:, :], self.sc1.ap, None,
                    ALU.mult, reads=[psb, self.sc1], writes=[self.qf])
            self.dma("cc", self.cct.ap, self.cc[t * P:(t + 1) * P, :],
                     reads=[self.cct], writes=[self.cct])
            self.dma("ss", self.sst.ap, self.ss[t * P:(t + 1) * P, :],
                     reads=[self.sst], writes=[self.sst])
            qf3 = self.qf.ap.rearrange("p (h d) -> p h d", d=D)
            tr3 = self.trt.ap.rearrange("p (h d) -> p h d", d=D)
            ccb = self.cct.ap.unsqueeze(1).broadcast_to([P, H, D])
            uu3 = self.uu.ap.rearrange("p (h d) -> p h d", d=D)
            self.V_("tensor_tensor", uu3, qf3, ccb, ALU.mult,
                    reads=[self.qf, self.cct], writes=[self.uu])
            sslo = self.sst.ap[:, 0:64].unsqueeze(1).broadcast_to([P, H, 64])
            sshi = self.sst.ap[:, 64:128].unsqueeze(1).broadcast_to([P, H, 64])
            self.V_("tensor_tensor", tr3[:, :, 0:64], qf3[:, :, 64:128], sslo,
                    ALU.mult, reads=[self.qf, self.sst], writes=[self.trt])
            self.V_("tensor_tensor", tr3[:, :, 64:128], qf3[:, :, 0:64], sshi,
                    ALU.mult, reads=[self.qf, self.sst], writes=[self.trt])
            self.V_("tensor_tensor", self.xqb.ap, self.uu.ap, self.trt.ap,
                    ALU.add, reads=[self.uu, self.trt], writes=[self.xqb])
            tr = self.tr[t % 2]
            self.transpose("trp%d" % (t % 2), tr.ap, self.xqb.ap,
                           reads=[self.xqb, tr], writes=[tr])
            if kind == "q":
                self.dma("sqrT", self.qrT_s[t], tr.ap, reads=[tr],
                         writes=[("qrT_s", t)])
            else:
                for h in range(H):
                    self.dma("skr%d" % h, self.krT_s[h, :, t * P:(t + 1) * P],
                             tr.ap[:, h, :], reads=[tr], writes=[("krT_s", h)])

    def a2(self):
        NT = self.NT
        for qt in range(NT):
            L = (qt + 1) * P
            ld = self.ld[qt % 2]
            self.dma("ld%d" % (qt % 2), ld.ap, self.qrT_s[qt],
                     reads=[ld, ("qrT_s", qt)], writes=[ld])
            ysc = self.y_scr[qt % 2]
            pTs = [self.pT, self.tr[qt % 2]]
            nch = (L + 511) // 512

            def qk(h):
                hg = h // 4
                if h % 2 == 0:
                    half = (h % 4) // 2
                    vb = self.vsh[half]
                    src = self.v_s[hg].rearrange("(k p) f -> p k f", p=P)
                    self.dma("vs%d" % half, vb.ap[:, 0:qt + 1, :],
                             src[:, 0:qt + 1, half * 256:(half + 1) * 256],
                             reads=[vb] + [("v_s", hg, tt) for tt in range(qt + 1)],
                             writes=[vb])
                ks = self.ks[h % 2]
                self.dma("ks%d" % (h % 2), ks.ap[:, 0:L], self.krT_s[h, :, 0:L],
                         reads=[ks, ("krT_s", h)], writes=[ks])
                Rn = "A" if h % 2 == 0 else "B"
                Rt = self.psA if Rn == "A" else self.psB
                for cc_ in range(nch):
                    n = min(512, L - cc_ * 512)
                    self.T_("matmul", Rt[:, cc_ * 512:cc_ * 512 + n],
                            ld.ap[:, h, :], ks.ap[:, cc_ * 512:cc_ * 512 + n],
                            start=True, stop=True,
                            reads=[ld, ks],
                            writes=[self.PSR(Rn, cc_ * 512, cc_ * 512 + n)])

            def smx(h):
                Rn = "A" if h % 2 == 0 else "B"
                Rt = self.psA if Rn == "A" else self.psB
                Rb = self.PSR(Rn, 0, L)
                nm = self.nmp[h % 2]; den = self.denp[h % 2]
                rden = self.rdenp[h % 2]; pb = self.pbp[h % 2]
                self.V_("tensor_tensor", Rt[:, qt * P:L], Rt[:, qt * P:L],
                        self.cb.ap, ALU.add,
                        reads=[self.PSR(Rn, qt * P, L), self.cb],
                        writes=[self.PSR(Rn, qt * P, L)])
                self.V_("reduce_max", nm.ap, Rt[:, 0:L], AX.X,
                        negate=True, reads=[Rb], writes=[nm])
                self.act(pb.ap[:, 0:L], Rt[:, 0:L], ACTF.Exp,
                         bias=nm.ap, scale=1.0, accum=den.ap,
                         reads=[Rb, nm], writes=[pb, den])
                self.V_("reciprocal", rden.ap, den.ap,
                        reads=[den], writes=[rden])
                pT = pTs[h % 2]
                self.transpose("trpP%d" % (h % 2), pT.ap[:, 0:qt + 1, :],
                               pb.ap[:, 0:L], reads=[pb, pT], writes=[pT])

            def pv(h):
                h4 = h % 4
                Rn = "A" if h % 2 == 0 else "B"
                Yt = self.psA if Rn == "A" else self.psB
                Yb = self.PSR(Rn, 1920, 2048)
                pT = pTs[h % 2]; rden = self.rdenp[h % 2]
                vb = self.vsh[h4 // 2]
                for kt in range(qt + 1):
                    self.T_("matmul", Yt[:, 1920:2048], pT.ap[:, kt, :],
                            vb.ap[:, kt, (h4 % 2) * P:(h4 % 2 + 1) * P],
                            start=(kt == 0), stop=(kt == qt),
                            reads=[pT, vb], writes=[Yb])
                self.V_("tensor_scalar", ysc.ap[:, h * P:(h + 1) * P],
                        Yt[:, 1920:2048], rden.ap, None, ALU.mult,
                        reads=[Yb, rden], writes=[ysc])

            qk(0)
            smx(0)
            for h in range(16):
                if h + 1 < 16:
                    qk(h + 1)
                    smx(h + 1)
                pv(h)
            self.V_("tensor_tensor", self.xn.ap, ysc.ap, ysc.ap, ALU.mult,
                    reads=[ysc], writes=[self.xn])
            self.V_("reduce_sum", self.ss2v.ap, self.xn.ap, AX.X,
                    reads=[self.xn], writes=[self.ss2v])
            self.act(self.lnv.ap, self.ss2v.ap, ACTF.Ln, bias=self.c_eps6.ap,
                     scale=float(1.0 / C), reads=[self.ss2v, self.c_eps6],
                     writes=[self.lnv])
            self.act(self.rr.ap, self.lnv.ap, ACTF.Exp, scale=-0.5,
                     reads=[self.lnv], writes=[self.rr])
            self.quant_store(ysc, self.samc_y, qt, self.yqT_s, "yqT_s",
                             scale_pre=self.rr)

    def a3(self):
        NT = self.NT
        for t in range(NT):
            ld = self.ld[t % 2]
            self.dma("ld%d" % (t % 2), ld.ap, self.yqT_s[t],
                     reads=[ld, ("yqT_s", t)], writes=[ld])
            xt = self.xt[t % 2]
            self.dma("xt%d" % (t % 2), xt.ap, self.x[t * P:(t + 1) * P, :],
                     reads=[xt], writes=[xt])
            psn = "A" if t % 2 == 0 else "B"
            pst = self.psA if psn == "A" else self.psB
            psb = self.PSR(psn, 0, 2048)
            for kc in range(NKC):
                for n in range(4):
                    self.T_("matmul", pst[:, n * 512:(n + 1) * 512],
                            ld.ap[:, kc, :], self.wtern.ap[:, kc, n * 512:(n + 1) * 512],
                            start=(kc == 0), stop=(kc == NKC - 1),
                            reads=[ld, self.wtern], writes=[psb])
            self.V_("tensor_scalar", self.sc1.ap, self.samc_y.ap[:, t:t + 1],
                    self.mqs.ap, None, ALU.mult,
                    reads=[self.samc_y, self.mqs], writes=[self.sc1])
            self.bn(psb, pst)
            self.V_("tensor_tensor", self.s2t.ap, self.sc1.ap, self.sc1.ap,
                    ALU.mult, reads=[self.sc1], writes=[self.s2t])
            self.V_("tensor_tensor", self.vv.ap, self.mv.ap[:, 1:2], self.s2t.ap,
                    ALU.mult, reads=[self.mv, self.s2t], writes=[self.vv])
            self.ln_rstd(self.vv.ap, self.c_eps5, self.rstd, [self.vv])
            self.V_("tensor_tensor", self.gv.ap, self.sc1.ap, self.rstd.ap,
                    ALU.mult, reads=[self.sc1, self.rstd], writes=[self.gv])
            self.V_("tensor_scalar", self.xn.ap, pst[:, :], self.mv.ap[:, 0:1],
                    self.gv.ap, ALU.subtract, ALU.mult,
                    reads=[psb, self.mv, self.gv], writes=[self.xn])
            self.V_("tensor_tensor", self.qf.ap, self.xn.ap, xt.ap, ALU.add,
                    reads=[self.xn, xt], writes=[self.qf])
            self.dma("sx1", self.x1_s[t], self.qf.ap, reads=[self.qf],
                     writes=[("x1_s", t)])

    def b0(self):
        NT = self.NT
        for kc in range(NKC):
            self.dma("fc1", self.fc1T.ap[:, kc, :],
                     self.fc1Td[kc * P:(kc + 1) * P, :],
                     reads=[self.fc1T], writes=[self.fc1T])
        for t in range(NT):
            self.b0_tile(t)
            if t % 4 == 3:
                self.b1_blk(t // 4)

    def b0_tile(self, t):
        if True:
            x1 = self.x1ld[t % 2]
            self.dma("x1ld%d" % (t % 2), x1.ap, self.x1_s[t],
                     reads=[x1, ("x1_s", t)], writes=[x1])
            self.bn(x1, x1.ap)
            self.ln_rstd(self.mv.ap[:, 1:2], self.c_eps5, self.rstd, [self.mv])
            self.V_("tensor_scalar", self.xqb.ap, x1.ap, self.mv.ap[:, 0:1],
                    self.rstd.ap, ALU.subtract, ALU.mult,
                    reads=[x1, self.mv, self.rstd], writes=[self.xqb])
            tr = self.tr[t % 2]
            self.transpose("trp%d" % (t % 2), tr.ap, self.xqb.ap,
                           reads=[self.xqb, tr], writes=[tr])
            self.dma("sln2T", self.ln2T_s[t], tr.ap, reads=[tr],
                     writes=[("ln2T_s", t)])
            psg = self.PSR("B", 0, 512)
            for kc in range(NKC):
                self.T_("matmul", self.psB[:, 0:NE], tr.ap[:, kc, :],
                        self.gw_sb.ap[:, kc, :],
                        start=(kc == 0), stop=(kc == NKC - 1),
                        reads=[tr, self.gw_sb], writes=[psg])
            self.act(self.et.ap, self.psB[:, 0:NE], ACTF.Exp, scale=-1.0,
                     reads=[psg], writes=[self.et])
            self.V_("tensor_single_scalar", self.d1.ap, self.et.ap, 1.0, ALU.add,
                    reads=[self.et], writes=[self.d1])
            self.V_("reciprocal", self.scrs.ap, self.d1.ap,
                    reads=[self.d1], writes=[self.scrs])
            self.V_("tensor_tensor", self.sbv.ap, self.scrs.ap, self.gbias.ap,
                    ALU.add, reads=[self.scrs, self.gbias], writes=[self.sbv])
            for g in range(4):
                sg = self.sbv.ap[:, g * 8:(g + 1) * 8]
                self.V_("reduce_max", self.m1g.ap, sg, AX.X,
                        reads=[self.sbv], writes=[self.m1g])
                self.V_("tensor_scalar", self.msk8.ap, sg, self.m1g.ap, None,
                        ALU.is_ge, reads=[self.sbv, self.m1g], writes=[self.msk8])
                self.V_("scalar_tensor_tensor", self.t8b.ap, self.msk8.ap,
                        self.c_n1e30.ap, sg, ALU.mult, ALU.add,
                        reads=[self.msk8, self.sbv, self.c_n1e30],
                        writes=[self.t8b])
                self.V_("reduce_max", self.m2g.ap, self.t8b.ap, AX.X,
                        reads=[self.t8b], writes=[self.m2g])
                self.V_("tensor_tensor", self.gsv.ap[:, g:g + 1], self.m1g.ap,
                        self.m2g.ap, ALU.add,
                        reads=[self.m1g, self.m2g], writes=[self.gsv])
            self.V_("reduce_max", self.g1v.ap, self.gsv.ap, AX.X,
                    reads=[self.gsv], writes=[self.g1v])
            self.V_("tensor_scalar", self.mask4.ap, self.gsv.ap, self.g1v.ap,
                    None, ALU.is_ge, reads=[self.gsv, self.g1v], writes=[self.mask4])
            self.V_("scalar_tensor_tensor", self.t4b.ap, self.mask4.ap,
                    self.c_n1e30.ap, self.gsv.ap, ALU.mult, ALU.add,
                    reads=[self.mask4, self.gsv, self.c_n1e30],
                    writes=[self.t4b])
            self.V_("reduce_max", self.g2v.ap, self.t4b.ap, AX.X,
                    reads=[self.t4b], writes=[self.g2v])
            self.V_("tensor_scalar", self.mask4.ap, self.gsv.ap, self.g2v.ap,
                    None, ALU.is_ge, reads=[self.gsv, self.g2v], writes=[self.mask4])
            self.V_("tensor_scalar", self.madd4.ap, self.mask4.ap, -1.0, 1e30,
                    ALU.add, ALU.mult, reads=[self.mask4], writes=[self.madd4])
            sb3 = self.sbv.ap.rearrange("p (g e) -> p g e", e=8)
            msb3 = self.msb.ap.rearrange("p (g e) -> p g e", e=8)
            self.V_("tensor_tensor", msb3, sb3,
                    self.madd4.ap.unsqueeze(2).broadcast_to([P, 4, 8]), ALU.add,
                    reads=[self.sbv, self.madd4], writes=[self.msb])
            self.V_("tensor_copy", self.cur.ap, self.msb.ap,
                    reads=[self.msb], writes=[self.cur])
            for _ in range(7):
                self.V_("reduce_max", self.mx1.ap, self.cur.ap, AX.X,
                        reads=[self.cur], writes=[self.mx1])
                self.V_("tensor_scalar", self.mk32.ap, self.cur.ap, self.mx1.ap,
                        None, ALU.is_ge, reads=[self.cur, self.mx1],
                        writes=[self.mk32])
                self.V_("scalar_tensor_tensor", self.cur.ap, self.mk32.ap,
                        self.c_n1e30.ap, self.cur.ap, ALU.mult, ALU.add,
                        reads=[self.mk32, self.cur, self.c_n1e30],
                        writes=[self.cur])
            self.V_("reduce_max", self.th8.ap, self.cur.ap, AX.X,
                    reads=[self.cur], writes=[self.th8])
            self.V_("tensor_scalar", self.selm.ap, self.msb.ap, self.th8.ap,
                    None, ALU.is_ge, reads=[self.msb, self.th8], writes=[self.selm])
            self.V_("tensor_tensor", self.wsl.ap, self.selm.ap, self.scrs.ap,
                    ALU.mult, reads=[self.selm, self.scrs], writes=[self.wsl])
            self.V_("reduce_sum", self.s8.ap, self.wsl.ap, AX.X,
                    reads=[self.wsl], writes=[self.s8])
            self.V_("reciprocal", self.rs8.ap, self.s8.ap,
                    reads=[self.s8], writes=[self.rs8])
            self.V_("tensor_scalar", self.rs25.ap, self.rs8.ap, 2.5, None,
                    ALU.mult, reads=[self.rs8], writes=[self.rs25])
            self.V_("tensor_scalar", self.cw.ap[:, t, :], self.wsl.ap,
                    self.rs25.ap, None, ALU.mult,
                    reads=[self.wsl, self.rs25], writes=[self.cw])

    def load_slab(self, lane, slab, blk):
        for j in range(4):
            t = blk * 4 + j
            self.dma(lane, slab.ap[:, :, j * P:(j + 1) * P], self.ln2T_s[t],
                     reads=[("ln2T_s", t)], writes=[slab])

    def b1_blk(self, blk):
        slab = self.slab[blk % 2]
        self.load_slab("slab%d" % (blk % 2), slab, blk)
        for lc in range(8):
            bank = lc % 4
            psb = self.PSR("A", bank * 512, (bank + 1) * 512)
            for kc in range(NKC):
                self.T_("matmul", self.psA[:, bank * 512:(bank + 1) * 512],
                        self.fc1T.ap[:, kc, lc * P:(lc + 1) * P],
                        slab.ap[:, kc, :],
                        start=(kc == 0), stop=(kc == NKC - 1),
                        reads=[self.fc1T, slab], writes=[psb])
            self.act(self.x_latT.ap[:, lc, blk * 512:(blk + 1) * 512],
                     self.psA[:, bank * 512:(bank + 1) * 512], ACTF.Copy,
                     reads=[psb], writes=[self.x_latT])

    def b2(self):
        NT = self.NT
        nblks = NT // 4
        for e in range(NE):
            w1 = self.w1b[e % 2]
            w2 = self.w2b[e % 2]
            for lc in range(8):
                self.dma("w1b%d" % (e % 2), w1.ap[:, lc, :],
                         self.w1Td[e, lc * P:(lc + 1) * P, :],
                         reads=[w1], writes=[w1])
            for ic in range(8):
                self.dma("w2b%d" % (e % 2), w2.ap[:, ic, :],
                         self.w2Td[e, ic * P:(ic + 1) * P, :],
                         reads=[w2], writes=[w2])
            for blk in range(nblks):
                toff = blk * 512
                for ic in range(8):
                    bank = ic % 4
                    psb = self.PSR("A", bank * 512, (bank + 1) * 512)
                    for lc in range(8):
                        self.T_("matmul",
                                self.psA[:, bank * 512:(bank + 1) * 512],
                                w1.ap[:, lc, ic * P:(ic + 1) * P],
                                self.x_latT.ap[:, lc, toff:toff + 512],
                                start=(lc == 0), stop=(lc == 7),
                                reads=[w1, self.x_latT], writes=[psb])
                    rt = self.rtmp2[ic % 2]
                    self.V_("tensor_scalar", rt.ap,
                            self.psA[:, bank * 512:(bank + 1) * 512],
                            0.0, None, ALU.max, reads=[psb], writes=[rt])
                    self.A_("activation", self.h_all.ap[:, ic, :], rt.ap,
                            ACTF.Square, reads=[rt], writes=[self.h_all])
                for lh in range(2):
                    for tt in range(4):
                        psb = self.PSR("B", tt * 512, (tt + 1) * 512)
                        for ic in range(8):
                            self.T_("matmul",
                                    self.psB[:, tt * 512:(tt + 1) * 512],
                                    self.h_all.ap[:, ic, tt * P:(tt + 1) * P],
                                    w2.ap[:, ic, lh * 512:(lh + 1) * 512],
                                    start=(ic == 0), stop=(ic == 7),
                                    reads=[self.h_all, w2], writes=[psb])
                        tg = blk * 4 + tt
                        cwap = self.cw.ap[:, tg, e:e + 1]
                        ydst = self.y_lat.ap[:, tg, lh * 512:(lh + 1) * 512]
                        if e == 0:
                            self.V_("tensor_scalar", ydst,
                                    self.psB[:, tt * 512:(tt + 1) * 512],
                                    cwap, None, ALU.mult,
                                    reads=[psb, self.cw], writes=[self.y_lat])
                        else:
                            self.V_("scalar_tensor_tensor", ydst,
                                    self.psB[:, tt * 512:(tt + 1) * 512],
                                    cwap, ydst, ALU.mult, ALU.add,
                                    reads=[psb, self.cw, self.y_lat],
                                    writes=[self.y_lat])
        for tg in range(NT):
            self.V_("tensor_copy", self.ylb.ap, self.y_lat.ap[:, tg, :],
                    reads=[self.y_lat], writes=[self.ylb])
            self.transpose("trpY", self.ylT.ap, self.ylb.ap,
                           reads=[self.ylb, self.ylT], writes=[self.ylT])
            self.dma("sylT", self.ylT_s[:, :, tg * P:(tg + 1) * P], self.ylT.ap,
                     reads=[self.ylT], writes=[("ylT_s", tg)])

    def b3(self):
        TP = 8
        upTd_r = self.upTd.rearrange("(a p) n -> p a n", p=P)
        for ps in range(self.NT // TP):
            slab = self.slab3
            for j in range(TP):
                t = ps * TP + j
                self.dma("slab3", slab.ap[:, :, j * P:(j + 1) * P],
                         self.ln2T_s[t], reads=[("ln2T_s", t)], writes=[slab])
            self.dma("yl3", self.yl3.ap,
                     self.ylT_s[:, :, ps * 1024:(ps + 1) * 1024],
                     reads=[self.yl3] + [("ylT_s", ps * TP + j) for j in range(TP)],
                     writes=[self.yl3])
            for sig in range(5):
                if sig < 4:
                    for sc in range(8):
                        self.dma("dnw", self.dnb.ap[:, sc, :],
                                 self.dfT[sig * 1024 + sc * P:
                                          sig * 1024 + (sc + 1) * P, :],
                                 reads=[self.dnb], writes=[self.dnb])
                    for si in range(8):
                        ub = self.upb[si % 2]
                        self.dma("upw", ub.ap,
                                 upTd_r[:, :, sig * 1024 + si * P:
                                        sig * 1024 + (si + 1) * P],
                                 reads=[ub], writes=[ub])
                        for th in range(2):
                            bank = (si * 2 + th) % 4
                            psb = self.PSR("A", bank * 512, (bank + 1) * 512)
                            for kc in range(NKC):
                                self.T_("matmul",
                                        self.psA[:, bank * 512:(bank + 1) * 512],
                                        ub.ap[:, kc, :],
                                        slab.ap[:, kc, th * 512:(th + 1) * 512],
                                        start=(kc == 0), stop=(kc == NKC - 1),
                                        reads=[ub, slab], writes=[psb])
                            rt = self.rtmp3[th]
                            self.V_("tensor_scalar", rt.ap,
                                    self.psA[:, bank * 512:(bank + 1) * 512],
                                    0.0, None, ALU.max, reads=[psb], writes=[rt])
                            self.A_("activation",
                                    self.sTs.ap[:, si, th * 512:(th + 1) * 512],
                                    rt.ap, ACTF.Square,
                                    reads=[rt], writes=[self.sTs])
                else:
                    for sc in range(8):
                        self.dma("dnw", self.dnb.ap[:, sc, :],
                                 self.dfT[SI + sc * P:SI + (sc + 1) * P, :],
                                 reads=[self.dnb], writes=[self.dnb])
                for t in range(TP):
                    psb = self.PSR("B", 0, 2048)
                    for sc in range(8):
                        if sig < 4:
                            lhs = self.sTs.ap[:, sc, t * P:(t + 1) * P]
                            lrd = self.sTs
                        else:
                            lhs = self.yl3.ap[:, sc, t * P:(t + 1) * P]
                            lrd = self.yl3
                        for co in range(4):
                            self.T_("matmul", self.psB[:, co * 512:(co + 1) * 512],
                                    lhs, self.dnb.ap[:, sc, co * 512:(co + 1) * 512],
                                    start=(sc == 0), stop=(sc == 7),
                                    reads=[lrd, self.dnb],
                                    writes=[self.PSR("B", co * 512, (co + 1) * 512)])
                    rdst = self.rout.ap[:, t, :]
                    if sig == 0:
                        self.V_("tensor_copy", rdst, self.psB[:, :],
                                reads=[psb], writes=[self.rout])
                    else:
                        self.V_("tensor_tensor", rdst, rdst, self.psB[:, :],
                                ALU.add, reads=[psb, self.rout], writes=[self.rout])
            for t in range(TP):
                tg = ps * TP + t
                rsrc = self.rout.ap[:, t, :]
                self.bn(self.rout, rsrc)
                self.ln_rstd(self.mv.ap[:, 1:2], self.c_eps5, self.rstd, [self.mv])
                self.dma("x1l3", self.x1l3.ap, self.x1_s[tg],
                         reads=[("x1_s", tg)], writes=[self.x1l3])
                self.V_("tensor_scalar", rsrc, rsrc, self.mv.ap[:, 0:1],
                        self.rstd.ap, ALU.subtract, ALU.mult,
                        reads=[self.rout, self.mv, self.rstd], writes=[self.rout])
                self.V_("tensor_tensor", rsrc, rsrc, self.x1l3.ap,
                        ALU.add, reads=[self.rout, self.x1l3], writes=[self.rout])
                self.dma("sout", self.out[tg * P:(tg + 1) * P, :], rsrc,
                         reads=[self.rout], writes=[("out", tg)])

    def build_all(self):
        self.declare()
        self.const_loads()
        self.a0()
        self.tern(self.wqT, self.mclip_q, CQ)
        self.a1("q")
        self.tern(self.wvT, self.mclip_v, CK)
        self.a1("v")
        self.tern(self.wkT, self.mclip_k, CK)
        self.a1("k")
        self.a2()
        self.tern(self.woT, self.mclip_o, CK)
        self.a3()
        self.b0()
        self.b2()
        self.b3()
        self.p.finish()


LANES = {}
for _l in (["xt0", "xt1", "ld0", "ld1", "x1ld0", "x1ld1",
            "slab0", "slab1", "slab3", "x1l3", "sxqT_s", "sqrT", "syqT_s",
            "sx1", "sln2T", "sout", "sylT", "yl3", "trp0", "trp1",
            "trpP0", "trpP1", "trpY"]
           + ["skr%d" % h for h in range(16)] + ["sv%d" % g for g in range(4)]):
    LANES[_l] = "S"
for _l in ("cb", "ones", "gbias", "cc", "ss", "ch0", "vs0", "vs1"):
    LANES[_l] = "A"
for _l in ("ch1", "ks0", "ks1"):
    LANES[_l] = "G"
for _l in ("gw", "fc1", "w1b0", "w1b1", "w2b0", "w2b1", "upw", "dnw"):
    LANES[_l] = "G"


def build_nc(NT=16):
    nc = bass.Bass("TRN2", target_bir_lowering=False)
    from contextlib import ExitStack
    with ExitStack() as st:
        sems = {}
        for name in ("T", "V", "A", "G"):
            sems[name] = st.enter_context(nc.semaphore("sem" + name))
        for name in LANES:
            sems[name] = st.enter_context(nc.semaphore("q" + name))
        prog = Prog(sems, LANES)
        b = Builder(nc, prog, NT)
        b.build_all()
        block = st.enter_context(nc.Block())
        prog.realize(block)
    return nc


def _host_prep(L0_attn_Wq, L0_attn_Wk, L0_attn_Wv, L0_attn_Wo, L0_ffn_gate_w,
               L0_ffn_gate_bias, L0_ffn_fc1_w, L0_ffn_fc2_w, L0_ffn_w1,
               L0_ffn_w2, L0_ffn_shared_up_w, L0_ffn_shared_down_w, T=2048):
    cont = np.ascontiguousarray
    perm = np.concatenate([
        np.concatenate([np.arange(h * 128, (h + 1) * 128, 2),
                        np.arange(h * 128 + 1, (h + 1) * 128, 2)])
        for h in range(16)])
    shared = {
        "wqT": cont(L0_attn_Wq[perm].T),
        "wkT": cont(L0_attn_Wk[perm].T),
        "wvT": cont(L0_attn_Wv.T),
        "woT": cont(L0_attn_Wo.T),
        "gwT": cont(L0_ffn_gate_w.T),
        "gbiasd": cont(np.tile(L0_ffn_gate_bias[None, :].astype(np.float32),
                               (128, 1))),
        "fc1Td": cont(L0_ffn_fc1_w.T),
        "dfT": cont(np.concatenate([L0_ffn_shared_down_w.T,
                                    L0_ffn_fc2_w.T], axis=0)),
        "w1Td": cont(L0_ffn_w1.transpose(0, 2, 1)),
        "w2Td": cont(L0_ffn_w2.transpose(0, 2, 1)),
        "upTd": cont(L0_ffn_shared_up_w.T),
    }
    import jax
    import jax.numpy as jnp
    with jax.default_device(jax.devices("cpu")[0]):
        freqs = 1.0 / (10000.0 ** (jnp.arange(0, 128, 2, dtype=jnp.float32) / 128))
        ang = jnp.arange(T, dtype=jnp.float32)[:, None] * freqs[None, :]
        cos = np.asarray(jnp.cos(ang), dtype=np.float32)
        sin = np.asarray(jnp.sin(ang), dtype=np.float32)
    shared["cc"] = cont(np.concatenate([cos, cos], axis=1))
    shared["ss"] = cont(np.concatenate([-sin, sin], axis=1))
    ii = np.arange(128)
    shared["cbm"] = np.where(ii[:, None] >= ii[None, :], 0.0,
                             -1e30).astype(np.float32)
    shared["onesd"] = np.ones((128, 128), np.float32)
    return shared


def kernel(L0_attn_x, L0_attn_Wq, L0_attn_Wk, L0_attn_Wv, L0_attn_Wo,
           L0_ffn_gate_w, L0_ffn_gate_bias, L0_ffn_fc1_w, L0_ffn_fc2_w,
           L0_ffn_w1, L0_ffn_w2, L0_ffn_shared_up_w, L0_ffn_shared_down_w):
    shared = _host_prep(L0_attn_Wq, L0_attn_Wk, L0_attn_Wv, L0_attn_Wo,
                        L0_ffn_gate_w, L0_ffn_gate_bias, L0_ffn_fc1_w,
                        L0_ffn_fc2_w, L0_ffn_w1, L0_ffn_w2,
                        L0_ffn_shared_up_w, L0_ffn_shared_down_w)
    nc = build_nc(NT=16)
    in_maps = []
    for c in range(8):
        m = dict(shared)
        m["x"] = np.ascontiguousarray(L0_attn_x[c])
        in_maps.append(m)
    res = run_bass_kernel_spmd(nc, in_maps, core_ids=list(range(8)))
    return np.stack([res.results[c]["out"] for c in range(8)], axis=0)


# version 14
# speedup vs baseline: 2.2617x; 1.1166x over previous
import numpy as np
import ml_dtypes
import concourse.bass as bass
import concourse.mybir as mybir
from concourse.bass_utils import run_bass_kernel_spmd

F32 = mybir.dt.float32
BF16 = mybir.dt.bfloat16
U8 = mybir.dt.uint8
AX = mybir.AxisListType
ALU = mybir.AluOpType
ACTF = mybir.ActivationFunctionType

P = 128
C = 2048
H = 16
D = 128
NKC = C // P
NE = 32
LAT = 1024
INTER = 1024
SI = 4096
KB = 1024
PGSZ = 2048

CQ = 1.0 / (127.0 * float(np.sqrt(np.float64(128.0))))
CK = 1.0 / 127.0
INV_N = 1.0 / (2048.0 * 2048.0)
M32 = float(2 ** 23 * 1.5)
M64 = float(2 ** 52 * 1.5)


class Buf:
    __slots__ = ("ap", "keys")

    def __init__(self, ap, keys):
        self.ap = ap
        self.keys = frozenset(keys)


class Prog:
    ENGS = ("T", "V", "A", "G", "S")

    def __init__(self, sems, lane_queue):
        self.sems = sems
        self.lane_queue = lane_queue
        self.cnt = {k: 0 for k in sems}
        self.steps = {e: [] for e in self.ENGS}
        self.seen = {e: {} for e in self.ENGS}
        self.pages = {}
        self.dirty = {e: set() for e in self.ENGS}
        self.rdirty = {e: set() for e in self.ENGS}

    def _rec(self, k):
        r = self.pages.get(k)
        if r is None:
            r = [None, []]
            self.pages[k] = r
        return r

    @staticmethod
    def _keys(bufs):
        s = set()
        for b in bufs:
            s |= b.keys if isinstance(b, Buf) else {b}
        return s

    def op(self, eng, fn, reads=(), writes=(), lane=None):
        rk = self._keys(reads)
        wk = self._keys(writes)
        waits = []
        seen = self.seen[eng]

        def need(sem, val, agent):
            if agent != eng and agent in self.lane_queue:
                val = self.cnt[agent]
            if val > 0 and seen.get(sem, 0) < val:
                waits.append((self.sems[sem], val))
                seen[sem] = val

        drain = False
        for k in rk | wk:
            rec = self.pages.get(k)
            if rec is None:
                continue
            w = rec[0]
            if w is not None:
                sem, val, agent = w
                if agent == eng:
                    if k in self.dirty[eng]:
                        drain = True
                else:
                    need(sem, val, agent)
            if k in wk:
                for sem, val, agent in rec[1]:
                    if agent == eng:
                        if k in self.rdirty[eng]:
                            drain = True
                    else:
                        need(sem, val, agent)
        if eng == "T":
            drain = False

        if lane is None:
            sem_name, inc = eng, 1
            agent = eng
        else:
            sem_name, inc = lane, 16
            agent = lane
        self.cnt[sem_name] += inc
        val = self.cnt[sem_name]

        if drain:
            self.dirty[eng].clear()
            self.rdirty[eng].clear()
        for k in wk:
            rec = self._rec(k)
            rec[0] = (sem_name, val, agent)
            rec[1] = []
        for k in rk:
            self._rec(k)[1].append((sem_name, val, agent))
        if lane is None:
            self.dirty[eng] |= wk
            self.rdirty[eng] |= rk
        self.steps[eng].append((tuple(waits), drain, fn, self.sems[sem_name], inc))
        return val

    def finish(self):
        waits = [(self.sems[k], v) for k, v in self.cnt.items() if v > 0]
        self.steps["S"].append((tuple(waits), False, None, None, 0))

    def realize(self, block):
        names = {"T": "tensor", "V": "vector", "A": "scalar", "G": "gpsimd", "S": "sync"}
        for e in self.ENGS:
            steps = self.steps[e]

            def mk(steps=steps):
                def body(eng):
                    for waits, drain, fn, sem, inc in steps:
                        for s, v in waits:
                            eng.wait_ge(s, v)
                        if drain:
                            eng.drain()
                        if fn is not None:
                            inst = fn(eng)
                            if sem is not None:
                                inst.then_inc(sem, inc)
                return body
            getattr(block, names[e])(mk())


class Builder:
    def __init__(self, nc, prog, NT):
        self.nc = nc
        self.p = prog
        self.NT = NT
        self.T = NT * P

    def T_(self, fname, *args, reads=(), writes=(), **kw):
        self.p.op("T", (lambda e, a=args, k=kw, f=fname: getattr(e, f)(*a, **k)),
                  reads, writes)

    def V_(self, fname, *args, reads=(), writes=(), **kw):
        self.p.op("V", (lambda e, a=args, k=kw, f=fname: getattr(e, f)(*a, **k)),
                  reads, writes)

    def A_(self, fname, *args, reads=(), writes=(), **kw):
        self.p.op("A", (lambda e, a=args, k=kw, f=fname: getattr(e, f)(*a, **k)),
                  reads, writes)

    def dma(self, lane, out_ap, in_ap, reads=(), writes=()):
        self.p.op(self.p.lane_queue[lane],
                  (lambda e, o=out_ap, i=in_ap: e.dma_start(out=o, in_=i)),
                  reads, writes, lane=lane)

    def transpose(self, lane, out_ap, in_ap, reads=(), writes=()):
        self.p.op("S", (lambda e, o=out_ap, i=in_ap: e.dma_start_transpose(o, i)),
                  reads, writes, lane=lane)

    def AR(self, off_kb, shape, dtype):
        nel = int(np.prod(shape[1:]))
        nby = nel * mybir.dt.size(dtype)
        if dtype == BF16:
            arena, tag, epk = self.arena16, "A6", 512
        else:
            arena, tag, epk = self.arena32, "A2", 256
        ap = arena[:, off_kb * epk: off_kb * epk + nel]
        if len(shape) == 3:
            ap = ap.rearrange("p (a b) -> p a b", b=shape[2])
        keys = {(tag, g) for g in range(off_kb * KB // PGSZ,
                                        (off_kb * KB + nby + PGSZ - 1) // PGSZ)}
        return Buf(ap, keys)

    def SB(self, name, shape, dtype=F32, tdest=False):
        h = self.nc.alloc_sbuf_tensor(name, shape, dtype, is_transpose_dest=tdest)
        return Buf(h[:], {name})

    def PSR(self, which, lo, hi):
        t = self.psA if which == "A" else self.psB
        keys = {("ps" + which, b) for b in range(lo // 512, (hi + 511) // 512)}
        return Buf(t[:, lo:hi], keys)

    def declare(self):
        nc = self.nc
        NT, T = self.NT, self.T
        dr = lambda n, s, dt=F32, kind="ExternalInput": nc.dram_tensor(n, s, dt, kind=kind).ap()
        self.x = dr("x", [T, C])
        self.wqT = dr("wqT", [C, C]); self.wkT = dr("wkT", [C, C])
        self.wvT = dr("wvT", [C, C]); self.woT = dr("woT", [C, C])
        self.cc = dr("cc", [T, D]); self.ss = dr("ss", [T, D])
        self.cbm = dr("cbm", [P, P]); self.onesd = dr("onesd", [P, P])
        self.gbiasd = dr("gbiasd", [P, NE])
        self.gwT = dr("gwT", [C, NE])
        self.fc1Td = dr("fc1Td", [C, LAT])
        self.dfT = dr("dfT", [SI + LAT, C])
        self.w1Td = dr("w1Td", [NE, LAT, INTER])
        self.w2Td = dr("w2Td", [NE, INTER, LAT])
        self.upTd = dr("upTd", [C, SI])
        self.out = dr("out", [T, C], F32, kind="ExternalOutput")
        idr = lambda n, s, dt: nc.dram_tensor(n, s, dt, kind="Internal").ap()
        self.xqT_s = idr("xqT_s", [NT, P, NKC, P], BF16)
        self.qrT_s = idr("qrT_s", [NT, P, NKC, P], BF16)
        self.krT_s = idr("krT_s", [H, P, T], BF16)
        self.v_s = idr("v_s", [4, T, 512], BF16)
        self.yqT_s = idr("yqT_s", [NT, P, NKC, P], BF16)
        self.x1_s = idr("x1_s", [NT, P, C], F32)
        self.ln2T_s = idr("ln2T_s", [NT, P, NKC, P], BF16)
        self.ylT_s = idr("ylT_s", [P, 8, T], BF16)

        self.psA = nc.alloc_psum_tensor("psA", [P, 2048], F32)
        self.psB = nc.alloc_psum_tensor("psB", [P, 2048], F32)

        S = self.SB
        self.tr = [S("tr0", [P, NKC, P], BF16, tdest=True),
                   S("tr1", [P, NKC, P], BF16, tdest=True)]
        self.pT = S("pT", [P, NT, P], BF16, tdest=True)
        self.ylT = S("ylT", [P, 8, P], BF16, tdest=True)
        self.cb = S("cb", [P, P]); self.ones = S("ones", [P, P])
        self.cct = S("cct", [P, D]); self.sst = S("sst", [P, D])
        self.gbias = S("gbias", [P, NE])
        self.gw_sb = S("gw_sb", [P, NKC, NE], BF16)
        self.samc = S("samc", [P, NT]); self.samc_y = S("samc_y", [P, NT])
        self.cw = S("cw", [P, NT, NE])
        self.bst = S("bst", [P, 4, 6]); self.mv = S("mv", [P, 2])
        self.cs16 = S("cs16", [P, NKC]); self.cstot = S("cstot", [P, 1])
        for n in ("mclip_q", "mclip_k", "mclip_v", "mclip_o", "thr", "nthr",
                  "lnv", "rstd", "rsq", "bq", "rr", "sfold", "am", "amc",
                  "rmc", "qsc", "qsc2", "sc1", "s2t", "vv", "gv",
                  "nm", "den", "rden", "ss2v",
                  "g1v", "g2v", "m1g", "m2g", "th8", "s8", "rs8", "mx1",
                  "nm2", "den2", "rden2"):
            setattr(self, n, S(n, [P, 1]))
        self.nmp = [self.nm, self.nm2]
        self.denp = [self.den, self.den2]
        self.rdenp = [self.rden, self.rden2]
        self.c_eps5 = S("c_eps5", [P, 1]); self.c_eps6 = S("c_eps6", [P, 1])
        self.c_2565 = S("c_2565", [P, 1])
        self.c_m256 = S("c_m256", [P, 1]); self.c_n1e30 = S("c_n1e30", [P, 1])
        self.c_one = S("c_one", [P, 1]); self.gdisc = S("gdisc", [P, 1])
        self.c_zero = S("c_zero", [P, 1])
        self.mqs = S("mqs", [P, 1]); self.rs25 = S("rs25", [P, 1])
        self.et = S("et", [P, NE]); self.d1 = S("d1", [P, NE])
        self.scrs = S("scrs", [P, NE]); self.sbv = S("sbv", [P, NE])
        self.msb = S("msb", [P, NE]); self.cur = S("cur", [P, NE])
        self.mk32 = S("mk32", [P, NE]); self.wsl = S("wsl", [P, NE])
        self.selm = S("selm", [P, NE])
        self.gsv = S("gsv", [P, 4]); self.mask4 = S("mask4", [P, 4])
        self.t4b = S("t4b", [P, 4]); self.madd4 = S("madd4", [P, 4])
        self.msk8 = S("msk8", [P, 8]); self.t8b = S("t8b", [P, 8])

        self.arena16 = nc.alloc_sbuf_tensor("arena16", [P, 108 * 512], BF16)
        self.arena32 = nc.alloc_sbuf_tensor("arena32", [P, 76 * 256], F32)
        A = self.AR
        self.wtern = A(0, [P, NKC, C], BF16)
        self.xqb = A(64, [P, C], BF16)
        self.vbb = A(68, [P, C], BF16)
        self.ld = [A(72, [P, NKC, P], BF16), A(76, [P, NKC, P], BF16)]
        self.pb = A(80, [P, 2048], BF16)
        self.ks = [A(84, [P, 2048], BF16), A(88, [P, 2048], BF16)]
        self.vsh = [A(92, [P, NT, 256], BF16),
                    A(100, [P, NT, 256], BF16)]
        self.pbp = [self.pb, self.vbb]
        self.chunk = [A(0, [P, C], F32), A(8, [P, C], F32)]
        self.xt = [A(16, [P, C], F32), A(24, [P, C], F32)]
        self.xn = A(32, [P, C], F32)
        self.qf = A(40, [P, C], F32)
        self.uu = A(48, [P, C], F32)
        self.trt = A(56, [P, C], F32)
        self.y_scr = [A(64, [P, C], F32), A(64, [P, C], F32)]
        self.fc1T = A(76, [P, NKC, LAT], BF16)
        self.x_latT = A(0, [P, 8, T], BF16)
        self.slab = [A(32, [P, NKC, 512], BF16), A(48, [P, NKC, 512], BF16)]
        self.w1b = [A(32, [P, 8, INTER], BF16), A(48, [P, 8, INTER], BF16)]
        self.w2b = [A(64, [P, 8, LAT], BF16), A(80, [P, 8, LAT], BF16)]
        self.h_all = A(96, [P, 8, 512], BF16)
        self.ylb = A(104, [P, LAT], BF16)
        self.dnb = A(0, [P, 8, C], BF16)
        self.slab3 = A(32, [P, NKC, 1024], BF16)
        self.sTs = A(64, [P, 8, 1024], BF16)
        self.upb = [A(80, [P, NKC, P], BF16),
                    A(84, [P, NKC, P], BF16)]
        self.yl3 = A(88, [P, 8, 1024], BF16)
        self.x1ld = [A(0, [P, C], F32), A(8, [P, C], F32)]
        self.y_lat = A(0, [P, NT, LAT], F32)
        self.rtmp2 = [A(64, [P, 512], F32), A(66, [P, 512], F32)]
        self.rout = A(0, [P, 8, C], F32)
        self.x1l3 = A(64, [P, C], F32)
        self.rtmp3 = [A(72, [P, 512], F32), A(74, [P, 512], F32)]

    def act(self, out, in_, func, bias=0.0, scale=1.0, accum=None,
            reads=(), writes=()):
        kw = dict(bias=bias, scale=scale)
        if accum is not None:
            kw["accum_out"] = accum
        self.A_("activation", out, in_, func, reads=reads, writes=writes, **kw)

    def bn(self, src_buf, src_ap2048, extra_reads=()):
        for i in range(4):
            self.V_("bn_stats", self.bst.ap[:, i, :], src_ap2048[:, i * 512:(i + 1) * 512],
                    reads=[src_buf, *extra_reads], writes=[self.bst])
        self.V_("bn_aggr", self.mv.ap, self.bst.ap,
                reads=[self.bst], writes=[self.mv])

    def ln_rstd(self, var_ap, eps_buf, out_buf, reads):
        self.act(self.lnv.ap, var_ap, ACTF.Ln, bias=eps_buf.ap, scale=1.0,
                 reads=list(reads) + [eps_buf], writes=[self.lnv])
        self.act(out_buf.ap, self.lnv.ap, ACTF.Exp, bias=0.0, scale=-0.5,
                 reads=[self.lnv], writes=[out_buf])

    def const_loads(self):
        self.V_("memset", self.c_eps5.ap, 1e-5, writes=[self.c_eps5])
        self.V_("memset", self.c_eps6.ap, 1e-6, writes=[self.c_eps6])
        self.V_("memset", self.c_2565.ap, 256.5, writes=[self.c_2565])
        self.V_("memset", self.c_m256.ap, -256.0, writes=[self.c_m256])
        self.V_("memset", self.c_n1e30.ap, -1e30, writes=[self.c_n1e30])
        self.V_("memset", self.c_one.ap, 1.0, writes=[self.c_one])
        self.V_("memset", self.c_zero.ap, 0.0, writes=[self.c_zero])
        self.V_("tensor_single_scalar", self.gdisc.ap, self.c_one.ap, M64,
                ALU.add, reads=[self.c_one], writes=[self.gdisc])
        self.V_("tensor_single_scalar", self.gdisc.ap, self.gdisc.ap, M64,
                ALU.subtract, reads=[self.gdisc], writes=[self.gdisc])
        self.dma("cb", self.cb.ap, self.cbm, writes=[self.cb])
        self.dma("ones", self.ones.ap, self.onesd, writes=[self.ones])
        self.dma("gbias", self.gbias.ap, self.gbiasd, writes=[self.gbias])
        self.dma("gw", self.gw_sb.ap,
                 self.gwT.rearrange("(a p) n -> p a n", p=P), writes=[self.gw_sb])
        from concourse.hw_specs import get_activation_tables
        idx = None
        for i, funcs in enumerate(get_activation_tables(self.nc.m.arch).values()):
            if ACTF.Exp in funcs and ACTF.Ln in funcs:
                idx = i
                break
        if idx is not None:
            nc = self.nc
            def fn(e, idx=idx, nc=nc):
                return e.add_instruction(mybir.InstLoadActFuncSet(
                    name=nc.get_next_instruction_name(),
                    act_func_set_id=idx, ins=[], outs=[]))
            self.p.op("A", fn, (), ())

    def tern(self, wdram, mclip, cmul):
        NCH = NKC
        for kc in range(NCH):
            ch = self.chunk[kc % 2]
            self.dma("ch%d" % (kc % 2), ch.ap, wdram[kc * P:(kc + 1) * P, :],
                     reads=[ch], writes=[ch])
            self.V_("reduce_sum", self.cs16.ap[:, kc:kc + 1], ch.ap, AX.X,
                    apply_absolute_value=True, reads=[ch], writes=[self.cs16])
        self.V_("reduce_sum", self.cstot.ap, self.cs16.ap, AX.X,
                reads=[self.cs16], writes=[self.cstot])
        ps1 = self.PSR("A", 0, 512)
        self.T_("matmul", self.psA[:, 0:1], self.ones.ap, self.cstot.ap,
                start=True, stop=True, reads=[self.ones, self.cstot], writes=[ps1])
        self.V_("tensor_scalar", mclip.ap, self.psA[:, 0:1], float(INV_N), 1e-5,
                ALU.mult, ALU.max, reads=[ps1], writes=[mclip])
        self.V_("tensor_scalar", self.thr.ap, mclip.ap, 0.5, None, ALU.mult,
                reads=[mclip], writes=[self.thr])
        self.V_("tensor_scalar", self.nthr.ap, mclip.ap, -0.5, None, ALU.mult,
                reads=[mclip], writes=[self.nthr])
        self.V_("tensor_scalar", self.mqs.ap, mclip.ap, float(cmul), None,
                ALU.mult, reads=[mclip], writes=[self.mqs])
        for kc in range(NCH):
            ch = self.chunk[kc % 2]
            self.dma("ch%d" % (kc % 2), ch.ap, wdram[kc * P:(kc + 1) * P, :],
                     reads=[ch], writes=[ch])
            self.V_("tensor_scalar", self.qf.ap, ch.ap, self.nthr.ap, None,
                    ALU.is_lt, reads=[ch, self.nthr], writes=[self.qf])
            self.V_("scalar_tensor_tensor", self.wtern.ap[:, kc, :], ch.ap,
                    self.thr.ap, self.qf.ap, ALU.is_gt, ALU.subtract,
                    reads=[ch, self.thr, self.qf], writes=[self.wtern])

    def a0(self):
        for t in range(self.NT):
            xt = self.xt[t % 2]
            self.dma("xt%d" % (t % 2), xt.ap, self.x[t * P:(t + 1) * P, :],
                     reads=[xt], writes=[xt])
            self.bn(xt, xt.ap)
            self.ln_rstd(self.mv.ap[:, 1:2], self.c_eps5, self.rstd, [self.mv])
            self.V_("tensor_tensor", self.rsq.ap, self.rstd.ap, self.rstd.ap,
                    ALU.mult, reads=[self.rstd], writes=[self.rsq])
            self.V_("tensor_tensor", self.bq.ap, self.mv.ap[:, 1:2], self.rsq.ap,
                    ALU.mult, reads=[self.mv, self.rsq], writes=[self.bq])
            self.act(self.lnv.ap, self.bq.ap, ACTF.Ln, bias=self.c_eps6.ap,
                     reads=[self.bq, self.c_eps6], writes=[self.lnv])
            self.act(self.rr.ap, self.lnv.ap, ACTF.Exp, scale=-0.5,
                     reads=[self.lnv], writes=[self.rr])
            self.V_("tensor_tensor", self.sfold.ap, self.rstd.ap, self.rr.ap,
                    ALU.mult, reads=[self.rstd, self.rr], writes=[self.sfold])
            self.V_("tensor_scalar", self.xn.ap, xt.ap, self.mv.ap[:, 0:1],
                    self.sfold.ap, ALU.subtract, ALU.mult,
                    reads=[xt, self.mv, self.sfold], writes=[self.xn])
            self.quant_store(self.xn, self.samc, t, self.xqT_s, "xqT_s",
                             scale_pre=None)

    def quant_store(self, src, samc, t, dst_dram, dst_key, scale_pre):
        self.V_("reduce_max", self.am.ap, src.ap, AX.X, apply_absolute_value=True,
                reads=[src], writes=[self.am])
        if scale_pre is not None:
            self.V_("tensor_tensor", self.amc.ap, self.am.ap, scale_pre.ap,
                    ALU.mult, reads=[self.am, scale_pre], writes=[self.amc])
            amsrc = self.amc
        else:
            amsrc = self.am
        self.V_("tensor_single_scalar", samc.ap[:, t:t + 1], amsrc.ap, 1e-5,
                ALU.max, reads=[amsrc], writes=[samc])
        self.V_("reciprocal", self.rmc.ap, samc.ap[:, t:t + 1],
                reads=[samc], writes=[self.rmc])
        self.V_("tensor_scalar", self.qsc.ap, self.rmc.ap, 127.0, None, ALU.mult,
                reads=[self.rmc], writes=[self.qsc])
        if scale_pre is not None:
            self.V_("tensor_tensor", self.qsc2.ap, self.qsc.ap, scale_pre.ap,
                    ALU.mult, reads=[self.qsc, scale_pre], writes=[self.qsc2])
            qs = self.qsc2
        else:
            qs = self.qsc
        self.V_("tensor_scalar", self.qf.ap, src.ap, qs.ap, None, ALU.mult,
                reads=[src, qs], writes=[self.qf])
        self.V_("tensor_single_scalar", self.uu.ap, self.qf.ap, M32, ALU.add,
                reads=[self.qf], writes=[self.uu])
        self.V_("tensor_single_scalar", self.uu.ap, self.uu.ap, M32,
                ALU.subtract, reads=[self.uu], writes=[self.uu])
        self.V_("tensor_single_scalar", self.trt.ap, self.qf.ap, M64, ALU.add,
                reads=[self.qf], writes=[self.trt])
        self.V_("tensor_single_scalar", self.trt.ap, self.trt.ap, M64,
                ALU.subtract, reads=[self.trt], writes=[self.trt])
        self.V_("tensor_tensor", self.trt.ap, self.trt.ap, self.uu.ap,
                ALU.subtract, reads=[self.trt, self.uu], writes=[self.trt])
        self.V_("scalar_tensor_tensor", self.xqb.ap, self.trt.ap,
                self.gdisc.ap, self.uu.ap, ALU.mult, ALU.add,
                reads=[self.trt, self.uu, self.gdisc], writes=[self.xqb])
        tr = self.tr[t % 2]
        self.transpose("trp%d" % (t % 2), tr.ap, self.xqb.ap,
                       reads=[self.xqb, tr], writes=[tr])
        self.dma("s" + dst_key, dst_dram[t], tr.ap, reads=[tr],
                 writes=[(dst_key, t)])

    def a1(self, kind):
        NT = self.NT
        self.dma("ld0", self.ld[0].ap, self.xqT_s[0],
                 reads=[self.ld[0], ("xqT_s", 0)], writes=[self.ld[0]])
        for t in range(NT):
            ld = self.ld[t % 2]
            if t + 1 < NT:
                nld = self.ld[(t + 1) % 2]
                self.dma("ld%d" % ((t + 1) % 2), nld.ap, self.xqT_s[t + 1],
                         reads=[nld, ("xqT_s", t + 1)], writes=[nld])
            psn = "A" if t % 2 == 0 else "B"
            pst = self.psA if psn == "A" else self.psB
            psb = self.PSR(psn, 0, 2048)
            for kc in range(NKC):
                for n in range(4):
                    self.T_("matmul", pst[:, n * 512:(n + 1) * 512],
                            ld.ap[:, kc, :], self.wtern.ap[:, kc, n * 512:(n + 1) * 512],
                            start=(kc == 0), stop=(kc == NKC - 1),
                            reads=[ld, self.wtern], writes=[psb])
            self.V_("tensor_scalar", self.sc1.ap, self.samc.ap[:, t:t + 1],
                    self.mqs.ap, None, ALU.mult,
                    reads=[self.samc, self.mqs], writes=[self.sc1])
            if kind == "v":
                self.V_("tensor_scalar", self.vbb.ap, pst[:, :], self.sc1.ap,
                        None, ALU.mult, reads=[psb, self.sc1], writes=[self.vbb])
                for hg in range(4):
                    self.dma("sv%d" % hg, self.v_s[hg, t * P:(t + 1) * P, :],
                             self.vbb.ap[:, hg * 512:(hg + 1) * 512],
                             reads=[self.vbb], writes=[("v_s", hg, t)])
                continue
            self.V_("tensor_scalar", self.qf.ap, pst[:, :], self.sc1.ap, None,
                    ALU.mult, reads=[psb, self.sc1], writes=[self.qf])
            self.dma("cc", self.cct.ap, self.cc[t * P:(t + 1) * P, :],
                     reads=[self.cct], writes=[self.cct])
            self.dma("ss", self.sst.ap, self.ss[t * P:(t + 1) * P, :],
                     reads=[self.sst], writes=[self.sst])
            qf3 = self.qf.ap.rearrange("p (h d) -> p h d", d=D)
            tr3 = self.trt.ap.rearrange("p (h d) -> p h d", d=D)
            ccb = self.cct.ap.unsqueeze(1).broadcast_to([P, H, D])
            uu3 = self.uu.ap.rearrange("p (h d) -> p h d", d=D)
            self.V_("tensor_tensor", uu3, qf3, ccb, ALU.mult,
                    reads=[self.qf, self.cct], writes=[self.uu])
            sslo = self.sst.ap[:, 0:64].unsqueeze(1).broadcast_to([P, H, 64])
            sshi = self.sst.ap[:, 64:128].unsqueeze(1).broadcast_to([P, H, 64])
            self.V_("tensor_tensor", tr3[:, :, 0:64], qf3[:, :, 64:128], sslo,
                    ALU.mult, reads=[self.qf, self.sst], writes=[self.trt])
            self.V_("tensor_tensor", tr3[:, :, 64:128], qf3[:, :, 0:64], sshi,
                    ALU.mult, reads=[self.qf, self.sst], writes=[self.trt])
            self.V_("tensor_tensor", self.xqb.ap, self.uu.ap, self.trt.ap,
                    ALU.add, reads=[self.uu, self.trt], writes=[self.xqb])
            tr = self.tr[t % 2]
            self.transpose("trp%d" % (t % 2), tr.ap, self.xqb.ap,
                           reads=[self.xqb, tr], writes=[tr])
            if kind == "q":
                self.dma("sqrT", self.qrT_s[t], tr.ap, reads=[tr],
                         writes=[("qrT_s", t)])
            else:
                for h in range(H):
                    self.dma("skr%d" % h, self.krT_s[h, :, t * P:(t + 1) * P],
                             tr.ap[:, h, :], reads=[tr], writes=[("krT_s", h)])

    def a2(self):
        NT = self.NT
        self.dma("ldq0", self.ld[0].ap, self.qrT_s[0],
                 reads=[self.ld[0], ("qrT_s", 0)], writes=[self.ld[0]])
        for qt in range(NT):
            L = (qt + 1) * P
            ld = self.ld[qt % 2]
            if qt + 1 < NT:
                nld = self.ld[(qt + 1) % 2]
                self.dma("ldq%d" % ((qt + 1) % 2), nld.ap, self.qrT_s[qt + 1],
                         reads=[nld, ("qrT_s", qt + 1)], writes=[nld])
            ysc = self.y_scr[qt % 2]
            pTs = [self.pT, self.tr[qt % 2]]
            nch = (L + 511) // 512

            def qk(h):
                hg = h // 4
                if h % 2 == 0:
                    half = (h % 4) // 2
                    vb = self.vsh[half]
                    src = self.v_s[hg].rearrange("(k p) f -> p k f", p=P)
                    self.dma("vs%d" % half, vb.ap[:, 0:qt + 1, :],
                             src[:, 0:qt + 1, half * 256:(half + 1) * 256],
                             reads=[vb] + [("v_s", hg, tt) for tt in range(qt + 1)],
                             writes=[vb])
                ks = self.ks[h % 2]
                self.dma("ks%d" % (h % 2), ks.ap[:, 0:L], self.krT_s[h, :, 0:L],
                         reads=[ks, ("krT_s", h)], writes=[ks])
                Rn = "A" if h % 2 == 0 else "B"
                Rt = self.psA if Rn == "A" else self.psB
                for cc_ in range(nch):
                    n = min(512, L - cc_ * 512)
                    self.T_("matmul", Rt[:, cc_ * 512:cc_ * 512 + n],
                            ld.ap[:, h, :], ks.ap[:, cc_ * 512:cc_ * 512 + n],
                            start=True, stop=True,
                            reads=[ld, ks],
                            writes=[self.PSR(Rn, cc_ * 512, cc_ * 512 + n)])

            def smx(h):
                Rn = "A" if h % 2 == 0 else "B"
                Rt = self.psA if Rn == "A" else self.psB
                Rb = self.PSR(Rn, 0, L)
                nm = self.nmp[h % 2]; den = self.denp[h % 2]
                rden = self.rdenp[h % 2]; pb = self.pbp[h % 2]
                self.V_("tensor_tensor", Rt[:, qt * P:L], Rt[:, qt * P:L],
                        self.cb.ap, ALU.add,
                        reads=[self.PSR(Rn, qt * P, L), self.cb],
                        writes=[self.PSR(Rn, qt * P, L)])
                self.V_("reduce_max", nm.ap, Rt[:, 0:L], AX.X,
                        negate=True, reads=[Rb], writes=[nm])
                self.act(pb.ap[:, 0:L], Rt[:, 0:L], ACTF.Exp,
                         bias=nm.ap, scale=1.0, accum=den.ap,
                         reads=[Rb, nm], writes=[pb, den])
                self.V_("reciprocal", rden.ap, den.ap,
                        reads=[den], writes=[rden])
                pT = pTs[h % 2]
                self.transpose("trpP%d" % (h % 2), pT.ap[:, 0:qt + 1, :],
                               pb.ap[:, 0:L], reads=[pb, pT], writes=[pT])

            def pv(h):
                h4 = h % 4
                Rn = "A" if h % 2 == 0 else "B"
                Yt = self.psA if Rn == "A" else self.psB
                Yb = self.PSR(Rn, 1920, 2048)
                pT = pTs[h % 2]; rden = self.rdenp[h % 2]
                vb = self.vsh[h4 // 2]
                for kt in range(qt + 1):
                    self.T_("matmul", Yt[:, 1920:2048], pT.ap[:, kt, :],
                            vb.ap[:, kt, (h4 % 2) * P:(h4 % 2 + 1) * P],
                            start=(kt == 0), stop=(kt == qt),
                            reads=[pT, vb], writes=[Yb])
                self.V_("tensor_scalar", ysc.ap[:, h * P:(h + 1) * P],
                        Yt[:, 1920:2048], rden.ap, None, ALU.mult,
                        reads=[Yb, rden], writes=[ysc])

            qk(0)
            smx(0)
            for h in range(16):
                if h + 1 < 16:
                    qk(h + 1)
                    smx(h + 1)
                pv(h)
            self.V_("tensor_tensor", self.xn.ap, ysc.ap, ysc.ap, ALU.mult,
                    reads=[ysc], writes=[self.xn])
            self.V_("reduce_sum", self.ss2v.ap, self.xn.ap, AX.X,
                    reads=[self.xn], writes=[self.ss2v])
            self.act(self.lnv.ap, self.ss2v.ap, ACTF.Ln, bias=self.c_eps6.ap,
                     scale=float(1.0 / C), reads=[self.ss2v, self.c_eps6],
                     writes=[self.lnv])
            self.act(self.rr.ap, self.lnv.ap, ACTF.Exp, scale=-0.5,
                     reads=[self.lnv], writes=[self.rr])
            self.quant_store(ysc, self.samc_y, qt, self.yqT_s, "yqT_s",
                             scale_pre=self.rr)

    def a3(self):
        NT = self.NT
        self.dma("ld0", self.ld[0].ap, self.yqT_s[0],
                 reads=[self.ld[0], ("yqT_s", 0)], writes=[self.ld[0]])
        self.dma("xt0", self.xt[0].ap, self.x[0:P, :],
                 reads=[self.xt[0]], writes=[self.xt[0]])
        for t in range(NT):
            ld = self.ld[t % 2]
            xt = self.xt[t % 2]
            if t + 1 < NT:
                nld = self.ld[(t + 1) % 2]
                self.dma("ld%d" % ((t + 1) % 2), nld.ap, self.yqT_s[t + 1],
                         reads=[nld, ("yqT_s", t + 1)], writes=[nld])
                nxt = self.xt[(t + 1) % 2]
                self.dma("xt%d" % ((t + 1) % 2), nxt.ap,
                         self.x[(t + 1) * P:(t + 2) * P, :],
                         reads=[nxt], writes=[nxt])
            psn = "A" if t % 2 == 0 else "B"
            pst = self.psA if psn == "A" else self.psB
            psb = self.PSR(psn, 0, 2048)
            for kc in range(NKC):
                for n in range(4):
                    self.T_("matmul", pst[:, n * 512:(n + 1) * 512],
                            ld.ap[:, kc, :], self.wtern.ap[:, kc, n * 512:(n + 1) * 512],
                            start=(kc == 0), stop=(kc == NKC - 1),
                            reads=[ld, self.wtern], writes=[psb])
            self.V_("tensor_scalar", self.sc1.ap, self.samc_y.ap[:, t:t + 1],
                    self.mqs.ap, None, ALU.mult,
                    reads=[self.samc_y, self.mqs], writes=[self.sc1])
            self.bn(psb, pst)
            self.V_("tensor_tensor", self.s2t.ap, self.sc1.ap, self.sc1.ap,
                    ALU.mult, reads=[self.sc1], writes=[self.s2t])
            self.V_("tensor_tensor", self.vv.ap, self.mv.ap[:, 1:2], self.s2t.ap,
                    ALU.mult, reads=[self.mv, self.s2t], writes=[self.vv])
            self.ln_rstd(self.vv.ap, self.c_eps5, self.rstd, [self.vv])
            self.V_("tensor_tensor", self.gv.ap, self.sc1.ap, self.rstd.ap,
                    ALU.mult, reads=[self.sc1, self.rstd], writes=[self.gv])
            self.V_("tensor_scalar", self.xn.ap, pst[:, :], self.mv.ap[:, 0:1],
                    self.gv.ap, ALU.subtract, ALU.mult,
                    reads=[psb, self.mv, self.gv], writes=[self.xn])
            self.V_("tensor_tensor", self.qf.ap, self.xn.ap, xt.ap, ALU.add,
                    reads=[self.xn, xt], writes=[self.qf])
            self.dma("sx1", self.x1_s[t], self.qf.ap, reads=[self.qf],
                     writes=[("x1_s", t)])

    def b0(self):
        NT = self.NT
        for kc in range(NKC):
            self.dma("fc1", self.fc1T.ap[:, kc, :],
                     self.fc1Td[kc * P:(kc + 1) * P, :],
                     reads=[self.fc1T], writes=[self.fc1T])
        for t in range(NT):
            self.b0_tile(t)
            if t % 4 == 3:
                self.b1_blk(t // 4)

    def b0_tile(self, t):
        if True:
            x1 = self.x1ld[t % 2]
            if t == 0:
                self.dma("x1ld0", x1.ap, self.x1_s[0],
                         reads=[x1, ("x1_s", 0)], writes=[x1])
            if t + 1 < self.NT:
                nx1 = self.x1ld[(t + 1) % 2]
                self.dma("x1ld%d" % ((t + 1) % 2), nx1.ap, self.x1_s[t + 1],
                         reads=[nx1, ("x1_s", t + 1)], writes=[nx1])
            self.bn(x1, x1.ap)
            self.ln_rstd(self.mv.ap[:, 1:2], self.c_eps5, self.rstd, [self.mv])
            self.V_("tensor_scalar", self.xqb.ap, x1.ap, self.mv.ap[:, 0:1],
                    self.rstd.ap, ALU.subtract, ALU.mult,
                    reads=[x1, self.mv, self.rstd], writes=[self.xqb])
            tr = self.tr[t % 2]
            self.transpose("trp%d" % (t % 2), tr.ap, self.xqb.ap,
                           reads=[self.xqb, tr], writes=[tr])
            self.dma("sln2T", self.ln2T_s[t], tr.ap, reads=[tr],
                     writes=[("ln2T_s", t)])
            psg = self.PSR("B", 0, 512)
            for kc in range(NKC):
                self.T_("matmul", self.psB[:, 0:NE], tr.ap[:, kc, :],
                        self.gw_sb.ap[:, kc, :],
                        start=(kc == 0), stop=(kc == NKC - 1),
                        reads=[tr, self.gw_sb], writes=[psg])
            self.act(self.et.ap, self.psB[:, 0:NE], ACTF.Exp, scale=-1.0,
                     reads=[psg], writes=[self.et])
            self.V_("tensor_single_scalar", self.d1.ap, self.et.ap, 1.0, ALU.add,
                    reads=[self.et], writes=[self.d1])
            self.V_("reciprocal", self.scrs.ap, self.d1.ap,
                    reads=[self.d1], writes=[self.scrs])
            self.V_("tensor_tensor", self.sbv.ap, self.scrs.ap, self.gbias.ap,
                    ALU.add, reads=[self.scrs, self.gbias], writes=[self.sbv])
            for g in range(4):
                sg = self.sbv.ap[:, g * 8:(g + 1) * 8]
                self.V_("reduce_max", self.m1g.ap, sg, AX.X,
                        reads=[self.sbv], writes=[self.m1g])
                self.V_("tensor_scalar", self.msk8.ap, sg, self.m1g.ap, None,
                        ALU.is_ge, reads=[self.sbv, self.m1g], writes=[self.msk8])
                self.V_("scalar_tensor_tensor", self.t8b.ap, self.msk8.ap,
                        self.c_n1e30.ap, sg, ALU.mult, ALU.add,
                        reads=[self.msk8, self.sbv, self.c_n1e30],
                        writes=[self.t8b])
                self.V_("reduce_max", self.m2g.ap, self.t8b.ap, AX.X,
                        reads=[self.t8b], writes=[self.m2g])
                self.V_("tensor_tensor", self.gsv.ap[:, g:g + 1], self.m1g.ap,
                        self.m2g.ap, ALU.add,
                        reads=[self.m1g, self.m2g], writes=[self.gsv])
            self.V_("reduce_max", self.g1v.ap, self.gsv.ap, AX.X,
                    reads=[self.gsv], writes=[self.g1v])
            self.V_("tensor_scalar", self.mask4.ap, self.gsv.ap, self.g1v.ap,
                    None, ALU.is_ge, reads=[self.gsv, self.g1v], writes=[self.mask4])
            self.V_("scalar_tensor_tensor", self.t4b.ap, self.mask4.ap,
                    self.c_n1e30.ap, self.gsv.ap, ALU.mult, ALU.add,
                    reads=[self.mask4, self.gsv, self.c_n1e30],
                    writes=[self.t4b])
            self.V_("reduce_max", self.g2v.ap, self.t4b.ap, AX.X,
                    reads=[self.t4b], writes=[self.g2v])
            self.V_("tensor_scalar", self.mask4.ap, self.gsv.ap, self.g2v.ap,
                    None, ALU.is_ge, reads=[self.gsv, self.g2v], writes=[self.mask4])
            self.V_("tensor_scalar", self.madd4.ap, self.mask4.ap, -1.0, 1e30,
                    ALU.add, ALU.mult, reads=[self.mask4], writes=[self.madd4])
            sb3 = self.sbv.ap.rearrange("p (g e) -> p g e", e=8)
            msb3 = self.msb.ap.rearrange("p (g e) -> p g e", e=8)
            self.V_("tensor_tensor", msb3, sb3,
                    self.madd4.ap.unsqueeze(2).broadcast_to([P, 4, 8]), ALU.add,
                    reads=[self.sbv, self.madd4], writes=[self.msb])
            self.V_("tensor_copy", self.cur.ap, self.msb.ap,
                    reads=[self.msb], writes=[self.cur])
            for _ in range(7):
                self.V_("reduce_max", self.mx1.ap, self.cur.ap, AX.X,
                        reads=[self.cur], writes=[self.mx1])
                self.V_("tensor_scalar", self.mk32.ap, self.cur.ap, self.mx1.ap,
                        None, ALU.is_ge, reads=[self.cur, self.mx1],
                        writes=[self.mk32])
                self.V_("scalar_tensor_tensor", self.cur.ap, self.mk32.ap,
                        self.c_n1e30.ap, self.cur.ap, ALU.mult, ALU.add,
                        reads=[self.mk32, self.cur, self.c_n1e30],
                        writes=[self.cur])
            self.V_("reduce_max", self.th8.ap, self.cur.ap, AX.X,
                    reads=[self.cur], writes=[self.th8])
            self.V_("tensor_scalar", self.selm.ap, self.msb.ap, self.th8.ap,
                    None, ALU.is_ge, reads=[self.msb, self.th8], writes=[self.selm])
            self.V_("tensor_tensor", self.wsl.ap, self.selm.ap, self.scrs.ap,
                    ALU.mult, reads=[self.selm, self.scrs], writes=[self.wsl])
            self.V_("reduce_sum", self.s8.ap, self.wsl.ap, AX.X,
                    reads=[self.wsl], writes=[self.s8])
            self.V_("reciprocal", self.rs8.ap, self.s8.ap,
                    reads=[self.s8], writes=[self.rs8])
            self.V_("tensor_scalar", self.rs25.ap, self.rs8.ap, 2.5, None,
                    ALU.mult, reads=[self.rs8], writes=[self.rs25])
            self.V_("tensor_scalar", self.cw.ap[:, t, :], self.wsl.ap,
                    self.rs25.ap, None, ALU.mult,
                    reads=[self.wsl, self.rs25], writes=[self.cw])

    def load_slab(self, lane, slab, blk):
        for j in range(4):
            t = blk * 4 + j
            self.dma(lane, slab.ap[:, :, j * P:(j + 1) * P], self.ln2T_s[t],
                     reads=[("ln2T_s", t)], writes=[slab])

    def b1_blk(self, blk):
        slab = self.slab[blk % 2]
        self.load_slab("slab%d" % (blk % 2), slab, blk)
        for lc in range(8):
            bank = lc % 4
            psb = self.PSR("A", bank * 512, (bank + 1) * 512)
            for kc in range(NKC):
                self.T_("matmul", self.psA[:, bank * 512:(bank + 1) * 512],
                        self.fc1T.ap[:, kc, lc * P:(lc + 1) * P],
                        slab.ap[:, kc, :],
                        start=(kc == 0), stop=(kc == NKC - 1),
                        reads=[self.fc1T, slab], writes=[psb])
            self.act(self.x_latT.ap[:, lc, blk * 512:(blk + 1) * 512],
                     self.psA[:, bank * 512:(bank + 1) * 512], ACTF.Copy,
                     reads=[psb], writes=[self.x_latT])

    def b2(self):
        NT = self.NT
        nblks = NT // 4
        for e in range(NE):
            w1 = self.w1b[e % 2]
            w2 = self.w2b[e % 2]
            for lc in range(8):
                self.dma("w1b%d" % (e % 2), w1.ap[:, lc, :],
                         self.w1Td[e, lc * P:(lc + 1) * P, :],
                         reads=[w1], writes=[w1])
            for ic in range(8):
                self.dma("w2b%d" % (e % 2), w2.ap[:, ic, :],
                         self.w2Td[e, ic * P:(ic + 1) * P, :],
                         reads=[w2], writes=[w2])
            for blk in range(nblks):
                toff = blk * 512
                for ic in range(8):
                    bank = ic % 4
                    psb = self.PSR("A", bank * 512, (bank + 1) * 512)
                    for lc in range(8):
                        self.T_("matmul",
                                self.psA[:, bank * 512:(bank + 1) * 512],
                                w1.ap[:, lc, ic * P:(ic + 1) * P],
                                self.x_latT.ap[:, lc, toff:toff + 512],
                                start=(lc == 0), stop=(lc == 7),
                                reads=[w1, self.x_latT], writes=[psb])
                    rt = self.rtmp2[ic % 2]
                    self.V_("tensor_scalar", rt.ap,
                            self.psA[:, bank * 512:(bank + 1) * 512],
                            0.0, None, ALU.max, reads=[psb], writes=[rt])
                    self.A_("activation", self.h_all.ap[:, ic, :], rt.ap,
                            ACTF.Square, reads=[rt], writes=[self.h_all])
                for lh in range(2):
                    for tt in range(4):
                        psb = self.PSR("B", tt * 512, (tt + 1) * 512)
                        for ic in range(8):
                            self.T_("matmul",
                                    self.psB[:, tt * 512:(tt + 1) * 512],
                                    self.h_all.ap[:, ic, tt * P:(tt + 1) * P],
                                    w2.ap[:, ic, lh * 512:(lh + 1) * 512],
                                    start=(ic == 0), stop=(ic == 7),
                                    reads=[self.h_all, w2], writes=[psb])
                        tg = blk * 4 + tt
                        cwap = self.cw.ap[:, tg, e:e + 1]
                        ydst = self.y_lat.ap[:, tg, lh * 512:(lh + 1) * 512]
                        if e == 0:
                            self.V_("tensor_scalar", ydst,
                                    self.psB[:, tt * 512:(tt + 1) * 512],
                                    cwap, None, ALU.mult,
                                    reads=[psb, self.cw], writes=[self.y_lat])
                        else:
                            self.V_("scalar_tensor_tensor", ydst,
                                    self.psB[:, tt * 512:(tt + 1) * 512],
                                    cwap, ydst, ALU.mult, ALU.add,
                                    reads=[psb, self.cw, self.y_lat],
                                    writes=[self.y_lat])
        for tg in range(NT):
            self.V_("tensor_copy", self.ylb.ap, self.y_lat.ap[:, tg, :],
                    reads=[self.y_lat], writes=[self.ylb])
            self.transpose("trpY", self.ylT.ap, self.ylb.ap,
                           reads=[self.ylb, self.ylT], writes=[self.ylT])
            self.dma("sylT", self.ylT_s[:, :, tg * P:(tg + 1) * P], self.ylT.ap,
                     reads=[self.ylT], writes=[("ylT_s", tg)])

    def b3(self):
        TP = 8
        upTd_r = self.upTd.rearrange("(a p) n -> p a n", p=P)
        for ps in range(self.NT // TP):
            slab = self.slab3
            for j in range(TP):
                t = ps * TP + j
                self.dma("slab3", slab.ap[:, :, j * P:(j + 1) * P],
                         self.ln2T_s[t], reads=[("ln2T_s", t)], writes=[slab])
            self.dma("yl3", self.yl3.ap,
                     self.ylT_s[:, :, ps * 1024:(ps + 1) * 1024],
                     reads=[self.yl3] + [("ylT_s", ps * TP + j) for j in range(TP)],
                     writes=[self.yl3])
            for sig in range(5):
                if sig < 4:
                    for sc in range(8):
                        self.dma("dnw", self.dnb.ap[:, sc, :],
                                 self.dfT[sig * 1024 + sc * P:
                                          sig * 1024 + (sc + 1) * P, :],
                                 reads=[self.dnb], writes=[self.dnb])
                    for si in range(8):
                        ub = self.upb[si % 2]
                        self.dma("upw", ub.ap,
                                 upTd_r[:, :, sig * 1024 + si * P:
                                        sig * 1024 + (si + 1) * P],
                                 reads=[ub], writes=[ub])
                        for th in range(2):
                            bank = (si * 2 + th) % 4
                            psb = self.PSR("A", bank * 512, (bank + 1) * 512)
                            for kc in range(NKC):
                                self.T_("matmul",
                                        self.psA[:, bank * 512:(bank + 1) * 512],
                                        ub.ap[:, kc, :],
                                        slab.ap[:, kc, th * 512:(th + 1) * 512],
                                        start=(kc == 0), stop=(kc == NKC - 1),
                                        reads=[ub, slab], writes=[psb])
                            rt = self.rtmp3[th]
                            self.V_("tensor_scalar", rt.ap,
                                    self.psA[:, bank * 512:(bank + 1) * 512],
                                    0.0, None, ALU.max, reads=[psb], writes=[rt])
                            self.A_("activation",
                                    self.sTs.ap[:, si, th * 512:(th + 1) * 512],
                                    rt.ap, ACTF.Square,
                                    reads=[rt], writes=[self.sTs])
                else:
                    for sc in range(8):
                        self.dma("dnw", self.dnb.ap[:, sc, :],
                                 self.dfT[SI + sc * P:SI + (sc + 1) * P, :],
                                 reads=[self.dnb], writes=[self.dnb])
                for t in range(TP):
                    psb = self.PSR("B", 0, 2048)
                    for sc in range(8):
                        if sig < 4:
                            lhs = self.sTs.ap[:, sc, t * P:(t + 1) * P]
                            lrd = self.sTs
                        else:
                            lhs = self.yl3.ap[:, sc, t * P:(t + 1) * P]
                            lrd = self.yl3
                        for co in range(4):
                            self.T_("matmul", self.psB[:, co * 512:(co + 1) * 512],
                                    lhs, self.dnb.ap[:, sc, co * 512:(co + 1) * 512],
                                    start=(sc == 0), stop=(sc == 7),
                                    reads=[lrd, self.dnb],
                                    writes=[self.PSR("B", co * 512, (co + 1) * 512)])
                    rdst = self.rout.ap[:, t, :]
                    if sig == 0:
                        self.V_("tensor_copy", rdst, self.psB[:, :],
                                reads=[psb], writes=[self.rout])
                    else:
                        self.V_("tensor_tensor", rdst, rdst, self.psB[:, :],
                                ALU.add, reads=[psb, self.rout], writes=[self.rout])
            for t in range(TP):
                tg = ps * TP + t
                rsrc = self.rout.ap[:, t, :]
                self.bn(self.rout, rsrc)
                self.ln_rstd(self.mv.ap[:, 1:2], self.c_eps5, self.rstd, [self.mv])
                self.dma("x1l3", self.x1l3.ap, self.x1_s[tg],
                         reads=[("x1_s", tg)], writes=[self.x1l3])
                self.V_("tensor_scalar", rsrc, rsrc, self.mv.ap[:, 0:1],
                        self.rstd.ap, ALU.subtract, ALU.mult,
                        reads=[self.rout, self.mv, self.rstd], writes=[self.rout])
                self.V_("tensor_tensor", rsrc, rsrc, self.x1l3.ap,
                        ALU.add, reads=[self.rout, self.x1l3], writes=[self.rout])
                self.dma("sout", self.out[tg * P:(tg + 1) * P, :], rsrc,
                         reads=[self.rout], writes=[("out", tg)])

    def build_all(self):
        self.declare()
        self.const_loads()
        self.a0()
        self.tern(self.wqT, self.mclip_q, CQ)
        self.a1("q")
        self.tern(self.wvT, self.mclip_v, CK)
        self.a1("v")
        self.tern(self.wkT, self.mclip_k, CK)
        self.a1("k")
        self.a2()
        self.tern(self.woT, self.mclip_o, CK)
        self.a3()
        self.b0()
        self.b2()
        self.b3()
        self.p.finish()


LANES = {}
for _l in (["xt0", "xt1", "ld0", "ld1", "x1ld0", "x1ld1",
            "slab0", "slab1", "slab3", "x1l3", "sxqT_s", "sqrT", "syqT_s",
            "sx1", "sln2T", "sout", "sylT", "yl3", "trp0", "trp1",
            "trpP0", "trpP1", "trpY"]
           + ["skr%d" % h for h in range(16)] + ["sv%d" % g for g in range(4)]):
    LANES[_l] = "S"
for _l in ("cb", "ones", "gbias", "cc", "ss", "ch0", "vs0", "vs1"):
    LANES[_l] = "A"
for _l in ("ch1", "ks0", "ks1", "ldq0", "ldq1"):
    LANES[_l] = "G"
for _l in ("gw", "fc1", "w1b0", "w1b1", "w2b0", "w2b1", "upw", "dnw"):
    LANES[_l] = "G"


def build_nc(NT=16):
    nc = bass.Bass("TRN2", target_bir_lowering=False)
    from contextlib import ExitStack
    with ExitStack() as st:
        sems = {}
        for name in ("T", "V", "A", "G"):
            sems[name] = st.enter_context(nc.semaphore("sem" + name))
        for name in LANES:
            sems[name] = st.enter_context(nc.semaphore("q" + name))
        prog = Prog(sems, LANES)
        b = Builder(nc, prog, NT)
        b.build_all()
        block = st.enter_context(nc.Block())
        prog.realize(block)
    return nc


def _host_prep(L0_attn_Wq, L0_attn_Wk, L0_attn_Wv, L0_attn_Wo, L0_ffn_gate_w,
               L0_ffn_gate_bias, L0_ffn_fc1_w, L0_ffn_fc2_w, L0_ffn_w1,
               L0_ffn_w2, L0_ffn_shared_up_w, L0_ffn_shared_down_w, T=2048):
    cont = np.ascontiguousarray
    perm = np.concatenate([
        np.concatenate([np.arange(h * 128, (h + 1) * 128, 2),
                        np.arange(h * 128 + 1, (h + 1) * 128, 2)])
        for h in range(16)])
    shared = {
        "wqT": cont(L0_attn_Wq[perm].T),
        "wkT": cont(L0_attn_Wk[perm].T),
        "wvT": cont(L0_attn_Wv.T),
        "woT": cont(L0_attn_Wo.T),
        "gwT": cont(L0_ffn_gate_w.T),
        "gbiasd": cont(np.tile(L0_ffn_gate_bias[None, :].astype(np.float32),
                               (128, 1))),
        "fc1Td": cont(L0_ffn_fc1_w.T),
        "dfT": cont(np.concatenate([L0_ffn_shared_down_w.T,
                                    L0_ffn_fc2_w.T], axis=0)),
        "w1Td": cont(L0_ffn_w1.transpose(0, 2, 1)),
        "w2Td": cont(L0_ffn_w2.transpose(0, 2, 1)),
        "upTd": cont(L0_ffn_shared_up_w.T),
    }
    import jax
    import jax.numpy as jnp
    with jax.default_device(jax.devices("cpu")[0]):
        freqs = 1.0 / (10000.0 ** (jnp.arange(0, 128, 2, dtype=jnp.float32) / 128))
        ang = jnp.arange(T, dtype=jnp.float32)[:, None] * freqs[None, :]
        cos = np.asarray(jnp.cos(ang), dtype=np.float32)
        sin = np.asarray(jnp.sin(ang), dtype=np.float32)
    shared["cc"] = cont(np.concatenate([cos, cos], axis=1))
    shared["ss"] = cont(np.concatenate([-sin, sin], axis=1))
    ii = np.arange(128)
    shared["cbm"] = np.where(ii[:, None] >= ii[None, :], 0.0,
                             -1e30).astype(np.float32)
    shared["onesd"] = np.ones((128, 128), np.float32)
    return shared


def kernel(L0_attn_x, L0_attn_Wq, L0_attn_Wk, L0_attn_Wv, L0_attn_Wo,
           L0_ffn_gate_w, L0_ffn_gate_bias, L0_ffn_fc1_w, L0_ffn_fc2_w,
           L0_ffn_w1, L0_ffn_w2, L0_ffn_shared_up_w, L0_ffn_shared_down_w):
    shared = _host_prep(L0_attn_Wq, L0_attn_Wk, L0_attn_Wv, L0_attn_Wo,
                        L0_ffn_gate_w, L0_ffn_gate_bias, L0_ffn_fc1_w,
                        L0_ffn_fc2_w, L0_ffn_w1, L0_ffn_w2,
                        L0_ffn_shared_up_w, L0_ffn_shared_down_w)
    nc = build_nc(NT=16)
    in_maps = []
    for c in range(8):
        m = dict(shared)
        m["x"] = np.ascontiguousarray(L0_attn_x[c])
        in_maps.append(m)
    res = run_bass_kernel_spmd(nc, in_maps, core_ids=list(range(8)))
    return np.stack([res.results[c]["out"] for c in range(8)], axis=0)


# version 15 (final)
# speedup vs baseline: 2.3166x; 1.0243x over previous
import numpy as np
import ml_dtypes
import concourse.bass as bass
import concourse.mybir as mybir
from concourse.bass_utils import run_bass_kernel_spmd

F32 = mybir.dt.float32
BF16 = mybir.dt.bfloat16
U8 = mybir.dt.uint8
AX = mybir.AxisListType
ALU = mybir.AluOpType
ACTF = mybir.ActivationFunctionType

P = 128
C = 2048
H = 16
D = 128
NKC = C // P
NE = 32
LAT = 1024
INTER = 1024
SI = 4096
KB = 1024
PGSZ = 2048

CQ = 1.0 / (127.0 * float(np.sqrt(np.float64(128.0))))
CK = 1.0 / 127.0
INV_N = 1.0 / (2048.0 * 2048.0)
M32 = float(2 ** 23 * 1.5)
M64 = float(2 ** 52 * 1.5)


class Buf:
    __slots__ = ("ap", "keys")

    def __init__(self, ap, keys):
        self.ap = ap
        self.keys = frozenset(keys)


class Prog:
    ENGS = ("T", "V", "A", "G", "S")

    def __init__(self, sems, lane_queue):
        self.sems = sems
        self.lane_queue = lane_queue
        self.cnt = {k: 0 for k in sems}
        self.steps = {e: [] for e in self.ENGS}
        self.seen = {e: {} for e in self.ENGS}
        self.pages = {}
        self.dirty = {e: set() for e in self.ENGS}
        self.rdirty = {e: set() for e in self.ENGS}

    def _rec(self, k):
        r = self.pages.get(k)
        if r is None:
            r = [None, []]
            self.pages[k] = r
        return r

    @staticmethod
    def _keys(bufs):
        s = set()
        for b in bufs:
            s |= b.keys if isinstance(b, Buf) else {b}
        return s

    def op(self, eng, fn, reads=(), writes=(), lane=None):
        rk = self._keys(reads)
        wk = self._keys(writes)
        waits = []
        seen = self.seen[eng]

        def need(sem, val, agent):
            if agent != eng and agent in self.lane_queue:
                val = self.cnt[agent]
            if val > 0 and seen.get(sem, 0) < val:
                waits.append((self.sems[sem], val))
                seen[sem] = val

        drain = False
        for k in rk | wk:
            rec = self.pages.get(k)
            if rec is None:
                continue
            w = rec[0]
            if w is not None:
                sem, val, agent = w
                if agent == eng:
                    if k in self.dirty[eng]:
                        drain = True
                else:
                    need(sem, val, agent)
            if k in wk:
                for sem, val, agent in rec[1]:
                    if agent == eng:
                        if k in self.rdirty[eng]:
                            drain = True
                    else:
                        need(sem, val, agent)
        if eng == "T":
            drain = False

        if lane is None:
            sem_name, inc = eng, 1
            agent = eng
        else:
            sem_name, inc = lane, 16
            agent = lane
        self.cnt[sem_name] += inc
        val = self.cnt[sem_name]

        if drain:
            self.dirty[eng].clear()
            self.rdirty[eng].clear()
        for k in wk:
            rec = self._rec(k)
            rec[0] = (sem_name, val, agent)
            rec[1] = []
        for k in rk:
            self._rec(k)[1].append((sem_name, val, agent))
        if lane is None:
            self.dirty[eng] |= wk
            self.rdirty[eng] |= rk
        self.steps[eng].append((tuple(waits), drain, fn, self.sems[sem_name], inc))
        return val

    def finish(self):
        waits = [(self.sems[k], v) for k, v in self.cnt.items() if v > 0]
        self.steps["S"].append((tuple(waits), False, None, None, 0))

    def realize(self, block):
        names = {"T": "tensor", "V": "vector", "A": "scalar", "G": "gpsimd", "S": "sync"}
        for e in self.ENGS:
            steps = self.steps[e]

            def mk(steps=steps):
                def body(eng):
                    for waits, drain, fn, sem, inc in steps:
                        for s, v in waits:
                            eng.wait_ge(s, v)
                        if drain:
                            eng.drain()
                        if fn is not None:
                            inst = fn(eng)
                            if sem is not None:
                                inst.then_inc(sem, inc)
                return body
            getattr(block, names[e])(mk())


class Builder:
    def __init__(self, nc, prog, NT):
        self.nc = nc
        self.p = prog
        self.NT = NT
        self.T = NT * P

    def T_(self, fname, *args, reads=(), writes=(), **kw):
        self.p.op("T", (lambda e, a=args, k=kw, f=fname: getattr(e, f)(*a, **k)),
                  reads, writes)

    def V_(self, fname, *args, reads=(), writes=(), **kw):
        self.p.op("V", (lambda e, a=args, k=kw, f=fname: getattr(e, f)(*a, **k)),
                  reads, writes)

    def A_(self, fname, *args, reads=(), writes=(), **kw):
        self.p.op("A", (lambda e, a=args, k=kw, f=fname: getattr(e, f)(*a, **k)),
                  reads, writes)

    def dma(self, lane, out_ap, in_ap, reads=(), writes=()):
        self.p.op(self.p.lane_queue[lane],
                  (lambda e, o=out_ap, i=in_ap: e.dma_start(out=o, in_=i)),
                  reads, writes, lane=lane)

    def transpose(self, lane, out_ap, in_ap, reads=(), writes=()):
        self.p.op("S", (lambda e, o=out_ap, i=in_ap: e.dma_start_transpose(o, i)),
                  reads, writes, lane=lane)

    def AR(self, off_kb, shape, dtype):
        nel = int(np.prod(shape[1:]))
        nby = nel * mybir.dt.size(dtype)
        if dtype == BF16:
            arena, tag, epk = self.arena16, "A6", 512
        else:
            arena, tag, epk = self.arena32, "A2", 256
        ap = arena[:, off_kb * epk: off_kb * epk + nel]
        if len(shape) == 3:
            ap = ap.rearrange("p (a b) -> p a b", b=shape[2])
        keys = {(tag, g) for g in range(off_kb * KB // PGSZ,
                                        (off_kb * KB + nby + PGSZ - 1) // PGSZ)}
        return Buf(ap, keys)

    def SB(self, name, shape, dtype=F32, tdest=False):
        h = self.nc.alloc_sbuf_tensor(name, shape, dtype, is_transpose_dest=tdest)
        return Buf(h[:], {name})

    def PSR(self, which, lo, hi):
        t = self.psA if which == "A" else self.psB
        keys = {("ps" + which, b) for b in range(lo // 512, (hi + 511) // 512)}
        return Buf(t[:, lo:hi], keys)

    def declare(self):
        nc = self.nc
        NT, T = self.NT, self.T
        dr = lambda n, s, dt=F32, kind="ExternalInput": nc.dram_tensor(n, s, dt, kind=kind).ap()
        self.x = dr("x", [T, C])
        self.wqT = dr("wqT", [C, C]); self.wkT = dr("wkT", [C, C])
        self.wvT = dr("wvT", [C, C]); self.woT = dr("woT", [C, C])
        self.cc = dr("cc", [T, D]); self.ss = dr("ss", [T, D])
        self.cbm = dr("cbm", [P, P]); self.onesd = dr("onesd", [P, P])
        self.gbiasd = dr("gbiasd", [P, NE])
        self.gwT = dr("gwT", [C, NE])
        self.fc1Td = dr("fc1Td", [C, LAT])
        self.dfT = dr("dfT", [SI + LAT, C])
        self.w1Td = dr("w1Td", [NE, LAT, INTER])
        self.w2Td = dr("w2Td", [NE, INTER, LAT])
        self.upTd = dr("upTd", [C, SI])
        self.out = dr("out", [T, C], F32, kind="ExternalOutput")
        idr = lambda n, s, dt: nc.dram_tensor(n, s, dt, kind="Internal").ap()
        self.xqT_s = idr("xqT_s", [NT, P, NKC, P], BF16)
        self.qrT_s = idr("qrT_s", [NT, P, NKC, P], BF16)
        self.krT_s = idr("krT_s", [H, P, T], BF16)
        self.v_s = idr("v_s", [4, T, 512], BF16)
        self.yqT_s = idr("yqT_s", [NT, P, NKC, P], BF16)
        self.x1_s = idr("x1_s", [NT, P, C], F32)
        self.ln2T_s = idr("ln2T_s", [NT, P, NKC, P], BF16)
        self.ylT_s = idr("ylT_s", [P, 8, T], BF16)

        self.psA = nc.alloc_psum_tensor("psA", [P, 2048], F32)
        self.psB = nc.alloc_psum_tensor("psB", [P, 2048], F32)

        S = self.SB
        self.tr = [S("tr0", [P, NKC, P], BF16, tdest=True),
                   S("tr1", [P, NKC, P], BF16, tdest=True)]
        self.pT = S("pT", [P, NT, P], BF16, tdest=True)
        self.ylT = S("ylT", [P, 8, P], BF16, tdest=True)
        self.cb = S("cb", [P, P]); self.ones = S("ones", [P, P])
        self.cct = S("cct", [P, D]); self.sst = S("sst", [P, D])
        self.gbias = S("gbias", [P, NE])
        self.gw_sb = S("gw_sb", [P, NKC, NE], BF16)
        self.samc = S("samc", [P, NT]); self.samc_y = S("samc_y", [P, NT])
        self.cw = S("cw", [P, NT, NE])
        self.bst = S("bst", [P, 4, 6]); self.mv = S("mv", [P, 2])
        self.cs16 = S("cs16", [P, NKC]); self.cstot = S("cstot", [P, 1])
        for n in ("mclip_q", "mclip_k", "mclip_v", "mclip_o", "thr", "nthr",
                  "lnv", "rstd", "rsq", "bq", "rr", "sfold", "am", "amc",
                  "rmc", "qsc", "qsc2", "sc1", "s2t", "vv", "gv",
                  "nm", "den", "rden", "ss2v",
                  "g1v", "g2v", "m1g", "m2g", "th8", "s8", "rs8", "mx1",
                  "nm2", "den2", "rden2"):
            setattr(self, n, S(n, [P, 1]))
        self.nmp = [self.nm, self.nm2]
        self.denp = [self.den, self.den2]
        self.rdenp = [self.rden, self.rden2]
        self.c_eps5 = S("c_eps5", [P, 1]); self.c_eps6 = S("c_eps6", [P, 1])
        self.c_2565 = S("c_2565", [P, 1])
        self.c_m256 = S("c_m256", [P, 1]); self.c_n1e30 = S("c_n1e30", [P, 1])
        self.c_one = S("c_one", [P, 1]); self.gdisc = S("gdisc", [P, 1])
        self.c_zero = S("c_zero", [P, 1])
        self.mqs = S("mqs", [P, 1]); self.rs25 = S("rs25", [P, 1])
        self.et = S("et", [P, NE]); self.d1 = S("d1", [P, NE])
        self.scrs = S("scrs", [P, NE]); self.sbv = S("sbv", [P, NE])
        self.msb = S("msb", [P, NE]); self.cur = S("cur", [P, NE])
        self.mk32 = S("mk32", [P, NE]); self.wsl = S("wsl", [P, NE])
        self.selm = S("selm", [P, NE])
        self.gsv = S("gsv", [P, 4]); self.mask4 = S("mask4", [P, 4])
        self.t4b = S("t4b", [P, 4]); self.madd4 = S("madd4", [P, 4])
        self.msk8 = S("msk8", [P, 8]); self.t8b = S("t8b", [P, 8])

        self.arena16 = nc.alloc_sbuf_tensor("arena16", [P, 108 * 512], BF16)
        self.arena32 = nc.alloc_sbuf_tensor("arena32", [P, 76 * 256], F32)
        A = self.AR
        self.wtern = A(0, [P, NKC, C], BF16)
        self.xqb = A(64, [P, C], BF16)
        self.vbb = A(68, [P, C], BF16)
        self.ld = [A(72, [P, NKC, P], BF16), A(76, [P, NKC, P], BF16)]
        self.pb = A(80, [P, 2048], BF16)
        self.ks = [A(84, [P, 2048], BF16), A(88, [P, 2048], BF16)]
        self.vsh = [A(92, [P, NT, 256], BF16),
                    A(100, [P, NT, 256], BF16)]
        self.pbp = [self.pb, self.vbb]
        self.chunk = [A(0, [P, C], F32), A(8, [P, C], F32)]
        self.xt = [A(16, [P, C], F32), A(24, [P, C], F32)]
        self.xn = A(32, [P, C], F32)
        self.qf = A(40, [P, C], F32)
        self.uu = A(48, [P, C], F32)
        self.trt = A(56, [P, C], F32)
        self.y_scr = [A(64, [P, C], F32), A(64, [P, C], F32)]
        self.fc1T = A(76, [P, NKC, LAT], BF16)
        self.x_latT = A(0, [P, 8, T], BF16)
        self.slab = [A(32, [P, NKC, 512], BF16), A(48, [P, NKC, 512], BF16)]
        self.w1b = [A(32, [P, 8, INTER], BF16), A(48, [P, 8, INTER], BF16)]
        self.w2b = [A(64, [P, 8, LAT], BF16), A(80, [P, 8, LAT], BF16)]
        self.h_all = A(96, [P, 8, 512], BF16)
        self.ylb = A(104, [P, LAT], BF16)
        self.dnb = A(0, [P, 8, C], BF16)
        self.slab3 = A(32, [P, NKC, 1024], BF16)
        self.sTs = A(64, [P, 8, 1024], BF16)
        self.upb = [A(80, [P, NKC, P], BF16),
                    A(84, [P, NKC, P], BF16)]
        self.yl3 = A(88, [P, 8, 1024], BF16)
        self.x1ld = [A(0, [P, C], F32), A(8, [P, C], F32)]
        self.y_lat = A(0, [P, NT, LAT], F32)
        self.rtmp2 = [A(64, [P, 512], F32), A(66, [P, 512], F32)]
        self.rout = A(0, [P, 8, C], F32)
        self.x1l3 = A(64, [P, C], F32)
        self.rtmp3 = [A(72, [P, 512], F32), A(74, [P, 512], F32)]

    def act(self, out, in_, func, bias=0.0, scale=1.0, accum=None,
            reads=(), writes=()):
        kw = dict(bias=bias, scale=scale)
        if accum is not None:
            kw["accum_out"] = accum
        self.A_("activation", out, in_, func, reads=reads, writes=writes, **kw)

    def bn(self, src_buf, src_ap2048, extra_reads=()):
        for i in range(4):
            self.V_("bn_stats", self.bst.ap[:, i, :], src_ap2048[:, i * 512:(i + 1) * 512],
                    reads=[src_buf, *extra_reads], writes=[self.bst])
        self.V_("bn_aggr", self.mv.ap, self.bst.ap,
                reads=[self.bst], writes=[self.mv])

    def ln_rstd(self, var_ap, eps_buf, out_buf, reads):
        self.act(self.lnv.ap, var_ap, ACTF.Ln, bias=eps_buf.ap, scale=1.0,
                 reads=list(reads) + [eps_buf], writes=[self.lnv])
        self.act(out_buf.ap, self.lnv.ap, ACTF.Exp, bias=0.0, scale=-0.5,
                 reads=[self.lnv], writes=[out_buf])

    def const_loads(self):
        self.V_("memset", self.c_eps5.ap, 1e-5, writes=[self.c_eps5])
        self.V_("memset", self.c_eps6.ap, 1e-6, writes=[self.c_eps6])
        self.V_("memset", self.c_2565.ap, 256.5, writes=[self.c_2565])
        self.V_("memset", self.c_m256.ap, -256.0, writes=[self.c_m256])
        self.V_("memset", self.c_n1e30.ap, -1e30, writes=[self.c_n1e30])
        self.V_("memset", self.c_one.ap, 1.0, writes=[self.c_one])
        self.V_("memset", self.c_zero.ap, 0.0, writes=[self.c_zero])
        self.V_("tensor_single_scalar", self.gdisc.ap, self.c_one.ap, M64,
                ALU.add, reads=[self.c_one], writes=[self.gdisc])
        self.V_("tensor_single_scalar", self.gdisc.ap, self.gdisc.ap, M64,
                ALU.subtract, reads=[self.gdisc], writes=[self.gdisc])
        self.dma("cb", self.cb.ap, self.cbm, writes=[self.cb])
        self.dma("ones", self.ones.ap, self.onesd, writes=[self.ones])
        self.dma("gbias", self.gbias.ap, self.gbiasd, writes=[self.gbias])
        self.dma("gw", self.gw_sb.ap,
                 self.gwT.rearrange("(a p) n -> p a n", p=P), writes=[self.gw_sb])
        from concourse.hw_specs import get_activation_tables
        idx = None
        for i, funcs in enumerate(get_activation_tables(self.nc.m.arch).values()):
            if ACTF.Exp in funcs and ACTF.Ln in funcs:
                idx = i
                break
        if idx is not None:
            nc = self.nc
            def fn(e, idx=idx, nc=nc):
                return e.add_instruction(mybir.InstLoadActFuncSet(
                    name=nc.get_next_instruction_name(),
                    act_func_set_id=idx, ins=[], outs=[]))
            self.p.op("A", fn, (), ())

    def tern(self, wdram, mclip, cmul):
        NCH = NKC
        for kc in range(NCH):
            ch = self.chunk[kc % 2]
            self.dma("ch%d" % (kc % 2), ch.ap, wdram[kc * P:(kc + 1) * P, :],
                     reads=[ch], writes=[ch])
            self.V_("reduce_sum", self.cs16.ap[:, kc:kc + 1], ch.ap, AX.X,
                    apply_absolute_value=True, reads=[ch], writes=[self.cs16])
        self.V_("reduce_sum", self.cstot.ap, self.cs16.ap, AX.X,
                reads=[self.cs16], writes=[self.cstot])
        ps1 = self.PSR("A", 0, 512)
        self.T_("matmul", self.psA[:, 0:1], self.ones.ap, self.cstot.ap,
                start=True, stop=True, reads=[self.ones, self.cstot], writes=[ps1])
        self.V_("tensor_scalar", mclip.ap, self.psA[:, 0:1], float(INV_N), 1e-5,
                ALU.mult, ALU.max, reads=[ps1], writes=[mclip])
        self.V_("tensor_scalar", self.thr.ap, mclip.ap, 0.5, None, ALU.mult,
                reads=[mclip], writes=[self.thr])
        self.V_("tensor_scalar", self.nthr.ap, mclip.ap, -0.5, None, ALU.mult,
                reads=[mclip], writes=[self.nthr])
        self.V_("tensor_scalar", self.mqs.ap, mclip.ap, float(cmul), None,
                ALU.mult, reads=[mclip], writes=[self.mqs])
        for kc in range(NCH):
            ch = self.chunk[kc % 2]
            self.dma("ch%d" % (kc % 2), ch.ap, wdram[kc * P:(kc + 1) * P, :],
                     reads=[ch], writes=[ch])
            self.V_("tensor_scalar", self.qf.ap, ch.ap, self.nthr.ap, None,
                    ALU.is_lt, reads=[ch, self.nthr], writes=[self.qf])
            self.V_("scalar_tensor_tensor", self.wtern.ap[:, kc, :], ch.ap,
                    self.thr.ap, self.qf.ap, ALU.is_gt, ALU.subtract,
                    reads=[ch, self.thr, self.qf], writes=[self.wtern])

    def a0(self):
        self.dma("xt0", self.xt[0].ap, self.x[0:P, :],
                 reads=[self.xt[0]], writes=[self.xt[0]])
        for t in range(self.NT):
            xt = self.xt[t % 2]
            if t + 1 < self.NT:
                nxt = self.xt[(t + 1) % 2]
                self.dma("xt%d" % ((t + 1) % 2), nxt.ap,
                         self.x[(t + 1) * P:(t + 2) * P, :],
                         reads=[nxt], writes=[nxt])
            self.bn(xt, xt.ap)
            self.ln_rstd(self.mv.ap[:, 1:2], self.c_eps5, self.rstd, [self.mv])
            self.V_("tensor_tensor", self.rsq.ap, self.rstd.ap, self.rstd.ap,
                    ALU.mult, reads=[self.rstd], writes=[self.rsq])
            self.V_("tensor_tensor", self.bq.ap, self.mv.ap[:, 1:2], self.rsq.ap,
                    ALU.mult, reads=[self.mv, self.rsq], writes=[self.bq])
            self.act(self.lnv.ap, self.bq.ap, ACTF.Ln, bias=self.c_eps6.ap,
                     reads=[self.bq, self.c_eps6], writes=[self.lnv])
            self.act(self.rr.ap, self.lnv.ap, ACTF.Exp, scale=-0.5,
                     reads=[self.lnv], writes=[self.rr])
            self.V_("tensor_tensor", self.sfold.ap, self.rstd.ap, self.rr.ap,
                    ALU.mult, reads=[self.rstd, self.rr], writes=[self.sfold])
            self.V_("tensor_scalar", self.xn.ap, xt.ap, self.mv.ap[:, 0:1],
                    self.sfold.ap, ALU.subtract, ALU.mult,
                    reads=[xt, self.mv, self.sfold], writes=[self.xn])
            self.quant_store(self.xn, self.samc, t, self.xqT_s, "xqT_s",
                             scale_pre=None)

    def quant_store(self, src, samc, t, dst_dram, dst_key, scale_pre):
        self.V_("reduce_max", self.am.ap, src.ap, AX.X, apply_absolute_value=True,
                reads=[src], writes=[self.am])
        if scale_pre is not None:
            self.V_("tensor_tensor", self.amc.ap, self.am.ap, scale_pre.ap,
                    ALU.mult, reads=[self.am, scale_pre], writes=[self.amc])
            amsrc = self.amc
        else:
            amsrc = self.am
        self.V_("tensor_single_scalar", samc.ap[:, t:t + 1], amsrc.ap, 1e-5,
                ALU.max, reads=[amsrc], writes=[samc])
        self.V_("reciprocal", self.rmc.ap, samc.ap[:, t:t + 1],
                reads=[samc], writes=[self.rmc])
        self.V_("tensor_scalar", self.qsc.ap, self.rmc.ap, 127.0, None, ALU.mult,
                reads=[self.rmc], writes=[self.qsc])
        if scale_pre is not None:
            self.V_("tensor_tensor", self.qsc2.ap, self.qsc.ap, scale_pre.ap,
                    ALU.mult, reads=[self.qsc, scale_pre], writes=[self.qsc2])
            qs = self.qsc2
        else:
            qs = self.qsc
        self.V_("tensor_scalar", self.qf.ap, src.ap, qs.ap, None, ALU.mult,
                reads=[src, qs], writes=[self.qf])
        self.V_("tensor_single_scalar", self.uu.ap, self.qf.ap, M32, ALU.add,
                reads=[self.qf], writes=[self.uu])
        self.V_("tensor_single_scalar", self.uu.ap, self.uu.ap, M32,
                ALU.subtract, reads=[self.uu], writes=[self.uu])
        self.V_("tensor_single_scalar", self.trt.ap, self.qf.ap, M64, ALU.add,
                reads=[self.qf], writes=[self.trt])
        self.V_("tensor_single_scalar", self.trt.ap, self.trt.ap, M64,
                ALU.subtract, reads=[self.trt], writes=[self.trt])
        self.V_("tensor_tensor", self.trt.ap, self.trt.ap, self.uu.ap,
                ALU.subtract, reads=[self.trt, self.uu], writes=[self.trt])
        self.V_("scalar_tensor_tensor", self.xqb.ap, self.trt.ap,
                self.gdisc.ap, self.uu.ap, ALU.mult, ALU.add,
                reads=[self.trt, self.uu, self.gdisc], writes=[self.xqb])
        tr = self.tr[t % 2]
        self.transpose("trp%d" % (t % 2), tr.ap, self.xqb.ap,
                       reads=[self.xqb, tr], writes=[tr])
        self.dma("s" + dst_key, dst_dram[t], tr.ap, reads=[tr],
                 writes=[(dst_key, t)])

    def a1(self, kind):
        NT = self.NT
        self.dma("ld0", self.ld[0].ap, self.xqT_s[0],
                 reads=[self.ld[0], ("xqT_s", 0)], writes=[self.ld[0]])
        for t in range(NT):
            ld = self.ld[t % 2]
            if t + 1 < NT:
                nld = self.ld[(t + 1) % 2]
                self.dma("ld%d" % ((t + 1) % 2), nld.ap, self.xqT_s[t + 1],
                         reads=[nld, ("xqT_s", t + 1)], writes=[nld])
            psn = "A" if t % 2 == 0 else "B"
            pst = self.psA if psn == "A" else self.psB
            psb = self.PSR(psn, 0, 2048)
            for kc in range(NKC):
                for n in range(4):
                    self.T_("matmul", pst[:, n * 512:(n + 1) * 512],
                            ld.ap[:, kc, :], self.wtern.ap[:, kc, n * 512:(n + 1) * 512],
                            start=(kc == 0), stop=(kc == NKC - 1),
                            reads=[ld, self.wtern], writes=[psb])
            self.V_("tensor_scalar", self.sc1.ap, self.samc.ap[:, t:t + 1],
                    self.mqs.ap, None, ALU.mult,
                    reads=[self.samc, self.mqs], writes=[self.sc1])
            if kind == "v":
                self.V_("tensor_scalar", self.vbb.ap, pst[:, :], self.sc1.ap,
                        None, ALU.mult, reads=[psb, self.sc1], writes=[self.vbb])
                for hg in range(4):
                    self.dma("sv%d" % hg, self.v_s[hg, t * P:(t + 1) * P, :],
                             self.vbb.ap[:, hg * 512:(hg + 1) * 512],
                             reads=[self.vbb], writes=[("v_s", hg, t)])
                continue
            self.V_("tensor_scalar", self.qf.ap, pst[:, :], self.sc1.ap, None,
                    ALU.mult, reads=[psb, self.sc1], writes=[self.qf])
            self.dma("cc", self.cct.ap, self.cc[t * P:(t + 1) * P, :],
                     reads=[self.cct], writes=[self.cct])
            self.dma("ss", self.sst.ap, self.ss[t * P:(t + 1) * P, :],
                     reads=[self.sst], writes=[self.sst])
            qf3 = self.qf.ap.rearrange("p (h d) -> p h d", d=D)
            tr3 = self.trt.ap.rearrange("p (h d) -> p h d", d=D)
            ccb = self.cct.ap.unsqueeze(1).broadcast_to([P, H, D])
            uu3 = self.uu.ap.rearrange("p (h d) -> p h d", d=D)
            self.V_("tensor_tensor", uu3, qf3, ccb, ALU.mult,
                    reads=[self.qf, self.cct], writes=[self.uu])
            sslo = self.sst.ap[:, 0:64].unsqueeze(1).broadcast_to([P, H, 64])
            sshi = self.sst.ap[:, 64:128].unsqueeze(1).broadcast_to([P, H, 64])
            self.V_("tensor_tensor", tr3[:, :, 0:64], qf3[:, :, 64:128], sslo,
                    ALU.mult, reads=[self.qf, self.sst], writes=[self.trt])
            self.V_("tensor_tensor", tr3[:, :, 64:128], qf3[:, :, 0:64], sshi,
                    ALU.mult, reads=[self.qf, self.sst], writes=[self.trt])
            self.V_("tensor_tensor", self.xqb.ap, self.uu.ap, self.trt.ap,
                    ALU.add, reads=[self.uu, self.trt], writes=[self.xqb])
            tr = self.tr[t % 2]
            self.transpose("trp%d" % (t % 2), tr.ap, self.xqb.ap,
                           reads=[self.xqb, tr], writes=[tr])
            if kind == "q":
                self.dma("sqrT", self.qrT_s[t], tr.ap, reads=[tr],
                         writes=[("qrT_s", t)])
            else:
                for h in range(H):
                    self.dma("skr%d" % h, self.krT_s[h, :, t * P:(t + 1) * P],
                             tr.ap[:, h, :], reads=[tr], writes=[("krT_s", h)])

    def a2(self):
        NT = self.NT
        self.dma("ldq0", self.ld[0].ap, self.qrT_s[0],
                 reads=[self.ld[0], ("qrT_s", 0)], writes=[self.ld[0]])
        for qt in range(NT):
            L = (qt + 1) * P
            ld = self.ld[qt % 2]
            if qt + 1 < NT:
                nld = self.ld[(qt + 1) % 2]
                self.dma("ldq%d" % ((qt + 1) % 2), nld.ap, self.qrT_s[qt + 1],
                         reads=[nld, ("qrT_s", qt + 1)], writes=[nld])
            ysc = self.y_scr[qt % 2]
            pTs = [self.pT, self.tr[qt % 2]]
            nch = (L + 511) // 512

            def qk(h):
                hg = h // 4
                if h % 2 == 0:
                    half = (h % 4) // 2
                    vb = self.vsh[half]
                    src = self.v_s[hg].rearrange("(k p) f -> p k f", p=P)
                    self.dma("vs%d" % half, vb.ap[:, 0:qt + 1, :],
                             src[:, 0:qt + 1, half * 256:(half + 1) * 256],
                             reads=[vb] + [("v_s", hg, tt) for tt in range(qt + 1)],
                             writes=[vb])
                ks = self.ks[h % 2]
                self.dma("ks%d" % (h % 2), ks.ap[:, 0:L], self.krT_s[h, :, 0:L],
                         reads=[ks, ("krT_s", h)], writes=[ks])
                Rn = "A" if h % 2 == 0 else "B"
                Rt = self.psA if Rn == "A" else self.psB
                for cc_ in range(nch):
                    n = min(512, L - cc_ * 512)
                    self.T_("matmul", Rt[:, cc_ * 512:cc_ * 512 + n],
                            ld.ap[:, h, :], ks.ap[:, cc_ * 512:cc_ * 512 + n],
                            start=True, stop=True,
                            reads=[ld, ks],
                            writes=[self.PSR(Rn, cc_ * 512, cc_ * 512 + n)])

            def smx(h):
                Rn = "A" if h % 2 == 0 else "B"
                Rt = self.psA if Rn == "A" else self.psB
                Rb = self.PSR(Rn, 0, L)
                nm = self.nmp[h % 2]; den = self.denp[h % 2]
                rden = self.rdenp[h % 2]; pb = self.pbp[h % 2]
                self.V_("tensor_tensor", Rt[:, qt * P:L], Rt[:, qt * P:L],
                        self.cb.ap, ALU.add,
                        reads=[self.PSR(Rn, qt * P, L), self.cb],
                        writes=[self.PSR(Rn, qt * P, L)])
                self.V_("reduce_max", nm.ap, Rt[:, 0:L], AX.X,
                        negate=True, reads=[Rb], writes=[nm])
                self.act(pb.ap[:, 0:L], Rt[:, 0:L], ACTF.Exp,
                         bias=nm.ap, scale=1.0, accum=den.ap,
                         reads=[Rb, nm], writes=[pb, den])
                self.V_("reciprocal", rden.ap, den.ap,
                        reads=[den], writes=[rden])
                pT = pTs[h % 2]
                self.transpose("trpP%d" % (h % 2), pT.ap[:, 0:qt + 1, :],
                               pb.ap[:, 0:L], reads=[pb, pT], writes=[pT])

            def pv(h):
                h4 = h % 4
                Rn = "A" if h % 2 == 0 else "B"
                Yt = self.psA if Rn == "A" else self.psB
                Yb = self.PSR(Rn, 1920, 2048)
                pT = pTs[h % 2]; rden = self.rdenp[h % 2]
                vb = self.vsh[h4 // 2]
                for kt in range(qt + 1):
                    self.T_("matmul", Yt[:, 1920:2048], pT.ap[:, kt, :],
                            vb.ap[:, kt, (h4 % 2) * P:(h4 % 2 + 1) * P],
                            start=(kt == 0), stop=(kt == qt),
                            reads=[pT, vb], writes=[Yb])
                self.V_("tensor_scalar", ysc.ap[:, h * P:(h + 1) * P],
                        Yt[:, 1920:2048], rden.ap, None, ALU.mult,
                        reads=[Yb, rden], writes=[ysc])

            qk(0)
            smx(0)
            for h in range(16):
                if h + 1 < 16:
                    qk(h + 1)
                    smx(h + 1)
                pv(h)
            self.V_("tensor_tensor", self.xn.ap, ysc.ap, ysc.ap, ALU.mult,
                    reads=[ysc], writes=[self.xn])
            self.V_("reduce_sum", self.ss2v.ap, self.xn.ap, AX.X,
                    reads=[self.xn], writes=[self.ss2v])
            self.act(self.lnv.ap, self.ss2v.ap, ACTF.Ln, bias=self.c_eps6.ap,
                     scale=float(1.0 / C), reads=[self.ss2v, self.c_eps6],
                     writes=[self.lnv])
            self.act(self.rr.ap, self.lnv.ap, ACTF.Exp, scale=-0.5,
                     reads=[self.lnv], writes=[self.rr])
            self.quant_store(ysc, self.samc_y, qt, self.yqT_s, "yqT_s",
                             scale_pre=self.rr)

    def a3(self):
        NT = self.NT
        self.dma("ld0", self.ld[0].ap, self.yqT_s[0],
                 reads=[self.ld[0], ("yqT_s", 0)], writes=[self.ld[0]])
        self.dma("xt0", self.xt[0].ap, self.x[0:P, :],
                 reads=[self.xt[0]], writes=[self.xt[0]])
        for t in range(NT):
            ld = self.ld[t % 2]
            xt = self.xt[t % 2]
            if t + 1 < NT:
                nld = self.ld[(t + 1) % 2]
                self.dma("ld%d" % ((t + 1) % 2), nld.ap, self.yqT_s[t + 1],
                         reads=[nld, ("yqT_s", t + 1)], writes=[nld])
                nxt = self.xt[(t + 1) % 2]
                self.dma("xt%d" % ((t + 1) % 2), nxt.ap,
                         self.x[(t + 1) * P:(t + 2) * P, :],
                         reads=[nxt], writes=[nxt])
            psn = "A" if t % 2 == 0 else "B"
            pst = self.psA if psn == "A" else self.psB
            psb = self.PSR(psn, 0, 2048)
            for kc in range(NKC):
                for n in range(4):
                    self.T_("matmul", pst[:, n * 512:(n + 1) * 512],
                            ld.ap[:, kc, :], self.wtern.ap[:, kc, n * 512:(n + 1) * 512],
                            start=(kc == 0), stop=(kc == NKC - 1),
                            reads=[ld, self.wtern], writes=[psb])
            self.V_("tensor_scalar", self.sc1.ap, self.samc_y.ap[:, t:t + 1],
                    self.mqs.ap, None, ALU.mult,
                    reads=[self.samc_y, self.mqs], writes=[self.sc1])
            self.bn(psb, pst)
            self.V_("tensor_tensor", self.s2t.ap, self.sc1.ap, self.sc1.ap,
                    ALU.mult, reads=[self.sc1], writes=[self.s2t])
            self.V_("tensor_tensor", self.vv.ap, self.mv.ap[:, 1:2], self.s2t.ap,
                    ALU.mult, reads=[self.mv, self.s2t], writes=[self.vv])
            self.ln_rstd(self.vv.ap, self.c_eps5, self.rstd, [self.vv])
            self.V_("tensor_tensor", self.gv.ap, self.sc1.ap, self.rstd.ap,
                    ALU.mult, reads=[self.sc1, self.rstd], writes=[self.gv])
            self.V_("tensor_scalar", self.xn.ap, pst[:, :], self.mv.ap[:, 0:1],
                    self.gv.ap, ALU.subtract, ALU.mult,
                    reads=[psb, self.mv, self.gv], writes=[self.xn])
            self.V_("tensor_tensor", self.qf.ap, self.xn.ap, xt.ap, ALU.add,
                    reads=[self.xn, xt], writes=[self.qf])
            self.dma("sx1", self.x1_s[t], self.qf.ap, reads=[self.qf],
                     writes=[("x1_s", t)])

    def b0(self):
        NT = self.NT
        for kc in range(NKC):
            self.dma("fc1", self.fc1T.ap[:, kc, :],
                     self.fc1Td[kc * P:(kc + 1) * P, :],
                     reads=[self.fc1T], writes=[self.fc1T])
        for t in range(NT):
            self.b0_tile(t)
            if t % 4 == 3:
                self.b1_blk(t // 4)

    def b0_tile(self, t):
        if True:
            x1 = self.x1ld[t % 2]
            if t == 0:
                self.dma("x1ld0", x1.ap, self.x1_s[0],
                         reads=[x1, ("x1_s", 0)], writes=[x1])
            if t + 1 < self.NT:
                nx1 = self.x1ld[(t + 1) % 2]
                self.dma("x1ld%d" % ((t + 1) % 2), nx1.ap, self.x1_s[t + 1],
                         reads=[nx1, ("x1_s", t + 1)], writes=[nx1])
            self.bn(x1, x1.ap)
            self.ln_rstd(self.mv.ap[:, 1:2], self.c_eps5, self.rstd, [self.mv])
            self.V_("tensor_scalar", self.xqb.ap, x1.ap, self.mv.ap[:, 0:1],
                    self.rstd.ap, ALU.subtract, ALU.mult,
                    reads=[x1, self.mv, self.rstd], writes=[self.xqb])
            tr = self.tr[t % 2]
            self.transpose("trp%d" % (t % 2), tr.ap, self.xqb.ap,
                           reads=[self.xqb, tr], writes=[tr])
            self.dma("sln2T", self.ln2T_s[t], tr.ap, reads=[tr],
                     writes=[("ln2T_s", t)])
            psg = self.PSR("B", 0, 512)
            for kc in range(NKC):
                self.T_("matmul", self.psB[:, 0:NE], tr.ap[:, kc, :],
                        self.gw_sb.ap[:, kc, :],
                        start=(kc == 0), stop=(kc == NKC - 1),
                        reads=[tr, self.gw_sb], writes=[psg])
            self.act(self.et.ap, self.psB[:, 0:NE], ACTF.Exp, scale=-1.0,
                     reads=[psg], writes=[self.et])
            self.V_("tensor_single_scalar", self.d1.ap, self.et.ap, 1.0, ALU.add,
                    reads=[self.et], writes=[self.d1])
            self.V_("reciprocal", self.scrs.ap, self.d1.ap,
                    reads=[self.d1], writes=[self.scrs])
            self.V_("tensor_tensor", self.sbv.ap, self.scrs.ap, self.gbias.ap,
                    ALU.add, reads=[self.scrs, self.gbias], writes=[self.sbv])
            for g in range(4):
                sg = self.sbv.ap[:, g * 8:(g + 1) * 8]
                self.V_("reduce_max", self.m1g.ap, sg, AX.X,
                        reads=[self.sbv], writes=[self.m1g])
                self.V_("tensor_scalar", self.msk8.ap, sg, self.m1g.ap, None,
                        ALU.is_ge, reads=[self.sbv, self.m1g], writes=[self.msk8])
                self.V_("scalar_tensor_tensor", self.t8b.ap, self.msk8.ap,
                        self.c_n1e30.ap, sg, ALU.mult, ALU.add,
                        reads=[self.msk8, self.sbv, self.c_n1e30],
                        writes=[self.t8b])
                self.V_("reduce_max", self.m2g.ap, self.t8b.ap, AX.X,
                        reads=[self.t8b], writes=[self.m2g])
                self.V_("tensor_tensor", self.gsv.ap[:, g:g + 1], self.m1g.ap,
                        self.m2g.ap, ALU.add,
                        reads=[self.m1g, self.m2g], writes=[self.gsv])
            self.V_("reduce_max", self.g1v.ap, self.gsv.ap, AX.X,
                    reads=[self.gsv], writes=[self.g1v])
            self.V_("tensor_scalar", self.mask4.ap, self.gsv.ap, self.g1v.ap,
                    None, ALU.is_ge, reads=[self.gsv, self.g1v], writes=[self.mask4])
            self.V_("scalar_tensor_tensor", self.t4b.ap, self.mask4.ap,
                    self.c_n1e30.ap, self.gsv.ap, ALU.mult, ALU.add,
                    reads=[self.mask4, self.gsv, self.c_n1e30],
                    writes=[self.t4b])
            self.V_("reduce_max", self.g2v.ap, self.t4b.ap, AX.X,
                    reads=[self.t4b], writes=[self.g2v])
            self.V_("tensor_scalar", self.mask4.ap, self.gsv.ap, self.g2v.ap,
                    None, ALU.is_ge, reads=[self.gsv, self.g2v], writes=[self.mask4])
            self.V_("tensor_scalar", self.madd4.ap, self.mask4.ap, -1.0, 1e30,
                    ALU.add, ALU.mult, reads=[self.mask4], writes=[self.madd4])
            sb3 = self.sbv.ap.rearrange("p (g e) -> p g e", e=8)
            msb3 = self.msb.ap.rearrange("p (g e) -> p g e", e=8)
            self.V_("tensor_tensor", msb3, sb3,
                    self.madd4.ap.unsqueeze(2).broadcast_to([P, 4, 8]), ALU.add,
                    reads=[self.sbv, self.madd4], writes=[self.msb])
            self.V_("tensor_copy", self.cur.ap, self.msb.ap,
                    reads=[self.msb], writes=[self.cur])
            for _ in range(7):
                self.V_("reduce_max", self.mx1.ap, self.cur.ap, AX.X,
                        reads=[self.cur], writes=[self.mx1])
                self.V_("tensor_scalar", self.mk32.ap, self.cur.ap, self.mx1.ap,
                        None, ALU.is_ge, reads=[self.cur, self.mx1],
                        writes=[self.mk32])
                self.V_("scalar_tensor_tensor", self.cur.ap, self.mk32.ap,
                        self.c_n1e30.ap, self.cur.ap, ALU.mult, ALU.add,
                        reads=[self.mk32, self.cur, self.c_n1e30],
                        writes=[self.cur])
            self.V_("reduce_max", self.th8.ap, self.cur.ap, AX.X,
                    reads=[self.cur], writes=[self.th8])
            self.V_("tensor_scalar", self.selm.ap, self.msb.ap, self.th8.ap,
                    None, ALU.is_ge, reads=[self.msb, self.th8], writes=[self.selm])
            self.V_("tensor_tensor", self.wsl.ap, self.selm.ap, self.scrs.ap,
                    ALU.mult, reads=[self.selm, self.scrs], writes=[self.wsl])
            self.V_("reduce_sum", self.s8.ap, self.wsl.ap, AX.X,
                    reads=[self.wsl], writes=[self.s8])
            self.V_("reciprocal", self.rs8.ap, self.s8.ap,
                    reads=[self.s8], writes=[self.rs8])
            self.V_("tensor_scalar", self.rs25.ap, self.rs8.ap, 2.5, None,
                    ALU.mult, reads=[self.rs8], writes=[self.rs25])
            self.V_("tensor_scalar", self.cw.ap[:, t, :], self.wsl.ap,
                    self.rs25.ap, None, ALU.mult,
                    reads=[self.wsl, self.rs25], writes=[self.cw])

    def load_slab(self, lane, slab, blk):
        for j in range(4):
            t = blk * 4 + j
            self.dma(lane, slab.ap[:, :, j * P:(j + 1) * P], self.ln2T_s[t],
                     reads=[("ln2T_s", t)], writes=[slab])

    def b1_blk(self, blk):
        slab = self.slab[blk % 2]
        self.load_slab("slab%d" % (blk % 2), slab, blk)
        for lc in range(8):
            bank = lc % 4
            psb = self.PSR("A", bank * 512, (bank + 1) * 512)
            for kc in range(NKC):
                self.T_("matmul", self.psA[:, bank * 512:(bank + 1) * 512],
                        self.fc1T.ap[:, kc, lc * P:(lc + 1) * P],
                        slab.ap[:, kc, :],
                        start=(kc == 0), stop=(kc == NKC - 1),
                        reads=[self.fc1T, slab], writes=[psb])
            self.act(self.x_latT.ap[:, lc, blk * 512:(blk + 1) * 512],
                     self.psA[:, bank * 512:(bank + 1) * 512], ACTF.Copy,
                     reads=[psb], writes=[self.x_latT])

    def b2(self):
        NT = self.NT
        nblks = NT // 4
        for e in range(NE):
            w1 = self.w1b[e % 2]
            w2 = self.w2b[e % 2]
            for lc in range(8):
                self.dma("w1b%d" % (e % 2), w1.ap[:, lc, :],
                         self.w1Td[e, lc * P:(lc + 1) * P, :],
                         reads=[w1], writes=[w1])
            for ic in range(8):
                self.dma("w2b%d" % (e % 2), w2.ap[:, ic, :],
                         self.w2Td[e, ic * P:(ic + 1) * P, :],
                         reads=[w2], writes=[w2])
            for blk in range(nblks):
                toff = blk * 512
                for ic in range(8):
                    bank = ic % 4
                    psb = self.PSR("A", bank * 512, (bank + 1) * 512)
                    for lc in range(8):
                        self.T_("matmul",
                                self.psA[:, bank * 512:(bank + 1) * 512],
                                w1.ap[:, lc, ic * P:(ic + 1) * P],
                                self.x_latT.ap[:, lc, toff:toff + 512],
                                start=(lc == 0), stop=(lc == 7),
                                reads=[w1, self.x_latT], writes=[psb])
                    rt = self.rtmp2[ic % 2]
                    self.V_("tensor_scalar", rt.ap,
                            self.psA[:, bank * 512:(bank + 1) * 512],
                            0.0, None, ALU.max, reads=[psb], writes=[rt])
                    self.A_("activation", self.h_all.ap[:, ic, :], rt.ap,
                            ACTF.Square, reads=[rt], writes=[self.h_all])
                for lh in range(2):
                    for tt in range(4):
                        psb = self.PSR("B", tt * 512, (tt + 1) * 512)
                        for ic in range(8):
                            self.T_("matmul",
                                    self.psB[:, tt * 512:(tt + 1) * 512],
                                    self.h_all.ap[:, ic, tt * P:(tt + 1) * P],
                                    w2.ap[:, ic, lh * 512:(lh + 1) * 512],
                                    start=(ic == 0), stop=(ic == 7),
                                    reads=[self.h_all, w2], writes=[psb])
                        tg = blk * 4 + tt
                        cwap = self.cw.ap[:, tg, e:e + 1]
                        ydst = self.y_lat.ap[:, tg, lh * 512:(lh + 1) * 512]
                        if e == 0:
                            self.V_("tensor_scalar", ydst,
                                    self.psB[:, tt * 512:(tt + 1) * 512],
                                    cwap, None, ALU.mult,
                                    reads=[psb, self.cw], writes=[self.y_lat])
                        else:
                            self.V_("scalar_tensor_tensor", ydst,
                                    self.psB[:, tt * 512:(tt + 1) * 512],
                                    cwap, ydst, ALU.mult, ALU.add,
                                    reads=[psb, self.cw, self.y_lat],
                                    writes=[self.y_lat])
        for tg in range(NT):
            self.V_("tensor_copy", self.ylb.ap, self.y_lat.ap[:, tg, :],
                    reads=[self.y_lat], writes=[self.ylb])
            self.transpose("trpY", self.ylT.ap, self.ylb.ap,
                           reads=[self.ylb, self.ylT], writes=[self.ylT])
            self.dma("sylT", self.ylT_s[:, :, tg * P:(tg + 1) * P], self.ylT.ap,
                     reads=[self.ylT], writes=[("ylT_s", tg)])

    def b3(self):
        TP = 8
        upTd_r = self.upTd.rearrange("(a p) n -> p a n", p=P)
        for ps in range(self.NT // TP):
            slab = self.slab3
            for j in range(TP):
                t = ps * TP + j
                self.dma("slab3", slab.ap[:, :, j * P:(j + 1) * P],
                         self.ln2T_s[t], reads=[("ln2T_s", t)], writes=[slab])
            self.dma("yl3", self.yl3.ap,
                     self.ylT_s[:, :, ps * 1024:(ps + 1) * 1024],
                     reads=[self.yl3] + [("ylT_s", ps * TP + j) for j in range(TP)],
                     writes=[self.yl3])
            for sig in range(5):
                if sig < 4:
                    for sc in range(8):
                        self.dma("dnw", self.dnb.ap[:, sc, :],
                                 self.dfT[sig * 1024 + sc * P:
                                          sig * 1024 + (sc + 1) * P, :],
                                 reads=[self.dnb], writes=[self.dnb])
                    for si in range(8):
                        ub = self.upb[si % 2]
                        self.dma("upw", ub.ap,
                                 upTd_r[:, :, sig * 1024 + si * P:
                                        sig * 1024 + (si + 1) * P],
                                 reads=[ub], writes=[ub])
                        for th in range(2):
                            bank = (si * 2 + th) % 4
                            psb = self.PSR("A", bank * 512, (bank + 1) * 512)
                            for kc in range(NKC):
                                self.T_("matmul",
                                        self.psA[:, bank * 512:(bank + 1) * 512],
                                        ub.ap[:, kc, :],
                                        slab.ap[:, kc, th * 512:(th + 1) * 512],
                                        start=(kc == 0), stop=(kc == NKC - 1),
                                        reads=[ub, slab], writes=[psb])
                            rt = self.rtmp3[th]
                            self.V_("tensor_scalar", rt.ap,
                                    self.psA[:, bank * 512:(bank + 1) * 512],
                                    0.0, None, ALU.max, reads=[psb], writes=[rt])
                            self.A_("activation",
                                    self.sTs.ap[:, si, th * 512:(th + 1) * 512],
                                    rt.ap, ACTF.Square,
                                    reads=[rt], writes=[self.sTs])
                else:
                    for sc in range(8):
                        self.dma("dnw", self.dnb.ap[:, sc, :],
                                 self.dfT[SI + sc * P:SI + (sc + 1) * P, :],
                                 reads=[self.dnb], writes=[self.dnb])
                for t in range(TP):
                    psb = self.PSR("B", 0, 2048)
                    for sc in range(8):
                        if sig < 4:
                            lhs = self.sTs.ap[:, sc, t * P:(t + 1) * P]
                            lrd = self.sTs
                        else:
                            lhs = self.yl3.ap[:, sc, t * P:(t + 1) * P]
                            lrd = self.yl3
                        for co in range(4):
                            self.T_("matmul", self.psB[:, co * 512:(co + 1) * 512],
                                    lhs, self.dnb.ap[:, sc, co * 512:(co + 1) * 512],
                                    start=(sc == 0), stop=(sc == 7),
                                    reads=[lrd, self.dnb],
                                    writes=[self.PSR("B", co * 512, (co + 1) * 512)])
                    rdst = self.rout.ap[:, t, :]
                    if sig == 0:
                        self.V_("tensor_copy", rdst, self.psB[:, :],
                                reads=[psb], writes=[self.rout])
                    else:
                        self.V_("tensor_tensor", rdst, rdst, self.psB[:, :],
                                ALU.add, reads=[psb, self.rout], writes=[self.rout])
            for t in range(TP):
                tg = ps * TP + t
                rsrc = self.rout.ap[:, t, :]
                self.bn(self.rout, rsrc)
                self.ln_rstd(self.mv.ap[:, 1:2], self.c_eps5, self.rstd, [self.mv])
                self.dma("x1l3", self.x1l3.ap, self.x1_s[tg],
                         reads=[("x1_s", tg)], writes=[self.x1l3])
                self.V_("tensor_scalar", rsrc, rsrc, self.mv.ap[:, 0:1],
                        self.rstd.ap, ALU.subtract, ALU.mult,
                        reads=[self.rout, self.mv, self.rstd], writes=[self.rout])
                self.V_("tensor_tensor", rsrc, rsrc, self.x1l3.ap,
                        ALU.add, reads=[self.rout, self.x1l3], writes=[self.rout])
                self.dma("sout", self.out[tg * P:(tg + 1) * P, :], rsrc,
                         reads=[self.rout], writes=[("out", tg)])

    def build_all(self):
        self.declare()
        self.const_loads()
        self.a0()
        self.tern(self.wqT, self.mclip_q, CQ)
        self.a1("q")
        self.tern(self.wvT, self.mclip_v, CK)
        self.a1("v")
        self.tern(self.wkT, self.mclip_k, CK)
        self.a1("k")
        self.a2()
        self.tern(self.woT, self.mclip_o, CK)
        self.a3()
        self.b0()
        self.b2()
        self.b3()
        self.p.finish()


LANES = {}
for _l in (["xt0", "xt1", "ld0", "ld1", "x1ld0", "x1ld1",
            "slab0", "slab1", "slab3", "x1l3", "sxqT_s", "sqrT", "syqT_s",
            "sx1", "sln2T", "sout", "sylT", "yl3", "trp0", "trp1",
            "trpP0", "trpP1", "trpY"]
           + ["skr%d" % h for h in range(16)] + ["sv%d" % g for g in range(4)]):
    LANES[_l] = "S"
for _l in ("cb", "ones", "gbias", "cc", "ss", "ch0", "vs0", "vs1"):
    LANES[_l] = "A"
for _l in ("ch1", "ks0", "ks1", "ldq0", "ldq1"):
    LANES[_l] = "G"
for _l in ("gw", "fc1", "w1b0", "w1b1", "w2b0", "w2b1", "upw", "dnw"):
    LANES[_l] = "G"


def build_nc(NT=16):
    nc = bass.Bass("TRN2", target_bir_lowering=False)
    from contextlib import ExitStack
    with ExitStack() as st:
        sems = {}
        for name in ("T", "V", "A", "G"):
            sems[name] = st.enter_context(nc.semaphore("sem" + name))
        for name in LANES:
            sems[name] = st.enter_context(nc.semaphore("q" + name))
        prog = Prog(sems, LANES)
        b = Builder(nc, prog, NT)
        b.build_all()
        block = st.enter_context(nc.Block())
        prog.realize(block)
    return nc


def _host_prep(L0_attn_Wq, L0_attn_Wk, L0_attn_Wv, L0_attn_Wo, L0_ffn_gate_w,
               L0_ffn_gate_bias, L0_ffn_fc1_w, L0_ffn_fc2_w, L0_ffn_w1,
               L0_ffn_w2, L0_ffn_shared_up_w, L0_ffn_shared_down_w, T=2048):
    cont = np.ascontiguousarray
    perm = np.concatenate([
        np.concatenate([np.arange(h * 128, (h + 1) * 128, 2),
                        np.arange(h * 128 + 1, (h + 1) * 128, 2)])
        for h in range(16)])
    shared = {
        "wqT": cont(L0_attn_Wq[perm].T),
        "wkT": cont(L0_attn_Wk[perm].T),
        "wvT": cont(L0_attn_Wv.T),
        "woT": cont(L0_attn_Wo.T),
        "gwT": cont(L0_ffn_gate_w.T),
        "gbiasd": cont(np.tile(L0_ffn_gate_bias[None, :].astype(np.float32),
                               (128, 1))),
        "fc1Td": cont(L0_ffn_fc1_w.T),
        "dfT": cont(np.concatenate([L0_ffn_shared_down_w.T,
                                    L0_ffn_fc2_w.T], axis=0)),
        "w1Td": cont(L0_ffn_w1.transpose(0, 2, 1)),
        "w2Td": cont(L0_ffn_w2.transpose(0, 2, 1)),
        "upTd": cont(L0_ffn_shared_up_w.T),
    }
    import jax
    import jax.numpy as jnp
    with jax.default_device(jax.devices("cpu")[0]):
        freqs = 1.0 / (10000.0 ** (jnp.arange(0, 128, 2, dtype=jnp.float32) / 128))
        ang = jnp.arange(T, dtype=jnp.float32)[:, None] * freqs[None, :]
        cos = np.asarray(jnp.cos(ang), dtype=np.float32)
        sin = np.asarray(jnp.sin(ang), dtype=np.float32)
    shared["cc"] = cont(np.concatenate([cos, cos], axis=1))
    shared["ss"] = cont(np.concatenate([-sin, sin], axis=1))
    ii = np.arange(128)
    shared["cbm"] = np.where(ii[:, None] >= ii[None, :], 0.0,
                             -1e30).astype(np.float32)
    shared["onesd"] = np.ones((128, 128), np.float32)
    return shared


def kernel(L0_attn_x, L0_attn_Wq, L0_attn_Wk, L0_attn_Wv, L0_attn_Wo,
           L0_ffn_gate_w, L0_ffn_gate_bias, L0_ffn_fc1_w, L0_ffn_fc2_w,
           L0_ffn_w1, L0_ffn_w2, L0_ffn_shared_up_w, L0_ffn_shared_down_w):
    shared = _host_prep(L0_attn_Wq, L0_attn_Wk, L0_attn_Wv, L0_attn_Wo,
                        L0_ffn_gate_w, L0_ffn_gate_bias, L0_ffn_fc1_w,
                        L0_ffn_fc2_w, L0_ffn_w1, L0_ffn_w2,
                        L0_ffn_shared_up_w, L0_ffn_shared_down_w)
    nc = build_nc(NT=16)
    in_maps = []
    for c in range(8):
        m = dict(shared)
        m["x"] = np.ascontiguousarray(L0_attn_x[c])
        in_maps.append(m)
    res = run_bass_kernel_spmd(nc, in_maps, core_ids=list(range(8)))
    return np.stack([res.results[c]["out"] for c in range(8)], axis=0)
